# version 05
# baseline (speedup 1.0000x reference)
import contextlib
import numpy as np
import ml_dtypes
import concourse.bass as bass
import concourse.mybir as mybir
from concourse.bass_utils import run_bass_kernel_spmd

F32 = mybir.dt.float32
BF16 = mybir.dt.bfloat16
I8 = mybir.dt.int8
AF = mybir.ActivationFunctionType
ALU = mybir.AluOpType
BF = ml_dtypes.bfloat16

D = 2048
T = 2048
B, S = 2, 8192
NCORE = 8
H, PP, NN, K4 = 32, 64, 128, 4
PROJ = 12320
QC = 128
NCH = T // QC
EXP, FF = 8, 512
NH, QLR, KVLR, DN, DR, DV = 16, 1536, 512, 128, 64, 128
DQK = DN + DR
HF, RLORA = 8192, 16
LNEPS = 1e-5

ENGS = ("sync", "gpsimd", "vector", "scalar", "tensor")

SCALAR_CONSTS = [1e-5, 1e-6, -0.5, 1.0, -1.0, -7.5, 0.07216878364870323, 0.5,
                 2.0, -40.0, 0.015625, 1.0 / 2048.0, 1.0 / 1536.0, 1.0 / 512.0,
                 1.0 / 64.0]


class Prog:

    def __init__(self, nc, es):
        self.nc = nc
        self.es = es
        self.steps = {e: [] for e in ENGS}
        self.sem_handles = {}
        self.sem_counts = {}

    def sem(self, name):
        if name not in self.sem_handles:
            self.sem_handles[name] = self.es.enter_context(self.nc.semaphore(name))
            self.sem_counts[name] = 0
        return name

    def count(self, name):
        self.sem(name)
        return self.sem_counts[name]

    def op(self, eng, fn, waits=(), inc=None, amt=1):
        w = [(s, v) for (s, v) in waits if v > 0]
        post = None
        if inc is not None:
            self.sem(inc)
            self.sem_counts[inc] += amt
            post = self.sem_counts[inc]
        self.steps[eng].append((fn, w, inc, amt if inc else 0))
        return post

    def dma(self, eng, fn, key, waits=()):
        s = "d_" + key
        v = self.op(eng, fn, waits=waits, inc=s, amt=16)
        return (s, v)

    def wait_only(self, eng, waits):
        self.steps[eng].append((None, [(s, v) for (s, v) in waits if v > 0], None, 0))

    def phase_barrier(self):
        for eng in ENGS:
            self.barrier_wait(eng)

    def barrier_wait(self, eng, sems=None):
        names = sems if sems is not None else list(self.sem_counts)
        self.steps[eng].append((None, [(s, self.sem_counts[s]) for s in names
                                       if self.sem_counts[s] > 0], None, 0))

    def emit(self, block):
        handles = self.sem_handles

        def mk(name):
            def run(e):
                for fn, waits, inc, amt in self.steps[name]:
                    for (s, v) in waits:
                        e.wait_ge(handles[s], v)
                    if fn is not None:
                        inst = fn(e)
                        if inc is not None:
                            inst.then_inc(handles[inc], amt)
            return run

        block.sync(mk("sync"))
        block.gpsimd(mk("gpsimd"))
        block.vector(mk("vector"))
        block.scalar(mk("scalar"))
        block.tensor(mk("tensor"))



def host_prep(inp):
    shared = {}
    shared["c_idf32"] = np.eye(128, dtype=np.float32)
    shared["c_idbf"] = np.eye(128, dtype=np.float32).astype(BF)
    shared["c_ones_1x128_f"] = np.ones((1, 128), np.float32)
    shared["c_ones_1x128_b"] = np.ones((1, 128), np.float32).astype(BF)
    shared["c_ones_128x1_b"] = np.ones((128, 1), np.float32).astype(BF)
    shared["c_ones_128x1_f"] = np.ones((128, 1), np.float32)
    jj, ii = np.meshgrid(np.arange(QC), np.arange(QC), indexing="ij")
    shared["c_scanmask"] = np.where(jj > ii, -40.0, 0.0).astype(np.float32)
    sel = np.zeros((32, 128 * 16), np.float32)
    for g in range(16):
        for p in range(128):
            sel[2 * g + p // 64, g * 128 + p] = 1.0
    shared["c_seldt"] = sel
    blk2 = np.zeros((2, 128), np.float32)
    blk2[0, 0:64] = 1.0
    blk2[1, 64:128] = 1.0
    shared["c_blk2"] = blk2
    shared["c_scalarpool"] = np.tile(np.array(SCALAR_CONSTS, np.float32), (128, 1))
    shared["c_onesrow_b"] = np.ones((1, S), np.float32).astype(BF)
    inv_freq = 1.0 / (10000.0 ** (np.arange(0, DR, 2, dtype=np.float64) / DR))
    t = np.arange(S, dtype=np.float64)
    emb = np.concatenate([np.outer(t, inv_freq)] * 2, axis=1)
    shared["c_cosT"] = np.ascontiguousarray(np.cos(emb).T.astype(np.float32))
    shared["c_sinT"] = np.ascontiguousarray(np.sin(emb).T.astype(np.float32))

    Win = np.asarray(inp["L0_attn_Win"])
    order = []
    for g in range(16):
        order += list(range(g * 128, g * 128 + 128))
        order += list(range(H * PP + g * 128, H * PP + g * 128 + 128))
        order += list(range(2 * H * PP + g * 256, 2 * H * PP + g * 256 + 256))
        order += list(range(2 * H * PP + H * NN + g * 256, 2 * H * PP + H * NN + g * 256 + 256))
    order += list(range(2 * H * PP + 2 * H * NN, PROJ))
    shared["w_winT"] = np.ascontiguousarray(Win[order, :].T)
    shared["w_woutT"] = np.ascontiguousarray(np.asarray(inp["L0_attn_Wout"]).T)
    shared["w_alog"] = np.asarray(inp["L0_attn_A_log"]).reshape(32, 1).astype(np.float32)
    shared["w_dtb"] = np.asarray(inp["L0_attn_dt_bias"]).reshape(32, 1).astype(np.float32)
    shared["w_cwx"] = np.ascontiguousarray(np.asarray(inp["L0_attn_conv_w_x"]).reshape(2048, 4))
    shared["w_cwb"] = np.ascontiguousarray(np.asarray(inp["L0_attn_conv_w_b"]).reshape(4096, 4))
    shared["w_cwc"] = np.ascontiguousarray(np.asarray(inp["L0_attn_conv_w_c"]).reshape(4096, 4))

    shared["w_routerT"] = np.ascontiguousarray(np.asarray(inp["L0_ffn_router_w"]).T)
    shared["w_moe1"] = np.ascontiguousarray(np.asarray(inp["L0_ffn_w1"]))
    shared["w_moe3"] = np.ascontiguousarray(np.asarray(inp["L0_ffn_w3"]))
    shared["w_moe2"] = np.ascontiguousarray(np.asarray(inp["L0_ffn_w2"]))

    shared["w_qaT"] = np.ascontiguousarray(np.asarray(inp["L1_attn_wq_a_w"]).T)
    shared["w_qab"] = np.ascontiguousarray(np.asarray(inp["L1_attn_wq_a_b"]).reshape(12, 128).T)
    shared["w_qnorm"] = np.ascontiguousarray(np.asarray(inp["L1_attn_q_norm_w"]).reshape(12, 128).T)
    shared["w_qbT"] = np.ascontiguousarray(np.asarray(inp["L1_attn_wq_b_w"]).T)
    qbb = np.zeros((128, 32), np.float32)
    _b = np.asarray(inp["L1_attn_wq_b_b"])
    for hh in range(16):
        qbb[:, 2 * hh] = _b[hh * 192:hh * 192 + 128]
        qbb[0:64, 2 * hh + 1] = _b[hh * 192 + 128:hh * 192 + 192]
    shared["w_qbb"] = np.ascontiguousarray(qbb)
    shared["w_kvaT"] = np.ascontiguousarray(np.asarray(inp["L1_attn_wkv_a_w"]).T)
    kvab = np.zeros((128, 5), np.float32)
    kvab[:, 0:4] = np.asarray(inp["L1_attn_wkv_a_b"])[0:512].reshape(4, 128).T
    kvab[0:64, 4] = np.asarray(inp["L1_attn_wkv_a_b"])[512:576]
    shared["w_kvab"] = np.ascontiguousarray(kvab)
    shared["w_kvnorm"] = np.ascontiguousarray(np.asarray(inp["L1_attn_kv_norm_w"]).reshape(4, 128).T)
    shared["w_kvbT"] = np.ascontiguousarray(np.asarray(inp["L1_attn_wkv_b_w"]).T)
    shared["w_kvbb"] = np.ascontiguousarray(np.asarray(inp["L1_attn_wkv_b_b"]).reshape(32, 128).T)
    shared["w_woT"] = np.ascontiguousarray(np.asarray(inp["L1_attn_wo_w"]).T)
    shared["w_wob"] = np.ascontiguousarray(np.asarray(inp["L1_attn_wo_b"]).reshape(16, 128).T)

    shared["w_upqT"] = np.ascontiguousarray(np.asarray(inp["L1_ffn_w_up_q"]).T.astype(np.int8))
    shared["w_upsT"] = np.ascontiguousarray(np.asarray(inp["L1_ffn_w_up_scale"]).T)
    shared["w_bup"] = np.ascontiguousarray(np.asarray(inp["L1_ffn_b_up"]).reshape(64, 128).T)
    shared["w_upla"] = np.ascontiguousarray(np.asarray(inp["L1_ffn_w_up_lora_a"]))
    shared["w_uplb"] = np.ascontiguousarray(np.asarray(inp["L1_ffn_w_up_lora_b"]))
    shared["w_dnqT"] = np.ascontiguousarray(np.asarray(inp["L1_ffn_w_down_q"]).T.astype(np.int8))
    shared["w_dnsT"] = np.ascontiguousarray(np.asarray(inp["L1_ffn_w_down_scale"]).T)
    shared["w_bdn"] = np.ascontiguousarray(np.asarray(inp["L1_ffn_b_down"]).reshape(16, 128).T)
    shared["w_dnla"] = np.ascontiguousarray(np.asarray(inp["L1_ffn_w_down_lora_a"]))
    shared["w_dnlb"] = np.ascontiguousarray(np.asarray(inp["L1_ffn_w_down_lora_b"]))

    x = np.asarray(inp["L0_attn_x"])
    percore = []
    for c in range(NCORE):
        b, q = c // 4, c % 4
        lo = q * T
        sl = x[b, max(lo - 3, 0):lo + T, :]
        if lo == 0:
            sl = np.concatenate([np.zeros((3, D), x.dtype), sl], axis=0)
        wc = np.zeros((4, 4), np.float32)
        jm = np.zeros((4, 1), np.float32)
        for j in range(4):
            if j < q:
                jm[j, 0] = 1.0
                for l in range(j + 1, q):
                    wc[l, j] = 1.0
        percore.append({
            "x_fm": np.ascontiguousarray(sl.T),
            "c_cosq": np.ascontiguousarray(shared["c_cosT"][:, lo:lo + T].astype(BF)),
            "c_sinq": np.ascontiguousarray(shared["c_sinT"][:, lo:lo + T].astype(BF)),
            "c_wcoef": wc, "c_jmask": jm,
        })
    return shared, percore


SHAPES = {
    "c_idf32": ([128, 128], F32), "c_idbf": ([128, 128], BF16),
    "c_ones_1x128_f": ([1, 128], F32), "c_ones_1x128_b": ([1, 128], BF16),
    "c_ones_128x1_b": ([128, 1], BF16), "c_ones_128x1_f": ([128, 1], F32),
    "c_scanmask": ([128, 128], F32), "c_seldt": ([32, 2048], F32),
    "c_blk2": ([2, 128], F32), "c_cosT": ([64, S], F32), "c_sinT": ([64, S], F32),
    "c_scalarpool": ([128, len(SCALAR_CONSTS)], F32),
    "w_winT": ([D, PROJ], F32), "w_woutT": ([D, D], F32),
    "w_alog": ([32, 1], F32), "w_dtb": ([32, 1], F32),
    "w_cwx": ([2048, 4], F32), "w_cwb": ([4096, 4], F32), "w_cwc": ([4096, 4], F32),
    "w_routerT": ([D, 8], F32),
    "w_moe1": ([EXP, D, FF], F32), "w_moe3": ([EXP, D, FF], F32), "w_moe2": ([EXP, FF, D], F32),
    "w_qaT": ([D, QLR], F32), "w_qab": ([128, 12], F32), "w_qnorm": ([128, 12], F32),
    "w_qbT": ([QLR, NH * DQK], F32), "w_qbb": ([128, 32], F32),
    "w_kvaT": ([D, KVLR + DR], F32), "w_kvab": ([128, 5], F32),
    "w_kvnorm": ([128, 4], F32),
    "w_kvbT": ([KVLR, NH * (DN + DV)], F32), "w_kvbb": ([128, 32], F32),
    "w_woT": ([D, D], F32), "w_wob": ([128, 16], F32),
    "w_upqT": ([D, HF], I8), "w_upsT": ([32, HF], F32), "w_bup": ([128, 64], F32),
    "w_upla": ([D, RLORA], F32), "w_uplb": ([RLORA, HF], F32),
    "w_dnqT": ([HF, D], I8), "w_dnsT": ([128, D], F32), "w_bdn": ([128, 16], F32),
    "w_dnla": ([HF, RLORA], F32), "w_dnlb": ([RLORA, D], F32),
    "x_fm": ([D, T + 3], F32),
    "c_cosq": ([64, T], BF16), "c_sinq": ([64, T], BF16),
    "c_wcoef": ([4, 4], F32), "c_jmask": ([4, 1], F32),
    "c_onesrow_b": ([1, S], BF16),
}


class Ctx:
    def __init__(self, nc, es):
        self.nc = nc
        self.es = es
        self.p = Prog(nc, es)
        self.dram = {}
        self.sbuf = {}
        self._cw = {"consts": ("d_consts", 0), "constsg": ("d_constsg", 0)}

    @property
    def consts_w(self):
        return [v for v in self._cw.values() if v[1] > 0]

    def dram_in(self, name, shape, dtype):
        ap = self.nc.dram_tensor(name, list(shape), dtype, kind="ExternalInput").ap()
        self.dram[name] = ap
        return ap

    def dram_out(self, name, shape, dtype):
        ap = self.nc.dram_tensor(name, list(shape), dtype, kind="ExternalOutput").ap()
        self.dram[name] = ap
        return ap

    def dram_tmp(self, name, shape, dtype):
        ap = self.nc.dram_tensor(name, list(shape), dtype, kind="Internal").ap()
        self.dram[name] = ap
        return ap

    def begin_arena(self, nbytes):
        self._arena = self.es.enter_context(
            self.nc.sbuf_tensor("ARENA", [128, nbytes], I8))
        self._aoff = 0
        self._arena_names = []

    def arena_reset(self):
        self._aoff = 0
        for n in self._arena_names:
            self.sbuf.pop(n, None)
        self._arena_names = []

    def sb(self, name, shape, dtype):
        if getattr(self, "_arena", None) is not None:
            size = {F32: 4, BF16: 2, I8: 1}[dtype]
            nb = shape[1] * size
            nb_al = (nb + 63) & ~63
            off = self._aoff
            self._aoff += nb_al
            assert self._aoff <= self._arena.shape[1], \
                "arena overflow at %s (%d)" % (name, self._aoff)
            v = self._arena[:, off:off + nb].bitcast(dtype)
            if shape[0] != 128:
                v = v[0:shape[0], :]
            self.sbuf[name] = v
            self._arena_names.append(name)
            return v
        h = self.es.enter_context(self.nc.sbuf_tensor(name, list(shape), dtype))
        self.sbuf[name] = h
        return h

    def ps(self, name, shape, dtype=F32):
        return self.es.enter_context(self.nc.psum_tensor(name, list(shape), dtype))

    def pbank(self, i):
        if not hasattr(self, "_pdbl"):
            self._pdbl = [self.ps("pd%d" % k, [128, 1024]) for k in range(4)]
        return self._pdbl[i // 2][:, (i % 2) * 512:(i % 2 + 1) * 512]

    def pdouble(self, j):
        self.pbank(0)
        return self._pdbl[j]

    def load_const(self, dname, cast=False):
        if dname in self.sbuf:
            return self.sbuf[dname]
        shape, dtype = SHAPES[dname]
        sbdt = dtype if not cast else BF16
        if getattr(self, "_arena", None) is not None:
            h = self.sb(dname, shape, sbdt)
            w0 = [("pe", self.p.count("pe")), ("dve", self.p.count("dve"))]
        else:
            h = self.es.enter_context(self.nc.sbuf_tensor("sb_" + dname, list(shape), sbdt))
            self.sbuf[dname] = h
            w0 = []
        eng = "gpsimd" if cast else "sync"
        key = "constsg" if cast else "consts"
        self._cw[key] = self.p.dma(eng, lambda e, d=h, a=self.dram[dname]:
                                   e.dma_start(out=d[:, :], in_=a), key, waits=w0)
        return h

    def register_scalar_pool(self):
        pool = self.sbuf["c_scalarpool"]
        for i, v in enumerate(SCALAR_CONSTS):
            self.nc.const_aps.register(F32, v, pool[:, i:i + 1])



def emit_ln_store(cx, stg, tqw, out_f32, out_bf, tcol0, bank, uid):
    p = cx.p
    pss, psq, psb1, psb2 = bank["pss"], bank["psq"], bank["psb1"], bank["psb2"]
    sqt, mrow, vrow, oout = bank["sqt"], bank["mrow"], bank["vrow"], bank["oout"]
    onesf = cx.sbuf["c_ones_128x1_f"]
    ones1x = cx.sbuf["c_ones_1x128_f"]
    stg_ready = [("dve", p.count("dve"))] + cx.consts_w
    prev_psq_mm = 0
    for dt in range(16):
        tile = stg[:, dt * tqw:(dt + 1) * tqw]
        p.op("tensor", lambda e, t=tile, k=dt: e.matmul(
            out=pss[:, :], lhsT=onesf[:, :], rhs=t, start=(k == 0), stop=(k == 15)),
            waits=(stg_ready if dt == 0 else []), inc="pe")
        av = p.op("scalar", lambda e, t=tile: e.activation(
            out=sqt[:, :], in_=t, func=AF.Square),
            waits=(stg_ready if dt == 0 else [("pe", prev_psq_mm)]), inc="act")
        prev_psq_mm = p.op("tensor", lambda e, k=dt: e.matmul(
            out=psq[:, :], lhsT=onesf[:, :], rhs=sqt[:, :], start=(k == 0), stop=(k == 15)),
            waits=[("act", av)], inc="pe")
    pe_stats = p.count("pe")
    mv = p.op("vector", lambda e: e.tensor_scalar(out=mrow[:, :], in0=pss[0:1, :],
              scalar1=1.0 / D, scalar2=None, op0=ALU.mult), waits=[("pe", pe_stats)], inc="dve")
    vv = p.op("vector", lambda e: e.tensor_scalar(out=vrow[:, :], in0=psq[0:1, :],
              scalar1=1.0 / D, scalar2=None, op0=ALU.mult), waits=[("pe", pe_stats)], inc="dve")
    m2 = p.op("scalar", lambda e: e.activation(out=sqt[0:1, 0:tqw], in_=mrow[:, :],
              func=AF.Square), waits=[("dve", mv)], inc="act")
    vv2 = p.op("vector", lambda e: e.tensor_tensor(out=vrow[:, :], in0=vrow[:, :],
               in1=sqt[0:1, 0:tqw], op=ALU.subtract), waits=[("act", m2), ("dve", vv)], inc="dve")
    l1 = p.op("scalar", lambda e: e.activation(out=vrow[:, :], in_=vrow[:, :],
              func=AF.Ln, bias=LNEPS), waits=[("dve", vv2)], inc="act")
    l2 = p.op("scalar", lambda e: e.activation(out=vrow[:, :], in_=vrow[:, :],
              func=AF.Exp, scale=-0.5), waits=[("act", l1)], inc="act")
    pb1 = p.op("tensor", lambda e: e.matmul(out=psb1[:, 0:tqw], lhsT=ones1x[:, :],
               rhs=mrow[:, :], start=True, stop=True), waits=[("dve", mv), ("act", l2)], inc="pe")
    pb2 = p.op("tensor", lambda e: e.matmul(out=psb2[:, 0:tqw], lhsT=ones1x[:, :],
               rhs=vrow[:, :], start=True, stop=True), waits=[("act", l2)], inc="pe")
    w1 = bank.get("last_w1")
    w2 = bank.get("last_w2")
    for dt in range(16):
        tile = stg[:, dt * tqw:(dt + 1) * tqw]
        gate = ([w1] if w1 else []) + ([w2] if w2 else [])
        sv = p.op("vector", lambda e, t=tile: e.tensor_tensor(
            out=sqt[:, :], in0=t, in1=psb1[:, 0:tqw], op=ALU.subtract),
            waits=[("pe", pb1)] + gate, inc="dve")
        ov = p.op("vector", lambda e: e.tensor_tensor(
            out=oout[:, :], in0=sqt[:, :], in1=psb2[:, 0:tqw], op=ALU.mult),
            waits=[("dve", sv), ("pe", pb2)], inc="dve")
        w1 = p.dma("gpsimd", lambda e, d=dt: e.dma_start(
            out=out_f32[d * 128:(d + 1) * 128, tcol0:tcol0 + tqw], in_=oout[:, :]),
            "st_" + uid + "_f", waits=[("dve", ov)])
        if out_bf is not None:
            bv = p.op("vector", lambda e: e.tensor_copy(out=bank["obf"][:, :], in_=oout[:, :]),
                      waits=[("dve", ov)], inc="dve")
            w2 = p.dma("gpsimd", lambda e, d=dt: e.dma_start(
                out=out_bf[d * 128:(d + 1) * 128, tcol0:tcol0 + tqw], in_=bank["obf"][:, :]),
                "st_" + uid + "_b", waits=[("dve", bv)])
    bank["last_w1"] = w1
    bank["last_w2"] = w2
    return [w1] + ([w2] if out_bf is not None else [])



def emit_dequant(cx, wup_bf, wdn_bf):
    p = cx.p
    q8a = cx.sb("dq_q8a", [128, 4096], I8)
    q8b = cx.sb("dq_q8b", [128, 4096], I8)
    dqoa = cx.sb("dq_oa", [128, 4096], BF16)
    dqob = cx.sb("dq_ob", [128, 4096], BF16)
    dqf = cx.sb("dq_f", [128, 512], F32)
    scbuf = cx.sb("dq_sc", [2, 4096], F32)
    psc = cx.pbank(7)
    blk2 = cx.sbuf["c_blk2"]
    q8v = [q8a, q8b]
    dqo = [dqoa, dqob]
    stw = {0: ("d_dq0", 0), 1: ("d_dq1", 0)}

    def dequant_matrix(qT, scT_dram, out_dram, nrow, ncol, tag):
        cw = 4096 if ncol % 4096 == 0 else ncol
        ncs = ncol // cw
        it = 0
        last_scmm = p.count("pe")
        for rt in range(nrow // 128):
            for cs in range(ncs):
                par = it % 2
                it += 1
                buf, obuf = q8v[par], dqo[par]
                ldw = p.dma("sync", lambda e, b=buf, r=rt, c=cs, cw=cw: e.dma_start(
                    out=b[:, 0:cw], in_=qT[r * 128:(r + 1) * 128, c * cw:(c + 1) * cw]),
                    "q8_%d" % par, waits=[stw[par], ("act", p.count("act"))])
                r0 = (rt * 128) // 64
                scw = p.dma("sync", lambda e, rr=r0, c=cs, cw=cw: e.dma_start(
                    out=scbuf[:, 0:cw], in_=scT_dram[rr:rr + 2, c * cw:(c + 1) * cw]),
                    "sc", waits=[("pe", last_scmm)])
                for j in range(cw // 512):
                    pev = p.op("tensor", lambda e, j=j: e.matmul(
                        out=psc[:, :], lhsT=blk2[:, :], rhs=scbuf[:, j * 512:(j + 1) * 512],
                        start=True, stop=True),
                        waits=[scw] + cx.consts_w + [("dve", p.count("dve"))], inc="pe")
                    acv = p.op("scalar", lambda e, b=buf, j=j: e.activation(
                        out=dqf[:, :], in_=b[:, j * 512:(j + 1) * 512], func=AF.Copy,
                        bias=-7.5, scale=1.0), waits=[ldw, ("dve", p.count("dve"))], inc="act")
                    p.op("vector", lambda e, o=obuf, j=j: e.tensor_tensor(
                        out=o[:, j * 512:(j + 1) * 512], in0=dqf[:, :], in1=psc[:, :], op=ALU.mult),
                        waits=[("pe", pev), ("act", acv)], inc="dve")
                last_scmm = p.count("pe")
                stw[par] = p.dma("gpsimd", lambda e, o=obuf, r=rt, c=cs, cw=cw: e.dma_start(
                    out=out_dram[r * 128:(r + 1) * 128, c * cw:(c + 1) * cw], in_=o[:, 0:cw]),
                    "dq%d" % par, waits=[("dve", p.count("dve"))])
        return [stw[0], stw[1]]

    dd = dequant_matrix(cx.dram["w_upqT"], cx.dram["w_upsT"], wup_bf, D, HF, "u")
    dd += dequant_matrix(cx.dram["w_dnqT"], cx.dram["w_dnsT"], wdn_bf, HF, D, "d")
    cx.deq_done = dd
    cx.deq_dve = p.count("dve")
    return dd


def phase_l1ffn(cx, stream_f32, stream_bf, out_f32, pre_dequant=False):
    nc, p = cx.nc, cx.p
    TQ = 512

    if pre_dequant:
        wup_bf = cx.dram["wup_bf"]
        wdn_bf = cx.dram["wdn_bf"]
    else:
        wup_bf = cx.dram_tmp("wup_bf", [D, HF], BF16)
        wdn_bf = cx.dram_tmp("wdn_bf", [HF, D], BF16)

    xs = cx.sb("xs", [128, 16 * TQ], BF16)
    x2 = cx.sb("x2", [128, 64 * TQ], BF16)
    stg = cx.sb("stg", [128, 16 * TQ], F32)
    wb1 = cx.sb("wb1", [128, 2 * 2048], BF16)
    wb2 = cx.sb("wb2", [128, 8192], BF16)
    xres = cx.sb("xres", [128, 2 * TQ], F32)
    labuf = cx.sb("labuf", [16, TQ], BF16)
    la2buf = cx.sb("la2buf", [16, TQ], BF16)
    sqt = cx.sb("sqt", [128, TQ], F32)
    mrow = cx.sb("mrow", [1, TQ], F32)
    vrow = cx.sb("vrow", [1, TQ], F32)
    oout = cx.sb("oout", [128, TQ], F32)
    dqf = cx.sb("dqf", [128, 512], F32)
    scbuf = cx.sb("scbuf", [2, 4096], F32)

    bup = cx.load_const("w_bup")
    bdn = cx.load_const("w_bdn")
    la_up = cx.sb("la_up", [128, 16 * 16], BF16)
    lb_up = cx.sb("lb_up", [16, HF], BF16)
    la_dn = cx.sb("la_dn", [128, 64 * 16], BF16)
    lb_dn = cx.sb("lb_dn", [16, D], BF16)
    p.dma("gpsimd", lambda e: e.dma_start(
        out=la_up[:, :].rearrange("p (a r) -> p a r", r=RLORA),
        in_=cx.dram["w_upla"].rearrange("(a p) r -> p a r", p=128)), "constsg")
    p.dma("gpsimd", lambda e: e.dma_start(out=lb_up[:, :], in_=cx.dram["w_uplb"]), "constsg")
    p.dma("gpsimd", lambda e: e.dma_start(
        out=la_dn[:, :].rearrange("p (a r) -> p a r", r=RLORA),
        in_=cx.dram["w_dnla"].rearrange("(a p) r -> p a r", p=128)), "constsg")
    cx._cw["constsg"] = p.dma("gpsimd", lambda e: e.dma_start(
        out=lb_dn[:, :], in_=cx.dram["w_dnlb"]), "constsg")

    blk2 = cx.sbuf["c_blk2"]

    psy0 = cx.pbank(0)
    psy1 = cx.pbank(1)
    psl = cx.pbank(2)[0:16, :]
    pss = cx.pbank(3)[0:1, :]
    psq = cx.pbank(4)[0:1, :]
    psb1 = cx.pbank(5)
    psb2 = cx.pbank(6)
    psc = cx.pbank(7)

    bank = dict(pss=pss, psq=psq, psb1=psb1, psb2=psb2, sqt=sqt, mrow=mrow,
                vrow=vrow, oout=oout, obf=None)

    q8v = [x2[:, 0:2048].bitcast(I8), x2[:, 2048:4096].bitcast(I8)]
    dqo = [x2[:, 4096:8192], x2[:, 8192:12288]]

    stw = {0: ("d_dq0", 0), 1: ("d_dq1", 0)}

    def dequant_matrix(qT, scT_dram, out_dram, nrow, ncol, tag):
        cw = 4096 if ncol % 4096 == 0 else ncol
        ncs = ncol // cw
        it = 0
        last_scmm = p.count("pe")
        for rt in range(nrow // 128):
            for cs in range(ncs):
                par = it % 2
                it += 1
                buf, obuf = q8v[par], dqo[par]
                ldw = p.dma("sync", lambda e, b=buf, r=rt, c=cs, cw=cw: e.dma_start(
                    out=b[:, 0:cw], in_=qT[r * 128:(r + 1) * 128, c * cw:(c + 1) * cw]),
                    "q8_%d" % par, waits=[stw[par], ("act", p.count("act"))])
                r0 = (rt * 128) // 64
                scw = p.dma("sync", lambda e, rr=r0, c=cs, cw=cw: e.dma_start(
                    out=scbuf[:, 0:cw], in_=scT_dram[rr:rr + 2, c * cw:(c + 1) * cw]),
                    "sc", waits=[("pe", last_scmm)])
                for j in range(cw // 512):
                    pev = p.op("tensor", lambda e, j=j: e.matmul(
                        out=psc[:, :], lhsT=blk2[:, :], rhs=scbuf[:, j * 512:(j + 1) * 512],
                        start=True, stop=True),
                        waits=[scw] + cx.consts_w + [("dve", p.count("dve"))], inc="pe")
                    acv = p.op("scalar", lambda e, b=buf, j=j: e.activation(
                        out=dqf[:, :], in_=b[:, j * 512:(j + 1) * 512], func=AF.Copy,
                        bias=-7.5, scale=1.0), waits=[ldw, ("dve", p.count("dve"))], inc="act")
                    p.op("vector", lambda e, o=obuf, j=j: e.tensor_tensor(
                        out=o[:, j * 512:(j + 1) * 512], in0=dqf[:, :], in1=psc[:, :], op=ALU.mult),
                        waits=[("pe", pev), ("act", acv)], inc="dve")
                last_scmm = p.count("pe")
                stw[par] = p.dma("gpsimd", lambda e, o=obuf, r=rt, c=cs, cw=cw: e.dma_start(
                    out=out_dram[r * 128:(r + 1) * 128, c * cw:(c + 1) * cw], in_=o[:, 0:cw]),
                    "dq%d" % par, waits=[("dve", p.count("dve"))])
        return [stw[0], stw[1]]

    if pre_dequant:
        deq_done = []
        deq_dve = 0
    else:
        deq_done = dequant_matrix(cx.dram["w_upqT"], cx.dram["w_upsT"], wup_bf, D, HF, "u")
        deq_done += dequant_matrix(cx.dram["w_dnqT"], cx.dram["w_dnsT"], wdn_bf, HF, D, "d")
        deq_dve = p.count("dve")

    last_pe_user_xs = 0
    wlast = None
    evac_dve2 = [deq_dve, deq_dve]
    for tq in range(4):
        tqs = slice(tq * TQ, (tq + 1) * TQ)
        ldx = p.dma("sync", lambda e, s=tqs: e.dma_start(
            out=xs[:, :].rearrange("p (a t) -> p a t", t=TQ),
            in_=stream_bf.rearrange("(a p) t -> p a t", p=128)[:, :, s]),
            "xs", waits=[("pe", last_pe_user_xs)] + (deq_done + [("dve", deq_dve)] if tq == 0 else []))

        lpe = None
        for kt in range(16):
            lpe = p.op("tensor", lambda e, k=kt: e.matmul(
                out=psl[:, :], lhsT=la_up[:, k * 16:(k + 1) * 16], rhs=xs[:, k * TQ:(k + 1) * TQ],
                start=(k == 0), stop=(k == 15)),
                waits=([ldx] + cx.consts_w + [("dve", p.count("dve"))] if kt == 0 else []),
                inc=("pe" if kt == 15 else None))
        lav = p.op("vector", lambda e: e.tensor_copy(out=labuf[:, :], in_=psl[:, :]),
                   waits=[("pe", lpe)], inc="dve")

        wld = [None, None]
        use_pe = [0, 0]
        use_act = [p.count("act"), p.count("act")]
        for ht in range(64):
            par = ht % 2
            wtile = wb1[:, par * 2048:(par + 1) * 2048]
            wld[par] = p.dma("sync", lambda e, w=wtile, h=ht: e.dma_start(
                out=w.rearrange("p (a h) -> p a h", h=128),
                in_=wup_bf.rearrange("(a p) h -> p a h", p=128)[:, :, h * 128:(h + 1) * 128]),
                "wb1_%d" % par, waits=[("pe", use_pe[par])])
            ps = psy0 if par == 0 else psy1
            for kt in range(16):
                p.op("tensor", lambda e, k=kt, w=wtile, ps=ps: e.matmul(
                    out=ps[:, :], lhsT=w[:, k * 128:(k + 1) * 128], rhs=xs[:, k * TQ:(k + 1) * TQ],
                    start=(k == 0), stop=False),
                    waits=([wld[par], ("act", use_act[par]), ("dve", lav)] if kt == 0 else []),
                    inc=None)
            pev = p.op("tensor", lambda e, h=ht, ps=ps: e.matmul(
                out=ps[:, :], lhsT=lb_up[:, h * 128:(h + 1) * 128], rhs=labuf[:, :],
                start=False, stop=True), inc="pe")
            use_pe[par] = pev
            use_act[par] = p.op("scalar", lambda e, h=ht, ps=ps: e.activation(
                out=x2[:, h * TQ:(h + 1) * TQ], in_=ps[:, :], func=AF.Relu,
                bias=bup[:, h:h + 1], scale=1.0), waits=[("pe", pev)], inc="act")
        y1_act = p.count("act")

        lpe2 = None
        for kt in range(64):
            lpe2 = p.op("tensor", lambda e, k=kt: e.matmul(
                out=psl[:, :], lhsT=la_dn[:, k * 16:(k + 1) * 16], rhs=x2[:, k * TQ:(k + 1) * TQ],
                start=(k == 0), stop=(k == 63)),
                waits=([("act", y1_act), ("dve", lav)] if kt == 0 else []),
                inc=("pe" if kt == 63 else None))
        la2v = p.op("vector", lambda e: e.tensor_copy(out=la2buf[:, :], in_=psl[:, :]),
                    waits=[("pe", lpe2)], inc="dve")

        half_done = 0
        grp_done = 0
        for dt in range(16):
            par = dt % 2
            wld2a = p.dma("sync", lambda e, d=dt: e.dma_start(
                out=wb2[:, 0:4096].rearrange("p (a h) -> p a h", h=128),
                in_=wdn_bf.rearrange("(a p) h -> p a h", p=128)[:, 0:32, d * 128:(d + 1) * 128]),
                "wb2a", waits=[("pe", half_done)] + (deq_done if (tq == 0 and dt == 0) else []))
            wld2b = p.dma("sync", lambda e, d=dt: e.dma_start(
                out=wb2[:, 4096:8192].rearrange("p (a h) -> p a h", h=128),
                in_=wdn_bf.rearrange("(a p) h -> p a h", p=128)[:, 32:64, d * 128:(d + 1) * 128]),
                "wb2b", waits=[("pe", grp_done)])
            ldr = p.dma("sync", lambda e, d=dt, s=tq, par=par: e.dma_start(
                out=xres[:, par * TQ:(par + 1) * TQ],
                in_=stream_f32[d * 128:(d + 1) * 128, s * TQ:(s + 1) * TQ]),
                "xres%d" % par, waits=[("dve", evac_dve2[par])])
            ps = psy0 if par == 0 else psy1
            for kt in range(64):
                inc = "pe" if kt == 31 else None
                v = p.op("tensor", lambda e, k=kt, ps=ps: e.matmul(
                    out=ps[:, :], lhsT=wb2[:, k * 128:(k + 1) * 128], rhs=x2[:, k * TQ:(k + 1) * TQ],
                    start=(k == 0), stop=False),
                    waits=([wld2a, ("dve", evac_dve2[par])] if kt == 0 else
                           ([wld2b] if kt == 32 else [])), inc=inc)
                if kt == 31:
                    half_done = v
            pev = p.op("tensor", lambda e, d=dt, ps=ps: e.matmul(
                out=ps[:, :], lhsT=lb_dn[:, d * 128:(d + 1) * 128], rhs=la2buf[:, :],
                start=False, stop=True), waits=[("dve", la2v)], inc="pe")
            grp_done = pev
            evac_dve2[par] = p.op("vector", lambda e, d=dt, ps=ps: e.scalar_tensor_tensor(
                out=stg[:, d * TQ:(d + 1) * TQ], in0=ps[:, :], scalar=bdn[:, d:d + 1],
                in1=xres[:, (d % 2) * TQ:((d % 2) + 1) * TQ], op0=ALU.add, op1=ALU.add),
                waits=[("pe", pev), ldr], inc="dve")
        last_pe_user_xs = p.count("pe")

        wlast = emit_ln_store(cx, stg, TQ, out_f32, None, tq * TQ, bank, "ffn")
    return wlast





def phase_moe(cx, stream_f32, stream_bf, out_f32, out_bf):
    nc, p = cx.nc, cx.p
    TW = 512

    g_dram = cx.dram_tmp("g_dram", [8, T], F32)

    xs = cx.sb("m_xs", [128, 16 * TW], BF16)
    Hp = cx.sb("m_Hp", [128, 32 * TW], BF16)
    stg = cx.sb("m_stg", [128, 16 * 512], F32)
    wq1 = cx.sb("m_wq1", [128, 2 * 2048], BF16)
    wq3 = cx.sb("m_wq3", [128, 2 * 2048], BF16)
    wq2 = cx.sb("m_wq2", [128, 2 * 4096], BF16)
    xres = cx.sb("m_xres", [128, 2 * 512], F32)
    gtm = cx.sb("m_gtm", [128, 64], F32)
    gfm = cx.sb("m_gfm", [8, TW], F32)
    grow = cx.sb("m_grow", [1, 8 * TW], F32)
    sc1 = cx.sb("m_sc1", [128, 8], F32)
    sc2 = cx.sb("m_sc2", [128, 8], F32)
    m1c = cx.sb("m_m1", [128, 4], F32)
    xsi = cx.sb("m_xsi", [128, 512], BF16)
    hh = cx.sb("m_hh", [128, 512], BF16)
    sqt = cx.sb("m_sqt", [128, 512], F32)
    mrow = cx.sb("m_mrow", [1, 512], F32)
    vrow = cx.sb("m_vrow", [1, 512], F32)
    oout = cx.sb("m_oout", [128, 512], F32)
    obf = cx.sb("m_obf", [128, 512], BF16)

    routT = cx.sb("m_routT", [128, 16 * 8], BF16)
    cx._cw["constsg"] = p.dma("gpsimd", lambda e: e.dma_start(
        out=routT[:, :].rearrange("p (a r) -> p a r", r=8),
        in_=cx.dram["w_routerT"].rearrange("(a p) r -> p a r", p=128)), "constsg")
    idf = cx.sbuf["c_idf32"]
    ones1x = cx.sbuf["c_ones_1x128_f"]

    pb = [cx.pbank(i) for i in range(8)]
    bank = dict(pss=pb[3][0:1, :], psq=pb[4][0:1, :], psb1=pb[5], psb2=pb[6],
                sqt=sqt, mrow=mrow, vrow=vrow, oout=oout, obf=obf)

    w1d = cx.dram["w_moe1"].rearrange("e (a p) f -> p e a f", p=128)
    w3d = cx.dram["w_moe3"].rearrange("e (a p) f -> p e a f", p=128)
    w2d = cx.dram["w_moe2"].rearrange("e (k p) d -> p e k d", p=128)

    wlast = None
    for th in range(T // TW):
        ths = slice(th * TW, (th + 1) * TW)
        ldx = p.dma("sync", lambda e, s=ths: e.dma_start(
            out=xs[:, :].rearrange("p (a t) -> p a t", t=TW),
            in_=stream_bf.rearrange("(a p) t -> p a t", p=128)[:, :, s]),
            "m_xs", waits=[("pe", p.count("pe"))])

        gtr_evac = None
        for tt in range(TW // 128):
            pr = pb[7][:, 0:8]
            lpe = None
            for kt in range(16):
                lpe = p.op("tensor", lambda e, k=kt, t=tt: e.matmul(
                    out=pr, lhsT=xs[:, k * TW + t * 128:k * TW + (t + 1) * 128],
                    rhs=routT[:, k * 8:(k + 1) * 8], start=(k == 0), stop=(k == 15)),
                    waits=([ldx] + cx.consts_w + [("dve", p.count("dve"))] if kt == 0 else []),
                    inc=("pe" if kt == 15 else None))
            c1 = p.op("vector", lambda e: e.tensor_copy(out=sc1[:, :], in_=pr),
                      waits=[("pe", lpe)], inc="dve")
            c2 = p.op("vector", lambda e: e.reduce_max(out=m1c[:, 0:1], in_=sc1[:, :],
                      axis=mybir.AxisListType.X), waits=[("dve", c1)], inc="dve")
            c3 = p.op("vector", lambda e: e.tensor_scalar(out=m1c[:, 1:2], in0=m1c[:, 0:1],
                      scalar1=-1.0, scalar2=None, op0=ALU.mult), waits=[("dve", c2)], inc="dve")
            c4 = p.op("vector", lambda e: e.tensor_scalar(out=sc2[:, :], in0=sc1[:, :],
                      scalar1=m1c[:, 0:1], scalar2=None, op0=ALU.is_ge),
                      waits=[("dve", c3)], inc="dve")
            c5 = p.op("vector", lambda e: e.scalar_tensor_tensor(out=sc2[:, :], in0=sc2[:, :],
                      scalar=-1e30, in1=sc1[:, :], op0=ALU.mult, op1=ALU.add),
                      waits=[("dve", c4)], inc="dve")
            c6 = p.op("vector", lambda e: e.reduce_max(out=m1c[:, 2:3], in_=sc2[:, :],
                      axis=mybir.AxisListType.X), waits=[("dve", c5)], inc="dve")
            a1 = p.op("scalar", lambda e: e.activation(out=sc2[:, :], in_=sc1[:, :],
                      func=AF.Exp, bias=m1c[:, 1:2], scale=1.0),
                      waits=[("dve", c6)], inc="act")
            c7 = p.op("vector", lambda e: e.tensor_scalar(out=sc1[:, :], in0=sc1[:, :],
                      scalar1=m1c[:, 2:3], scalar2=None, op0=ALU.is_ge),
                      waits=[("act", a1)], inc="dve")
            c8 = p.op("vector", lambda e: e.tensor_tensor(out=sc2[:, :], in0=sc2[:, :],
                      in1=sc1[:, :], op=ALU.mult), waits=[("dve", c7)], inc="dve")
            c9 = p.op("vector", lambda e: e.reduce_sum(out=m1c[:, 3:4], in_=sc2[:, :],
                      axis=mybir.AxisListType.X), waits=[("dve", c8)], inc="dve")
            c10 = p.op("vector", lambda e: e.reciprocal(out=m1c[:, 3:4], in_=m1c[:, 3:4]),
                       waits=[("dve", c9)], inc="dve")
            c11 = p.op("vector", lambda e, t=tt: e.tensor_scalar(out=gtm[:, t * 8:(t + 1) * 8],
                       in0=sc2[:, :], scalar1=m1c[:, 3:4], scalar2=None, op0=ALU.mult),
                       waits=[("dve", c10)], inc="dve")
            tpe = p.op("tensor", lambda e, t=tt: e.transpose(
                out=pb[6][0:8, 0:128], in_=gtm[:, t * 8:(t + 1) * 8], identity=idf[:, :]),
                waits=[("dve", c11)], inc="pe")
            gtr_evac = p.op("vector", lambda e, t=tt: e.tensor_copy(
                out=gfm[:, t * 128:(t + 1) * 128], in_=pb[6][0:8, 0:128]),
                waits=[("pe", tpe)], inc="dve")
        stg_w = p.dma("gpsimd", lambda e, h=th: e.dma_start(
            out=g_dram[:, h * TW:(h + 1) * TW], in_=gfm[:, :]),
            "m_g%d" % th, waits=[("dve", gtr_evac)])
        groww = None
        for ee in range(8):
            groww = p.dma("sync", lambda e, ee=ee, h=th: e.dma_start(
                out=grow[0:1, ee * TW:(ee + 1) * TW], in_=g_dram[ee:ee + 1, h * TW:(h + 1) * TW]),
                "m_grow%d" % th, waits=([stg_w, ("pe", p.count("pe"))] if ee == 0 else []))

        w1w = [None, None]
        w3w = [None, None]
        w_use = [0, 0]
        for e_i in range(EXP):
            gmm = p.op("tensor", lambda e, ei=e_i: e.matmul(
                out=pb[2][:, :], lhsT=ones1x[:, :],
                rhs=grow[0:1, ei * TW:ei * TW + 512],
                start=True, stop=True),
                waits=[groww, ("dve", p.count("dve"))], inc="pe")
            for mt in range(4):
                wpar = mt % 2
                w1w[wpar] = p.dma("gpsimd", lambda e, ei=e_i, m=mt, wpar=wpar: e.dma_start(
                    out=wq1[:, wpar * 2048:(wpar + 1) * 2048].rearrange("p (a f) -> p a f", f=128),
                    in_=w1d[:, ei, :, m * 128:(m + 1) * 128]),
                    "m_w1%d" % wpar, waits=[("pe", w_use[wpar])])
                w3w[wpar] = p.dma("gpsimd", lambda e, ei=e_i, m=mt, wpar=wpar: e.dma_start(
                    out=wq3[:, wpar * 2048:(wpar + 1) * 2048].rearrange("p (a f) -> p a f", f=128),
                    in_=w3d[:, ei, :, m * 128:(m + 1) * 128]),
                    "m_w3%d" % wpar)
                x1pe = x3pe = None
                actw = p.count("act")
                dvew = p.count("dve")
                for kt in range(16):
                    x1pe = p.op("tensor", lambda e, k=kt, wpar=wpar: e.matmul(
                        out=pb[0][:, :], lhsT=wq1[:, wpar * 2048 + k * 128:wpar * 2048 + (k + 1) * 128],
                        rhs=xs[:, k * TW:k * TW + 512],
                        start=(k == 0), stop=(k == 15)),
                        waits=([w1w[wpar], ("act", actw)] if kt == 0 else []),
                        inc=("pe" if kt == 15 else None))
                for kt in range(16):
                    x3pe = p.op("tensor", lambda e, k=kt, wpar=wpar: e.matmul(
                        out=pb[1][:, :], lhsT=wq3[:, wpar * 2048 + k * 128:wpar * 2048 + (k + 1) * 128],
                        rhs=xs[:, k * TW:k * TW + 512],
                        start=(k == 0), stop=(k == 15)),
                        waits=([w3w[wpar], ("dve", dvew)] if kt == 0 else []),
                        inc=("pe" if kt == 15 else None))
                w_use[wpar] = x3pe
                av = p.op("scalar", lambda e: e.activation(out=xsi[:, :], in_=pb[0][:, :],
                          func=AF.Silu), waits=[("pe", x1pe), ("dve", p.count("dve"))], inc="act")
                hv = p.op("vector", lambda e: e.tensor_tensor(out=hh[:, :], in0=xsi[:, :],
                          in1=pb[1][:, :], op=ALU.mult),
                          waits=[("act", av), ("pe", x3pe)], inc="dve")
                p.op("vector", lambda e, ei=e_i, m=mt: e.tensor_tensor(
                    out=Hp[:, (ei * 4 + m) * TW:(ei * 4 + m) * TW + 512],
                    in0=hh[:, :], in1=pb[2][:, :], op=ALU.mult),
                    waits=[("dve", hv), ("pe", gmm)], inc="dve")
        hp_dve = p.count("dve")

        for tc in range(TW // 512):
            w2w = [None, None]
            w2use = [0, 0]
            evac_dve = [p.count("dve"), p.count("dve")]
            for dt in range(16):
                par = dt % 2
                w2w[par] = p.dma("gpsimd", lambda e, d=dt, par=par: e.dma_start(
                    out=wq2[:, par * 4096:(par + 1) * 4096].rearrange("p (ek d) -> p ek d", d=128),
                    in_=w2d[:, :, :, d * 128:(d + 1) * 128].rearrange("p e k d -> p (e k) d")),
                    "m_w2%d" % par, waits=[("pe", w2use[par])])
                ldr = p.dma("sync", lambda e, d=dt, c=tc, h=th, par=par: e.dma_start(
                    out=xres[:, par * 512:(par + 1) * 512],
                    in_=stream_f32[d * 128:(d + 1) * 128, h * TW + c * 512:h * TW + (c + 1) * 512]),
                    "m_xr%d" % par, waits=[("dve", p.count("dve"))])
                ps = pb[0] if par == 0 else pb[1]
                ype = None
                act_war = p.count("act")
                for kt in range(32):
                    ype = p.op("tensor", lambda e, k=kt, c=tc, par=par, ps=ps: e.matmul(
                        out=ps[:, :], lhsT=wq2[:, par * 4096 + k * 128:par * 4096 + (k + 1) * 128],
                        rhs=Hp[:, k * TW + c * 512:k * TW + (c + 1) * 512],
                        start=(k == 0), stop=(k == 31)),
                        waits=([w2w[par], ("dve", max(hp_dve, evac_dve[par])),
                                ("act", act_war)] if kt == 0 else []),
                        inc=("pe" if kt == 31 else None))
                w2use[par] = ype
                evac_dve[par] = p.op("vector", lambda e, d=dt, ps=ps, par=par: e.tensor_tensor(
                    out=stg[:, d * 512:(d + 1) * 512], in0=ps[:, :],
                    in1=xres[:, par * 512:(par + 1) * 512], op=ALU.add),
                    waits=[("pe", ype), ldr], inc="dve")
            wlast = emit_ln_store(cx, stg, 512, out_f32, out_bf, th * TW + tc * 512, bank, "moe")
    return wlast





def rms_bcast(cx, psq_ap, inv_n_const, pbb, tag):
    p = cx.p
    vr = cx.sbuf["vrow_s"]
    w = psq_ap.shape[1]
    v1 = p.op("vector", lambda e: e.tensor_scalar(out=vr[:, 0:w], in0=psq_ap,
              scalar1=inv_n_const, scalar2=None, op0=ALU.mult),
              waits=[("pe", p.count("pe"))], inc="dve")
    a1 = p.op("scalar", lambda e: e.activation(out=vr[:, 0:w], in_=vr[:, 0:w],
              func=AF.Ln, bias=1e-6), waits=[("dve", v1)], inc="act")
    a2 = p.op("scalar", lambda e: e.activation(out=vr[:, 0:w], in_=vr[:, 0:w],
              func=AF.Exp, scale=-0.5), waits=[("act", a1)], inc="act")
    pbv = p.op("tensor", lambda e: e.matmul(out=pbb[:, 0:w], lhsT=cx.sbuf["c_ones_1x128_f"][:, :],
               rhs=vr[:, 0:w], start=True, stop=True),
               waits=[("act", a2), ("dve", p.count("dve"))], inc="pe")
    return pbv


def phase_l1qkv(cx, stream_bf, qn_dram, kvp_dram):
    nc, p = cx.nc, cx.p
    TC = 512

    xs = cx.sb("a_xs", [128, 16 * T], BF16)
    qraw = cx.sb("a_qraw", [128, 12 * TC], F32)
    kvraw = cx.sb("a_kvraw", [128, 5 * TC], F32)
    sq = cx.sb("a_sq", [128, TC], F32)
    vrow_s = cx.sb("vrow_s", [1, TC], F32)
    cx.sbuf["vrow_s"] = vrow_s
    obf = cx.sb("a_obf", [128, TC], BF16)
    rot = cx.sb("a_rot", [64, TC], F32)
    rop1 = cx.sb("a_rop1", [64, TC], F32)
    wqa = cx.sb("a_wqa", [128, 2 * 2048], BF16)
    wkva = cx.sb("a_wkva", [128, 2 * 2048], BF16)

    qab = cx.load_const("w_qab")
    qnw = cx.load_const("w_qnorm")
    kvab = cx.load_const("w_kvab")
    kvnw = cx.load_const("w_kvnorm")
    cosq = cx.load_const("c_cosq")
    sinq = cx.load_const("c_sinq")

    pb = [cx.pbank(i) for i in range(8)]

    ldx = p.dma("sync", lambda e: e.dma_start(
        out=xs[:, :].rearrange("p (a t) -> p a t", t=T),
        in_=stream_bf.rearrange("(a p) t -> p a t", p=128)), "a_xs",
        waits=[("pe", p.count("pe"))])

    qaT = cx.dram["w_qaT"].rearrange("(a p) n -> p a n", p=128)
    kvaT = cx.dram["w_kvaT"].rearrange("(a p) n -> p a n", p=128)

    obf_gate = None
    for tc in range(4):
        wq = [None, None]
        ev = [0, 0]
        wquse = [0, 0]
        for mt in range(12):
            par = mt % 2
            wq[par] = p.dma("gpsimd", lambda e, m=mt, par=par: e.dma_start(
                out=wqa[:, par * 2048:(par + 1) * 2048].rearrange("p (a n) -> p a n", n=128),
                in_=qaT[:, :, m * 128:(m + 1) * 128]), "a_wqa%d" % par,
                waits=[("pe", wquse[par])])
            ps = pb[par]
            qpe = None
            for kt in range(16):
                qpe = p.op("tensor", lambda e, k=kt, par=par, c=tc, ps=ps: e.matmul(
                    out=ps[:, :], lhsT=wqa[:, par * 2048 + k * 128:par * 2048 + (k + 1) * 128],
                    rhs=xs[:, k * T + c * TC:k * T + (c + 1) * TC],
                    start=(k == 0), stop=(k == 15)),
                    waits=([wq[par], ldx, ("act", ev[par])] + cx.consts_w if kt == 0 else []),
                    inc=("pe" if kt == 15 else None))
            wquse[par] = qpe
            ev[par] = p.op("scalar", lambda e, m=mt, ps=ps: e.activation(
                out=qraw[:, m * TC:(m + 1) * TC], in_=ps[:, :], func=AF.Identity,
                bias=qab[:, m:m + 1], scale=1.0), waits=[("pe", qpe)], inc="act")
        qraw_act = p.count("act")
        prevmm = 0
        for mt in range(12):
            av = p.op("scalar", lambda e, m=mt: e.activation(
                out=sq[:, :], in_=qraw[:, m * TC:(m + 1) * TC], func=AF.Square),
                waits=([("act", qraw_act), ("pe", prevmm), ("dve", p.count("dve"))]), inc="act")
            prevmm = p.op("tensor", lambda e, m=mt: e.matmul(
                out=pb[2][0:1, :], lhsT=cx.sbuf["c_ones_128x1_f"][:, :], rhs=sq[:, :],
                start=(m == 0), stop=(m == 11)), waits=[("act", av)], inc="pe")
        pbv = rms_bcast(cx, pb[2][0:1, 0:TC], 1.0 / 1536.0, pb[3], "q")
        for mt in range(12):
            ov = p.op("vector", lambda e, m=mt: e.scalar_tensor_tensor(
                out=obf[:, :], in0=qraw[:, m * TC:(m + 1) * TC], scalar=qnw[:, m:m + 1],
                in1=pb[3][:, 0:TC], op0=ALU.mult, op1=ALU.mult),
                waits=[("pe", pbv)] + ([obf_gate] if obf_gate else []), inc="dve")
            obf_gate = p.dma("gpsimd", lambda e, m=mt, c=tc: e.dma_start(
                out=qn_dram[m * 128:(m + 1) * 128, c * TC:(c + 1) * TC], in_=obf[:, :]),
                "a_qn", waits=[("dve", ov)])

        wk = [None, None]
        evk = [qraw_act, qraw_act]
        wkuse = [0, 0]
        for mt in range(5):
            par = mt % 2
            mw = 128 if mt < 4 else 64
            wk[par] = p.dma("gpsimd", lambda e, m=mt, par=par, mw=mw: e.dma_start(
                out=wkva[:, par * 2048:par * 2048 + 16 * mw].rearrange("p (a n) -> p a n", n=mw),
                in_=kvaT[:, :, m * 128:m * 128 + mw]), "a_wkva%d" % par,
                waits=[("pe", wkuse[par])])
            ps = pb[par]
            kpe_mm = None
            for kt in range(16):
                kpe_mm = p.op("tensor", lambda e, k=kt, par=par, c=tc, ps=ps, mw=mw: e.matmul(
                    out=ps[0:mw, 0:TC], lhsT=wkva[:, par * 2048 + k * mw:par * 2048 + (k + 1) * mw],
                    rhs=xs[:, k * T + c * TC:k * T + (c + 1) * TC],
                    start=(k == 0), stop=(k == 15)),
                    waits=([wk[par], ("act", evk[par])] if kt == 0 else []),
                    inc=("pe" if kt == 15 else None))
            wkuse[par] = kpe_mm
            evk[par] = p.op("scalar", lambda e, m=mt, ps=ps, mw=mw: e.activation(
                out=kvraw[0:mw, m * TC:(m + 1) * TC], in_=ps[0:mw, 0:TC], func=AF.Identity,
                bias=kvab[0:mw, m:m + 1], scale=1.0), waits=[("pe", kpe_mm)], inc="act")
        kvraw_act = p.count("act")
        prevmm = 0
        for mt in range(4):
            av = p.op("scalar", lambda e, m=mt: e.activation(
                out=sq[:, :], in_=kvraw[:, m * TC:(m + 1) * TC], func=AF.Square),
                waits=[("act", kvraw_act), ("pe", prevmm), ("dve", p.count("dve"))], inc="act")
            prevmm = p.op("tensor", lambda e, m=mt: e.matmul(
                out=pb[2][0:1, :], lhsT=cx.sbuf["c_ones_128x1_f"][:, :], rhs=sq[:, :],
                start=(m == 0), stop=(m == 3)), waits=[("act", av)], inc="pe")
        pbv = rms_bcast(cx, pb[2][0:1, 0:TC], 1.0 / 512.0, pb[3], "kv")
        for mt in range(4):
            ov = p.op("vector", lambda e, m=mt: e.scalar_tensor_tensor(
                out=obf[:, :], in0=kvraw[:, m * TC:(m + 1) * TC], scalar=kvnw[:, m:m + 1],
                in1=pb[3][:, 0:TC], op0=ALU.mult, op1=ALU.mult),
                waits=[("pe", pbv)] + ([obf_gate] if obf_gate else []), inc="dve")
            obf_gate = p.dma("gpsimd", lambda e, m=mt, c=tc: e.dma_start(
                out=kvp_dram[m * 128:(m + 1) * 128, c * TC:(c + 1) * TC], in_=obf[:, :]),
                "a_kv", waits=[("dve", ov)])
        kp = kvraw[0:64, 4 * TC:5 * TC]
        r1 = p.op("vector", lambda e: e.tensor_scalar(
            out=rot[0:32, :], in0=kvraw[32:64, 4 * TC:5 * TC], scalar1=-1.0, scalar2=None,
            op0=ALU.mult), waits=[("act", kvraw_act), ("dve", p.count("dve"))], inc="dve")
        r2 = p.op("vector", lambda e: e.tensor_copy(
            out=rot[32:64, :], in_=kvraw[0:32, 4 * TC:5 * TC]), waits=[("dve", r1)], inc="dve")
        r3 = p.op("vector", lambda e, c=tc: e.tensor_tensor(
            out=rop1[:, :], in0=kp, in1=cosq[:, c * TC:(c + 1) * TC], op=ALU.mult),
            waits=[("dve", r2)] + cx.consts_w, inc="dve")
        r4 = p.op("vector", lambda e, c=tc: e.tensor_tensor(
            out=rot[:, :], in0=rot[:, :], in1=sinq[:, c * TC:(c + 1) * TC], op=ALU.mult),
            waits=[("dve", r3)], inc="dve")
        r5 = p.op("vector", lambda e: e.tensor_tensor(
            out=obf[0:64, :], in0=rop1[:, :], in1=rot[:, :], op=ALU.add),
            waits=[("dve", r4)] + ([obf_gate] if obf_gate else []), inc="dve")
        obf_gate = p.dma("gpsimd", lambda e, c=tc: e.dma_start(
            out=kvp_dram[512:576, c * TC:(c + 1) * TC], in_=obf[0:64, :]),
            "a_kv", waits=[("dve", r5)])
    return [obf_gate]



def phase_l1attn(cx, qn_dram, kvall_dram, attn_dram):
    nc, p = cx.nc, cx.p
    TC = 512
    SCALE = 0.07216878364870323

    qn = cx.sb("t_qn", [128, 12 * T], BF16)
    kvn = cx.sb("t_kvn", [128, 4 * S], BF16)
    kB = cx.sb("t_kB", [65, S], BF16)
    kp2m = cx.sb("t_kp2m", [1, 1], F32)
    kA = cx.sb("t_kA", [128, S], BF16)
    VT = cx.sb("t_VT", [128, S], BF16)
    vtmp = cx.sb("t_vtmp", [128, TC], BF16)
    wqb_ = cx.sb("t_wqb", [128, 2 * 12 * 192], BF16)
    wkvb_ = cx.sb("t_wkvb", [128, 4 * 256], BF16)
    qA = cx.sb("t_qA", [128, T], BF16)
    qB = cx.sb("t_qB", [65, T], BF16)
    qpef = cx.sb("t_qpef", [64, TC], F32)
    rot = cx.sb("t_rot", [64, TC], F32)
    sq = cx.sb("t_sq", [128, TC], F32)
    kmaxr = cx.sb("t_kmaxr", [1, TC], F32)
    km2 = cx.sb("t_km2", [1, 1], F32)
    brow = cx.sb("t_brow", [1, TC], F32)
    PT = cx.sb("t_PT", [128, 2 * 1024], BF16)
    rb_sb = cx.sb("t_rb", [128, TC], BF16)
    oot = cx.sb("t_oot", [128, TC], BF16)
    cosq = cx.load_const("c_cosq")
    sinq = cx.load_const("c_sinq")
    qbb = cx.load_const("w_qbb")
    kvbb = cx.load_const("w_kvbb")
    idb = cx.sbuf["c_idbf"]
    onesf = cx.sbuf["c_ones_128x1_f"]
    ones64 = onesf[0:64, :]
    onesb = cx.sbuf["c_ones_128x1_b"]

    pb = [cx.pbank(i) for i in range(8)]

    ldq = p.dma("sync", lambda e: e.dma_start(
        out=qn[:, :].rearrange("p (a t) -> p a t", t=T),
        in_=qn_dram.rearrange("(a p) t -> p a t", p=128)), "t_qn",
        waits=[("pe", p.count("pe"))])
    ldkv = None
    for r in range(4):
        for kvt in range(4):
            ldkv = p.dma("sync", lambda e, r=r, k=kvt: e.dma_start(
                out=kvn[:, k * S + r * T:k * S + (r + 1) * T],
                in_=kvall_dram[r * 576 + k * 128:r * 576 + (k + 1) * 128, :]), "t_kvn",
                waits=([("pe", p.count("pe"))] if (r == 0 and kvt == 0) else []))
        ldkv = p.dma("sync", lambda e, r=r: e.dma_start(
            out=kB[0:64, r * T:(r + 1) * T],
            in_=kvall_dram[r * 576 + 512:r * 576 + 576, :]), "t_kvn")
    ldone = p.dma("sync", lambda e: e.dma_start(
        out=kB[64:65, :], in_=cx.dram["c_onesrow_b"]), "t_kvn")
    kvn_w = ldone
    kp2mm = None
    kmx = None
    for c in range(16):
        a1 = p.op("scalar", lambda e, c=c: e.activation(
            out=sq[0:64, :], in_=kB[0:64, c * TC:(c + 1) * TC], func=AF.Square),
            waits=[kvn_w, ("pe", kp2mm or 0)], inc="act")
        kp2mm = p.op("tensor", lambda e, c=c: e.matmul(
            out=pb[2][0:1, :], lhsT=ones64, rhs=sq[0:64, :], start=True, stop=True),
            waits=[("act", a1), ("dve", p.count("dve"))] + cx.consts_w, inc="pe")
        if c == 0:
            kmx = p.op("vector", lambda e: e.tensor_copy(out=kmaxr[:, :], in_=pb[2][0:1, :]),
                       waits=[("pe", kp2mm)], inc="dve")
        else:
            kmx = p.op("vector", lambda e: e.tensor_tensor(out=kmaxr[:, :], in0=kmaxr[:, :],
                       in1=pb[2][0:1, :], op=ALU.max), waits=[("pe", kp2mm)], inc="dve")
    kmx = p.op("vector", lambda e: e.reduce_max(out=kp2m[:, :], in_=kmaxr[:, :],
               axis=mybir.AxisListType.X), waits=[("dve", kmx)], inc="dve")
    kpe2_dve = p.count("dve")

    qbT = cx.dram["w_qbT"].rearrange("(a p) n -> p a n", p=128)
    kvbT = cx.dram["w_kvbT"].rearrange("(a p) n -> p a n", p=128)

    attn_st = None
    huse = [0, 0]
    for h in range(NH):
        hp_ = h % 2
        wqb = wqb_[:, hp_ * 2304:(hp_ + 1) * 2304]
        wkvb = wkvb_
        wqw = p.dma("gpsimd", lambda e, h=h, w=wqb: e.dma_start(
            out=w.rearrange("p (a n) -> p a n", n=192),
            in_=qbT[:, :, h * 192:(h + 1) * 192]), "t_wqb%d" % hp_,
            waits=[("pe", huse[hp_])])
        wkw = p.dma("gpsimd", lambda e, h=h, w=wkvb: e.dma_start(
            out=w.rearrange("p (a n) -> p a n", n=256),
            in_=kvbT[:, :, h * 256:(h + 1) * 256]), "t_wkvb",
            waits=[("pe", huse[1 - hp_])])

        kmax_dve = None
        for c in range(16):
            kmm = None
            act_war = p.count("act")
            for a in range(4):
                kmm = p.op("tensor", lambda e, a=a, c=c, wkvb=wkvb: e.matmul(
                    out=pb[0][:, :], lhsT=wkvb[:, a * 256:a * 256 + 128],
                    rhs=kvn[:, a * S + c * TC:a * S + (c + 1) * TC],
                    start=(a == 0), stop=(a == 3)),
                    waits=([wkw, kvn_w, ("act", act_war), ("dve", kpe2_dve)] if a == 0 else []),
                    inc=("pe" if a == 3 else None))
            av = p.op("scalar", lambda e, c=c, h=h: e.activation(
                out=kA[:, c * TC:(c + 1) * TC], in_=pb[0][:, :], func=AF.Identity,
                bias=kvbb[:, 2 * h:2 * h + 1], scale=1.0), waits=[("pe", kmm)], inc="act")
            a2 = p.op("scalar", lambda e, c=c: e.activation(
                out=sq[:, :], in_=kA[:, c * TC:(c + 1) * TC], func=AF.Square),
                waits=[("act", av), ("pe", p.count("pe"))], inc="act")
            nmm = p.op("tensor", lambda e: e.matmul(
                out=pb[2][0:1, :], lhsT=onesf[:, :], rhs=sq[:, :], start=True, stop=True),
                waits=[("act", a2), ("dve", p.count("dve"))] + cx.consts_w, inc="pe")
            if c == 0:
                kmax_dve = p.op("vector", lambda e: e.tensor_copy(out=kmaxr[:, :],
                                in_=pb[2][0:1, :]), waits=[("pe", nmm)], inc="dve")
            else:
                kmax_dve = p.op("vector", lambda e: e.tensor_tensor(out=kmaxr[:, :],
                                in0=kmaxr[:, :], in1=pb[2][0:1, :], op=ALU.max),
                                waits=[("pe", nmm)], inc="dve")
            vmm = None
            for a in range(4):
                vmm = p.op("tensor", lambda e, a=a, c=c, wkvb=wkvb: e.matmul(
                    out=pb[1][:, :], lhsT=wkvb[:, a * 256 + 128:a * 256 + 256],
                    rhs=kvn[:, a * S + c * TC:a * S + (c + 1) * TC],
                    start=(a == 0), stop=(a == 3)),
                    waits=([("dve", p.count("dve"))] if a == 0 else []),
                    inc=("pe" if a == 3 else None))
            vv = p.op("scalar", lambda e, c=c, h=h: e.activation(
                out=vtmp[:, :], in_=pb[1][:, :], func=AF.Identity,
                bias=kvbb[:, 2 * h + 1:2 * h + 2], scale=1.0), waits=[("pe", vmm)], inc="act")
            for s4 in range(4):
                tmm = p.op("tensor", lambda e, s4=s4: e.transpose(
                    out=pb[3][:, 0:64].bitcast(BF16), in_=vtmp[:, s4 * 128:(s4 + 1) * 128],
                    identity=idb[:, :]),
                    waits=[("act", vv), ("dve", p.count("dve"))] + cx.consts_w, inc="pe")
                p.op("vector", lambda e, c=c, s4=s4: e.tensor_copy(
                    out=VT[:, (c * 4 + s4) * 128:(c * 4 + s4 + 1) * 128],
                    in_=pb[3][:, 0:64].bitcast(BF16)), waits=[("pe", tmm)], inc="dve")
        km = p.op("vector", lambda e: e.reduce_max(out=km2[:, :], in_=kmaxr[:, :],
                  axis=mybir.AxisListType.X), waits=[("dve", kmax_dve)], inc="dve")
        km = p.op("vector", lambda e: e.tensor_tensor(out=km2[:, :], in0=km2[:, :],
                  in1=kp2m[:, :], op=ALU.add), waits=[("dve", km)], inc="dve")

        on_prev = 0
        for c in range(4):
            qmm = None
            for a in range(12):
                qmm = p.op("tensor", lambda e, a=a, c=c, wqb=wqb: e.matmul(
                    out=pb[0][:, :], lhsT=wqb[:, a * 192:a * 192 + 128],
                    rhs=qn[:, a * T + c * TC:a * T + (c + 1) * TC],
                    start=(a == 0), stop=(a == 11)),
                    waits=([wqw, ldq, ("act", p.count("act")), ("dve", p.count("dve"))]
                           if a == 0 else []), inc=("pe" if a == 11 else None))
            qav = p.op("scalar", lambda e, c=c, h=h: e.activation(
                out=qA[:, c * TC:(c + 1) * TC], in_=pb[0][:, :], func=AF.Identity,
                bias=qbb[:, 2 * h:2 * h + 1], scale=1.0), waits=[("pe", qmm)], inc="act")
            qpmm = None
            for a in range(12):
                qpmm = p.op("tensor", lambda e, a=a, c=c, wqb=wqb: e.matmul(
                    out=pb[1][0:64, :], lhsT=wqb[:, a * 192 + 128:a * 192 + 192],
                    rhs=qn[:, a * T + c * TC:a * T + (c + 1) * TC],
                    start=(a == 0), stop=(a == 11)),
                    waits=([("dve", p.count("dve"))] if a == 0 else []),
                    inc=("pe" if a == 11 else None))
            qpv = p.op("scalar", lambda e, c=c, h=h: e.activation(
                out=qpef[:, :], in_=pb[1][0:64, :], func=AF.Identity,
                bias=qbb[0:64, 2 * h + 1:2 * h + 2], scale=1.0), waits=[("pe", qpmm)], inc="act")
            r1 = p.op("vector", lambda e: e.tensor_scalar(out=rot[0:32, :], in0=qpef[32:64, :],
                      scalar1=-1.0, scalar2=None, op0=ALU.mult), waits=[("act", qpv)], inc="dve")
            r2 = p.op("vector", lambda e: e.tensor_copy(out=rot[32:64, :], in_=qpef[0:32, :]),
                      waits=[("dve", r1)], inc="dve")
            r3 = p.op("vector", lambda e, c=c: e.tensor_tensor(out=qpef[:, :], in0=qpef[:, :],
                      in1=cosq[:, c * TC:(c + 1) * TC], op=ALU.mult),
                      waits=[("dve", r2)] + cx.consts_w, inc="dve")
            r4 = p.op("vector", lambda e, c=c: e.tensor_tensor(out=rot[:, :], in0=rot[:, :],
                      in1=sinq[:, c * TC:(c + 1) * TC], op=ALU.mult), waits=[("dve", r3)], inc="dve")
            r5 = p.op("vector", lambda e, c=c: e.tensor_tensor(out=qB[0:64, c * TC:(c + 1) * TC],
                      in0=qpef[:, :], in1=rot[:, :], op=ALU.add), waits=[("dve", r4)], inc="dve")
            s1 = p.op("scalar", lambda e, c=c: e.activation(out=sq[:, :],
                      in_=qA[:, c * TC:(c + 1) * TC], func=AF.Square),
                      waits=[("act", qav), ("pe", p.count("pe"))], inc="act")
            b1 = p.op("tensor", lambda e: e.matmul(out=pb[2][0:1, :], lhsT=onesf[:, :],
                      rhs=sq[:, :], start=True, stop=False),
                      waits=[("act", s1)] + cx.consts_w, inc="pe")
            s2 = p.op("scalar", lambda e, c=c: e.activation(out=sq[0:64, :],
                      in_=qB[0:64, c * TC:(c + 1) * TC], func=AF.Square),
                      waits=[("dve", r5), ("pe", b1)], inc="act")
            b2 = p.op("tensor", lambda e: e.matmul(out=pb[2][0:1, :], lhsT=ones64,
                      rhs=sq[0:64, :], start=False, stop=True), waits=[("act", s2)], inc="pe")
            b3 = p.op("vector", lambda e: e.tensor_scalar(out=brow[:, :], in0=pb[2][0:1, :],
                      scalar1=km2[0:1, 0:1], scalar2=None, op0=ALU.mult),
                      waits=[("pe", b2), ("dve", km)], inc="dve")
            b4 = p.op("scalar", lambda e: e.activation(out=brow[:, :], in_=brow[:, :],
                      func=AF.Ln, bias=1e-6), waits=[("dve", b3)], inc="act")
            b5 = p.op("scalar", lambda e, c=c: e.activation(out=qB[64:65, c * TC:(c + 1) * TC],
                      in_=brow[:, :], func=AF.Exp, scale=0.5), waits=[("act", b4)], inc="act")
            b6 = p.op("vector", lambda e, c=c: e.tensor_scalar(
                out=qB[64:65, c * TC:(c + 1) * TC], in0=qB[64:65, c * TC:(c + 1) * TC],
                scalar1=-1.0, scalar2=None, op0=ALU.mult), waits=[("act", b5)], inc="dve")

            expw = [0, 0]
            opv = None
            dnm = None
            for kt2 in range(32):
                par = kt2 % 2
                std = cx.pdouble(par)
                mm1 = mm2 = None
                for half in range(2):
                    kt = kt2 * 2 + half
                    tgt = std[:, half * 512:(half + 1) * 512]
                    mm1 = p.op("tensor", lambda e, kt=kt, c=c, tgt=tgt: e.matmul(
                        out=tgt, lhsT=kA[:, kt * 128:(kt + 1) * 128],
                        rhs=qA[:, c * TC:(c + 1) * TC], start=True, stop=False),
                        waits=([("act", expw[par]), ("dve", b6)] if half == 0 else []),
                        inc=None)
                    mm2 = p.op("tensor", lambda e, kt=kt, c=c, tgt=tgt: e.matmul(
                        out=tgt, lhsT=kB[:, kt * 128:(kt + 1) * 128],
                        rhs=qB[:, c * TC:(c + 1) * TC], start=False, stop=True),
                        inc=("pe" if half == 1 else None))
                expw[par] = p.op("scalar", lambda e, par=par, std=std: e.activation(
                    out=PT[:, par * 1024:(par + 1) * 1024], in_=std[:, 0:1024],
                    func=AF.Exp, scale=SCALE), waits=[("pe", mm2)], inc="act")
                for half in range(2):
                    kt = kt2 * 2 + half
                    opv = p.op("tensor", lambda e, kt=kt, par=par, half=half: e.matmul(
                        out=pb[4][:, :], lhsT=VT[:, kt * 128:(kt + 1) * 128],
                        rhs=PT[:, par * 1024 + half * 512:par * 1024 + (half + 1) * 512],
                        start=(kt == 0), stop=(kt == 63)),
                        waits=(([("act", expw[par]), ("dve", on_prev)]) if half == 0 else []),
                        inc=("pe" if kt == 63 else None))
                    dnm = p.op("tensor", lambda e, kt=kt, par=par, half=half: e.matmul(
                        out=pb[5][0:1, :], lhsT=onesb[:, :],
                        rhs=PT[:, par * 1024 + half * 512:par * 1024 + (half + 1) * 512],
                        start=(kt == 0), stop=(kt == 63)),
                        waits=((cx.consts_w + [("dve", on_prev)]) if kt == 0 else []),
                        inc=("pe" if kt == 63 else None))
            rc = p.op("vector", lambda e: e.reciprocal(out=brow[:, :], in_=pb[5][0:1, :]),
                      waits=[("pe", dnm)], inc="dve")
            rbc = p.op("tensor", lambda e: e.matmul(out=pb[6][:, :],
                       lhsT=cx.sbuf["c_ones_1x128_f"][:, :], rhs=brow[:, :],
                       start=True, stop=True), waits=[("dve", rc)] + cx.consts_w, inc="pe")
            rv = p.op("vector", lambda e: e.tensor_copy(out=rb_sb[:, :], in_=pb[6][:, :]),
                      waits=[("pe", rbc)], inc="dve")
            on = p.op("vector", lambda e: e.tensor_tensor(out=oot[:, :], in0=rb_sb[:, :],
                      in1=pb[4][:, :], op=ALU.mult),
                      waits=[("dve", rv), ("pe", opv)] + ([attn_st] if attn_st else []), inc="dve")
            on_prev = on
            attn_st = p.dma("gpsimd", lambda e, h=h, c=c: e.dma_start(
                out=attn_dram[h * 128:(h + 1) * 128, c * TC:(c + 1) * TC], in_=oot[:, :]),
                "t_attn", waits=[("dve", on)])
        huse[hp_] = p.count("pe")
    return [attn_st]



def phase_l1wo(cx, attn_dram, stream_f32, out_f32, out_bf):
    nc, p = cx.nc, cx.p
    TC = 512
    av_ = cx.sb("w_av", [128, 16 * TC], BF16)
    stg = cx.sb("w_stg", [128, 16 * TC], F32)
    wwo = cx.sb("w_wwo", [128, 2 * 2048], BF16)
    xres = cx.sb("w_xres", [128, 2 * TC], F32)
    sqt = cx.sb("w_sqt", [128, TC], F32)
    mrow = cx.sb("w_mrow", [1, TC], F32)
    vrow = cx.sb("w_vrow", [1, TC], F32)
    oout = cx.sb("w_oout", [128, TC], F32)
    obf = cx.sb("w_obf", [128, TC], BF16)
    wob = cx.load_const("w_wob")
    pb = [cx.pbank(i) for i in range(8)]
    bank = dict(pss=pb[3][0:1, :], psq=pb[4][0:1, :], psb1=pb[5], psb2=pb[6],
                sqt=sqt, mrow=mrow, vrow=vrow, oout=oout, obf=obf)
    woT = cx.dram["w_woT"].rearrange("(a p) n -> p a n", p=128)
    wlast = None
    wuse = [0, 0]
    evac_dve = [p.count("dve"), p.count("dve")]
    wld = [None, None]
    for tc in range(4):
        lda = p.dma("sync", lambda e, c=tc: e.dma_start(
            out=av_[:, :].rearrange("p (a t) -> p a t", t=TC),
            in_=attn_dram.rearrange("(a p) t -> p a t", p=128)[:, :, c * TC:(c + 1) * TC]),
            "w_av", waits=[("pe", p.count("pe"))])
        for dt in range(16):
            par = dt % 2
            wld[par] = p.dma("gpsimd", lambda e, d=dt, par=par: e.dma_start(
                out=wwo[:, par * 2048:(par + 1) * 2048].rearrange("p (a n) -> p a n", n=128),
                in_=woT[:, :, d * 128:(d + 1) * 128]), "w_wwo%d" % par,
                waits=[("pe", wuse[par])])
            ldr = p.dma("sync", lambda e, d=dt, c=tc, par=par: e.dma_start(
                out=xres[:, par * TC:(par + 1) * TC],
                in_=stream_f32[d * 128:(d + 1) * 128, c * TC:(c + 1) * TC]),
                "w_xr%d" % par, waits=[("dve", evac_dve[par])])
            ps = pb[par]
            ype = None
            for kt in range(16):
                ype = p.op("tensor", lambda e, k=kt, ps=ps, par=par: e.matmul(
                    out=ps[:, :], lhsT=wwo[:, par * 2048 + k * 128:par * 2048 + (k + 1) * 128],
                    rhs=av_[:, k * TC:(k + 1) * TC], start=(k == 0), stop=(k == 15)),
                    waits=([wld[par], lda, ("dve", evac_dve[par])] + cx.consts_w
                           if kt == 0 else []), inc=("pe" if kt == 15 else None))
            wuse[par] = ype
            evac_dve[par] = p.op("vector", lambda e, d=dt, ps=ps, par=par: e.scalar_tensor_tensor(
                out=stg[:, d * TC:(d + 1) * TC], in0=ps[:, :], scalar=wob[:, d:d + 1],
                in1=xres[:, par * TC:(par + 1) * TC], op0=ALU.add, op1=ALU.add),
                waits=[("pe", ype), ldr], inc="dve")
        wlast = emit_ln_store(cx, stg, TC, out_f32, out_bf, tc * TC, bank, "wo")
    return wlast





def phase_l0a(cx, x_fm, y_raw, sz_tm, Cc_d, S_out, dectot_d, expSgT_d):
    nc, p = cx.nc, cx.p
    TH = T + 3

    xT = cx.sb("l_xT", [128, 16 * TH], BF16)
    winb = cx.sb("l_winb", [128, 2 * 2048], BF16)
    wdt = cx.sb("l_wdt", [128, 16 * 32], BF16)
    xssm = cx.sb("l_xssm", [128, TH], BF16)
    Braw = cx.sb("l_Braw", [128, 2 * TH], BF16)
    Craw = cx.sb("l_Craw", [128, 2 * TH], BF16)
    cacc = cx.sb("l_cacc", [128, T], F32)
    xc = cx.sb("l_xc", [128, T], F32)
    Bc = cx.sb("l_Bc", [128, 2 * T], BF16)
    Cc = cx.sb("l_Cc", [128, 2 * T], BF16)
    dtx = cx.sb("l_dtx", [128, T], BF16)
    dtb_ = cx.sb("l_dt", [32, T], F32)
    sR = cx.sb("l_sR", [32, T], F32)
    sL = cx.sb("l_sL", [32, T], F32)
    sTl = cx.sb("l_sTl", [128, 512], F32)
    sgT = cx.sb("l_sgT", [128, 512], F32)
    eSl = cx.sb("l_eSl", [128, 512], F32)
    wsT = cx.sb("l_wsT", [128, 512], F32)
    erep = cx.sb("l_erep", [128, 512], F32)
    eend = cx.sb("l_eend", [32, 16], F32)
    erow = cx.sb("l_erow", [1, 512], F32)
    srow = cx.sb("l_srow", [1, 128], F32)
    tA = cx.sb("l_tA", [32, 512], F32)
    tBq = cx.sb("l_tB", [32, 512], F32)
    cwx = cx.sb("l_cwx", [128, 4], F32)
    cwb = cx.sb("l_cwb", [128, 2 * 4], F32)
    cwc = cx.sb("l_cwc", [128, 2 * 4], F32)
    Dm = cx.sb("l_Dm", [128, 128], F32)
    eD = cx.sb("l_eD", [128, 128], F32)
    PT = cx.sb("l_PT", [128, 128], BF16)
    dtxT = cx.sb("l_dtxT", [128, 64], BF16)
    BcTs = cx.sb("l_BcTs", [128, 128], BF16)
    ytmp = cx.sb("l_ytmp", [128, 64], F32)
    yg = cx.sb("l_yg", [128, 16 * 128], BF16)
    zg = cx.sb("l_zg", [128, 16 * 128], BF16)
    hT = cx.sb("l_hT", [128, 2 * 2 * 64], F32)
    hTb = cx.sb("l_hTb", [128, 2 * 64], BF16)

    alog = cx.load_const("w_alog")
    dtbias = cx.load_const("w_dtb")
    idf = cx.sbuf["c_idf32"]
    idb = cx.sbuf["c_idbf"]
    seldt = cx.load_const("c_seldt")
    scanmask = cx.load_const("c_scanmask")
    ones1x = cx.sbuf["c_ones_1x128_f"]
    eb_dram = cx.dram_tmp("l_eend_d", [16, 32], F32)

    pb = [cx.pbank(i) for i in range(8)]

    winT = cx.dram["w_winT"].rearrange("(a p) n -> p a n", p=128)
    ldx = p.dma("gpsimd", lambda e: e.dma_start(
        out=xT[:, :].rearrange("p (a t) -> p a t", t=TH),
        in_=x_fm.rearrange("(a p) t -> p a t", p=128)), "l_xT",
        waits=[("pe", p.count("pe"))])
    ldwdt = p.dma("gpsimd", lambda e: e.dma_start(
        out=wdt[:, :].rearrange("p (a n) -> p a n", n=32),
        in_=winT[:, :, 12288:12320]), "l_wdt")

    sp_dve = 0
    for tc in range(4):
        dmm = None
        for kt in range(16):
            dmm = p.op("tensor", lambda e, k=kt, c=tc: e.matmul(
                out=pb[2][0:32, :], lhsT=wdt[:, k * 32:(k + 1) * 32],
                rhs=xT[:, k * TH + 3 + c * 512:k * TH + 3 + (c + 1) * 512],
                start=(k == 0), stop=(k == 15)),
                waits=([ldx, ldwdt, ("dve", p.count("dve")),
                        ("act", p.count("act"))] + cx.consts_w if kt == 0 else []),
                inc=("pe" if kt == 15 else None))
        a1 = p.op("scalar", lambda e: e.activation(out=tA[:, :], in_=pb[2][0:32, :],
                  func=AF.Abs, bias=dtbias[:, 0:1], scale=1.0), waits=[("pe", dmm)], inc="act")
        a2 = p.op("scalar", lambda e: e.activation(out=tA[:, :], in_=tA[:, :],
                  func=AF.Exp, scale=-1.0), waits=[("act", a1)], inc="act")
        a3 = p.op("scalar", lambda e: e.activation(out=tA[:, :], in_=tA[:, :],
                  func=AF.Ln, bias=1.0), waits=[("act", a2)], inc="act")
        a4 = p.op("scalar", lambda e: e.activation(out=tBq[:, :], in_=pb[2][0:32, :],
                  func=AF.Relu, bias=dtbias[:, 0:1], scale=1.0), waits=[("act", a3)], inc="act")
        sp_dve = p.op("vector", lambda e, c=tc: e.tensor_tensor(
            out=dtb_[:, c * 512:(c + 1) * 512], in0=tA[:, :], in1=tBq[:, :], op=ALU.add),
            waits=[("act", a4)], inc="dve")
    ae = p.op("scalar", lambda e: e.activation(out=tA[:, 0:1], in_=alog[:, 0:1],
              func=AF.Exp), waits=[("dve", sp_dve)] + cx.consts_w, inc="act")
    ne = p.op("vector", lambda e: e.tensor_scalar(out=tA[:, 0:1], in0=tA[:, 0:1],
              scalar1=-1.0, scalar2=None, op0=ALU.mult), waits=[("act", ae)], inc="dve")
    ld_ = p.op("vector", lambda e: e.tensor_scalar(out=sR[:, :], in0=dtb_[:, :],
               scalar1=tA[:, 0:1], scalar2=None, op0=ALU.mult),
               waits=[("dve", ne)], inc="dve")
    sc = p.op("vector", lambda e: e.tensor_tensor_scan(out=sR[:, :], data0=sR[:, :],
              data1=dtb_[:, :], initial=0.0, op0=ALU.add, op1=ALU.bypass),
              waits=[("dve", ld_)], inc="dve")
    dst = p.dma("gpsimd", lambda e: e.dma_start(out=dectot_d, in_=sR[:, T - 1:T]),
                "l_dec", waits=[("dve", sc)])
    slv = None
    for c in range(NCH):
        if c == 0:
            slv = p.op("vector", lambda e: e.tensor_copy(out=sL[:, 0:QC], in_=sR[:, 0:QC]),
                       waits=[("dve", sc)], inc="dve")
        else:
            slv = p.op("vector", lambda e, c=c: e.tensor_scalar(
                out=sL[:, c * QC:(c + 1) * QC], in0=sR[:, c * QC:(c + 1) * QC],
                scalar1=sR[:, c * QC - 1:c * QC], scalar2=None, op0=ALU.subtract),
                waits=[("dve", sc)], inc="dve")
    wrow = xc[0:32, :]
    wrv = None
    for c in range(NCH):
        wrv = p.op("vector", lambda e, c=c: e.tensor_scalar(
            out=wrow[:, c * QC:(c + 1) * QC], in0=sL[:, c * QC:(c + 1) * QC],
            scalar1=sL[:, c * QC + QC - 1:c * QC + QC], scalar2=None, op0=ALU.subtract),
            waits=[("dve", slv)], inc="dve")
    tpv = None
    for c in range(NCH):
        t1 = p.op("tensor", lambda e, c=c: e.transpose(
            out=pb[4][:, 0:32], in_=sL[:, c * QC:(c + 1) * QC], identity=idf[0:32, 0:32]),
            waits=[("dve", wrv), ("dve", p.count("dve"))] + cx.consts_w, inc="pe")
        v1 = p.op("vector", lambda e, c=c: e.tensor_copy(out=sTl[:, c * 32:(c + 1) * 32],
                  in_=pb[4][:, 0:32]), waits=[("pe", t1)], inc="dve")
        t2 = p.op("tensor", lambda e, c=c: e.transpose(
            out=pb[4][:, 0:32], in_=sR[:, c * QC:(c + 1) * QC], identity=idf[0:32, 0:32]),
            waits=[("dve", v1)], inc="pe")
        v2 = p.op("vector", lambda e, c=c: e.tensor_copy(out=sgT[:, c * 32:(c + 1) * 32],
                  in_=pb[4][:, 0:32]), waits=[("pe", t2)], inc="dve")
        t3 = p.op("tensor", lambda e, c=c: e.transpose(
            out=pb[4][:, 0:32], in_=wrow[:, c * QC:(c + 1) * QC], identity=idf[0:32, 0:32]),
            waits=[("dve", v2)], inc="pe")
        tpv = p.op("vector", lambda e, c=c: e.tensor_copy(out=wsT[:, c * 32:(c + 1) * 32],
                  in_=pb[4][:, 0:32]), waits=[("pe", t3)], inc="dve")
    e1 = p.op("scalar", lambda e: e.activation(out=eSl[:, :], in_=sTl[:, :], func=AF.Exp),
              waits=[("dve", tpv)], inc="act")
    e2 = p.op("scalar", lambda e: e.activation(out=wsT[:, :], in_=wsT[:, :],
              func=AF.Exp, scale=-1.0), waits=[("act", e1)], inc="act")
    e3 = p.op("scalar", lambda e: e.activation(out=sgT[:, :], in_=sgT[:, :], func=AF.Exp),
              waits=[("act", e2)], inc="act")
    est = p.dma("gpsimd", lambda e: e.dma_start(out=expSgT_d, in_=sgT[:, :]), "l_esg",
                waits=[("act", e3)])
    ee1 = p.op("scalar", lambda e: e.activation(
        out=eend[:, :], in_=sL[:, QC - 1:T:QC], func=AF.Exp),
        waits=[("dve", slv)], inc="act")
    te = p.op("tensor", lambda e: e.transpose(out=pb[5][0:16, 0:32], in_=eend[:, :],
              identity=idf[0:32, 0:32]), waits=[("act", ee1)], inc="pe")
    ve = p.op("vector", lambda e: e.tensor_copy(out=tA[0:16, 0:32], in_=pb[5][0:16, 0:32]),
              waits=[("pe", te)], inc="dve")
    we = p.dma("gpsimd", lambda e: e.dma_start(out=eb_dram, in_=tA[0:16, 0:32]), "l_eb",
               waits=[("dve", ve)])
    re = p.dma("sync", lambda e: e.dma_start(
        out=erow[:, :], in_=eb_dram.rearrange("a b -> (a b)").unsqueeze(0)), "l_er",
        waits=[we])
    me = p.op("tensor", lambda e: e.matmul(out=pb[5][:, 0:512], lhsT=ones1x[:, :],
              rhs=erow[:, :], start=True, stop=True), waits=[re] + cx.consts_w, inc="pe")
    ve2 = p.op("vector", lambda e: e.tensor_copy(out=erep[:, :], in_=pb[5][:, 0:512]),
               waits=[("pe", me)], inc="dve")
    prep_dve = p.count("dve")
    prep_act = p.count("act")

    winw = None
    zstw = None
    ystw = None
    winuse = [0, 0]
    soutw = {0: None, 1: None}
    for g in range(16):
        pe_war = p.count("pe")
        dvw = p.count("dve")
        cw1 = p.dma("sync", lambda e, g=g: e.dma_start(
            out=cwx[:, :], in_=cx.dram["w_cwx"][g * 128:(g + 1) * 128, :]), "l_cw1",
            waits=[("dve", dvw)])
        cw2 = p.dma("sync", lambda e, g=g: e.dma_start(
            out=cwb[:, :].rearrange("p (a n) -> p a n", n=4),
            in_=cx.dram["w_cwb"].rearrange("(a p) n -> p a n", p=128)[:, 2 * g:2 * g + 2, :]),
            "l_cw2", waits=[("dve", dvw)])
        cw3 = p.dma("sync", lambda e, g=g: e.dma_start(
            out=cwc[:, :].rearrange("p (a n) -> p a n", n=4),
            in_=cx.dram["w_cwc"].rearrange("(a p) n -> p a n", p=128)[:, 2 * g:2 * g + 2, :]),
            "l_cw3", waits=[("dve", dvw)])

        outs = [(0, xssm, 0), (256, Braw, 0), (384, Braw, 1), (512, Craw, 0), (640, Craw, 1)]
        chunks = [(0, 512), (512, 512), (1024, 512), (1536, 512), (2048, 3)]
        group_dve = p.count("dve")
        ev = 0
        winw2 = [None, None]
        for oi, (off, buf, half) in enumerate(outs):
            wpar = oi % 2
            winw2[wpar] = p.dma("gpsimd", lambda e, g=g, off=off, wpar=wpar: e.dma_start(
                out=winb[:, wpar * 2048:(wpar + 1) * 2048].rearrange("p (a n) -> p a n", n=128),
                in_=winT[:, :, g * 768 + off:g * 768 + off + 128]), "l_win%d" % wpar,
                waits=[("pe", winuse[wpar])])
            for ci, (c0, cwid) in enumerate(chunks):
                par = ci % 2
                ps = pb[par]
                mmv = None
                for kt in range(16):
                    mmv = p.op("tensor", lambda e, k=kt, wpar=wpar, c0=c0, cwid=cwid, ps=ps: e.matmul(
                        out=ps[:, 0:cwid],
                        lhsT=winb[:, wpar * 2048 + k * 128:wpar * 2048 + (k + 1) * 128],
                        rhs=xT[:, k * TH + c0:k * TH + c0 + cwid],
                        start=(k == 0), stop=(k == 15)),
                        waits=([winw2[wpar], ("act", p.count("act"))] if kt == 0 else []),
                        inc=("pe" if kt == 15 else None))
                ev = p.op("scalar", lambda e, buf=buf, half=half, c0=c0, cwid=cwid, ps=ps:
                          e.activation(out=buf[:, half * TH + c0:half * TH + c0 + cwid],
                                       in_=ps[:, 0:cwid], func=AF.Copy),
                          waits=[("pe", mmv), ("dve", group_dve)], inc="act")
            winuse[wpar] = mmv
        proj_act = p.count("act")

        zwin = p.dma("gpsimd", lambda e, g=g: e.dma_start(
            out=winb[:, 0:2048].rearrange("p (a n) -> p a n", n=128),
            in_=winT[:, :, g * 768 + 128:g * 768 + 256]), "l_win0",
            waits=[("pe", winuse[0])])
        zevs = None
        for tt in range(16):
            par = tt % 2
            ps = pb[4] if par == 0 else pb[5]
            zmm = None
            for kt in range(16):
                zmm = p.op("tensor", lambda e, k=kt, t=tt, ps=ps: e.matmul(
                    out=ps[:, 0:128], lhsT=xT[:, k * TH + 3 + t * 128:k * TH + 3 + (t + 1) * 128],
                    rhs=winb[:, k * 128:(k + 1) * 128],
                    start=(k == 0), stop=(k == 15)),
                    waits=([zwin, ("act", p.count("act"))] if kt == 0 else []),
                    inc=("pe" if kt == 15 else None))
            zevs = p.op("scalar", lambda e, t=tt, ps=ps: e.activation(
                out=zg[:, t * 128:(t + 1) * 128], in_=ps[:, 0:128], func=AF.Silu),
                waits=[("pe", zmm)] + ([zstw] if (tt == 0 and zstw) else []), inc="act")
            winuse[0] = zmm
        zstw = p.dma("gpsimd", lambda e, g=g: e.dma_start(
            out=sz_tm.rearrange("(a p) hp -> p a hp", p=128)[:, :, g * 128:(g + 1) * 128],
            in_=zg[:, :].rearrange("p (a n) -> p a n", n=128)), "l_zst",
            waits=[("act", zevs)])

        def conv(buf, half, cw, cwcol, outbuf, outbf, outoff):
            v0 = p.op("vector", lambda e, buf=buf, half=half, cw=cw, cwcol=cwcol: e.tensor_scalar(
                out=cacc[:, :], in0=buf[:, half * TH + 0:half * TH + T],
                scalar1=cw[:, cwcol * 4:cwcol * 4 + 1], scalar2=None, op0=ALU.mult),
                waits=[("act", proj_act), cw1, cw2, cw3, ("act", p.count("act"))] + extra_gate,
                inc="dve")
            for k in range(1, 4):
                v0 = p.op("vector", lambda e, buf=buf, half=half, cw=cw, cwcol=cwcol, k=k:
                          e.scalar_tensor_tensor(
                              out=cacc[:, :], in0=buf[:, half * TH + k:half * TH + k + T],
                              scalar=cw[:, cwcol * 4 + k:cwcol * 4 + k + 1], in1=cacc[:, :],
                              op0=ALU.mult, op1=ALU.add), waits=[("dve", v0)], inc="dve")
            av = p.op("scalar", lambda e, outbuf=outbuf, outoff=outoff: e.activation(
                out=outbuf[:, outoff:outoff + T], in_=cacc[:, :], func=AF.Silu),
                waits=[("dve", v0), ("pe", pe_war)], inc="act")
            return av

        extra_gate = ([ccstw, ccst2w] if g > 0 else [])
        cxv = conv(xssm, 0, cwx, 0, xc, False, 0)
        cb0 = conv(Braw, 0, cwb, 0, Bc, True, 0)
        cb1 = conv(Braw, 1, cwb, 1, Bc, True, T)
        cc0 = conv(Craw, 0, cwc, 0, Cc, True, 0)
        cc1 = conv(Craw, 1, cwc, 1, Cc, True, T)
        dxv = None
        for tc4 in range(4):
            smm = p.op("tensor", lambda e, c=tc4, g=g: e.matmul(
                out=pb[6][:, :], lhsT=seldt[:, g * 128:(g + 1) * 128],
                rhs=dtb_[:, c * 512:(c + 1) * 512], start=True, stop=True),
                waits=[("dve", p.count("dve")), ("act", cxv)] + cx.consts_w, inc="pe")
            dxv = p.op("vector", lambda e, c=tc4: e.tensor_tensor(
                out=dtx[:, c * 512:(c + 1) * 512], in0=xc[:, c * 512:(c + 1) * 512],
                in1=pb[6][:, :], op=ALU.mult), waits=[("pe", smm), ("act", cxv)], inc="dve")

        ccstw = p.dma("gpsimd", lambda e, g=g: e.dma_start(
            out=Cc_d[g * 256:g * 256 + 128, :], in_=Cc[:, 0:T]), "l_cc0",
            waits=[("act", cc0)])
        ccst2w = p.dma("gpsimd", lambda e, g=g: e.dma_start(
            out=Cc_d[g * 256 + 128:g * 256 + 256, :], in_=Cc[:, T:2 * T]), "l_cc1",
            waits=[("act", cc1)])

        bc_act = {0: cb0, 1: cb1}
        cc_act = {0: cc0, 1: cc1}
        for hp in range(2):
            h = 2 * g + hp
            for c in range(NCH):
                col = c * 32 + h
                scol = sTl[:, col:col + 1]
                st1 = p.op("tensor", lambda e, col=col: e.transpose(
                    out=pb[6][0:1, 0:128], in_=sTl[:, col:col + 1], identity=idf[:, :]),
                    waits=[("dve", p.count("dve"))], inc="pe")
                sv1 = p.op("vector", lambda e: e.tensor_copy(out=srow[:, :],
                           in_=pb[6][0:1, 0:128]), waits=[("pe", st1)], inc="dve")
                db = p.op("tensor", lambda e: e.matmul(out=pb[6][:, 0:128], lhsT=ones1x[:, :],
                          rhs=srow[:, :], start=True, stop=True), waits=[("dve", sv1)], inc="pe")
                d1 = p.op("vector", lambda e, col=col: e.tensor_scalar(
                    out=Dm[:, :], in0=pb[6][:, 0:128], scalar1=sTl[:, col:col + 1],
                    scalar2=0.0, op0=ALU.subtract, op1=ALU.min),
                    waits=[("pe", db)], inc="dve")
                d2 = p.op("vector", lambda e: e.tensor_tensor(out=Dm[:, :], in0=Dm[:, :],
                          in1=scanmask[:, :], op=ALU.add), waits=[("dve", d1)], inc="dve")
                d3 = p.op("scalar", lambda e: e.activation(out=eD[:, :], in_=Dm[:, :],
                          func=AF.Exp), waits=[("dve", d2)], inc="act")
                gt = p.op("tensor", lambda e, hp=hp, c=c: e.matmul(
                    out=pb[7][:, 0:128], lhsT=Bc[:, hp * T + c * QC:hp * T + (c + 1) * QC],
                    rhs=Cc[:, hp * T + c * QC:hp * T + (c + 1) * QC],
                    start=True, stop=True),
                    waits=[("act", bc_act[hp]), ("dve", p.count("dve"))], inc="pe")
                pt = p.op("vector", lambda e: e.tensor_tensor(out=PT[:, :], in0=eD[:, :],
                          in1=pb[7][:, 0:128], op=ALU.mult),
                          waits=[("act", d3), ("pe", gt)], inc="dve")
                dt1 = p.op("tensor", lambda e, hp=hp, c=c: e.transpose(
                    out=pb[6][:, 0:32].bitcast(BF16), in_=dtx[hp * 64:(hp + 1) * 64,
                    c * QC:(c + 1) * QC],
                    identity=idb[hp * 64:(hp + 1) * 64, hp * 64:(hp + 1) * 64]),
                    waits=[("dve", dxv), ("dve", p.count("dve"))], inc="pe")
                dv1 = p.op("vector", lambda e: e.tensor_copy(out=dtxT[:, :],
                           in_=pb[6][:, 0:32].bitcast(BF16)), waits=[("pe", dt1)], inc="dve")
                bt1 = p.op("tensor", lambda e, hp=hp, c=c: e.transpose(
                    out=pb[6][:, 0:64].bitcast(BF16), in_=Bc[:, hp * T + c * QC:hp * T + (c + 1) * QC],
                    identity=idb[:, :]), waits=[("dve", dv1)], inc="pe")
                bv1 = p.op("vector", lambda e, col=col: e.tensor_scalar(
                    out=BcTs[:, :], in0=pb[6][:, 0:64].bitcast(BF16),
                    scalar1=wsT[:, col:col + 1], scalar2=None, op0=ALU.mult),
                    waits=[("pe", bt1)], inc="dve")
                sm = p.op("tensor", lambda e: e.matmul(out=pb[4][:, 0:64], lhsT=BcTs[:, :],
                          rhs=dtxT[:, :], start=True, stop=True),
                          waits=[("dve", bv1)], inc="pe")
                ym1 = p.op("tensor", lambda e: e.matmul(out=pb[5][:, 0:64], lhsT=PT[:, :],
                           rhs=dtxT[:, :], start=True, stop=True),
                           waits=[("dve", pt)], inc="pe")
                if c > 0:
                    ym2 = p.op("tensor", lambda e, hp=hp, c=c: e.matmul(
                        out=pb[7][:, 0:64], lhsT=Cc[:, hp * T + c * QC:hp * T + (c + 1) * QC],
                        rhs=hTb[:, hp * 64:(hp + 1) * 64], start=True, stop=True),
                        waits=[("dve", p.count("dve")), ("act", cc_act[hp])], inc="pe")
                    yv1 = p.op("vector", lambda e, col=col: e.tensor_scalar(
                        out=ytmp[:, :], in0=pb[7][:, 0:64], scalar1=eSl[:, col:col + 1],
                        scalar2=None, op0=ALU.mult), waits=[("pe", ym2)], inc="dve")
                    yv2 = p.op("vector", lambda e, c=c, hp=hp: e.tensor_tensor(
                        out=yg[:, c * 128 + hp * 64:c * 128 + (hp + 1) * 64],
                        in0=pb[5][:, 0:64], in1=ytmp[:, :], op=ALU.add),
                        waits=[("dve", yv1), ("pe", ym1)], inc="dve")
                else:
                    yv2 = p.op("vector", lambda e, c=c, hp=hp: e.tensor_copy(
                        out=yg[:, c * 128 + hp * 64:c * 128 + (hp + 1) * 64],
                        in_=pb[5][:, 0:64]),
                        waits=[("pe", ym1)] + ([ystw] if (hp == 0 and ystw) else []), inc="dve")
                par = c % 2
                if c > 0:
                    hv = p.op("vector", lambda e, hp=hp, col=col, par=par: e.scalar_tensor_tensor(
                        out=hT[:, (hp * 2 + par) * 64:(hp * 2 + par + 1) * 64],
                        in0=hT[:, (hp * 2 + 1 - par) * 64:(hp * 2 + 1 - par + 1) * 64],
                        scalar=erep[:, col:col + 1], in1=pb[4][:, 0:64],
                        op0=ALU.mult, op1=ALU.add), waits=[("pe", sm)], inc="dve")
                else:
                    hv = p.op("vector", lambda e, hp=hp, par=par: e.tensor_copy(
                        out=hT[:, (hp * 2 + par) * 64:(hp * 2 + par + 1) * 64],
                        in_=pb[4][:, 0:64]),
                        waits=[("pe", sm)] + ([soutw[hp]] if soutw[hp] else []), inc="dve")
                if c < NCH - 1:
                    p.op("vector", lambda e, hp=hp, par=par: e.tensor_copy(
                        out=hTb[:, hp * 64:(hp + 1) * 64],
                        in_=hT[:, (hp * 2 + par) * 64:(hp * 2 + par + 1) * 64]),
                        waits=[("dve", hv)], inc="dve")
                else:
                    soutw[hp] = p.dma("gpsimd", lambda e, h=h, hp=hp, par=par: e.dma_start(
                        out=S_out[h * 128:(h + 1) * 128, :],
                        in_=hT[:, (hp * 2 + par) * 64:(hp * 2 + par + 1) * 64]),
                        "l_sout%d" % hp, waits=[("dve", hv)])
        ystw = p.dma("gpsimd", lambda e, g=g: e.dma_start(
            out=y_raw.rearrange("(a p) hp -> p a hp", p=128)[:, :, g * 128:(g + 1) * 128],
            in_=yg[:, :].rearrange("p (a n) -> p a n", n=128)), "l_yst",
            waits=[("dve", p.count("dve"))])
    return [ystw]





def phase_l0b(cx, S_all, dect_all, y_raw, sz_tm, Cc_d, expSgT_d, x_fm, out_f32, out_bf):
    nc, p = cx.nc, cx.p
    TC = 512

    wc = cx.load_const("c_wcoef")
    jm = cx.load_const("c_jmask")
    crow = cx.sb("b_crow", [1, 128], F32)
    crep = cx.sb("b_crep", [128, 128], F32)
    coefsT = cx.sb("b_coefsT", [4, 32], F32)
    dect = cx.sb("b_dect", [4, 32], F32)
    sjt = cx.sb("b_sjt", [128, 2 * 64], F32)
    hacc = cx.sb("b_hacc", [128, 64], F32)
    hin = cx.sb("b_hin", [128, 32 * 64], BF16)
    eSg = cx.sb("b_eSg", [128, 512], F32)
    Cc_sb = cx.sb("b_Cc", [128, 2 * T], BF16)
    y_sb = cx.sb("b_y", [128, 16 * 128], BF16)
    sz_sb = cx.sb("b_sz", [128, 16 * 128], BF16)
    ybat = cx.sb("b_ybat", [128, 16 * 64], F32)
    sqb = cx.sb("b_sqb", [128, 16 * 64], F32)
    rsb = cx.sb("b_rsb", [128, 16], F32)
    ygc = cx.sb("b_ygc", [128, 64], BF16)
    ygfm = cx.sb("b_ygfm", [128, 16 * T], BF16)
    stg = cx.sb("b_stg", [128, 16 * TC], F32)
    wwo = cx.sb("b_wwo", [128, 2 * 2048], BF16)
    xres = cx.sb("b_xres", [128, 2 * TC], F32)
    sqt = cx.sb("b_sqt", [128, TC], F32)
    mrow = cx.sb("b_mrow", [1, TC], F32)
    vrow = cx.sb("b_vrow", [1, TC], F32)
    oout = cx.sb("b_oout", [128, TC], F32)
    obf = cx.sb("b_obf", [128, TC], BF16)
    idb = cx.sbuf["c_idbf"]
    ones1x = cx.sbuf["c_ones_1x128_f"]
    cb_dram = cx.dram_tmp("b_cb_d", [4, 32], F32)

    pb = [cx.pbank(i) for i in range(8)]
    bank = dict(pss=pb[3][0:1, :], psq=pb[4][0:1, :], psb1=pb[5], psb2=pb[6],
                sqt=sqt, mrow=mrow, vrow=vrow, oout=oout, obf=obf)

    ldd = p.dma("sync", lambda e: e.dma_start(
        out=dect[:, :], in_=dect_all.rearrange("(j h) o -> j (h o)", j=4)), "b_dect",
        waits=[("pe", p.count("pe"))])
    cmm = p.op("tensor", lambda e: e.matmul(out=pb[0][0:4, 0:32], lhsT=wc[:, :],
               rhs=dect[:, :], start=True, stop=True),
               waits=[ldd] + cx.consts_w + [("dve", p.count("dve"))], inc="pe")
    ce = p.op("scalar", lambda e: e.activation(out=coefsT[:, :], in_=pb[0][0:4, 0:32],
              func=AF.Exp), waits=[("pe", cmm)], inc="act")
    cm = p.op("vector", lambda e: e.tensor_scalar(out=coefsT[:, :], in0=coefsT[:, :],
              scalar1=jm[:, 0:1], scalar2=None, op0=ALU.mult), waits=[("act", ce)], inc="dve")
    cw_ = p.dma("gpsimd", lambda e: e.dma_start(out=cb_dram, in_=coefsT[:, :]), "b_cb",
                waits=[("dve", cm)])
    cr = p.dma("sync", lambda e: e.dma_start(
        out=crow[:, :], in_=cb_dram.rearrange("a b -> (a b)").unsqueeze(0)), "b_cr",
        waits=[cw_])
    cbm = p.op("tensor", lambda e: e.matmul(out=pb[0][:, 0:128], lhsT=ones1x[:, :],
               rhs=crow[:, :], start=True, stop=True),
               waits=[cr, ("act", p.count("act"))], inc="pe")
    crv = p.op("vector", lambda e: e.tensor_copy(out=crep[:, :], in_=pb[0][:, 0:128]),
               waits=[("pe", cbm)], inc="dve")
    hv = crv
    ldsw = [None, None]
    for h in range(32):
        for j in range(4):
            par = j % 2
            ldsw[par] = p.dma("sync", lambda e, h=h, j=j, par=par: e.dma_start(
                out=sjt[:, par * 64:(par + 1) * 64],
                in_=S_all[j * 4096 + h * 128:j * 4096 + (h + 1) * 128, :]),
                "b_sj%d" % par, waits=[("dve", p.count("dve"))])
            if j == 0:
                hv = p.op("vector", lambda e, h=h, par=par: e.tensor_scalar(
                    out=hacc[:, :], in0=sjt[:, par * 64:(par + 1) * 64],
                    scalar1=crep[:, h:h + 1], scalar2=None, op0=ALU.mult),
                    waits=[ldsw[par], ("dve", crv)], inc="dve")
            else:
                hv = p.op("vector", lambda e, h=h, j=j, par=par: e.scalar_tensor_tensor(
                    out=hacc[:, :], in0=sjt[:, par * 64:(par + 1) * 64],
                    scalar=crep[:, j * 32 + h:j * 32 + h + 1], in1=hacc[:, :],
                    op0=ALU.mult, op1=ALU.add), waits=[ldsw[par]], inc="dve")
        hv = p.op("vector", lambda e, h=h: e.tensor_copy(out=hin[:, h * 64:(h + 1) * 64],
                  in_=hacc[:, :]), waits=[("dve", hv)], inc="dve")
    hin_dve = p.count("dve")
    lde = p.dma("sync", lambda e: e.dma_start(out=eSg[:, :], in_=expSgT_d), "b_esg")

    ldy = ldz = ldc = None
    for g in range(16):
        pe_war = p.count("pe")
        dve_war = p.count("dve")
        ldc = p.dma("sync", lambda e, g=g: e.dma_start(
            out=Cc_sb[:, :].rearrange("p (a t) -> p a t", t=T),
            in_=Cc_d.rearrange("(a p) t -> p a t", p=128)[:, 2 * g:2 * g + 2, :]),
            "b_cc", waits=[("pe", pe_war)])
        ldy = p.dma("sync", lambda e, g=g: e.dma_start(
            out=y_sb[:, :].rearrange("p (a n) -> p a n", n=128),
            in_=y_raw.rearrange("(a p) hp -> p a hp", p=128)[:, :, g * 128:(g + 1) * 128]),
            "b_yld", waits=[("dve", dve_war)])
        ldz = p.dma("sync", lambda e, g=g: e.dma_start(
            out=sz_sb[:, :].rearrange("p (a n) -> p a n", n=128),
            in_=sz_tm.rearrange("(a p) hp -> p a hp", p=128)[:, :, g * 128:(g + 1) * 128]),
            "b_zld", waits=[("dve", dve_war)])
        for hp in range(2):
            h = 2 * g + hp
            cmm_prev = [0, 0]
            yb_dve = None
            for c in range(NCH):
                col = c * 32 + h
                par = c % 2
                cmm2 = p.op("tensor", lambda e, hp=hp, c=c, h=h, par=par: e.matmul(
                    out=pb[par][:, 0:64], lhsT=Cc_sb[:, hp * T + c * QC:hp * T + (c + 1) * QC],
                    rhs=hin[:, h * 64:(h + 1) * 64], start=True, stop=True),
                    waits=[ldc, ("dve", max(hin_dve, cmm_prev[par]))], inc="pe")
                yb_dve = p.op("vector", lambda e, col=col, c=c, hp=hp, par=par: e.scalar_tensor_tensor(
                    out=ybat[:, c * 64:(c + 1) * 64], in0=pb[par][:, 0:64],
                    scalar=eSg[:, col:col + 1],
                    in1=y_sb[:, c * 128 + hp * 64:c * 128 + (hp + 1) * 64],
                    op0=ALU.mult, op1=ALU.add),
                    waits=[("pe", cmm2), ldy, lde], inc="dve")
                cmm_prev[par] = yb_dve
            s1 = p.op("scalar", lambda e: e.activation(out=sqb[:, :], in_=ybat[:, :],
                      func=AF.Square), waits=[("dve", yb_dve)], inc="act")
            s2 = p.op("vector", lambda e: e.reduce_sum(
                out=rsb[:, :], in_=sqb[:, :].rearrange("p (c v) -> p c v", v=64),
                axis=mybir.AxisListType.X), waits=[("act", s1)], inc="dve")
            s3 = p.op("vector", lambda e: e.tensor_scalar(out=rsb[:, :], in0=rsb[:, :],
                      scalar1=0.015625, scalar2=1e-6, op0=ALU.mult, op1=ALU.add),
                      waits=[("dve", s2)], inc="dve")
            s4 = p.op("scalar", lambda e: e.activation(out=rsb[:, :], in_=rsb[:, :],
                      func=AF.Ln), waits=[("dve", s3)], inc="act")
            s5 = p.op("scalar", lambda e: e.activation(out=rsb[:, :], in_=rsb[:, :],
                      func=AF.Exp, scale=-0.5), waits=[("act", s4)], inc="act")
            tp_prev = 0
            for c in range(NCH):
                g1 = p.op("vector", lambda e, c=c: e.tensor_scalar(
                    out=ybat[:, c * 64:(c + 1) * 64], in0=ybat[:, c * 64:(c + 1) * 64],
                    scalar1=rsb[:, c:c + 1], scalar2=None, op0=ALU.mult),
                    waits=([("act", s5)] if c == 0 else []), inc="dve")
                g2 = p.op("vector", lambda e, c=c, hp=hp: e.tensor_tensor(
                    out=ygc[:, :], in0=ybat[:, c * 64:(c + 1) * 64],
                    in1=sz_sb[:, c * 128 + hp * 64:c * 128 + (hp + 1) * 64], op=ALU.mult),
                    waits=[("dve", g1), ldz, ("pe", tp_prev)], inc="dve")
                tp_prev = p.op("tensor", lambda e: e.transpose(
                    out=pb[2][0:64, 0:64].bitcast(BF16), in_=ygc[:, :], identity=idb[:, :]),
                    waits=[("dve", g2)] + cx.consts_w, inc="pe")
                p.op("vector", lambda e, g=g, hp=hp, c=c: e.tensor_copy(
                    out=ygfm[hp * 64:(hp + 1) * 64, g * T + c * QC:g * T + (c + 1) * QC],
                    in_=pb[2][0:64, 0:64].bitcast(BF16)), waits=[("pe", tp_prev)], inc="dve")
    yg_dve = p.count("dve")

    woutT = cx.dram["w_woutT"].rearrange("(a p) n -> p a n", p=128)
    wlast = None
    wuse = [0, 0]
    evac_dve = [yg_dve, yg_dve]
    for tc in range(4):
        wld = [None, None]
        for dt in range(16):
            par = dt % 2
            wld[par] = p.dma("gpsimd", lambda e, d=dt, par=par: e.dma_start(
                out=wwo[:, par * 2048:(par + 1) * 2048].rearrange("p (a n) -> p a n", n=128),
                in_=woutT[:, :, d * 128:(d + 1) * 128]), "b_wo%d" % par,
                waits=[("pe", wuse[par])])
            ldr = p.dma("sync", lambda e, d=dt, c=tc, par=par: e.dma_start(
                out=xres[:, par * TC:(par + 1) * TC],
                in_=x_fm[d * 128:(d + 1) * 128, 3 + c * TC:3 + (c + 1) * TC]),
                "b_xr%d" % par, waits=[("dve", evac_dve[par])])
            ps = pb[par]
            ype = None
            for kt in range(16):
                ype = p.op("tensor", lambda e, k=kt, c=tc, ps=ps, par=par: e.matmul(
                    out=ps[:, :],
                    lhsT=wwo[:, par * 2048 + k * 128:par * 2048 + (k + 1) * 128],
                    rhs=ygfm[:, k * T + c * TC:k * T + (c + 1) * TC],
                    start=(k == 0), stop=(k == 15)),
                    waits=([wld[par], ("dve", max(yg_dve, evac_dve[par]))] if kt == 0 else []),
                    inc=("pe" if kt == 15 else None))
            wuse[par] = ype
            evac_dve[par] = p.op("vector", lambda e, d=dt, ps=ps, par=par: e.tensor_tensor(
                out=stg[:, d * TC:(d + 1) * TC], in0=ps[:, :],
                in1=xres[:, par * TC:(par + 1) * TC], op=ALU.add),
                waits=[("pe", ype), ldr], inc="dve")
        wlast = emit_ln_store(cx, stg, TC, out_f32, out_bf, tc * TC, bank, "l0")
    return wlast



def run_phase_l1ffn(stream_f32_np, stream_bf_np, shared):
    nc = bass.Bass("TRN2", target_bir_lowering=False)
    names = ("c_blk2", "c_ones_128x1_f", "c_ones_1x128_f", "c_scalarpool",
             "w_upqT", "w_upsT", "w_bup", "w_upla", "w_uplb",
             "w_dnqT", "w_dnsT", "w_bdn", "w_dnla", "w_dnlb")
    with contextlib.ExitStack() as es:
        cx = Ctx(nc, es)
        for n in names:
            cx.dram_in(n, *SHAPES[n])
        sf = cx.dram_in("stream_f32", [D, T], F32)
        sb_ = cx.dram_in("stream_bf", [D, T], BF16)
        out = cx.dram_out("out_f32", [D, T], F32)
        for n in ("c_blk2", "c_ones_128x1_f", "c_ones_1x128_f", "c_scalarpool"):
            cx.load_const(n)
        cx.register_scalar_pool()
        phase_l1ffn(cx, sf, sb_, out)
        cx.p.barrier_wait("gpsimd")
        blk = es.enter_context(nc.Block())
        cx.p.emit(blk)
    in_maps = []
    for c in range(NCORE):
        m = {n: shared[n] for n in names}
        m["stream_f32"] = stream_f32_np[c]
        m["stream_bf"] = stream_bf_np[c]
        in_maps.append(m)
    res = run_bass_kernel_spmd(nc, in_maps, core_ids=list(range(NCORE)))
    return [r["out_f32"] for r in res.results]


MOE_NAMES = ("c_idf32", "c_ones_128x1_f", "c_ones_1x128_f", "c_scalarpool", "w_routerT",
             "w_moe1", "w_moe3", "w_moe2")


def run_phase_moe(stream_f32_np, stream_bf_np, shared):
    nc = bass.Bass("TRN2", target_bir_lowering=False)
    with contextlib.ExitStack() as es:
        cx = Ctx(nc, es)
        for n in MOE_NAMES:
            cx.dram_in(n, *SHAPES[n])
        sf = cx.dram_in("stream_f32", [D, T], F32)
        sb_ = cx.dram_in("stream_bf", [D, T], BF16)
        out = cx.dram_out("out_f32", [D, T], F32)
        outb = cx.dram_out("out_bf", [D, T], BF16)
        for n in ("c_idf32", "c_ones_128x1_f", "c_ones_1x128_f", "c_scalarpool"):
            cx.load_const(n)
        cx.register_scalar_pool()
        phase_moe(cx, sf, sb_, out, outb)
        cx.p.barrier_wait("gpsimd")
        blk = es.enter_context(nc.Block())
        cx.p.emit(blk)
    in_maps = []
    for c in range(NCORE):
        m = {n: shared[n] for n in MOE_NAMES}
        m["stream_f32"] = stream_f32_np[c]
        m["stream_bf"] = stream_bf_np[c]
        in_maps.append(m)
    res = run_bass_kernel_spmd(nc, in_maps, core_ids=list(range(NCORE)))
    return [r["out_f32"] for r in res.results]




QKV_NAMES = ("c_ones_128x1_f", "c_ones_1x128_f", "c_scalarpool",
             "w_qaT", "w_qab", "w_qnorm", "w_kvaT", "w_kvab", "w_kvnorm",
             "c_cosq", "c_sinq")
ATTN_NAMES = ("c_idbf", "c_ones_128x1_f", "c_ones_128x1_b", "c_ones_1x128_f",
              "c_scalarpool", "w_qbT", "w_qbb", "w_kvbT", "w_kvbb",
              "c_cosq", "c_sinq", "c_onesrow_b")
WO_NAMES = ("c_ones_128x1_f", "c_ones_1x128_f", "c_scalarpool", "w_woT", "w_wob")


def _run_simple_phase(build_fn, names, io_spec, in_maps_extra, shared, percore=None):
    nc = bass.Bass("TRN2", target_bir_lowering=False)
    with contextlib.ExitStack() as es:
        cx = Ctx(nc, es)
        for n in names:
            cx.dram_in(n, *SHAPES[n])
        for (n, shape, dtype, kind) in io_spec:
            if kind == "in":
                cx.dram_in(n, shape, dtype)
            else:
                cx.dram_out(n, shape, dtype)
        for n in names:
            if n.startswith("c_") and n not in ("c_cosq", "c_sinq", "c_onesrow_b"):
                cx.load_const(n)
        cx.register_scalar_pool()
        build_fn(cx)
        cx.p.barrier_wait("gpsimd")
        cx.p.barrier_wait("sync")
        blk = es.enter_context(nc.Block())
        cx.p.emit(blk)
    in_maps = []
    for c in range(NCORE):
        m = {}
        for n in names:
            m[n] = percore[c][n] if (percore is not None and n in percore[c]) else shared[n]
        m.update(in_maps_extra[c])
        in_maps.append(m)
    res = run_bass_kernel_spmd(nc, in_maps, core_ids=list(range(NCORE)))
    return res.results


def run_phase_l1qkv(stream_bf_np, shared, percore):
    def build(cx):
        phase_l1qkv(cx, cx.dram["stream_bf"], cx.dram["qn_d"], cx.dram["kvp_d"])
    io = [("stream_bf", [D, T], BF16, "in"),
          ("qn_d", [QLR, T], BF16, "out"), ("kvp_d", [576, T], BF16, "out")]
    extra = [{"stream_bf": stream_bf_np[c]} for c in range(NCORE)]
    res = _run_simple_phase(build, QKV_NAMES, io, extra, shared, percore)
    return [r["qn_d"] for r in res], [r["kvp_d"] for r in res]


def run_phase_l1attn(qn_np, kvall_np, shared, percore):
    def build(cx):
        phase_l1attn(cx, cx.dram["qn_d"], cx.dram["kvall_d"], cx.dram["attn_d"])
    io = [("qn_d", [QLR, T], BF16, "in"), ("kvall_d", [4 * 576, T], BF16, "in"),
          ("attn_d", [D, T], BF16, "out")]
    extra = [{"qn_d": qn_np[c], "kvall_d": kvall_np[c]} for c in range(NCORE)]
    res = _run_simple_phase(build, ATTN_NAMES, io, extra, shared, percore)
    return [r["attn_d"] for r in res]


def run_phase_l1wo(attn_np, stream_f32_np, shared):
    def build(cx):
        phase_l1wo(cx, cx.dram["attn_d"], cx.dram["stream_f32"],
                   cx.dram["out_f32"], cx.dram["out_bf"])
    io = [("attn_d", [D, T], BF16, "in"), ("stream_f32", [D, T], F32, "in"),
          ("out_f32", [D, T], F32, "out"), ("out_bf", [D, T], BF16, "out")]
    extra = [{"attn_d": attn_np[c], "stream_f32": stream_f32_np[c]} for c in range(NCORE)]
    res = _run_simple_phase(build, WO_NAMES, io, extra, shared)
    return [r["out_f32"] for r in res]




L0A_NAMES = ("c_idf32", "c_idbf", "c_ones_1x128_f", "c_ones_128x1_f", "c_scalarpool",
             "c_seldt", "c_scanmask", "w_winT", "w_alog", "w_dtb",
             "w_cwx", "w_cwb", "w_cwc")
L0B_NAMES = ("c_idbf", "c_ones_1x128_f", "c_ones_128x1_f", "c_scalarpool",
             "c_wcoef", "c_jmask", "w_woutT")


def _launch(build_fn, names, io_spec, extra, shared, percore):
    nc = bass.Bass("TRN2", target_bir_lowering=False)
    with contextlib.ExitStack() as es:
        cx = Ctx(nc, es)
        for n in names:
            cx.dram_in(n, *SHAPES[n])
        for (n, shape, dtype, kind) in io_spec:
            if kind == "in":
                cx.dram_in(n, shape, dtype)
            else:
                cx.dram_out(n, shape, dtype)
        for n in names:
            if n.startswith("c_") and n not in ("c_cosq", "c_sinq", "c_onesrow_b",
                                                "c_wcoef", "c_jmask"):
                cx.load_const(n)
        cx.register_scalar_pool()
        build_fn(cx)
        cx.p.barrier_wait("gpsimd")
        cx.p.barrier_wait("sync")
        blk = es.enter_context(nc.Block())
        cx.p.emit(blk)
    in_maps = []
    for c in range(NCORE):
        m = {}
        for n in names:
            m[n] = percore[c][n] if (percore is not None and n in percore[c]) else shared[n]
        m.update(extra[c])
        in_maps.append(m)
    res = run_bass_kernel_spmd(nc, in_maps, core_ids=list(range(NCORE)))
    return res.results




FUSED = True
PERSISTENT_CONSTS = ("c_idf32", "c_idbf", "c_ones_1x128_f", "c_ones_1x128_b",
                     "c_ones_128x1_b", "c_ones_128x1_f", "c_blk2", "c_scalarpool")
ARENA_BYTES = 210600


def kernel_fused(inputs):
    shared, percore = host_prep(inputs)
    nc = bass.Bass("TRN2", target_bir_lowering=False)
    allnames = set(PERSISTENT_CONSTS)
    for ns in (L0A_NAMES, L0B_NAMES, MOE_NAMES, QKV_NAMES, ATTN_NAMES, WO_NAMES):
        allnames |= set(ns)
    allnames |= {"w_upqT", "w_upsT", "w_bup", "w_upla", "w_uplb",
                 "w_dnqT", "w_dnsT", "w_bdn", "w_dnla", "w_dnlb"}
    percore_names = ("x_fm", "c_cosq", "c_sinq", "c_wcoef", "c_jmask")
    allnames |= set(percore_names)
    with contextlib.ExitStack() as es:
        cx = Ctx(nc, es)
        p = cx.p
        for n in sorted(allnames):
            cx.dram_in(n, *SHAPES[n])
        out = cx.dram_out("out_f32", [D, T], F32)
        y_raw = cx.dram_tmp("y_raw", [T, D], BF16)
        sz_tm = cx.dram_tmp("sz_tm", [T, D], BF16)
        Cc_d = cx.dram_tmp("Cc_d", [4096, T], BF16)
        S_out = cx.dram_tmp("S_out", [4096, 64], F32)
        dectot = cx.dram_tmp("dectot", [32, 1], F32)
        expSgT = cx.dram_tmp("expSgT", [128, 512], F32)
        S_all = cx.dram_tmp("S_all", [4 * 4096, 64], F32)
        dect_all = cx.dram_tmp("dect_all", [128, 1], F32)
        s1f = cx.dram_tmp("s1f", [D, T], F32)
        s1b = cx.dram_tmp("s1b", [D, T], BF16)
        s2f = cx.dram_tmp("s2f", [D, T], F32)
        s2b = cx.dram_tmp("s2b", [D, T], BF16)
        qn_d = cx.dram_tmp("qn_d", [QLR, T], BF16)
        kvp_d = cx.dram_tmp("kvp_d", [576, T], BF16)
        kvall_d = cx.dram_tmp("kvall_d", [4 * 576, T], BF16)
        attn_d = cx.dram_tmp("attn_d", [D, T], BF16)
        s3f = cx.dram_tmp("s3f", [D, T], F32)
        s3b = cx.dram_tmp("s3b", [D, T], BF16)

        for n in PERSISTENT_CONSTS:
            cx.load_const(n)
        cx.register_scalar_pool()
        cx.begin_arena(ARENA_BYTES)

        phase_l0a(cx, cx.dram["x_fm"], y_raw, sz_tm, Cc_d, S_out, dectot, expSgT)
        p.phase_barrier()
        cx.arena_reset()
        p.op("gpsimd", lambda e: e.collective_compute(
            "AllGather", ALU.bypass, replica_groups=[[0, 1, 2, 3], [4, 5, 6, 7]],
            ins=[S_out.opt()], outs=[S_all.opt()]), inc="cc")
        p.op("gpsimd", lambda e: e.collective_compute(
            "AllGather", ALU.bypass, replica_groups=[[0, 1, 2, 3], [4, 5, 6, 7]],
            ins=[dectot.opt()], outs=[dect_all.opt()]), inc="cc")
        p.phase_barrier()

        wup_bf = cx.dram_tmp("wup_bf", [D, HF], BF16)
        wdn_bf = cx.dram_tmp("wdn_bf", [HF, D], BF16)
        emit_dequant(cx, wup_bf, wdn_bf)
        phase_l0b(cx, S_all, dect_all, y_raw, sz_tm, Cc_d, expSgT,
                  cx.dram["x_fm"], s1f, s1b)
        p.phase_barrier()
        cx.arena_reset()

        phase_moe(cx, s1f, s1b, s2f, s2b)
        p.phase_barrier()
        cx.arena_reset()

        phase_l1qkv(cx, s2b, qn_d, kvp_d)
        p.phase_barrier()
        cx.arena_reset()
        p.op("gpsimd", lambda e: e.collective_compute(
            "AllGather", ALU.bypass, replica_groups=[[0, 1, 2, 3], [4, 5, 6, 7]],
            ins=[kvp_d.opt()], outs=[kvall_d.opt()]), inc="cc")
        p.phase_barrier()

        phase_l1attn(cx, qn_d, kvall_d, attn_d)
        p.phase_barrier()
        cx.arena_reset()

        phase_l1wo(cx, attn_d, s2f, s3f, s3b)
        p.phase_barrier()
        cx.arena_reset()

        phase_l1ffn(cx, s3f, s3b, out, pre_dequant=True)
        p.barrier_wait("gpsimd")
        p.barrier_wait("sync")
        nsem = len(p.sem_handles)
        blk = es.enter_context(nc.Block())
        p.emit(blk)
    in_maps = []
    for c in range(NCORE):
        m = {}
        for n in sorted(allnames):
            m[n] = percore[c][n] if n in percore[c] else shared[n]
        in_maps.append(m)
    res = run_bass_kernel_spmd(nc, in_maps, core_ids=list(range(NCORE)))
    outfm = [r["out_f32"] for r in res.results]
    outa = np.empty((B, S, D), np.float32)
    for c in range(NCORE):
        b, q = c // 4, c % 4
        outa[b, q * T:(q + 1) * T, :] = outfm[c].T
    return outa


_DBG = {}


def kernel(**inputs):
    if FUSED:
        return kernel_fused(inputs)
    shared, percore = host_prep(inputs)

    def bA(cx):
        phase_l0a(cx, cx.dram["x_fm"], cx.dram["y_raw"], cx.dram["sz_tm"],
                  cx.dram["Cc_d"], cx.dram["S_out"], cx.dram["dectot"], cx.dram["expSgT"])
    ioA = [("x_fm", [D, T + 3], F32, "in"),
           ("y_raw", [T, D], BF16, "out"), ("sz_tm", [T, D], BF16, "out"),
           ("Cc_d", [4096, T], BF16, "out"), ("S_out", [4096, 64], F32, "out"),
           ("dectot", [32, 1], F32, "out"), ("expSgT", [128, 512], F32, "out")]
    extA = [{"x_fm": percore[c]["x_fm"]} for c in range(NCORE)]
    rA = _launch(bA, L0A_NAMES, ioA, extA, shared, percore)
    _DBG["rA"] = rA

    S_alls, de_alls = [], []
    for c in range(NCORE):
        grp = range(0, 4) if c < 4 else range(4, 8)
        S_alls.append(np.ascontiguousarray(np.concatenate([rA[j]["S_out"] for j in grp], axis=0)))
        de_alls.append(np.ascontiguousarray(np.concatenate([rA[j]["dectot"] for j in grp], axis=0)))

    def bB(cx):
        phase_l0b(cx, cx.dram["S_all"], cx.dram["dect_all"], cx.dram["y_raw"],
                  cx.dram["sz_tm"], cx.dram["Cc_d"], cx.dram["expSgT"],
                  cx.dram["x_fm"], cx.dram["out_f32"], cx.dram["out_bf"])
    ioB = [("S_all", [4 * 4096, 64], F32, "in"), ("dect_all", [128, 1], F32, "in"),
           ("y_raw", [T, D], BF16, "in"), ("sz_tm", [T, D], BF16, "in"),
           ("Cc_d", [4096, T], BF16, "in"), ("expSgT", [128, 512], F32, "in"),
           ("x_fm", [D, T + 3], F32, "in"),
           ("out_f32", [D, T], F32, "out"), ("out_bf", [D, T], BF16, "out")]
    extB = [{"S_all": S_alls[c], "dect_all": de_alls[c], "y_raw": rA[c]["y_raw"],
             "sz_tm": rA[c]["sz_tm"], "Cc_d": rA[c]["Cc_d"], "expSgT": rA[c]["expSgT"],
             "x_fm": percore[c]["x_fm"]} for c in range(NCORE)]
    rB = _launch(bB, L0B_NAMES, ioB, extB, shared, percore)
    _DBG["rB"] = rB

    def bC(cx):
        phase_moe(cx, cx.dram["stream_f32"], cx.dram["stream_bf"],
                  cx.dram["out_f32"], cx.dram["out_bf"])
    ioC = [("stream_f32", [D, T], F32, "in"), ("stream_bf", [D, T], BF16, "in"),
           ("out_f32", [D, T], F32, "out"), ("out_bf", [D, T], BF16, "out")]
    extC = [{"stream_f32": rB[c]["out_f32"], "stream_bf": rB[c]["out_bf"]}
            for c in range(NCORE)]
    rC = _launch(bC, MOE_NAMES, ioC, extC, shared, percore)
    _DBG["rC"] = rC

    def bD(cx):
        phase_l1qkv(cx, cx.dram["stream_bf"], cx.dram["qn_d"], cx.dram["kvp_d"])
    ioD = [("stream_bf", [D, T], BF16, "in"),
           ("qn_d", [QLR, T], BF16, "out"), ("kvp_d", [576, T], BF16, "out")]
    extD = [{"stream_bf": rC[c]["out_bf"]} for c in range(NCORE)]
    rD = _launch(bD, QKV_NAMES, ioD, extD, shared, percore)
    _DBG["rD"] = rD

    kvalls = []
    for c in range(NCORE):
        grp = range(0, 4) if c < 4 else range(4, 8)
        kvalls.append(np.ascontiguousarray(np.concatenate([rD[j]["kvp_d"] for j in grp], axis=0)))

    def bE(cx):
        phase_l1attn(cx, cx.dram["qn_d"], cx.dram["kvall_d"], cx.dram["attn_d"])
    ioE = [("qn_d", [QLR, T], BF16, "in"), ("kvall_d", [4 * 576, T], BF16, "in"),
           ("attn_d", [D, T], BF16, "out")]
    extE = [{"qn_d": rD[c]["qn_d"], "kvall_d": kvalls[c]} for c in range(NCORE)]
    rE = _launch(bE, ATTN_NAMES, ioE, extE, shared, percore)
    _DBG["rE"] = rE

    def bF(cx):
        phase_l1wo(cx, cx.dram["attn_d"], cx.dram["stream_f32"],
                   cx.dram["out_f32"], cx.dram["out_bf"])
    ioF = [("attn_d", [D, T], BF16, "in"), ("stream_f32", [D, T], F32, "in"),
           ("out_f32", [D, T], F32, "out"), ("out_bf", [D, T], BF16, "out")]
    extF = [{"attn_d": rE[c]["attn_d"], "stream_f32": rC[c]["out_f32"]}
            for c in range(NCORE)]
    rF = _launch(bF, WO_NAMES, ioF, extF, shared, percore)
    _DBG["rF"] = rF

    def bG(cx):
        phase_l1ffn(cx, cx.dram["stream_f32"], cx.dram["stream_bf"], cx.dram["out_f32"])
    FFN_NAMES = ("c_blk2", "c_ones_128x1_f", "c_ones_1x128_f", "c_scalarpool",
                 "w_upqT", "w_upsT", "w_bup", "w_upla", "w_uplb",
                 "w_dnqT", "w_dnsT", "w_bdn", "w_dnla", "w_dnlb")
    ioG = [("stream_f32", [D, T], F32, "in"), ("stream_bf", [D, T], BF16, "in"),
           ("out_f32", [D, T], F32, "out")]
    extG = [{"stream_f32": rF[c]["out_f32"], "stream_bf": rF[c]["out_bf"]}
            for c in range(NCORE)]
    rG = _launch(bG, FFN_NAMES, ioG, extG, shared, percore)

    out = np.empty((B, S, D), np.float32)
    for c in range(NCORE):
        b, q = c // 4, c % 4
        out[b, q * T:(q + 1) * T, :] = rG[c]["out_f32"].T
    return out



# version 06 (final)
# speedup vs baseline: 1.0403x; 1.0403x over previous
import contextlib
import numpy as np
import ml_dtypes
import concourse.bass as bass
import concourse.mybir as mybir
from concourse.bass_utils import run_bass_kernel_spmd

F32 = mybir.dt.float32
BF16 = mybir.dt.bfloat16
I8 = mybir.dt.int8
AF = mybir.ActivationFunctionType
ALU = mybir.AluOpType
BF = ml_dtypes.bfloat16

D = 2048
T = 2048
B, S = 2, 8192
NCORE = 8
H, PP, NN, K4 = 32, 64, 128, 4
PROJ = 12320
QC = 128
NCH = T // QC
EXP, FF = 8, 512
NH, QLR, KVLR, DN, DR, DV = 16, 1536, 512, 128, 64, 128
DQK = DN + DR
HF, RLORA = 8192, 16
LNEPS = 1e-5

ENGS = ("sync", "gpsimd", "vector", "scalar", "tensor")

SCALAR_CONSTS = [1e-5, 1e-6, -0.5, 1.0, -1.0, -7.5, 0.07216878364870323, 0.5,
                 2.0, -40.0, 0.015625, 1.0 / 2048.0, 1.0 / 1536.0, 1.0 / 512.0,
                 1.0 / 64.0]


class Prog:

    def __init__(self, nc, es):
        self.nc = nc
        self.es = es
        self.steps = {e: [] for e in ENGS}
        self.sem_handles = {}
        self.sem_counts = {}

    def sem(self, name):
        if name not in self.sem_handles:
            self.sem_handles[name] = self.es.enter_context(self.nc.semaphore(name))
            self.sem_counts[name] = 0
        return name

    def count(self, name):
        self.sem(name)
        return self.sem_counts[name]

    def op(self, eng, fn, waits=(), inc=None, amt=1):
        w = [(s, v) for (s, v) in waits if v > 0]
        post = None
        if inc is not None:
            self.sem(inc)
            self.sem_counts[inc] += amt
            post = self.sem_counts[inc]
        self.steps[eng].append((fn, w, inc, amt if inc else 0))
        return post

    def dma(self, eng, fn, key, waits=()):
        s = "d_" + key
        v = self.op(eng, fn, waits=waits, inc=s, amt=16)
        return (s, v)

    def wait_only(self, eng, waits):
        self.steps[eng].append((None, [(s, v) for (s, v) in waits if v > 0], None, 0))

    def phase_barrier(self):
        for eng in ENGS:
            self.barrier_wait(eng)

    def barrier_wait(self, eng, sems=None):
        names = sems if sems is not None else list(self.sem_counts)
        self.steps[eng].append((None, [(s, self.sem_counts[s]) for s in names
                                       if self.sem_counts[s] > 0], None, 0))

    def emit(self, block):
        handles = self.sem_handles

        def mk(name):
            def run(e):
                for fn, waits, inc, amt in self.steps[name]:
                    for (s, v) in waits:
                        e.wait_ge(handles[s], v)
                    if fn is not None:
                        inst = fn(e)
                        if inc is not None:
                            inst.then_inc(handles[inc], amt)
            return run

        block.sync(mk("sync"))
        block.gpsimd(mk("gpsimd"))
        block.vector(mk("vector"))
        block.scalar(mk("scalar"))
        block.tensor(mk("tensor"))



def host_prep(inp):
    shared = {}
    shared["c_idf32"] = np.eye(128, dtype=np.float32)
    shared["c_idbf"] = np.eye(128, dtype=np.float32).astype(BF)
    shared["c_ones_1x128_f"] = np.ones((1, 128), np.float32)
    shared["c_ones_1x128_b"] = np.ones((1, 128), np.float32).astype(BF)
    shared["c_ones_128x1_b"] = np.ones((128, 1), np.float32).astype(BF)
    shared["c_ones_128x1_f"] = np.ones((128, 1), np.float32)
    jj, ii = np.meshgrid(np.arange(QC), np.arange(QC), indexing="ij")
    shared["c_scanmask"] = np.where(jj > ii, -40.0, 0.0).astype(np.float32)
    sel = np.zeros((32, 128 * 16), np.float32)
    for g in range(16):
        for p in range(128):
            sel[2 * g + p // 64, g * 128 + p] = 1.0
    shared["c_seldt"] = sel
    blk2 = np.zeros((2, 128), np.float32)
    blk2[0, 0:64] = 1.0
    blk2[1, 64:128] = 1.0
    shared["c_blk2"] = blk2
    shared["c_scalarpool"] = np.tile(np.array(SCALAR_CONSTS, np.float32), (128, 1))
    shared["c_onesrow_b"] = np.ones((1, S), np.float32).astype(BF)
    inv_freq = 1.0 / (10000.0 ** (np.arange(0, DR, 2, dtype=np.float64) / DR))
    t = np.arange(S, dtype=np.float64)
    emb = np.concatenate([np.outer(t, inv_freq)] * 2, axis=1)
    shared["c_cosT"] = np.ascontiguousarray(np.cos(emb).T.astype(np.float32))
    shared["c_sinT"] = np.ascontiguousarray(np.sin(emb).T.astype(np.float32))

    Win = np.asarray(inp["L0_attn_Win"])
    order = []
    for g in range(16):
        order += list(range(g * 128, g * 128 + 128))
        order += list(range(H * PP + g * 128, H * PP + g * 128 + 128))
        order += list(range(2 * H * PP + g * 256, 2 * H * PP + g * 256 + 256))
        order += list(range(2 * H * PP + H * NN + g * 256, 2 * H * PP + H * NN + g * 256 + 256))
    order += list(range(2 * H * PP + 2 * H * NN, PROJ))
    shared["w_winT"] = np.ascontiguousarray(Win[order, :].T)
    shared["w_woutT"] = np.ascontiguousarray(np.asarray(inp["L0_attn_Wout"]).T)
    shared["w_alog"] = np.asarray(inp["L0_attn_A_log"]).reshape(32, 1).astype(np.float32)
    shared["w_dtb"] = np.asarray(inp["L0_attn_dt_bias"]).reshape(32, 1).astype(np.float32)
    shared["w_cwx"] = np.ascontiguousarray(np.asarray(inp["L0_attn_conv_w_x"]).reshape(2048, 4))
    shared["w_cwb"] = np.ascontiguousarray(np.asarray(inp["L0_attn_conv_w_b"]).reshape(4096, 4))
    shared["w_cwc"] = np.ascontiguousarray(np.asarray(inp["L0_attn_conv_w_c"]).reshape(4096, 4))

    shared["w_routerT"] = np.ascontiguousarray(np.asarray(inp["L0_ffn_router_w"]).T)
    shared["w_moe1"] = np.ascontiguousarray(np.asarray(inp["L0_ffn_w1"]))
    shared["w_moe3"] = np.ascontiguousarray(np.asarray(inp["L0_ffn_w3"]))
    shared["w_moe2"] = np.ascontiguousarray(np.asarray(inp["L0_ffn_w2"]))

    shared["w_qaT"] = np.ascontiguousarray(np.asarray(inp["L1_attn_wq_a_w"]).T)
    shared["w_qab"] = np.ascontiguousarray(np.asarray(inp["L1_attn_wq_a_b"]).reshape(12, 128).T)
    shared["w_qnorm"] = np.ascontiguousarray(np.asarray(inp["L1_attn_q_norm_w"]).reshape(12, 128).T)
    shared["w_qbT"] = np.ascontiguousarray(np.asarray(inp["L1_attn_wq_b_w"]).T)
    qbb = np.zeros((128, 32), np.float32)
    _b = np.asarray(inp["L1_attn_wq_b_b"])
    for hh in range(16):
        qbb[:, 2 * hh] = _b[hh * 192:hh * 192 + 128]
        qbb[0:64, 2 * hh + 1] = _b[hh * 192 + 128:hh * 192 + 192]
    shared["w_qbb"] = np.ascontiguousarray(qbb)
    shared["w_kvaT"] = np.ascontiguousarray(np.asarray(inp["L1_attn_wkv_a_w"]).T)
    kvab = np.zeros((128, 5), np.float32)
    kvab[:, 0:4] = np.asarray(inp["L1_attn_wkv_a_b"])[0:512].reshape(4, 128).T
    kvab[0:64, 4] = np.asarray(inp["L1_attn_wkv_a_b"])[512:576]
    shared["w_kvab"] = np.ascontiguousarray(kvab)
    shared["w_kvnorm"] = np.ascontiguousarray(np.asarray(inp["L1_attn_kv_norm_w"]).reshape(4, 128).T)
    shared["w_kvbT"] = np.ascontiguousarray(np.asarray(inp["L1_attn_wkv_b_w"]).T)
    shared["w_kvbb"] = np.ascontiguousarray(np.asarray(inp["L1_attn_wkv_b_b"]).reshape(32, 128).T)
    shared["w_woT"] = np.ascontiguousarray(np.asarray(inp["L1_attn_wo_w"]).T)
    shared["w_wob"] = np.ascontiguousarray(np.asarray(inp["L1_attn_wo_b"]).reshape(16, 128).T)

    shared["w_upqT"] = np.ascontiguousarray(np.asarray(inp["L1_ffn_w_up_q"]).T.astype(np.int8))
    shared["w_upsT"] = np.ascontiguousarray(np.asarray(inp["L1_ffn_w_up_scale"]).T)
    shared["w_bup"] = np.ascontiguousarray(np.asarray(inp["L1_ffn_b_up"]).reshape(64, 128).T)
    shared["w_upla"] = np.ascontiguousarray(np.asarray(inp["L1_ffn_w_up_lora_a"]))
    shared["w_uplb"] = np.ascontiguousarray(np.asarray(inp["L1_ffn_w_up_lora_b"]))
    shared["w_dnqT"] = np.ascontiguousarray(np.asarray(inp["L1_ffn_w_down_q"]).T.astype(np.int8))
    shared["w_dnsT"] = np.ascontiguousarray(np.asarray(inp["L1_ffn_w_down_scale"]).T)
    shared["w_bdn"] = np.ascontiguousarray(np.asarray(inp["L1_ffn_b_down"]).reshape(16, 128).T)
    shared["w_dnla"] = np.ascontiguousarray(np.asarray(inp["L1_ffn_w_down_lora_a"]))
    shared["w_dnlb"] = np.ascontiguousarray(np.asarray(inp["L1_ffn_w_down_lora_b"]))

    x = np.asarray(inp["L0_attn_x"])
    percore = []
    for c in range(NCORE):
        b, q = c // 4, c % 4
        lo = q * T
        sl = x[b, max(lo - 3, 0):lo + T, :]
        if lo == 0:
            sl = np.concatenate([np.zeros((3, D), x.dtype), sl], axis=0)
        wc = np.zeros((4, 4), np.float32)
        jm = np.zeros((4, 1), np.float32)
        for j in range(4):
            if j < q:
                jm[j, 0] = 1.0
                for l in range(j + 1, q):
                    wc[l, j] = 1.0
        percore.append({
            "x_fm": np.ascontiguousarray(sl.T),
            "c_cosq": np.ascontiguousarray(shared["c_cosT"][:, lo:lo + T].astype(BF)),
            "c_sinq": np.ascontiguousarray(shared["c_sinT"][:, lo:lo + T].astype(BF)),
            "c_wcoef": wc, "c_jmask": jm,
        })
    return shared, percore


SHAPES = {
    "c_idf32": ([128, 128], F32), "c_idbf": ([128, 128], BF16),
    "c_ones_1x128_f": ([1, 128], F32), "c_ones_1x128_b": ([1, 128], BF16),
    "c_ones_128x1_b": ([128, 1], BF16), "c_ones_128x1_f": ([128, 1], F32),
    "c_scanmask": ([128, 128], F32), "c_seldt": ([32, 2048], F32),
    "c_blk2": ([2, 128], F32), "c_cosT": ([64, S], F32), "c_sinT": ([64, S], F32),
    "c_scalarpool": ([128, len(SCALAR_CONSTS)], F32),
    "w_winT": ([D, PROJ], F32), "w_woutT": ([D, D], F32),
    "w_alog": ([32, 1], F32), "w_dtb": ([32, 1], F32),
    "w_cwx": ([2048, 4], F32), "w_cwb": ([4096, 4], F32), "w_cwc": ([4096, 4], F32),
    "w_routerT": ([D, 8], F32),
    "w_moe1": ([EXP, D, FF], F32), "w_moe3": ([EXP, D, FF], F32), "w_moe2": ([EXP, FF, D], F32),
    "w_qaT": ([D, QLR], F32), "w_qab": ([128, 12], F32), "w_qnorm": ([128, 12], F32),
    "w_qbT": ([QLR, NH * DQK], F32), "w_qbb": ([128, 32], F32),
    "w_kvaT": ([D, KVLR + DR], F32), "w_kvab": ([128, 5], F32),
    "w_kvnorm": ([128, 4], F32),
    "w_kvbT": ([KVLR, NH * (DN + DV)], F32), "w_kvbb": ([128, 32], F32),
    "w_woT": ([D, D], F32), "w_wob": ([128, 16], F32),
    "w_upqT": ([D, HF], I8), "w_upsT": ([32, HF], F32), "w_bup": ([128, 64], F32),
    "w_upla": ([D, RLORA], F32), "w_uplb": ([RLORA, HF], F32),
    "w_dnqT": ([HF, D], I8), "w_dnsT": ([128, D], F32), "w_bdn": ([128, 16], F32),
    "w_dnla": ([HF, RLORA], F32), "w_dnlb": ([RLORA, D], F32),
    "x_fm": ([D, T + 3], F32),
    "c_cosq": ([64, T], BF16), "c_sinq": ([64, T], BF16),
    "c_wcoef": ([4, 4], F32), "c_jmask": ([4, 1], F32),
    "c_onesrow_b": ([1, S], BF16),
}


class Ctx:
    def __init__(self, nc, es):
        self.nc = nc
        self.es = es
        self.p = Prog(nc, es)
        self.dram = {}
        self.sbuf = {}
        self._cw = {"consts": ("d_consts", 0), "constsg": ("d_constsg", 0)}

    @property
    def consts_w(self):
        return [v for v in self._cw.values() if v[1] > 0]

    def dram_in(self, name, shape, dtype):
        ap = self.nc.dram_tensor(name, list(shape), dtype, kind="ExternalInput").ap()
        self.dram[name] = ap
        return ap

    def dram_out(self, name, shape, dtype):
        ap = self.nc.dram_tensor(name, list(shape), dtype, kind="ExternalOutput").ap()
        self.dram[name] = ap
        return ap

    def dram_tmp(self, name, shape, dtype):
        ap = self.nc.dram_tensor(name, list(shape), dtype, kind="Internal").ap()
        self.dram[name] = ap
        return ap

    def begin_arena(self, nbytes):
        self._arena = self.es.enter_context(
            self.nc.sbuf_tensor("ARENA", [128, nbytes], I8))
        self._aoff = 0
        self._arena_names = []

    def arena_reset(self):
        self._aoff = 0
        for n in self._arena_names:
            self.sbuf.pop(n, None)
        self._arena_names = []

    def sb(self, name, shape, dtype):
        if getattr(self, "_arena", None) is not None:
            size = {F32: 4, BF16: 2, I8: 1}[dtype]
            nb = shape[1] * size
            nb_al = (nb + 63) & ~63
            off = self._aoff
            self._aoff += nb_al
            assert self._aoff <= self._arena.shape[1], \
                "arena overflow at %s (%d)" % (name, self._aoff)
            v = self._arena[:, off:off + nb].bitcast(dtype)
            if shape[0] != 128:
                v = v[0:shape[0], :]
            self.sbuf[name] = v
            self._arena_names.append(name)
            return v
        h = self.es.enter_context(self.nc.sbuf_tensor(name, list(shape), dtype))
        self.sbuf[name] = h
        return h

    def ps(self, name, shape, dtype=F32):
        return self.es.enter_context(self.nc.psum_tensor(name, list(shape), dtype))

    def pbank(self, i):
        if not hasattr(self, "_pdbl"):
            self._pdbl = [self.ps("pd%d" % k, [128, 1024]) for k in range(4)]
        return self._pdbl[i // 2][:, (i % 2) * 512:(i % 2 + 1) * 512]

    def pdouble(self, j):
        self.pbank(0)
        return self._pdbl[j]

    def load_const(self, dname, cast=False):
        if dname in self.sbuf:
            return self.sbuf[dname]
        shape, dtype = SHAPES[dname]
        sbdt = dtype if not cast else BF16
        if getattr(self, "_arena", None) is not None:
            h = self.sb(dname, shape, sbdt)
            w0 = [("pe", self.p.count("pe")), ("dve", self.p.count("dve"))]
        else:
            h = self.es.enter_context(self.nc.sbuf_tensor("sb_" + dname, list(shape), sbdt))
            self.sbuf[dname] = h
            w0 = []
        eng = "gpsimd" if cast else "sync"
        key = "constsg" if cast else "consts"
        self._cw[key] = self.p.dma(eng, lambda e, d=h, a=self.dram[dname]:
                                   e.dma_start(out=d[:, :], in_=a), key, waits=w0)
        return h

    def register_scalar_pool(self):
        pool = self.sbuf["c_scalarpool"]
        for i, v in enumerate(SCALAR_CONSTS):
            self.nc.const_aps.register(F32, v, pool[:, i:i + 1])



def emit_ln_store(cx, stg, tqw, out_f32, out_bf, tcol0, bank, uid):
    p = cx.p
    pss, psq, psb1, psb2 = bank["pss"], bank["psq"], bank["psb1"], bank["psb2"]
    sqt, mrow, vrow, oout = bank["sqt"], bank["mrow"], bank["vrow"], bank["oout"]
    onesf = cx.sbuf["c_ones_128x1_f"]
    ones1x = cx.sbuf["c_ones_1x128_f"]
    stg_ready = [("dve", p.count("dve"))] + cx.consts_w
    prev_psq_mm = 0
    for dt in range(16):
        tile = stg[:, dt * tqw:(dt + 1) * tqw]
        p.op("tensor", lambda e, t=tile, k=dt: e.matmul(
            out=pss[:, :], lhsT=onesf[:, :], rhs=t, start=(k == 0), stop=(k == 15)),
            waits=(stg_ready if dt == 0 else []), inc="pe")
        av = p.op("scalar", lambda e, t=tile: e.activation(
            out=sqt[:, :], in_=t, func=AF.Square),
            waits=(stg_ready if dt == 0 else [("pe", prev_psq_mm)]), inc="act")
        prev_psq_mm = p.op("tensor", lambda e, k=dt: e.matmul(
            out=psq[:, :], lhsT=onesf[:, :], rhs=sqt[:, :], start=(k == 0), stop=(k == 15)),
            waits=[("act", av)], inc="pe")
    pe_stats = p.count("pe")
    mv = p.op("vector", lambda e: e.tensor_scalar(out=mrow[:, :], in0=pss[0:1, :],
              scalar1=1.0 / D, scalar2=None, op0=ALU.mult), waits=[("pe", pe_stats)], inc="dve")
    vv = p.op("vector", lambda e: e.tensor_scalar(out=vrow[:, :], in0=psq[0:1, :],
              scalar1=1.0 / D, scalar2=None, op0=ALU.mult), waits=[("pe", pe_stats)], inc="dve")
    m2 = p.op("scalar", lambda e: e.activation(out=sqt[0:1, 0:tqw], in_=mrow[:, :],
              func=AF.Square), waits=[("dve", mv)], inc="act")
    vv2 = p.op("vector", lambda e: e.tensor_tensor(out=vrow[:, :], in0=vrow[:, :],
               in1=sqt[0:1, 0:tqw], op=ALU.subtract), waits=[("act", m2), ("dve", vv)], inc="dve")
    l1 = p.op("scalar", lambda e: e.activation(out=vrow[:, :], in_=vrow[:, :],
              func=AF.Ln, bias=LNEPS), waits=[("dve", vv2)], inc="act")
    l2 = p.op("scalar", lambda e: e.activation(out=vrow[:, :], in_=vrow[:, :],
              func=AF.Exp, scale=-0.5), waits=[("act", l1)], inc="act")
    pb1 = p.op("tensor", lambda e: e.matmul(out=psb1[:, 0:tqw], lhsT=ones1x[:, :],
               rhs=mrow[:, :], start=True, stop=True), waits=[("dve", mv), ("act", l2)], inc="pe")
    pb2 = p.op("tensor", lambda e: e.matmul(out=psb2[:, 0:tqw], lhsT=ones1x[:, :],
               rhs=vrow[:, :], start=True, stop=True), waits=[("act", l2)], inc="pe")
    w1 = bank.get("last_w1")
    w2 = bank.get("last_w2")
    for dt in range(16):
        tile = stg[:, dt * tqw:(dt + 1) * tqw]
        gate = ([w1] if w1 else []) + ([w2] if w2 else [])
        sv = p.op("vector", lambda e, t=tile: e.tensor_tensor(
            out=sqt[:, :], in0=t, in1=psb1[:, 0:tqw], op=ALU.subtract),
            waits=[("pe", pb1)] + gate, inc="dve")
        ov = p.op("vector", lambda e: e.tensor_tensor(
            out=oout[:, :], in0=sqt[:, :], in1=psb2[:, 0:tqw], op=ALU.mult),
            waits=[("dve", sv), ("pe", pb2)], inc="dve")
        w1 = p.dma("gpsimd", lambda e, d=dt: e.dma_start(
            out=out_f32[d * 128:(d + 1) * 128, tcol0:tcol0 + tqw], in_=oout[:, :]),
            "st_" + uid + "_f", waits=[("dve", ov)])
        if out_bf is not None:
            bv = p.op("vector", lambda e: e.tensor_copy(out=bank["obf"][:, :], in_=oout[:, :]),
                      waits=[("dve", ov)], inc="dve")
            w2 = p.dma("gpsimd", lambda e, d=dt: e.dma_start(
                out=out_bf[d * 128:(d + 1) * 128, tcol0:tcol0 + tqw], in_=bank["obf"][:, :]),
                "st_" + uid + "_b", waits=[("dve", bv)])
    bank["last_w1"] = w1
    bank["last_w2"] = w2
    return [w1] + ([w2] if out_bf is not None else [])



def phase_l1ffn(cx, stream_f32, stream_bf, out_f32):
    nc, p = cx.nc, cx.p
    TQ = 512

    wup_bf = cx.dram_tmp("wup_bf", [D, HF], BF16)
    wdn_bf = cx.dram_tmp("wdn_bf", [HF, D], BF16)

    xs = cx.sb("xs", [128, 16 * TQ], BF16)
    x2 = cx.sb("x2", [128, 64 * TQ], BF16)
    stg = cx.sb("stg", [128, 16 * TQ], F32)
    wb1 = cx.sb("wb1", [128, 2 * 2048], BF16)
    wb2 = cx.sb("wb2", [128, 8192], BF16)
    xres = cx.sb("xres", [128, 2 * TQ], F32)
    labuf = cx.sb("labuf", [16, TQ], BF16)
    la2buf = cx.sb("la2buf", [16, TQ], BF16)
    sqt = cx.sb("sqt", [128, TQ], F32)
    mrow = cx.sb("mrow", [1, TQ], F32)
    vrow = cx.sb("vrow", [1, TQ], F32)
    oout = cx.sb("oout", [128, TQ], F32)
    dqf = cx.sb("dqf", [128, 512], F32)
    scbuf = cx.sb("scbuf", [2, 4096], F32)

    bup = cx.load_const("w_bup")
    bdn = cx.load_const("w_bdn")
    la_up = cx.sb("la_up", [128, 16 * 16], BF16)
    lb_up = cx.sb("lb_up", [16, HF], BF16)
    la_dn = cx.sb("la_dn", [128, 64 * 16], BF16)
    lb_dn = cx.sb("lb_dn", [16, D], BF16)
    p.dma("gpsimd", lambda e: e.dma_start(
        out=la_up[:, :].rearrange("p (a r) -> p a r", r=RLORA),
        in_=cx.dram["w_upla"].rearrange("(a p) r -> p a r", p=128)), "constsg")
    p.dma("gpsimd", lambda e: e.dma_start(out=lb_up[:, :], in_=cx.dram["w_uplb"]), "constsg")
    p.dma("gpsimd", lambda e: e.dma_start(
        out=la_dn[:, :].rearrange("p (a r) -> p a r", r=RLORA),
        in_=cx.dram["w_dnla"].rearrange("(a p) r -> p a r", p=128)), "constsg")
    cx._cw["constsg"] = p.dma("gpsimd", lambda e: e.dma_start(
        out=lb_dn[:, :], in_=cx.dram["w_dnlb"]), "constsg")

    blk2 = cx.sbuf["c_blk2"]

    psy0 = cx.pbank(0)
    psy1 = cx.pbank(1)
    psl = cx.pbank(2)[0:16, :]
    pss = cx.pbank(3)[0:1, :]
    psq = cx.pbank(4)[0:1, :]
    psb1 = cx.pbank(5)
    psb2 = cx.pbank(6)
    psc = cx.pbank(7)

    bank = dict(pss=pss, psq=psq, psb1=psb1, psb2=psb2, sqt=sqt, mrow=mrow,
                vrow=vrow, oout=oout, obf=None)

    q8v = [x2[:, 0:2048].bitcast(I8), x2[:, 2048:4096].bitcast(I8)]
    dqo = [x2[:, 4096:8192], x2[:, 8192:12288]]

    stw = {0: ("d_dq0", 0), 1: ("d_dq1", 0)}

    def dequant_matrix(qT, scT_dram, out_dram, nrow, ncol, tag):
        cw = 4096 if ncol % 4096 == 0 else ncol
        ncs = ncol // cw
        it = 0
        last_scmm = p.count("pe")
        for rt in range(nrow // 128):
            for cs in range(ncs):
                par = it % 2
                it += 1
                buf, obuf = q8v[par], dqo[par]
                ldw = p.dma("sync", lambda e, b=buf, r=rt, c=cs, cw=cw: e.dma_start(
                    out=b[:, 0:cw], in_=qT[r * 128:(r + 1) * 128, c * cw:(c + 1) * cw]),
                    "q8_%d" % par, waits=[stw[par], ("act", p.count("act"))])
                r0 = (rt * 128) // 64
                scw = p.dma("sync", lambda e, rr=r0, c=cs, cw=cw: e.dma_start(
                    out=scbuf[:, 0:cw], in_=scT_dram[rr:rr + 2, c * cw:(c + 1) * cw]),
                    "sc", waits=[("pe", last_scmm)])
                for j in range(cw // 512):
                    pev = p.op("tensor", lambda e, j=j: e.matmul(
                        out=psc[:, :], lhsT=blk2[:, :], rhs=scbuf[:, j * 512:(j + 1) * 512],
                        start=True, stop=True),
                        waits=[scw] + cx.consts_w + [("dve", p.count("dve"))], inc="pe")
                    acv = p.op("scalar", lambda e, b=buf, j=j: e.activation(
                        out=dqf[:, :], in_=b[:, j * 512:(j + 1) * 512], func=AF.Copy,
                        bias=-7.5, scale=1.0), waits=[ldw, ("dve", p.count("dve"))], inc="act")
                    p.op("vector", lambda e, o=obuf, j=j: e.tensor_tensor(
                        out=o[:, j * 512:(j + 1) * 512], in0=dqf[:, :], in1=psc[:, :], op=ALU.mult),
                        waits=[("pe", pev), ("act", acv)], inc="dve")
                last_scmm = p.count("pe")
                stw[par] = p.dma("gpsimd", lambda e, o=obuf, r=rt, c=cs, cw=cw: e.dma_start(
                    out=out_dram[r * 128:(r + 1) * 128, c * cw:(c + 1) * cw], in_=o[:, 0:cw]),
                    "dq%d" % par, waits=[("dve", p.count("dve"))])
        return [stw[0], stw[1]]

    deq_done = dequant_matrix(cx.dram["w_upqT"], cx.dram["w_upsT"], wup_bf, D, HF, "u")
    deq_done += dequant_matrix(cx.dram["w_dnqT"], cx.dram["w_dnsT"], wdn_bf, HF, D, "d")
    deq_dve = p.count("dve")

    last_pe_user_xs = 0
    wlast = None
    evac_dve2 = [deq_dve, deq_dve]
    for tq in range(4):
        tqs = slice(tq * TQ, (tq + 1) * TQ)
        ldx = p.dma("sync", lambda e, s=tqs: e.dma_start(
            out=xs[:, :].rearrange("p (a t) -> p a t", t=TQ),
            in_=stream_bf.rearrange("(a p) t -> p a t", p=128)[:, :, s]),
            "xs", waits=[("pe", last_pe_user_xs)] + (deq_done + [("dve", deq_dve)] if tq == 0 else []))

        lpe = None
        for kt in range(16):
            lpe = p.op("tensor", lambda e, k=kt: e.matmul(
                out=psl[:, :], lhsT=la_up[:, k * 16:(k + 1) * 16], rhs=xs[:, k * TQ:(k + 1) * TQ],
                start=(k == 0), stop=(k == 15)),
                waits=([ldx] + cx.consts_w + [("dve", p.count("dve"))] if kt == 0 else []),
                inc=("pe" if kt == 15 else None))
        lav = p.op("vector", lambda e: e.tensor_copy(out=labuf[:, :], in_=psl[:, :]),
                   waits=[("pe", lpe)], inc="dve")

        wld = [None, None]
        use_pe = [0, 0]
        use_act = [p.count("act"), p.count("act")]
        for ht in range(64):
            par = ht % 2
            wtile = wb1[:, par * 2048:(par + 1) * 2048]
            wld[par] = p.dma("sync", lambda e, w=wtile, h=ht: e.dma_start(
                out=w.rearrange("p (a h) -> p a h", h=128),
                in_=wup_bf.rearrange("(a p) h -> p a h", p=128)[:, :, h * 128:(h + 1) * 128]),
                "wb1_%d" % par, waits=[("pe", use_pe[par])])
            ps = psy0 if par == 0 else psy1
            for kt in range(16):
                p.op("tensor", lambda e, k=kt, w=wtile, ps=ps: e.matmul(
                    out=ps[:, :], lhsT=w[:, k * 128:(k + 1) * 128], rhs=xs[:, k * TQ:(k + 1) * TQ],
                    start=(k == 0), stop=False),
                    waits=([wld[par], ("act", use_act[par]), ("dve", lav)] if kt == 0 else []),
                    inc=None)
            pev = p.op("tensor", lambda e, h=ht, ps=ps: e.matmul(
                out=ps[:, :], lhsT=lb_up[:, h * 128:(h + 1) * 128], rhs=labuf[:, :],
                start=False, stop=True), inc="pe")
            use_pe[par] = pev
            use_act[par] = p.op("scalar", lambda e, h=ht, ps=ps: e.activation(
                out=x2[:, h * TQ:(h + 1) * TQ], in_=ps[:, :], func=AF.Relu,
                bias=bup[:, h:h + 1], scale=1.0), waits=[("pe", pev)], inc="act")
        y1_act = p.count("act")

        lpe2 = None
        for kt in range(64):
            lpe2 = p.op("tensor", lambda e, k=kt: e.matmul(
                out=psl[:, :], lhsT=la_dn[:, k * 16:(k + 1) * 16], rhs=x2[:, k * TQ:(k + 1) * TQ],
                start=(k == 0), stop=(k == 63)),
                waits=([("act", y1_act), ("dve", lav)] if kt == 0 else []),
                inc=("pe" if kt == 63 else None))
        la2v = p.op("vector", lambda e: e.tensor_copy(out=la2buf[:, :], in_=psl[:, :]),
                    waits=[("pe", lpe2)], inc="dve")

        half_done = 0
        grp_done = 0
        for dt in range(16):
            par = dt % 2
            wld2a = p.dma("sync", lambda e, d=dt: e.dma_start(
                out=wb2[:, 0:4096].rearrange("p (a h) -> p a h", h=128),
                in_=wdn_bf.rearrange("(a p) h -> p a h", p=128)[:, 0:32, d * 128:(d + 1) * 128]),
                "wb2a", waits=[("pe", half_done)] + (deq_done if (tq == 0 and dt == 0) else []))
            wld2b = p.dma("sync", lambda e, d=dt: e.dma_start(
                out=wb2[:, 4096:8192].rearrange("p (a h) -> p a h", h=128),
                in_=wdn_bf.rearrange("(a p) h -> p a h", p=128)[:, 32:64, d * 128:(d + 1) * 128]),
                "wb2b", waits=[("pe", grp_done)])
            ldr = p.dma("sync", lambda e, d=dt, s=tq, par=par: e.dma_start(
                out=xres[:, par * TQ:(par + 1) * TQ],
                in_=stream_f32[d * 128:(d + 1) * 128, s * TQ:(s + 1) * TQ]),
                "xres%d" % par, waits=[("dve", evac_dve2[par])])
            ps = psy0 if par == 0 else psy1
            for kt in range(64):
                inc = "pe" if kt == 31 else None
                v = p.op("tensor", lambda e, k=kt, ps=ps: e.matmul(
                    out=ps[:, :], lhsT=wb2[:, k * 128:(k + 1) * 128], rhs=x2[:, k * TQ:(k + 1) * TQ],
                    start=(k == 0), stop=False),
                    waits=([wld2a, ("dve", evac_dve2[par])] if kt == 0 else
                           ([wld2b] if kt == 32 else [])), inc=inc)
                if kt == 31:
                    half_done = v
            pev = p.op("tensor", lambda e, d=dt, ps=ps: e.matmul(
                out=ps[:, :], lhsT=lb_dn[:, d * 128:(d + 1) * 128], rhs=la2buf[:, :],
                start=False, stop=True), waits=[("dve", la2v)], inc="pe")
            grp_done = pev
            evac_dve2[par] = p.op("vector", lambda e, d=dt, ps=ps: e.scalar_tensor_tensor(
                out=stg[:, d * TQ:(d + 1) * TQ], in0=ps[:, :], scalar=bdn[:, d:d + 1],
                in1=xres[:, (d % 2) * TQ:((d % 2) + 1) * TQ], op0=ALU.add, op1=ALU.add),
                waits=[("pe", pev), ldr], inc="dve")
        last_pe_user_xs = p.count("pe")

        wlast = emit_ln_store(cx, stg, TQ, out_f32, None, tq * TQ, bank, "ffn")
    return wlast





def phase_moe(cx, stream_f32, stream_bf, out_f32, out_bf):
    nc, p = cx.nc, cx.p
    TW = 512

    g_dram = cx.dram_tmp("g_dram", [8, T], F32)

    xs = cx.sb("m_xs", [128, 16 * TW], BF16)
    Hp = cx.sb("m_Hp", [128, 32 * TW], BF16)
    stg = cx.sb("m_stg", [128, 16 * 512], F32)
    wq1 = cx.sb("m_wq1", [128, 2 * 2048], BF16)
    wq3 = cx.sb("m_wq3", [128, 2 * 2048], BF16)
    wq2 = cx.sb("m_wq2", [128, 2 * 4096], BF16)
    xres = cx.sb("m_xres", [128, 2 * 512], F32)
    gtm = cx.sb("m_gtm", [128, 64], F32)
    gfm = cx.sb("m_gfm", [8, TW], F32)
    grow = cx.sb("m_grow", [1, 8 * TW], F32)
    sc1 = cx.sb("m_sc1", [128, 8], F32)
    sc2 = cx.sb("m_sc2", [128, 8], F32)
    m1c = cx.sb("m_m1", [128, 4], F32)
    xsi = cx.sb("m_xsi", [128, 512], BF16)
    hh = cx.sb("m_hh", [128, 512], BF16)
    sqt = cx.sb("m_sqt", [128, 512], F32)
    mrow = cx.sb("m_mrow", [1, 512], F32)
    vrow = cx.sb("m_vrow", [1, 512], F32)
    oout = cx.sb("m_oout", [128, 512], F32)
    obf = cx.sb("m_obf", [128, 512], BF16)

    routT = cx.sb("m_routT", [128, 16 * 8], BF16)
    cx._cw["constsg"] = p.dma("gpsimd", lambda e: e.dma_start(
        out=routT[:, :].rearrange("p (a r) -> p a r", r=8),
        in_=cx.dram["w_routerT"].rearrange("(a p) r -> p a r", p=128)), "constsg")
    idf = cx.sbuf["c_idf32"]
    ones1x = cx.sbuf["c_ones_1x128_f"]

    pb = [cx.pbank(i) for i in range(8)]
    bank = dict(pss=pb[3][0:1, :], psq=pb[4][0:1, :], psb1=pb[5], psb2=pb[6],
                sqt=sqt, mrow=mrow, vrow=vrow, oout=oout, obf=obf)

    w1d = cx.dram["w_moe1"].rearrange("e (a p) f -> p e a f", p=128)
    w3d = cx.dram["w_moe3"].rearrange("e (a p) f -> p e a f", p=128)
    w2d = cx.dram["w_moe2"].rearrange("e (k p) d -> p e k d", p=128)

    wlast = None
    for th in range(T // TW):
        ths = slice(th * TW, (th + 1) * TW)
        ldx = p.dma("sync", lambda e, s=ths: e.dma_start(
            out=xs[:, :].rearrange("p (a t) -> p a t", t=TW),
            in_=stream_bf.rearrange("(a p) t -> p a t", p=128)[:, :, s]),
            "m_xs", waits=[("pe", p.count("pe"))])

        gtr_evac = None
        for tt in range(TW // 128):
            pr = pb[7][:, 0:8]
            lpe = None
            for kt in range(16):
                lpe = p.op("tensor", lambda e, k=kt, t=tt: e.matmul(
                    out=pr, lhsT=xs[:, k * TW + t * 128:k * TW + (t + 1) * 128],
                    rhs=routT[:, k * 8:(k + 1) * 8], start=(k == 0), stop=(k == 15)),
                    waits=([ldx] + cx.consts_w + [("dve", p.count("dve"))] if kt == 0 else []),
                    inc=("pe" if kt == 15 else None))
            c1 = p.op("vector", lambda e: e.tensor_copy(out=sc1[:, :], in_=pr),
                      waits=[("pe", lpe)], inc="dve")
            c2 = p.op("vector", lambda e: e.reduce_max(out=m1c[:, 0:1], in_=sc1[:, :],
                      axis=mybir.AxisListType.X), waits=[("dve", c1)], inc="dve")
            c3 = p.op("vector", lambda e: e.tensor_scalar(out=m1c[:, 1:2], in0=m1c[:, 0:1],
                      scalar1=-1.0, scalar2=None, op0=ALU.mult), waits=[("dve", c2)], inc="dve")
            c4 = p.op("vector", lambda e: e.tensor_scalar(out=sc2[:, :], in0=sc1[:, :],
                      scalar1=m1c[:, 0:1], scalar2=None, op0=ALU.is_ge),
                      waits=[("dve", c3)], inc="dve")
            c5 = p.op("vector", lambda e: e.scalar_tensor_tensor(out=sc2[:, :], in0=sc2[:, :],
                      scalar=-1e30, in1=sc1[:, :], op0=ALU.mult, op1=ALU.add),
                      waits=[("dve", c4)], inc="dve")
            c6 = p.op("vector", lambda e: e.reduce_max(out=m1c[:, 2:3], in_=sc2[:, :],
                      axis=mybir.AxisListType.X), waits=[("dve", c5)], inc="dve")
            a1 = p.op("scalar", lambda e: e.activation(out=sc2[:, :], in_=sc1[:, :],
                      func=AF.Exp, bias=m1c[:, 1:2], scale=1.0),
                      waits=[("dve", c6)], inc="act")
            c7 = p.op("vector", lambda e: e.tensor_scalar(out=sc1[:, :], in0=sc1[:, :],
                      scalar1=m1c[:, 2:3], scalar2=None, op0=ALU.is_ge),
                      waits=[("act", a1)], inc="dve")
            c8 = p.op("vector", lambda e: e.tensor_tensor(out=sc2[:, :], in0=sc2[:, :],
                      in1=sc1[:, :], op=ALU.mult), waits=[("dve", c7)], inc="dve")
            c9 = p.op("vector", lambda e: e.reduce_sum(out=m1c[:, 3:4], in_=sc2[:, :],
                      axis=mybir.AxisListType.X), waits=[("dve", c8)], inc="dve")
            c10 = p.op("vector", lambda e: e.reciprocal(out=m1c[:, 3:4], in_=m1c[:, 3:4]),
                       waits=[("dve", c9)], inc="dve")
            c11 = p.op("vector", lambda e, t=tt: e.tensor_scalar(out=gtm[:, t * 8:(t + 1) * 8],
                       in0=sc2[:, :], scalar1=m1c[:, 3:4], scalar2=None, op0=ALU.mult),
                       waits=[("dve", c10)], inc="dve")
            tpe = p.op("tensor", lambda e, t=tt: e.transpose(
                out=pb[6][0:8, 0:128], in_=gtm[:, t * 8:(t + 1) * 8], identity=idf[:, :]),
                waits=[("dve", c11)], inc="pe")
            gtr_evac = p.op("vector", lambda e, t=tt: e.tensor_copy(
                out=gfm[:, t * 128:(t + 1) * 128], in_=pb[6][0:8, 0:128]),
                waits=[("pe", tpe)], inc="dve")
        stg_w = p.dma("gpsimd", lambda e, h=th: e.dma_start(
            out=g_dram[:, h * TW:(h + 1) * TW], in_=gfm[:, :]),
            "m_g%d" % th, waits=[("dve", gtr_evac)])
        groww = None
        for ee in range(8):
            groww = p.dma("sync", lambda e, ee=ee, h=th: e.dma_start(
                out=grow[0:1, ee * TW:(ee + 1) * TW], in_=g_dram[ee:ee + 1, h * TW:(h + 1) * TW]),
                "m_grow%d" % th, waits=([stg_w, ("pe", p.count("pe"))] if ee == 0 else []))

        w1w = [None, None]
        w3w = [None, None]
        w_use = [0, 0]
        for e_i in range(EXP):
            gmm = p.op("tensor", lambda e, ei=e_i: e.matmul(
                out=pb[2][:, :], lhsT=ones1x[:, :],
                rhs=grow[0:1, ei * TW:ei * TW + 512],
                start=True, stop=True),
                waits=[groww, ("dve", p.count("dve"))], inc="pe")
            for mt in range(4):
                wpar = mt % 2
                w1w[wpar] = p.dma("gpsimd", lambda e, ei=e_i, m=mt, wpar=wpar: e.dma_start(
                    out=wq1[:, wpar * 2048:(wpar + 1) * 2048].rearrange("p (a f) -> p a f", f=128),
                    in_=w1d[:, ei, :, m * 128:(m + 1) * 128]),
                    "m_w1%d" % wpar, waits=[("pe", w_use[wpar])])
                w3w[wpar] = p.dma("gpsimd", lambda e, ei=e_i, m=mt, wpar=wpar: e.dma_start(
                    out=wq3[:, wpar * 2048:(wpar + 1) * 2048].rearrange("p (a f) -> p a f", f=128),
                    in_=w3d[:, ei, :, m * 128:(m + 1) * 128]),
                    "m_w3%d" % wpar)
                x1pe = x3pe = None
                actw = p.count("act")
                dvew = p.count("dve")
                for kt in range(16):
                    x1pe = p.op("tensor", lambda e, k=kt, wpar=wpar: e.matmul(
                        out=pb[0][:, :], lhsT=wq1[:, wpar * 2048 + k * 128:wpar * 2048 + (k + 1) * 128],
                        rhs=xs[:, k * TW:k * TW + 512],
                        start=(k == 0), stop=(k == 15)),
                        waits=([w1w[wpar], ("act", actw)] if kt == 0 else []),
                        inc=("pe" if kt == 15 else None))
                for kt in range(16):
                    x3pe = p.op("tensor", lambda e, k=kt, wpar=wpar: e.matmul(
                        out=pb[1][:, :], lhsT=wq3[:, wpar * 2048 + k * 128:wpar * 2048 + (k + 1) * 128],
                        rhs=xs[:, k * TW:k * TW + 512],
                        start=(k == 0), stop=(k == 15)),
                        waits=([w3w[wpar], ("dve", dvew)] if kt == 0 else []),
                        inc=("pe" if kt == 15 else None))
                w_use[wpar] = x3pe
                av = p.op("scalar", lambda e: e.activation(out=xsi[:, :], in_=pb[0][:, :],
                          func=AF.Silu), waits=[("pe", x1pe), ("dve", p.count("dve"))], inc="act")
                hv = p.op("vector", lambda e: e.tensor_tensor(out=hh[:, :], in0=xsi[:, :],
                          in1=pb[1][:, :], op=ALU.mult),
                          waits=[("act", av), ("pe", x3pe)], inc="dve")
                p.op("vector", lambda e, ei=e_i, m=mt: e.tensor_tensor(
                    out=Hp[:, (ei * 4 + m) * TW:(ei * 4 + m) * TW + 512],
                    in0=hh[:, :], in1=pb[2][:, :], op=ALU.mult),
                    waits=[("dve", hv), ("pe", gmm)], inc="dve")
        hp_dve = p.count("dve")

        for tc in range(TW // 512):
            w2w = [None, None]
            w2use = [0, 0]
            evac_dve = [p.count("dve"), p.count("dve")]
            for dt in range(16):
                par = dt % 2
                w2w[par] = p.dma("gpsimd", lambda e, d=dt, par=par: e.dma_start(
                    out=wq2[:, par * 4096:(par + 1) * 4096].rearrange("p (ek d) -> p ek d", d=128),
                    in_=w2d[:, :, :, d * 128:(d + 1) * 128].rearrange("p e k d -> p (e k) d")),
                    "m_w2%d" % par, waits=[("pe", w2use[par])])
                ldr = p.dma("sync", lambda e, d=dt, c=tc, h=th, par=par: e.dma_start(
                    out=xres[:, par * 512:(par + 1) * 512],
                    in_=stream_f32[d * 128:(d + 1) * 128, h * TW + c * 512:h * TW + (c + 1) * 512]),
                    "m_xr%d" % par, waits=[("dve", p.count("dve"))])
                ps = pb[0] if par == 0 else pb[1]
                ype = None
                act_war = p.count("act")
                for kt in range(32):
                    ype = p.op("tensor", lambda e, k=kt, c=tc, par=par, ps=ps: e.matmul(
                        out=ps[:, :], lhsT=wq2[:, par * 4096 + k * 128:par * 4096 + (k + 1) * 128],
                        rhs=Hp[:, k * TW + c * 512:k * TW + (c + 1) * 512],
                        start=(k == 0), stop=(k == 31)),
                        waits=([w2w[par], ("dve", max(hp_dve, evac_dve[par])),
                                ("act", act_war)] if kt == 0 else []),
                        inc=("pe" if kt == 31 else None))
                w2use[par] = ype
                evac_dve[par] = p.op("vector", lambda e, d=dt, ps=ps, par=par: e.tensor_tensor(
                    out=stg[:, d * 512:(d + 1) * 512], in0=ps[:, :],
                    in1=xres[:, par * 512:(par + 1) * 512], op=ALU.add),
                    waits=[("pe", ype), ldr], inc="dve")
            wlast = emit_ln_store(cx, stg, 512, out_f32, out_bf, th * TW + tc * 512, bank, "moe")
    return wlast





def rms_bcast(cx, psq_ap, inv_n_const, pbb, tag):
    p = cx.p
    vr = cx.sbuf["vrow_s"]
    w = psq_ap.shape[1]
    v1 = p.op("vector", lambda e: e.tensor_scalar(out=vr[:, 0:w], in0=psq_ap,
              scalar1=inv_n_const, scalar2=None, op0=ALU.mult),
              waits=[("pe", p.count("pe"))], inc="dve")
    a1 = p.op("scalar", lambda e: e.activation(out=vr[:, 0:w], in_=vr[:, 0:w],
              func=AF.Ln, bias=1e-6), waits=[("dve", v1)], inc="act")
    a2 = p.op("scalar", lambda e: e.activation(out=vr[:, 0:w], in_=vr[:, 0:w],
              func=AF.Exp, scale=-0.5), waits=[("act", a1)], inc="act")
    pbv = p.op("tensor", lambda e: e.matmul(out=pbb[:, 0:w], lhsT=cx.sbuf["c_ones_1x128_f"][:, :],
               rhs=vr[:, 0:w], start=True, stop=True),
               waits=[("act", a2), ("dve", p.count("dve"))], inc="pe")
    return pbv


def phase_l1qkv(cx, stream_bf, qn_dram, kvp_dram):
    nc, p = cx.nc, cx.p
    TC = 512

    xs = cx.sb("a_xs", [128, 16 * T], BF16)
    qraw = cx.sb("a_qraw", [128, 12 * TC], F32)
    kvraw = cx.sb("a_kvraw", [128, 5 * TC], F32)
    sq = cx.sb("a_sq", [128, TC], F32)
    vrow_s = cx.sb("vrow_s", [1, TC], F32)
    cx.sbuf["vrow_s"] = vrow_s
    obf = cx.sb("a_obf", [128, TC], BF16)
    rot = cx.sb("a_rot", [64, TC], F32)
    rop1 = cx.sb("a_rop1", [64, TC], F32)
    wqa = cx.sb("a_wqa", [128, 2 * 2048], BF16)
    wkva = cx.sb("a_wkva", [128, 2 * 2048], BF16)

    qab = cx.load_const("w_qab")
    qnw = cx.load_const("w_qnorm")
    kvab = cx.load_const("w_kvab")
    kvnw = cx.load_const("w_kvnorm")
    cosq = cx.load_const("c_cosq")
    sinq = cx.load_const("c_sinq")

    pb = [cx.pbank(i) for i in range(8)]

    ldx = p.dma("sync", lambda e: e.dma_start(
        out=xs[:, :].rearrange("p (a t) -> p a t", t=T),
        in_=stream_bf.rearrange("(a p) t -> p a t", p=128)), "a_xs",
        waits=[("pe", p.count("pe"))])

    qaT = cx.dram["w_qaT"].rearrange("(a p) n -> p a n", p=128)
    kvaT = cx.dram["w_kvaT"].rearrange("(a p) n -> p a n", p=128)

    obf_gate = None
    for tc in range(4):
        wq = [None, None]
        ev = [0, 0]
        wquse = [0, 0]
        for mt in range(12):
            par = mt % 2
            wq[par] = p.dma("gpsimd", lambda e, m=mt, par=par: e.dma_start(
                out=wqa[:, par * 2048:(par + 1) * 2048].rearrange("p (a n) -> p a n", n=128),
                in_=qaT[:, :, m * 128:(m + 1) * 128]), "a_wqa%d" % par,
                waits=[("pe", wquse[par])])
            ps = pb[par]
            qpe = None
            for kt in range(16):
                qpe = p.op("tensor", lambda e, k=kt, par=par, c=tc, ps=ps: e.matmul(
                    out=ps[:, :], lhsT=wqa[:, par * 2048 + k * 128:par * 2048 + (k + 1) * 128],
                    rhs=xs[:, k * T + c * TC:k * T + (c + 1) * TC],
                    start=(k == 0), stop=(k == 15)),
                    waits=([wq[par], ldx, ("act", ev[par])] + cx.consts_w if kt == 0 else []),
                    inc=("pe" if kt == 15 else None))
            wquse[par] = qpe
            ev[par] = p.op("scalar", lambda e, m=mt, ps=ps: e.activation(
                out=qraw[:, m * TC:(m + 1) * TC], in_=ps[:, :], func=AF.Identity,
                bias=qab[:, m:m + 1], scale=1.0), waits=[("pe", qpe)], inc="act")
        qraw_act = p.count("act")
        prevmm = 0
        for mt in range(12):
            av = p.op("scalar", lambda e, m=mt: e.activation(
                out=sq[:, :], in_=qraw[:, m * TC:(m + 1) * TC], func=AF.Square),
                waits=([("act", qraw_act), ("pe", prevmm), ("dve", p.count("dve"))]), inc="act")
            prevmm = p.op("tensor", lambda e, m=mt: e.matmul(
                out=pb[2][0:1, :], lhsT=cx.sbuf["c_ones_128x1_f"][:, :], rhs=sq[:, :],
                start=(m == 0), stop=(m == 11)), waits=[("act", av)], inc="pe")
        pbv = rms_bcast(cx, pb[2][0:1, 0:TC], 1.0 / 1536.0, pb[3], "q")
        for mt in range(12):
            ov = p.op("vector", lambda e, m=mt: e.scalar_tensor_tensor(
                out=obf[:, :], in0=qraw[:, m * TC:(m + 1) * TC], scalar=qnw[:, m:m + 1],
                in1=pb[3][:, 0:TC], op0=ALU.mult, op1=ALU.mult),
                waits=[("pe", pbv)] + ([obf_gate] if obf_gate else []), inc="dve")
            obf_gate = p.dma("gpsimd", lambda e, m=mt, c=tc: e.dma_start(
                out=qn_dram[m * 128:(m + 1) * 128, c * TC:(c + 1) * TC], in_=obf[:, :]),
                "a_qn", waits=[("dve", ov)])

        wk = [None, None]
        evk = [qraw_act, qraw_act]
        wkuse = [0, 0]
        for mt in range(5):
            par = mt % 2
            mw = 128 if mt < 4 else 64
            wk[par] = p.dma("gpsimd", lambda e, m=mt, par=par, mw=mw: e.dma_start(
                out=wkva[:, par * 2048:par * 2048 + 16 * mw].rearrange("p (a n) -> p a n", n=mw),
                in_=kvaT[:, :, m * 128:m * 128 + mw]), "a_wkva%d" % par,
                waits=[("pe", wkuse[par])])
            ps = pb[par]
            kpe_mm = None
            for kt in range(16):
                kpe_mm = p.op("tensor", lambda e, k=kt, par=par, c=tc, ps=ps, mw=mw: e.matmul(
                    out=ps[0:mw, 0:TC], lhsT=wkva[:, par * 2048 + k * mw:par * 2048 + (k + 1) * mw],
                    rhs=xs[:, k * T + c * TC:k * T + (c + 1) * TC],
                    start=(k == 0), stop=(k == 15)),
                    waits=([wk[par], ("act", evk[par])] if kt == 0 else []),
                    inc=("pe" if kt == 15 else None))
            wkuse[par] = kpe_mm
            evk[par] = p.op("scalar", lambda e, m=mt, ps=ps, mw=mw: e.activation(
                out=kvraw[0:mw, m * TC:(m + 1) * TC], in_=ps[0:mw, 0:TC], func=AF.Identity,
                bias=kvab[0:mw, m:m + 1], scale=1.0), waits=[("pe", kpe_mm)], inc="act")
        kvraw_act = p.count("act")
        prevmm = 0
        for mt in range(4):
            av = p.op("scalar", lambda e, m=mt: e.activation(
                out=sq[:, :], in_=kvraw[:, m * TC:(m + 1) * TC], func=AF.Square),
                waits=[("act", kvraw_act), ("pe", prevmm), ("dve", p.count("dve"))], inc="act")
            prevmm = p.op("tensor", lambda e, m=mt: e.matmul(
                out=pb[2][0:1, :], lhsT=cx.sbuf["c_ones_128x1_f"][:, :], rhs=sq[:, :],
                start=(m == 0), stop=(m == 3)), waits=[("act", av)], inc="pe")
        pbv = rms_bcast(cx, pb[2][0:1, 0:TC], 1.0 / 512.0, pb[3], "kv")
        for mt in range(4):
            ov = p.op("vector", lambda e, m=mt: e.scalar_tensor_tensor(
                out=obf[:, :], in0=kvraw[:, m * TC:(m + 1) * TC], scalar=kvnw[:, m:m + 1],
                in1=pb[3][:, 0:TC], op0=ALU.mult, op1=ALU.mult),
                waits=[("pe", pbv)] + ([obf_gate] if obf_gate else []), inc="dve")
            obf_gate = p.dma("gpsimd", lambda e, m=mt, c=tc: e.dma_start(
                out=kvp_dram[m * 128:(m + 1) * 128, c * TC:(c + 1) * TC], in_=obf[:, :]),
                "a_kv", waits=[("dve", ov)])
        kp = kvraw[0:64, 4 * TC:5 * TC]
        r1 = p.op("vector", lambda e: e.tensor_scalar(
            out=rot[0:32, :], in0=kvraw[32:64, 4 * TC:5 * TC], scalar1=-1.0, scalar2=None,
            op0=ALU.mult), waits=[("act", kvraw_act), ("dve", p.count("dve"))], inc="dve")
        r2 = p.op("vector", lambda e: e.tensor_copy(
            out=rot[32:64, :], in_=kvraw[0:32, 4 * TC:5 * TC]), waits=[("dve", r1)], inc="dve")
        r3 = p.op("vector", lambda e, c=tc: e.tensor_tensor(
            out=rop1[:, :], in0=kp, in1=cosq[:, c * TC:(c + 1) * TC], op=ALU.mult),
            waits=[("dve", r2)] + cx.consts_w, inc="dve")
        r4 = p.op("vector", lambda e, c=tc: e.tensor_tensor(
            out=rot[:, :], in0=rot[:, :], in1=sinq[:, c * TC:(c + 1) * TC], op=ALU.mult),
            waits=[("dve", r3)], inc="dve")
        r5 = p.op("vector", lambda e: e.tensor_tensor(
            out=obf[0:64, :], in0=rop1[:, :], in1=rot[:, :], op=ALU.add),
            waits=[("dve", r4)] + ([obf_gate] if obf_gate else []), inc="dve")
        obf_gate = p.dma("gpsimd", lambda e, c=tc: e.dma_start(
            out=kvp_dram[512:576, c * TC:(c + 1) * TC], in_=obf[0:64, :]),
            "a_kv", waits=[("dve", r5)])
    return [obf_gate]



def phase_l1attn(cx, qn_dram, kvall_dram, attn_dram):
    nc, p = cx.nc, cx.p
    TC = 512
    SCALE = 0.07216878364870323

    qn = cx.sb("t_qn", [128, 12 * T], BF16)
    kvn = cx.sb("t_kvn", [128, 4 * S], BF16)
    kB = cx.sb("t_kB", [65, S], BF16)
    kp2m = cx.sb("t_kp2m", [1, 1], F32)
    kA = cx.sb("t_kA", [128, S], BF16)
    VT = cx.sb("t_VT", [128, S], BF16)
    vtmp = cx.sb("t_vtmp", [128, TC], BF16)
    wqb_ = cx.sb("t_wqb", [128, 2 * 12 * 192], BF16)
    wkvb_ = cx.sb("t_wkvb", [128, 4 * 256], BF16)
    qA = cx.sb("t_qA", [128, T], BF16)
    qB = cx.sb("t_qB", [65, T], BF16)
    qpef = cx.sb("t_qpef", [64, TC], F32)
    rot = cx.sb("t_rot", [64, TC], F32)
    sq = cx.sb("t_sq", [128, TC], F32)
    kmaxr = cx.sb("t_kmaxr", [1, TC], F32)
    km2 = cx.sb("t_km2", [1, 1], F32)
    brow = cx.sb("t_brow", [1, TC], F32)
    PT = cx.sb("t_PT", [128, 2 * 1024], BF16)
    rb_sb = cx.sb("t_rb", [128, TC], BF16)
    oot = cx.sb("t_oot", [128, TC], BF16)
    cosq = cx.load_const("c_cosq")
    sinq = cx.load_const("c_sinq")
    qbb = cx.load_const("w_qbb")
    kvbb = cx.load_const("w_kvbb")
    idb = cx.sbuf["c_idbf"]
    onesf = cx.sbuf["c_ones_128x1_f"]
    ones64 = onesf[0:64, :]
    onesb = cx.sbuf["c_ones_128x1_b"]

    pb = [cx.pbank(i) for i in range(8)]

    ldq = p.dma("sync", lambda e: e.dma_start(
        out=qn[:, :].rearrange("p (a t) -> p a t", t=T),
        in_=qn_dram.rearrange("(a p) t -> p a t", p=128)), "t_qn",
        waits=[("pe", p.count("pe"))])
    ldkv = None
    for r in range(4):
        for kvt in range(4):
            ldkv = p.dma("sync", lambda e, r=r, k=kvt: e.dma_start(
                out=kvn[:, k * S + r * T:k * S + (r + 1) * T],
                in_=kvall_dram[r * 576 + k * 128:r * 576 + (k + 1) * 128, :]), "t_kvn",
                waits=([("pe", p.count("pe"))] if (r == 0 and kvt == 0) else []))
        ldkv = p.dma("sync", lambda e, r=r: e.dma_start(
            out=kB[0:64, r * T:(r + 1) * T],
            in_=kvall_dram[r * 576 + 512:r * 576 + 576, :]), "t_kvn")
    ldone = p.dma("sync", lambda e: e.dma_start(
        out=kB[64:65, :], in_=cx.dram["c_onesrow_b"]), "t_kvn")
    kvn_w = ldone
    kp2mm = None
    kmx = None
    for c in range(16):
        a1 = p.op("scalar", lambda e, c=c: e.activation(
            out=sq[0:64, :], in_=kB[0:64, c * TC:(c + 1) * TC], func=AF.Square),
            waits=[kvn_w, ("pe", kp2mm or 0)], inc="act")
        kp2mm = p.op("tensor", lambda e, c=c: e.matmul(
            out=pb[2][0:1, :], lhsT=ones64, rhs=sq[0:64, :], start=True, stop=True),
            waits=[("act", a1), ("dve", p.count("dve"))] + cx.consts_w, inc="pe")
        if c == 0:
            kmx = p.op("vector", lambda e: e.tensor_copy(out=kmaxr[:, :], in_=pb[2][0:1, :]),
                       waits=[("pe", kp2mm)], inc="dve")
        else:
            kmx = p.op("vector", lambda e: e.tensor_tensor(out=kmaxr[:, :], in0=kmaxr[:, :],
                       in1=pb[2][0:1, :], op=ALU.max), waits=[("pe", kp2mm)], inc="dve")
    kmx = p.op("vector", lambda e: e.reduce_max(out=kp2m[:, :], in_=kmaxr[:, :],
               axis=mybir.AxisListType.X), waits=[("dve", kmx)], inc="dve")
    kpe2_dve = p.count("dve")

    qbT = cx.dram["w_qbT"].rearrange("(a p) n -> p a n", p=128)
    kvbT = cx.dram["w_kvbT"].rearrange("(a p) n -> p a n", p=128)

    attn_st = None
    huse = [0, 0]
    for h in range(NH):
        hp_ = h % 2
        wqb = wqb_[:, hp_ * 2304:(hp_ + 1) * 2304]
        wkvb = wkvb_
        wqw = p.dma("gpsimd", lambda e, h=h, w=wqb: e.dma_start(
            out=w.rearrange("p (a n) -> p a n", n=192),
            in_=qbT[:, :, h * 192:(h + 1) * 192]), "t_wqb%d" % hp_,
            waits=[("pe", huse[hp_])])
        wkw = p.dma("gpsimd", lambda e, h=h, w=wkvb: e.dma_start(
            out=w.rearrange("p (a n) -> p a n", n=256),
            in_=kvbT[:, :, h * 256:(h + 1) * 256]), "t_wkvb",
            waits=[("pe", huse[1 - hp_])])

        kmax_dve = None
        for c in range(16):
            kmm = None
            act_war = p.count("act")
            for a in range(4):
                kmm = p.op("tensor", lambda e, a=a, c=c, wkvb=wkvb: e.matmul(
                    out=pb[0][:, :], lhsT=wkvb[:, a * 256:a * 256 + 128],
                    rhs=kvn[:, a * S + c * TC:a * S + (c + 1) * TC],
                    start=(a == 0), stop=(a == 3)),
                    waits=([wkw, kvn_w, ("act", act_war), ("dve", kpe2_dve)] if a == 0 else []),
                    inc=("pe" if a == 3 else None))
            av = p.op("scalar", lambda e, c=c, h=h: e.activation(
                out=kA[:, c * TC:(c + 1) * TC], in_=pb[0][:, :], func=AF.Identity,
                bias=kvbb[:, 2 * h:2 * h + 1], scale=1.0), waits=[("pe", kmm)], inc="act")
            a2 = p.op("scalar", lambda e, c=c: e.activation(
                out=sq[:, :], in_=kA[:, c * TC:(c + 1) * TC], func=AF.Square),
                waits=[("act", av), ("pe", p.count("pe"))], inc="act")
            nmm = p.op("tensor", lambda e: e.matmul(
                out=pb[2][0:1, :], lhsT=onesf[:, :], rhs=sq[:, :], start=True, stop=True),
                waits=[("act", a2), ("dve", p.count("dve"))] + cx.consts_w, inc="pe")
            if c == 0:
                kmax_dve = p.op("vector", lambda e: e.tensor_copy(out=kmaxr[:, :],
                                in_=pb[2][0:1, :]), waits=[("pe", nmm)], inc="dve")
            else:
                kmax_dve = p.op("vector", lambda e: e.tensor_tensor(out=kmaxr[:, :],
                                in0=kmaxr[:, :], in1=pb[2][0:1, :], op=ALU.max),
                                waits=[("pe", nmm)], inc="dve")
            vmm = None
            for a in range(4):
                vmm = p.op("tensor", lambda e, a=a, c=c, wkvb=wkvb: e.matmul(
                    out=pb[1][:, :], lhsT=wkvb[:, a * 256 + 128:a * 256 + 256],
                    rhs=kvn[:, a * S + c * TC:a * S + (c + 1) * TC],
                    start=(a == 0), stop=(a == 3)),
                    waits=([("dve", p.count("dve"))] if a == 0 else []),
                    inc=("pe" if a == 3 else None))
            vv = p.op("scalar", lambda e, c=c, h=h: e.activation(
                out=vtmp[:, :], in_=pb[1][:, :], func=AF.Identity,
                bias=kvbb[:, 2 * h + 1:2 * h + 2], scale=1.0), waits=[("pe", vmm)], inc="act")
            for s4 in range(4):
                tmm = p.op("tensor", lambda e, s4=s4: e.transpose(
                    out=pb[3][:, 0:64].bitcast(BF16), in_=vtmp[:, s4 * 128:(s4 + 1) * 128],
                    identity=idb[:, :]),
                    waits=[("act", vv), ("dve", p.count("dve"))] + cx.consts_w, inc="pe")
                p.op("vector", lambda e, c=c, s4=s4: e.tensor_copy(
                    out=VT[:, (c * 4 + s4) * 128:(c * 4 + s4 + 1) * 128],
                    in_=pb[3][:, 0:64].bitcast(BF16)), waits=[("pe", tmm)], inc="dve")
        km = p.op("vector", lambda e: e.reduce_max(out=km2[:, :], in_=kmaxr[:, :],
                  axis=mybir.AxisListType.X), waits=[("dve", kmax_dve)], inc="dve")
        km = p.op("vector", lambda e: e.tensor_tensor(out=km2[:, :], in0=km2[:, :],
                  in1=kp2m[:, :], op=ALU.add), waits=[("dve", km)], inc="dve")

        on_prev = 0
        for c in range(4):
            qmm = None
            for a in range(12):
                qmm = p.op("tensor", lambda e, a=a, c=c, wqb=wqb: e.matmul(
                    out=pb[0][:, :], lhsT=wqb[:, a * 192:a * 192 + 128],
                    rhs=qn[:, a * T + c * TC:a * T + (c + 1) * TC],
                    start=(a == 0), stop=(a == 11)),
                    waits=([wqw, ldq, ("act", p.count("act")), ("dve", p.count("dve"))]
                           if a == 0 else []), inc=("pe" if a == 11 else None))
            qav = p.op("scalar", lambda e, c=c, h=h: e.activation(
                out=qA[:, c * TC:(c + 1) * TC], in_=pb[0][:, :], func=AF.Identity,
                bias=qbb[:, 2 * h:2 * h + 1], scale=1.0), waits=[("pe", qmm)], inc="act")
            qpmm = None
            for a in range(12):
                qpmm = p.op("tensor", lambda e, a=a, c=c, wqb=wqb: e.matmul(
                    out=pb[1][0:64, :], lhsT=wqb[:, a * 192 + 128:a * 192 + 192],
                    rhs=qn[:, a * T + c * TC:a * T + (c + 1) * TC],
                    start=(a == 0), stop=(a == 11)),
                    waits=([("dve", p.count("dve"))] if a == 0 else []),
                    inc=("pe" if a == 11 else None))
            qpv = p.op("scalar", lambda e, c=c, h=h: e.activation(
                out=qpef[:, :], in_=pb[1][0:64, :], func=AF.Identity,
                bias=qbb[0:64, 2 * h + 1:2 * h + 2], scale=1.0), waits=[("pe", qpmm)], inc="act")
            r1 = p.op("vector", lambda e: e.tensor_scalar(out=rot[0:32, :], in0=qpef[32:64, :],
                      scalar1=-1.0, scalar2=None, op0=ALU.mult), waits=[("act", qpv)], inc="dve")
            r2 = p.op("vector", lambda e: e.tensor_copy(out=rot[32:64, :], in_=qpef[0:32, :]),
                      waits=[("dve", r1)], inc="dve")
            r3 = p.op("vector", lambda e, c=c: e.tensor_tensor(out=qpef[:, :], in0=qpef[:, :],
                      in1=cosq[:, c * TC:(c + 1) * TC], op=ALU.mult),
                      waits=[("dve", r2)] + cx.consts_w, inc="dve")
            r4 = p.op("vector", lambda e, c=c: e.tensor_tensor(out=rot[:, :], in0=rot[:, :],
                      in1=sinq[:, c * TC:(c + 1) * TC], op=ALU.mult), waits=[("dve", r3)], inc="dve")
            r5 = p.op("vector", lambda e, c=c: e.tensor_tensor(out=qB[0:64, c * TC:(c + 1) * TC],
                      in0=qpef[:, :], in1=rot[:, :], op=ALU.add), waits=[("dve", r4)], inc="dve")
            s1 = p.op("scalar", lambda e, c=c: e.activation(out=sq[:, :],
                      in_=qA[:, c * TC:(c + 1) * TC], func=AF.Square),
                      waits=[("act", qav), ("pe", p.count("pe"))], inc="act")
            b1 = p.op("tensor", lambda e: e.matmul(out=pb[2][0:1, :], lhsT=onesf[:, :],
                      rhs=sq[:, :], start=True, stop=False),
                      waits=[("act", s1)] + cx.consts_w, inc="pe")
            s2 = p.op("scalar", lambda e, c=c: e.activation(out=sq[0:64, :],
                      in_=qB[0:64, c * TC:(c + 1) * TC], func=AF.Square),
                      waits=[("dve", r5), ("pe", b1)], inc="act")
            b2 = p.op("tensor", lambda e: e.matmul(out=pb[2][0:1, :], lhsT=ones64,
                      rhs=sq[0:64, :], start=False, stop=True), waits=[("act", s2)], inc="pe")
            b3 = p.op("vector", lambda e: e.tensor_scalar(out=brow[:, :], in0=pb[2][0:1, :],
                      scalar1=km2[0:1, 0:1], scalar2=None, op0=ALU.mult),
                      waits=[("pe", b2), ("dve", km)], inc="dve")
            b4 = p.op("scalar", lambda e: e.activation(out=brow[:, :], in_=brow[:, :],
                      func=AF.Ln, bias=1e-6), waits=[("dve", b3)], inc="act")
            b5 = p.op("scalar", lambda e, c=c: e.activation(out=qB[64:65, c * TC:(c + 1) * TC],
                      in_=brow[:, :], func=AF.Exp, scale=0.5), waits=[("act", b4)], inc="act")
            b6 = p.op("vector", lambda e, c=c: e.tensor_scalar(
                out=qB[64:65, c * TC:(c + 1) * TC], in0=qB[64:65, c * TC:(c + 1) * TC],
                scalar1=-1.0, scalar2=None, op0=ALU.mult), waits=[("act", b5)], inc="dve")

            expw = [0, 0]
            opv = None
            dnm = None
            for kt2 in range(32):
                par = kt2 % 2
                std = cx.pdouble(par)
                mm1 = mm2 = None
                for half in range(2):
                    kt = kt2 * 2 + half
                    tgt = std[:, half * 512:(half + 1) * 512]
                    mm1 = p.op("tensor", lambda e, kt=kt, c=c, tgt=tgt: e.matmul(
                        out=tgt, lhsT=kA[:, kt * 128:(kt + 1) * 128],
                        rhs=qA[:, c * TC:(c + 1) * TC], start=True, stop=False),
                        waits=([("act", expw[par]), ("dve", b6)] if half == 0 else []),
                        inc=None)
                    mm2 = p.op("tensor", lambda e, kt=kt, c=c, tgt=tgt: e.matmul(
                        out=tgt, lhsT=kB[:, kt * 128:(kt + 1) * 128],
                        rhs=qB[:, c * TC:(c + 1) * TC], start=False, stop=True),
                        inc=("pe" if half == 1 else None))
                expw[par] = p.op("scalar", lambda e, par=par, std=std: e.activation(
                    out=PT[:, par * 1024:(par + 1) * 1024], in_=std[:, 0:1024],
                    func=AF.Exp, scale=SCALE), waits=[("pe", mm2)], inc="act")
                for half in range(2):
                    kt = kt2 * 2 + half
                    opv = p.op("tensor", lambda e, kt=kt, par=par, half=half: e.matmul(
                        out=pb[4][:, :], lhsT=VT[:, kt * 128:(kt + 1) * 128],
                        rhs=PT[:, par * 1024 + half * 512:par * 1024 + (half + 1) * 512],
                        start=(kt == 0), stop=(kt == 63)),
                        waits=(([("act", expw[par]), ("dve", on_prev)]) if half == 0 else []),
                        inc=("pe" if kt == 63 else None))
                    dnm = p.op("tensor", lambda e, kt=kt, par=par, half=half: e.matmul(
                        out=pb[5][0:1, :], lhsT=onesb[:, :],
                        rhs=PT[:, par * 1024 + half * 512:par * 1024 + (half + 1) * 512],
                        start=(kt == 0), stop=(kt == 63)),
                        waits=((cx.consts_w + [("dve", on_prev)]) if kt == 0 else []),
                        inc=("pe" if kt == 63 else None))
            rc = p.op("vector", lambda e: e.reciprocal(out=brow[:, :], in_=pb[5][0:1, :]),
                      waits=[("pe", dnm)], inc="dve")
            rbc = p.op("tensor", lambda e: e.matmul(out=pb[6][:, :],
                       lhsT=cx.sbuf["c_ones_1x128_f"][:, :], rhs=brow[:, :],
                       start=True, stop=True), waits=[("dve", rc)] + cx.consts_w, inc="pe")
            rv = p.op("vector", lambda e: e.tensor_copy(out=rb_sb[:, :], in_=pb[6][:, :]),
                      waits=[("pe", rbc)], inc="dve")
            on = p.op("vector", lambda e: e.tensor_tensor(out=oot[:, :], in0=rb_sb[:, :],
                      in1=pb[4][:, :], op=ALU.mult),
                      waits=[("dve", rv), ("pe", opv)] + ([attn_st] if attn_st else []), inc="dve")
            on_prev = on
            attn_st = p.dma("gpsimd", lambda e, h=h, c=c: e.dma_start(
                out=attn_dram[h * 128:(h + 1) * 128, c * TC:(c + 1) * TC], in_=oot[:, :]),
                "t_attn", waits=[("dve", on)])
        huse[hp_] = p.count("pe")
    return [attn_st]



def phase_l1wo(cx, attn_dram, stream_f32, out_f32, out_bf):
    nc, p = cx.nc, cx.p
    TC = 512
    av_ = cx.sb("w_av", [128, 16 * TC], BF16)
    stg = cx.sb("w_stg", [128, 16 * TC], F32)
    wwo = cx.sb("w_wwo", [128, 2 * 2048], BF16)
    xres = cx.sb("w_xres", [128, 2 * TC], F32)
    sqt = cx.sb("w_sqt", [128, TC], F32)
    mrow = cx.sb("w_mrow", [1, TC], F32)
    vrow = cx.sb("w_vrow", [1, TC], F32)
    oout = cx.sb("w_oout", [128, TC], F32)
    obf = cx.sb("w_obf", [128, TC], BF16)
    wob = cx.load_const("w_wob")
    pb = [cx.pbank(i) for i in range(8)]
    bank = dict(pss=pb[3][0:1, :], psq=pb[4][0:1, :], psb1=pb[5], psb2=pb[6],
                sqt=sqt, mrow=mrow, vrow=vrow, oout=oout, obf=obf)
    woT = cx.dram["w_woT"].rearrange("(a p) n -> p a n", p=128)
    wlast = None
    wuse = [0, 0]
    evac_dve = [p.count("dve"), p.count("dve")]
    wld = [None, None]
    for tc in range(4):
        lda = p.dma("sync", lambda e, c=tc: e.dma_start(
            out=av_[:, :].rearrange("p (a t) -> p a t", t=TC),
            in_=attn_dram.rearrange("(a p) t -> p a t", p=128)[:, :, c * TC:(c + 1) * TC]),
            "w_av", waits=[("pe", p.count("pe"))])
        for dt in range(16):
            par = dt % 2
            wld[par] = p.dma("gpsimd", lambda e, d=dt, par=par: e.dma_start(
                out=wwo[:, par * 2048:(par + 1) * 2048].rearrange("p (a n) -> p a n", n=128),
                in_=woT[:, :, d * 128:(d + 1) * 128]), "w_wwo%d" % par,
                waits=[("pe", wuse[par])])
            ldr = p.dma("sync", lambda e, d=dt, c=tc, par=par: e.dma_start(
                out=xres[:, par * TC:(par + 1) * TC],
                in_=stream_f32[d * 128:(d + 1) * 128, c * TC:(c + 1) * TC]),
                "w_xr%d" % par, waits=[("dve", evac_dve[par])])
            ps = pb[par]
            ype = None
            for kt in range(16):
                ype = p.op("tensor", lambda e, k=kt, ps=ps, par=par: e.matmul(
                    out=ps[:, :], lhsT=wwo[:, par * 2048 + k * 128:par * 2048 + (k + 1) * 128],
                    rhs=av_[:, k * TC:(k + 1) * TC], start=(k == 0), stop=(k == 15)),
                    waits=([wld[par], lda, ("dve", evac_dve[par])] + cx.consts_w
                           if kt == 0 else []), inc=("pe" if kt == 15 else None))
            wuse[par] = ype
            evac_dve[par] = p.op("vector", lambda e, d=dt, ps=ps, par=par: e.scalar_tensor_tensor(
                out=stg[:, d * TC:(d + 1) * TC], in0=ps[:, :], scalar=wob[:, d:d + 1],
                in1=xres[:, par * TC:(par + 1) * TC], op0=ALU.add, op1=ALU.add),
                waits=[("pe", ype), ldr], inc="dve")
        wlast = emit_ln_store(cx, stg, TC, out_f32, out_bf, tc * TC, bank, "wo")
    return wlast





def phase_l0a(cx, x_fm, y_raw, sz_tm, Cc_d, S_out, dectot_d, expSgT_d):
    nc, p = cx.nc, cx.p
    TH = T + 3

    xT = cx.sb("l_xT", [128, 16 * TH], BF16)
    winb = cx.sb("l_winb", [128, 2 * 2048], BF16)
    wdt = cx.sb("l_wdt", [128, 16 * 32], BF16)
    xssm = cx.sb("l_xssm", [128, TH], BF16)
    Braw = cx.sb("l_Braw", [128, 2 * TH], BF16)
    Craw = cx.sb("l_Craw", [128, 2 * TH], BF16)
    cacc = cx.sb("l_cacc", [128, T], F32)
    xc = cx.sb("l_xc", [128, T], F32)
    Bc = cx.sb("l_Bc", [128, 2 * T], BF16)
    Cc = cx.sb("l_Cc", [128, 2 * T], BF16)
    dtx = cx.sb("l_dtx", [128, T], BF16)
    dtb_ = cx.sb("l_dt", [32, T], F32)
    sR = cx.sb("l_sR", [32, T], F32)
    sL = cx.sb("l_sL", [32, T], F32)
    sTl = cx.sb("l_sTl", [128, 512], F32)
    sgT = cx.sb("l_sgT", [128, 512], F32)
    eSl = cx.sb("l_eSl", [128, 512], F32)
    wsT = cx.sb("l_wsT", [128, 512], F32)
    erep = cx.sb("l_erep", [128, 512], F32)
    eend = cx.sb("l_eend", [32, 16], F32)
    erow = cx.sb("l_erow", [1, 512], F32)
    srow = cx.sb("l_srow", [1, 2 * 128], F32)
    tA = cx.sb("l_tA", [32, 512], F32)
    tBq = cx.sb("l_tB", [32, 512], F32)
    cwx = cx.sb("l_cwx", [128, 4], F32)
    cwb = cx.sb("l_cwb", [128, 2 * 4], F32)
    cwc = cx.sb("l_cwc", [128, 2 * 4], F32)
    Dm = cx.sb("l_Dm", [128, 2 * 128], F32)
    eD = cx.sb("l_eD", [128, 2 * 128], F32)
    PT = cx.sb("l_PT", [128, 2 * 128], BF16)
    dtxT = cx.sb("l_dtxT", [128, 2 * 64], BF16)
    BcTs = cx.sb("l_BcTs", [128, 2 * 128], BF16)
    ytmp = cx.sb("l_ytmp", [128, 2 * 64], F32)
    yg = cx.sb("l_yg", [128, 16 * 128], BF16)
    zg = cx.sb("l_zg", [128, 16 * 128], BF16)
    hT = cx.sb("l_hT", [128, 2 * 2 * 64], F32)
    hTb = cx.sb("l_hTb", [128, 2 * 64], BF16)

    alog = cx.load_const("w_alog")
    dtbias = cx.load_const("w_dtb")
    idf = cx.sbuf["c_idf32"]
    idb = cx.sbuf["c_idbf"]
    seldt = cx.load_const("c_seldt")
    scanmask = cx.load_const("c_scanmask")
    ones1x = cx.sbuf["c_ones_1x128_f"]
    eb_dram = cx.dram_tmp("l_eend_d", [16, 32], F32)

    pb = [cx.pbank(i) for i in range(8)]

    winT = cx.dram["w_winT"].rearrange("(a p) n -> p a n", p=128)
    ldx = p.dma("gpsimd", lambda e: e.dma_start(
        out=xT[:, :].rearrange("p (a t) -> p a t", t=TH),
        in_=x_fm.rearrange("(a p) t -> p a t", p=128)), "l_xT",
        waits=[("pe", p.count("pe"))])
    ldwdt = p.dma("gpsimd", lambda e: e.dma_start(
        out=wdt[:, :].rearrange("p (a n) -> p a n", n=32),
        in_=winT[:, :, 12288:12320]), "l_wdt")

    sp_dve = 0
    for tc in range(4):
        dmm = None
        for kt in range(16):
            dmm = p.op("tensor", lambda e, k=kt, c=tc: e.matmul(
                out=pb[2][0:32, :], lhsT=wdt[:, k * 32:(k + 1) * 32],
                rhs=xT[:, k * TH + 3 + c * 512:k * TH + 3 + (c + 1) * 512],
                start=(k == 0), stop=(k == 15)),
                waits=([ldx, ldwdt, ("dve", p.count("dve")),
                        ("act", p.count("act"))] + cx.consts_w if kt == 0 else []),
                inc=("pe" if kt == 15 else None))
        a1 = p.op("scalar", lambda e: e.activation(out=tA[:, :], in_=pb[2][0:32, :],
                  func=AF.Abs, bias=dtbias[:, 0:1], scale=1.0), waits=[("pe", dmm)], inc="act")
        a2 = p.op("scalar", lambda e: e.activation(out=tA[:, :], in_=tA[:, :],
                  func=AF.Exp, scale=-1.0), waits=[("act", a1)], inc="act")
        a3 = p.op("scalar", lambda e: e.activation(out=tA[:, :], in_=tA[:, :],
                  func=AF.Ln, bias=1.0), waits=[("act", a2)], inc="act")
        a4 = p.op("scalar", lambda e: e.activation(out=tBq[:, :], in_=pb[2][0:32, :],
                  func=AF.Relu, bias=dtbias[:, 0:1], scale=1.0), waits=[("act", a3)], inc="act")
        sp_dve = p.op("vector", lambda e, c=tc: e.tensor_tensor(
            out=dtb_[:, c * 512:(c + 1) * 512], in0=tA[:, :], in1=tBq[:, :], op=ALU.add),
            waits=[("act", a4)], inc="dve")
    ae = p.op("scalar", lambda e: e.activation(out=tA[:, 0:1], in_=alog[:, 0:1],
              func=AF.Exp), waits=[("dve", sp_dve)] + cx.consts_w, inc="act")
    ne = p.op("vector", lambda e: e.tensor_scalar(out=tA[:, 0:1], in0=tA[:, 0:1],
              scalar1=-1.0, scalar2=None, op0=ALU.mult), waits=[("act", ae)], inc="dve")
    ld_ = p.op("vector", lambda e: e.tensor_scalar(out=sR[:, :], in0=dtb_[:, :],
               scalar1=tA[:, 0:1], scalar2=None, op0=ALU.mult),
               waits=[("dve", ne)], inc="dve")
    sc = p.op("vector", lambda e: e.tensor_tensor_scan(out=sR[:, :], data0=sR[:, :],
              data1=dtb_[:, :], initial=0.0, op0=ALU.add, op1=ALU.bypass),
              waits=[("dve", ld_)], inc="dve")
    dst = p.dma("gpsimd", lambda e: e.dma_start(out=dectot_d, in_=sR[:, T - 1:T]),
                "l_dec", waits=[("dve", sc)])
    slv = None
    for c in range(NCH):
        if c == 0:
            slv = p.op("vector", lambda e: e.tensor_copy(out=sL[:, 0:QC], in_=sR[:, 0:QC]),
                       waits=[("dve", sc)], inc="dve")
        else:
            slv = p.op("vector", lambda e, c=c: e.tensor_scalar(
                out=sL[:, c * QC:(c + 1) * QC], in0=sR[:, c * QC:(c + 1) * QC],
                scalar1=sR[:, c * QC - 1:c * QC], scalar2=None, op0=ALU.subtract),
                waits=[("dve", sc)], inc="dve")
    wrow = xc[0:32, :]
    wrv = None
    for c in range(NCH):
        wrv = p.op("vector", lambda e, c=c: e.tensor_scalar(
            out=wrow[:, c * QC:(c + 1) * QC], in0=sL[:, c * QC:(c + 1) * QC],
            scalar1=sL[:, c * QC + QC - 1:c * QC + QC], scalar2=None, op0=ALU.subtract),
            waits=[("dve", slv)], inc="dve")
    tpv = None
    for c in range(NCH):
        t1 = p.op("tensor", lambda e, c=c: e.transpose(
            out=pb[4][:, 0:32], in_=sL[:, c * QC:(c + 1) * QC], identity=idf[0:32, 0:32]),
            waits=[("dve", wrv), ("dve", p.count("dve"))] + cx.consts_w, inc="pe")
        v1 = p.op("vector", lambda e, c=c: e.tensor_copy(out=sTl[:, c * 32:(c + 1) * 32],
                  in_=pb[4][:, 0:32]), waits=[("pe", t1)], inc="dve")
        t2 = p.op("tensor", lambda e, c=c: e.transpose(
            out=pb[4][:, 0:32], in_=sR[:, c * QC:(c + 1) * QC], identity=idf[0:32, 0:32]),
            waits=[("dve", v1)], inc="pe")
        v2 = p.op("vector", lambda e, c=c: e.tensor_copy(out=sgT[:, c * 32:(c + 1) * 32],
                  in_=pb[4][:, 0:32]), waits=[("pe", t2)], inc="dve")
        t3 = p.op("tensor", lambda e, c=c: e.transpose(
            out=pb[4][:, 0:32], in_=wrow[:, c * QC:(c + 1) * QC], identity=idf[0:32, 0:32]),
            waits=[("dve", v2)], inc="pe")
        tpv = p.op("vector", lambda e, c=c: e.tensor_copy(out=wsT[:, c * 32:(c + 1) * 32],
                  in_=pb[4][:, 0:32]), waits=[("pe", t3)], inc="dve")
    e1 = p.op("scalar", lambda e: e.activation(out=eSl[:, :], in_=sTl[:, :], func=AF.Exp),
              waits=[("dve", tpv)], inc="act")
    e2 = p.op("scalar", lambda e: e.activation(out=wsT[:, :], in_=wsT[:, :],
              func=AF.Exp, scale=-1.0), waits=[("act", e1)], inc="act")
    e3 = p.op("scalar", lambda e: e.activation(out=sgT[:, :], in_=sgT[:, :], func=AF.Exp),
              waits=[("act", e2)], inc="act")
    est = p.dma("gpsimd", lambda e: e.dma_start(out=expSgT_d, in_=sgT[:, :]), "l_esg",
                waits=[("act", e3)])
    ee1 = p.op("scalar", lambda e: e.activation(
        out=eend[:, :], in_=sL[:, QC - 1:T:QC], func=AF.Exp),
        waits=[("dve", slv)], inc="act")
    te = p.op("tensor", lambda e: e.transpose(out=pb[5][0:16, 0:32], in_=eend[:, :],
              identity=idf[0:32, 0:32]), waits=[("act", ee1)], inc="pe")
    ve = p.op("vector", lambda e: e.tensor_copy(out=tA[0:16, 0:32], in_=pb[5][0:16, 0:32]),
              waits=[("pe", te)], inc="dve")
    we = p.dma("gpsimd", lambda e: e.dma_start(out=eb_dram, in_=tA[0:16, 0:32]), "l_eb",
               waits=[("dve", ve)])
    re = p.dma("sync", lambda e: e.dma_start(
        out=erow[:, :], in_=eb_dram.rearrange("a b -> (a b)").unsqueeze(0)), "l_er",
        waits=[we])
    me = p.op("tensor", lambda e: e.matmul(out=pb[5][:, 0:512], lhsT=ones1x[:, :],
              rhs=erow[:, :], start=True, stop=True), waits=[re] + cx.consts_w, inc="pe")
    ve2 = p.op("vector", lambda e: e.tensor_copy(out=erep[:, :], in_=pb[5][:, 0:512]),
               waits=[("pe", me)], inc="dve")
    prep_dve = p.count("dve")
    prep_act = p.count("act")

    winw = None
    zstw = None
    ystw = None
    winuse = [0, 0]
    soutw = {0: None, 1: None}
    for g in range(16):
        pe_war = p.count("pe")
        dvw = p.count("dve")
        cw1 = p.dma("sync", lambda e, g=g: e.dma_start(
            out=cwx[:, :], in_=cx.dram["w_cwx"][g * 128:(g + 1) * 128, :]), "l_cw1",
            waits=[("dve", dvw)])
        cw2 = p.dma("sync", lambda e, g=g: e.dma_start(
            out=cwb[:, :].rearrange("p (a n) -> p a n", n=4),
            in_=cx.dram["w_cwb"].rearrange("(a p) n -> p a n", p=128)[:, 2 * g:2 * g + 2, :]),
            "l_cw2", waits=[("dve", dvw)])
        cw3 = p.dma("sync", lambda e, g=g: e.dma_start(
            out=cwc[:, :].rearrange("p (a n) -> p a n", n=4),
            in_=cx.dram["w_cwc"].rearrange("(a p) n -> p a n", p=128)[:, 2 * g:2 * g + 2, :]),
            "l_cw3", waits=[("dve", dvw)])

        outs = [(0, xssm, 0), (256, Braw, 0), (384, Braw, 1), (512, Craw, 0), (640, Craw, 1)]
        chunks = [(0, 512), (512, 512), (1024, 512), (1536, 512), (2048, 3)]
        group_dve = p.count("dve")
        ev = 0
        winw2 = [None, None]
        for oi, (off, buf, half) in enumerate(outs):
            wpar = oi % 2
            winw2[wpar] = p.dma("gpsimd", lambda e, g=g, off=off, wpar=wpar: e.dma_start(
                out=winb[:, wpar * 2048:(wpar + 1) * 2048].rearrange("p (a n) -> p a n", n=128),
                in_=winT[:, :, g * 768 + off:g * 768 + off + 128]), "l_win%d" % wpar,
                waits=[("pe", winuse[wpar])])
            for ci, (c0, cwid) in enumerate(chunks):
                par = ci % 2
                ps = pb[par]
                mmv = None
                for kt in range(16):
                    mmv = p.op("tensor", lambda e, k=kt, wpar=wpar, c0=c0, cwid=cwid, ps=ps: e.matmul(
                        out=ps[:, 0:cwid],
                        lhsT=winb[:, wpar * 2048 + k * 128:wpar * 2048 + (k + 1) * 128],
                        rhs=xT[:, k * TH + c0:k * TH + c0 + cwid],
                        start=(k == 0), stop=(k == 15)),
                        waits=([winw2[wpar], ("act", p.count("act")),
                                ("dve", group_dve)] if kt == 0 else []),
                        inc=("pe" if kt == 15 else None))
                ev = p.op("scalar", lambda e, buf=buf, half=half, c0=c0, cwid=cwid, ps=ps:
                          e.activation(out=buf[:, half * TH + c0:half * TH + c0 + cwid],
                                       in_=ps[:, 0:cwid], func=AF.Copy),
                          waits=[("pe", mmv), ("dve", group_dve)], inc="act")
            winuse[wpar] = mmv
        proj_act = p.count("act")

        zwin = p.dma("gpsimd", lambda e, g=g: e.dma_start(
            out=winb[:, 0:2048].rearrange("p (a n) -> p a n", n=128),
            in_=winT[:, :, g * 768 + 128:g * 768 + 256]), "l_win0",
            waits=[("pe", winuse[0])])
        zevs = None
        for tt in range(16):
            par = tt % 2
            ps = pb[4] if par == 0 else pb[5]
            zmm = None
            for kt in range(16):
                zmm = p.op("tensor", lambda e, k=kt, t=tt, ps=ps: e.matmul(
                    out=ps[:, 0:128], lhsT=xT[:, k * TH + 3 + t * 128:k * TH + 3 + (t + 1) * 128],
                    rhs=winb[:, k * 128:(k + 1) * 128],
                    start=(k == 0), stop=(k == 15)),
                    waits=([zwin, ("act", p.count("act")),
                            ("dve", p.count("dve"))] if kt == 0 else []),
                    inc=("pe" if kt == 15 else None))
            zevs = p.op("scalar", lambda e, t=tt, ps=ps: e.activation(
                out=zg[:, t * 128:(t + 1) * 128], in_=ps[:, 0:128], func=AF.Silu),
                waits=[("pe", zmm)] + ([zstw] if (tt == 0 and zstw) else []), inc="act")
            winuse[0] = zmm
        zstw = p.dma("gpsimd", lambda e, g=g: e.dma_start(
            out=sz_tm.rearrange("(a p) hp -> p a hp", p=128)[:, :, g * 128:(g + 1) * 128],
            in_=zg[:, :].rearrange("p (a n) -> p a n", n=128)), "l_zst",
            waits=[("act", zevs)])

        def conv(buf, half, cw, cwcol, outbuf, outbf, outoff):
            v0 = p.op("vector", lambda e, buf=buf, half=half, cw=cw, cwcol=cwcol: e.tensor_scalar(
                out=cacc[:, :], in0=buf[:, half * TH + 0:half * TH + T],
                scalar1=cw[:, cwcol * 4:cwcol * 4 + 1], scalar2=None, op0=ALU.mult),
                waits=[("act", proj_act), cw1, cw2, cw3, ("act", p.count("act"))] + extra_gate,
                inc="dve")
            for k in range(1, 4):
                v0 = p.op("vector", lambda e, buf=buf, half=half, cw=cw, cwcol=cwcol, k=k:
                          e.scalar_tensor_tensor(
                              out=cacc[:, :], in0=buf[:, half * TH + k:half * TH + k + T],
                              scalar=cw[:, cwcol * 4 + k:cwcol * 4 + k + 1], in1=cacc[:, :],
                              op0=ALU.mult, op1=ALU.add), waits=[("dve", v0)], inc="dve")
            av = p.op("scalar", lambda e, outbuf=outbuf, outoff=outoff: e.activation(
                out=outbuf[:, outoff:outoff + T], in_=cacc[:, :], func=AF.Silu),
                waits=[("dve", v0), ("pe", pe_war)], inc="act")
            return av

        extra_gate = ([ccstw, ccst2w] if g > 0 else [])
        cxv = conv(xssm, 0, cwx, 0, xc, False, 0)
        cb0 = conv(Braw, 0, cwb, 0, Bc, True, 0)
        cb1 = conv(Braw, 1, cwb, 1, Bc, True, T)
        cc0 = conv(Craw, 0, cwc, 0, Cc, True, 0)
        cc1 = conv(Craw, 1, cwc, 1, Cc, True, T)
        dxv = None
        for tc4 in range(4):
            smm = p.op("tensor", lambda e, c=tc4, g=g: e.matmul(
                out=pb[6][:, :], lhsT=seldt[:, g * 128:(g + 1) * 128],
                rhs=dtb_[:, c * 512:(c + 1) * 512], start=True, stop=True),
                waits=[("dve", p.count("dve")), ("act", cxv)] + cx.consts_w, inc="pe")
            dxv = p.op("vector", lambda e, c=tc4: e.tensor_tensor(
                out=dtx[:, c * 512:(c + 1) * 512], in0=xc[:, c * 512:(c + 1) * 512],
                in1=pb[6][:, :], op=ALU.mult), waits=[("pe", smm), ("act", cxv)], inc="dve")

        ccstw = p.dma("gpsimd", lambda e, g=g: e.dma_start(
            out=Cc_d[g * 256:g * 256 + 128, :], in_=Cc[:, 0:T]), "l_cc0",
            waits=[("act", cc0)])
        ccst2w = p.dma("gpsimd", lambda e, g=g: e.dma_start(
            out=Cc_d[g * 256 + 128:g * 256 + 256, :], in_=Cc[:, T:2 * T]), "l_cc1",
            waits=[("act", cc1)])

        bc_act = {0: cb0, 1: cb1}
        cc_act = {0: cc0, 1: cc1}
        PBA = {0: pb[6], 1: pb[2]}
        PBG = {0: pb[7], 1: pb[3]}
        PBS = {0: pb[4], 1: pb[0]}
        PBY = {0: pb[5], 1: pb[1]}
        t_ = [dict(), dict()]
        for c in range(NCH):
            for hp in range(2):
                h = 2 * g + hp
                col = c * 32 + h
                t_[hp]["st1"] = p.op("tensor", lambda e, col=col, hp=hp: e.transpose(
                    out=PBA[hp][0:1, 0:128], in_=sTl[:, col:col + 1], identity=idf[:, :]),
                    waits=[("dve", p.count("dve"))], inc="pe")
            for hp in range(2):
                t_[hp]["sv1"] = p.op("vector", lambda e, hp=hp: e.tensor_copy(
                    out=srow[:, hp * 128:(hp + 1) * 128], in_=PBA[hp][0:1, 0:128]),
                    waits=[("pe", t_[hp]["st1"])], inc="dve")
            for hp in range(2):
                t_[hp]["db"] = p.op("tensor", lambda e, hp=hp: e.matmul(
                    out=PBA[hp][:, 0:128], lhsT=ones1x[:, :],
                    rhs=srow[:, hp * 128:(hp + 1) * 128], start=True, stop=True),
                    waits=[("dve", t_[hp]["sv1"])], inc="pe")
            for hp in range(2):
                col = c * 32 + 2 * g + hp
                t_[hp]["d1"] = p.op("vector", lambda e, col=col, hp=hp: e.tensor_scalar(
                    out=Dm[:, hp * 128:(hp + 1) * 128], in0=PBA[hp][:, 0:128],
                    scalar1=sTl[:, col:col + 1], scalar2=0.0,
                    op0=ALU.subtract, op1=ALU.min),
                    waits=[("pe", t_[hp]["db"])], inc="dve")
            for hp in range(2):
                t_[hp]["d2"] = p.op("vector", lambda e, hp=hp: e.tensor_tensor(
                    out=Dm[:, hp * 128:(hp + 1) * 128], in0=Dm[:, hp * 128:(hp + 1) * 128],
                    in1=scanmask[:, :], op=ALU.add), waits=[("dve", t_[hp]["d1"])], inc="dve")
            for hp in range(2):
                t_[hp]["d3"] = p.op("scalar", lambda e, hp=hp: e.activation(
                    out=eD[:, hp * 128:(hp + 1) * 128], in_=Dm[:, hp * 128:(hp + 1) * 128],
                    func=AF.Exp), waits=[("dve", t_[hp]["d2"])], inc="act")
            for hp in range(2):
                t_[hp]["gt"] = p.op("tensor", lambda e, hp=hp, c=c: e.matmul(
                    out=PBG[hp][:, 0:128], lhsT=Bc[:, hp * T + c * QC:hp * T + (c + 1) * QC],
                    rhs=Cc[:, hp * T + c * QC:hp * T + (c + 1) * QC],
                    start=True, stop=True),
                    waits=[("act", bc_act[hp]), ("dve", p.count("dve"))], inc="pe")
            for hp in range(2):
                t_[hp]["pt"] = p.op("vector", lambda e, hp=hp: e.tensor_tensor(
                    out=PT[:, hp * 128:(hp + 1) * 128], in0=eD[:, hp * 128:(hp + 1) * 128],
                    in1=PBG[hp][:, 0:128], op=ALU.mult),
                    waits=[("act", t_[hp]["d3"]), ("pe", t_[hp]["gt"])], inc="dve")
            for hp in range(2):
                t_[hp]["dt1"] = p.op("tensor", lambda e, hp=hp, c=c: e.transpose(
                    out=PBA[hp][:, 0:32].bitcast(BF16),
                    in_=dtx[hp * 64:(hp + 1) * 64, c * QC:(c + 1) * QC],
                    identity=idb[hp * 64:(hp + 1) * 64, hp * 64:(hp + 1) * 64]),
                    waits=[("dve", dxv), ("dve", p.count("dve"))], inc="pe")
            for hp in range(2):
                t_[hp]["dv1"] = p.op("vector", lambda e, hp=hp: e.tensor_copy(
                    out=dtxT[:, hp * 64:(hp + 1) * 64], in_=PBA[hp][:, 0:32].bitcast(BF16)),
                    waits=[("pe", t_[hp]["dt1"])], inc="dve")
            for hp in range(2):
                t_[hp]["bt1"] = p.op("tensor", lambda e, hp=hp, c=c: e.transpose(
                    out=PBA[hp][:, 0:64].bitcast(BF16),
                    in_=Bc[:, hp * T + c * QC:hp * T + (c + 1) * QC],
                    identity=idb[:, :]), waits=[("dve", t_[hp]["dv1"])], inc="pe")
            for hp in range(2):
                col = c * 32 + 2 * g + hp
                t_[hp]["bv1"] = p.op("vector", lambda e, col=col, hp=hp: e.tensor_scalar(
                    out=BcTs[:, hp * 128:(hp + 1) * 128], in0=PBA[hp][:, 0:64].bitcast(BF16),
                    scalar1=wsT[:, col:col + 1], scalar2=None, op0=ALU.mult),
                    waits=[("pe", t_[hp]["bt1"])], inc="dve")
            for hp in range(2):
                t_[hp]["sm"] = p.op("tensor", lambda e, hp=hp: e.matmul(
                    out=PBS[hp][:, 0:64], lhsT=BcTs[:, hp * 128:(hp + 1) * 128],
                    rhs=dtxT[:, hp * 64:(hp + 1) * 64], start=True, stop=True),
                    waits=[("dve", t_[hp]["bv1"])], inc="pe")
            for hp in range(2):
                t_[hp]["ym1"] = p.op("tensor", lambda e, hp=hp: e.matmul(
                    out=PBY[hp][:, 0:64], lhsT=PT[:, hp * 128:(hp + 1) * 128],
                    rhs=dtxT[:, hp * 64:(hp + 1) * 64], start=True, stop=True),
                    waits=[("dve", t_[hp]["pt"])], inc="pe")
            if c > 0:
                for hp in range(2):
                    t_[hp]["ym2"] = p.op("tensor", lambda e, hp=hp, c=c: e.matmul(
                        out=PBG[hp][:, 0:64], lhsT=Cc[:, hp * T + c * QC:hp * T + (c + 1) * QC],
                        rhs=hTb[:, hp * 64:(hp + 1) * 64], start=True, stop=True),
                        waits=[("dve", p.count("dve")), ("act", cc_act[hp])], inc="pe")
                for hp in range(2):
                    col = c * 32 + 2 * g + hp
                    t_[hp]["yv1"] = p.op("vector", lambda e, col=col, hp=hp: e.tensor_scalar(
                        out=ytmp[:, hp * 64:(hp + 1) * 64], in0=PBG[hp][:, 0:64],
                        scalar1=eSl[:, col:col + 1], scalar2=None, op0=ALU.mult),
                        waits=[("pe", t_[hp]["ym2"])], inc="dve")
                for hp in range(2):
                    t_[hp]["yv2"] = p.op("vector", lambda e, c=c, hp=hp: e.tensor_tensor(
                        out=yg[:, c * 128 + hp * 64:c * 128 + (hp + 1) * 64],
                        in0=PBY[hp][:, 0:64], in1=ytmp[:, hp * 64:(hp + 1) * 64], op=ALU.add),
                        waits=[("dve", t_[hp]["yv1"]), ("pe", t_[hp]["ym1"])], inc="dve")
            else:
                for hp in range(2):
                    t_[hp]["yv2"] = p.op("vector", lambda e, c=c, hp=hp: e.tensor_copy(
                        out=yg[:, c * 128 + hp * 64:c * 128 + (hp + 1) * 64],
                        in_=PBY[hp][:, 0:64]),
                        waits=[("pe", t_[hp]["ym1"])] + ([ystw] if (hp == 0 and ystw) else []),
                        inc="dve")
            par = c % 2
            for hp in range(2):
                col = c * 32 + 2 * g + hp
                if c > 0:
                    t_[hp]["hv"] = p.op("vector", lambda e, hp=hp, col=col, par=par:
                        e.scalar_tensor_tensor(
                            out=hT[:, (hp * 2 + par) * 64:(hp * 2 + par + 1) * 64],
                            in0=hT[:, (hp * 2 + 1 - par) * 64:(hp * 2 + 1 - par + 1) * 64],
                            scalar=erep[:, col:col + 1], in1=PBS[hp][:, 0:64],
                            op0=ALU.mult, op1=ALU.add),
                        waits=[("pe", t_[hp]["sm"])], inc="dve")
                else:
                    t_[hp]["hv"] = p.op("vector", lambda e, hp=hp, par=par: e.tensor_copy(
                        out=hT[:, (hp * 2 + par) * 64:(hp * 2 + par + 1) * 64],
                        in_=PBS[hp][:, 0:64]),
                        waits=[("pe", t_[hp]["sm"])] + ([soutw[hp]] if soutw[hp] else []),
                        inc="dve")
            for hp in range(2):
                h = 2 * g + hp
                if c < NCH - 1:
                    p.op("vector", lambda e, hp=hp, par=par: e.tensor_copy(
                        out=hTb[:, hp * 64:(hp + 1) * 64],
                        in_=hT[:, (hp * 2 + par) * 64:(hp * 2 + par + 1) * 64]),
                        waits=[("dve", t_[hp]["hv"])], inc="dve")
                else:
                    soutw[hp] = p.dma("gpsimd", lambda e, h=h, hp=hp, par=par: e.dma_start(
                        out=S_out[h * 128:(h + 1) * 128, :],
                        in_=hT[:, (hp * 2 + par) * 64:(hp * 2 + par + 1) * 64]),
                        "l_sout%d" % hp, waits=[("dve", t_[hp]["hv"])])
        ystw = p.dma("gpsimd", lambda e, g=g: e.dma_start(
            out=y_raw.rearrange("(a p) hp -> p a hp", p=128)[:, :, g * 128:(g + 1) * 128],
            in_=yg[:, :].rearrange("p (a n) -> p a n", n=128)), "l_yst",
            waits=[("dve", p.count("dve"))])
    return [ystw]





def phase_l0b(cx, S_all, dect_all, y_raw, sz_tm, Cc_d, expSgT_d, x_fm, out_f32, out_bf):
    nc, p = cx.nc, cx.p
    TC = 512

    wc = cx.load_const("c_wcoef")
    jm = cx.load_const("c_jmask")
    crow = cx.sb("b_crow", [1, 128], F32)
    crep = cx.sb("b_crep", [128, 128], F32)
    coefsT = cx.sb("b_coefsT", [4, 32], F32)
    dect = cx.sb("b_dect", [4, 32], F32)
    sjt = cx.sb("b_sjt", [128, 2 * 64], F32)
    hacc = cx.sb("b_hacc", [128, 64], F32)
    hin = cx.sb("b_hin", [128, 32 * 64], BF16)
    eSg = cx.sb("b_eSg", [128, 512], F32)
    Cc_sb = cx.sb("b_Cc", [128, 2 * T], BF16)
    y_sb = cx.sb("b_y", [128, 16 * 128], BF16)
    sz_sb = cx.sb("b_sz", [128, 16 * 128], BF16)
    ybat = cx.sb("b_ybat", [128, 16 * 64], F32)
    sqb = cx.sb("b_sqb", [128, 16 * 64], F32)
    rsb = cx.sb("b_rsb", [128, 16], F32)
    ygc = cx.sb("b_ygc", [128, 64], BF16)
    ygfm = cx.sb("b_ygfm", [128, 16 * T], BF16)
    stg = cx.sb("b_stg", [128, 16 * TC], F32)
    wwo = cx.sb("b_wwo", [128, 2 * 2048], BF16)
    xres = cx.sb("b_xres", [128, 2 * TC], F32)
    sqt = cx.sb("b_sqt", [128, TC], F32)
    mrow = cx.sb("b_mrow", [1, TC], F32)
    vrow = cx.sb("b_vrow", [1, TC], F32)
    oout = cx.sb("b_oout", [128, TC], F32)
    obf = cx.sb("b_obf", [128, TC], BF16)
    idb = cx.sbuf["c_idbf"]
    ones1x = cx.sbuf["c_ones_1x128_f"]
    cb_dram = cx.dram_tmp("b_cb_d", [4, 32], F32)

    pb = [cx.pbank(i) for i in range(8)]
    bank = dict(pss=pb[3][0:1, :], psq=pb[4][0:1, :], psb1=pb[5], psb2=pb[6],
                sqt=sqt, mrow=mrow, vrow=vrow, oout=oout, obf=obf)

    ldd = p.dma("sync", lambda e: e.dma_start(
        out=dect[:, :], in_=dect_all.rearrange("(j h) o -> j (h o)", j=4)), "b_dect",
        waits=[("pe", p.count("pe"))])
    cmm = p.op("tensor", lambda e: e.matmul(out=pb[0][0:4, 0:32], lhsT=wc[:, :],
               rhs=dect[:, :], start=True, stop=True),
               waits=[ldd] + cx.consts_w + [("dve", p.count("dve"))], inc="pe")
    ce = p.op("scalar", lambda e: e.activation(out=coefsT[:, :], in_=pb[0][0:4, 0:32],
              func=AF.Exp), waits=[("pe", cmm)], inc="act")
    cm = p.op("vector", lambda e: e.tensor_scalar(out=coefsT[:, :], in0=coefsT[:, :],
              scalar1=jm[:, 0:1], scalar2=None, op0=ALU.mult), waits=[("act", ce)], inc="dve")
    cw_ = p.dma("gpsimd", lambda e: e.dma_start(out=cb_dram, in_=coefsT[:, :]), "b_cb",
                waits=[("dve", cm)])
    cr = p.dma("sync", lambda e: e.dma_start(
        out=crow[:, :], in_=cb_dram.rearrange("a b -> (a b)").unsqueeze(0)), "b_cr",
        waits=[cw_])
    cbm = p.op("tensor", lambda e: e.matmul(out=pb[0][:, 0:128], lhsT=ones1x[:, :],
               rhs=crow[:, :], start=True, stop=True),
               waits=[cr, ("act", p.count("act"))], inc="pe")
    crv = p.op("vector", lambda e: e.tensor_copy(out=crep[:, :], in_=pb[0][:, 0:128]),
               waits=[("pe", cbm)], inc="dve")
    hv = crv
    ldsw = [None, None]
    for h in range(32):
        for j in range(4):
            par = j % 2
            ldsw[par] = p.dma("sync", lambda e, h=h, j=j, par=par: e.dma_start(
                out=sjt[:, par * 64:(par + 1) * 64],
                in_=S_all[j * 4096 + h * 128:j * 4096 + (h + 1) * 128, :]),
                "b_sj%d" % par, waits=[("dve", p.count("dve"))])
            if j == 0:
                hv = p.op("vector", lambda e, h=h, par=par: e.tensor_scalar(
                    out=hacc[:, :], in0=sjt[:, par * 64:(par + 1) * 64],
                    scalar1=crep[:, h:h + 1], scalar2=None, op0=ALU.mult),
                    waits=[ldsw[par], ("dve", crv)], inc="dve")
            else:
                hv = p.op("vector", lambda e, h=h, j=j, par=par: e.scalar_tensor_tensor(
                    out=hacc[:, :], in0=sjt[:, par * 64:(par + 1) * 64],
                    scalar=crep[:, j * 32 + h:j * 32 + h + 1], in1=hacc[:, :],
                    op0=ALU.mult, op1=ALU.add), waits=[ldsw[par]], inc="dve")
        hv = p.op("vector", lambda e, h=h: e.tensor_copy(out=hin[:, h * 64:(h + 1) * 64],
                  in_=hacc[:, :]), waits=[("dve", hv)], inc="dve")
    hin_dve = p.count("dve")
    lde = p.dma("sync", lambda e: e.dma_start(out=eSg[:, :], in_=expSgT_d), "b_esg")

    ldy = ldz = ldc = None
    for g in range(16):
        pe_war = p.count("pe")
        dve_war = p.count("dve")
        ldc = p.dma("sync", lambda e, g=g: e.dma_start(
            out=Cc_sb[:, :].rearrange("p (a t) -> p a t", t=T),
            in_=Cc_d.rearrange("(a p) t -> p a t", p=128)[:, 2 * g:2 * g + 2, :]),
            "b_cc", waits=[("pe", pe_war)])
        ldy = p.dma("sync", lambda e, g=g: e.dma_start(
            out=y_sb[:, :].rearrange("p (a n) -> p a n", n=128),
            in_=y_raw.rearrange("(a p) hp -> p a hp", p=128)[:, :, g * 128:(g + 1) * 128]),
            "b_yld", waits=[("dve", dve_war)])
        ldz = p.dma("sync", lambda e, g=g: e.dma_start(
            out=sz_sb[:, :].rearrange("p (a n) -> p a n", n=128),
            in_=sz_tm.rearrange("(a p) hp -> p a hp", p=128)[:, :, g * 128:(g + 1) * 128]),
            "b_zld", waits=[("dve", dve_war)])
        for hp in range(2):
            h = 2 * g + hp
            cmm_prev = [0, 0]
            yb_dve = None
            for c in range(NCH):
                col = c * 32 + h
                par = c % 2
                cmm2 = p.op("tensor", lambda e, hp=hp, c=c, h=h, par=par: e.matmul(
                    out=pb[par][:, 0:64], lhsT=Cc_sb[:, hp * T + c * QC:hp * T + (c + 1) * QC],
                    rhs=hin[:, h * 64:(h + 1) * 64], start=True, stop=True),
                    waits=[ldc, ("dve", max(hin_dve, cmm_prev[par]))], inc="pe")
                yb_dve = p.op("vector", lambda e, col=col, c=c, hp=hp, par=par: e.scalar_tensor_tensor(
                    out=ybat[:, c * 64:(c + 1) * 64], in0=pb[par][:, 0:64],
                    scalar=eSg[:, col:col + 1],
                    in1=y_sb[:, c * 128 + hp * 64:c * 128 + (hp + 1) * 64],
                    op0=ALU.mult, op1=ALU.add),
                    waits=[("pe", cmm2), ldy, lde], inc="dve")
                cmm_prev[par] = yb_dve
            s1 = p.op("scalar", lambda e: e.activation(out=sqb[:, :], in_=ybat[:, :],
                      func=AF.Square), waits=[("dve", yb_dve)], inc="act")
            s2 = p.op("vector", lambda e: e.reduce_sum(
                out=rsb[:, :], in_=sqb[:, :].rearrange("p (c v) -> p c v", v=64),
                axis=mybir.AxisListType.X), waits=[("act", s1)], inc="dve")
            s3 = p.op("vector", lambda e: e.tensor_scalar(out=rsb[:, :], in0=rsb[:, :],
                      scalar1=0.015625, scalar2=1e-6, op0=ALU.mult, op1=ALU.add),
                      waits=[("dve", s2)], inc="dve")
            s4 = p.op("scalar", lambda e: e.activation(out=rsb[:, :], in_=rsb[:, :],
                      func=AF.Ln), waits=[("dve", s3)], inc="act")
            s5 = p.op("scalar", lambda e: e.activation(out=rsb[:, :], in_=rsb[:, :],
                      func=AF.Exp, scale=-0.5), waits=[("act", s4)], inc="act")
            tp_prev = 0
            for c in range(NCH):
                g1 = p.op("vector", lambda e, c=c: e.tensor_scalar(
                    out=ybat[:, c * 64:(c + 1) * 64], in0=ybat[:, c * 64:(c + 1) * 64],
                    scalar1=rsb[:, c:c + 1], scalar2=None, op0=ALU.mult),
                    waits=([("act", s5)] if c == 0 else []), inc="dve")
                g2 = p.op("vector", lambda e, c=c, hp=hp: e.tensor_tensor(
                    out=ygc[:, :], in0=ybat[:, c * 64:(c + 1) * 64],
                    in1=sz_sb[:, c * 128 + hp * 64:c * 128 + (hp + 1) * 64], op=ALU.mult),
                    waits=[("dve", g1), ldz, ("pe", tp_prev)], inc="dve")
                tp_prev = p.op("tensor", lambda e: e.transpose(
                    out=pb[2][0:64, 0:64].bitcast(BF16), in_=ygc[:, :], identity=idb[:, :]),
                    waits=[("dve", g2)] + cx.consts_w, inc="pe")
                p.op("vector", lambda e, g=g, hp=hp, c=c: e.tensor_copy(
                    out=ygfm[hp * 64:(hp + 1) * 64, g * T + c * QC:g * T + (c + 1) * QC],
                    in_=pb[2][0:64, 0:64].bitcast(BF16)), waits=[("pe", tp_prev)], inc="dve")
    yg_dve = p.count("dve")

    woutT = cx.dram["w_woutT"].rearrange("(a p) n -> p a n", p=128)
    wlast = None
    wuse = [0, 0]
    evac_dve = [yg_dve, yg_dve]
    for tc in range(4):
        wld = [None, None]
        for dt in range(16):
            par = dt % 2
            wld[par] = p.dma("gpsimd", lambda e, d=dt, par=par: e.dma_start(
                out=wwo[:, par * 2048:(par + 1) * 2048].rearrange("p (a n) -> p a n", n=128),
                in_=woutT[:, :, d * 128:(d + 1) * 128]), "b_wo%d" % par,
                waits=[("pe", wuse[par])])
            ldr = p.dma("sync", lambda e, d=dt, c=tc, par=par: e.dma_start(
                out=xres[:, par * TC:(par + 1) * TC],
                in_=x_fm[d * 128:(d + 1) * 128, 3 + c * TC:3 + (c + 1) * TC]),
                "b_xr%d" % par, waits=[("dve", evac_dve[par])])
            ps = pb[par]
            ype = None
            for kt in range(16):
                ype = p.op("tensor", lambda e, k=kt, c=tc, ps=ps, par=par: e.matmul(
                    out=ps[:, :],
                    lhsT=wwo[:, par * 2048 + k * 128:par * 2048 + (k + 1) * 128],
                    rhs=ygfm[:, k * T + c * TC:k * T + (c + 1) * TC],
                    start=(k == 0), stop=(k == 15)),
                    waits=([wld[par], ("dve", max(yg_dve, evac_dve[par]))] if kt == 0 else []),
                    inc=("pe" if kt == 15 else None))
            wuse[par] = ype
            evac_dve[par] = p.op("vector", lambda e, d=dt, ps=ps, par=par: e.tensor_tensor(
                out=stg[:, d * TC:(d + 1) * TC], in0=ps[:, :],
                in1=xres[:, par * TC:(par + 1) * TC], op=ALU.add),
                waits=[("pe", ype), ldr], inc="dve")
        wlast = emit_ln_store(cx, stg, TC, out_f32, out_bf, tc * TC, bank, "l0")
    return wlast



def run_phase_l1ffn(stream_f32_np, stream_bf_np, shared):
    nc = bass.Bass("TRN2", target_bir_lowering=False)
    names = ("c_blk2", "c_ones_128x1_f", "c_ones_1x128_f", "c_scalarpool",
             "w_upqT", "w_upsT", "w_bup", "w_upla", "w_uplb",
             "w_dnqT", "w_dnsT", "w_bdn", "w_dnla", "w_dnlb")
    with contextlib.ExitStack() as es:
        cx = Ctx(nc, es)
        for n in names:
            cx.dram_in(n, *SHAPES[n])
        sf = cx.dram_in("stream_f32", [D, T], F32)
        sb_ = cx.dram_in("stream_bf", [D, T], BF16)
        out = cx.dram_out("out_f32", [D, T], F32)
        for n in ("c_blk2", "c_ones_128x1_f", "c_ones_1x128_f", "c_scalarpool"):
            cx.load_const(n)
        cx.register_scalar_pool()
        phase_l1ffn(cx, sf, sb_, out)
        cx.p.barrier_wait("gpsimd")
        blk = es.enter_context(nc.Block())
        cx.p.emit(blk)
    in_maps = []
    for c in range(NCORE):
        m = {n: shared[n] for n in names}
        m["stream_f32"] = stream_f32_np[c]
        m["stream_bf"] = stream_bf_np[c]
        in_maps.append(m)
    res = run_bass_kernel_spmd(nc, in_maps, core_ids=list(range(NCORE)))
    return [r["out_f32"] for r in res.results]


MOE_NAMES = ("c_idf32", "c_ones_128x1_f", "c_ones_1x128_f", "c_scalarpool", "w_routerT",
             "w_moe1", "w_moe3", "w_moe2")


def run_phase_moe(stream_f32_np, stream_bf_np, shared):
    nc = bass.Bass("TRN2", target_bir_lowering=False)
    with contextlib.ExitStack() as es:
        cx = Ctx(nc, es)
        for n in MOE_NAMES:
            cx.dram_in(n, *SHAPES[n])
        sf = cx.dram_in("stream_f32", [D, T], F32)
        sb_ = cx.dram_in("stream_bf", [D, T], BF16)
        out = cx.dram_out("out_f32", [D, T], F32)
        outb = cx.dram_out("out_bf", [D, T], BF16)
        for n in ("c_idf32", "c_ones_128x1_f", "c_ones_1x128_f", "c_scalarpool"):
            cx.load_const(n)
        cx.register_scalar_pool()
        phase_moe(cx, sf, sb_, out, outb)
        cx.p.barrier_wait("gpsimd")
        blk = es.enter_context(nc.Block())
        cx.p.emit(blk)
    in_maps = []
    for c in range(NCORE):
        m = {n: shared[n] for n in MOE_NAMES}
        m["stream_f32"] = stream_f32_np[c]
        m["stream_bf"] = stream_bf_np[c]
        in_maps.append(m)
    res = run_bass_kernel_spmd(nc, in_maps, core_ids=list(range(NCORE)))
    return [r["out_f32"] for r in res.results]




QKV_NAMES = ("c_ones_128x1_f", "c_ones_1x128_f", "c_scalarpool",
             "w_qaT", "w_qab", "w_qnorm", "w_kvaT", "w_kvab", "w_kvnorm",
             "c_cosq", "c_sinq")
ATTN_NAMES = ("c_idbf", "c_ones_128x1_f", "c_ones_128x1_b", "c_ones_1x128_f",
              "c_scalarpool", "w_qbT", "w_qbb", "w_kvbT", "w_kvbb",
              "c_cosq", "c_sinq", "c_onesrow_b")
WO_NAMES = ("c_ones_128x1_f", "c_ones_1x128_f", "c_scalarpool", "w_woT", "w_wob")


def _run_simple_phase(build_fn, names, io_spec, in_maps_extra, shared, percore=None):
    nc = bass.Bass("TRN2", target_bir_lowering=False)
    with contextlib.ExitStack() as es:
        cx = Ctx(nc, es)
        for n in names:
            cx.dram_in(n, *SHAPES[n])
        for (n, shape, dtype, kind) in io_spec:
            if kind == "in":
                cx.dram_in(n, shape, dtype)
            else:
                cx.dram_out(n, shape, dtype)
        for n in names:
            if n.startswith("c_") and n not in ("c_cosq", "c_sinq", "c_onesrow_b"):
                cx.load_const(n)
        cx.register_scalar_pool()
        build_fn(cx)
        cx.p.barrier_wait("gpsimd")
        cx.p.barrier_wait("sync")
        blk = es.enter_context(nc.Block())
        cx.p.emit(blk)
    in_maps = []
    for c in range(NCORE):
        m = {}
        for n in names:
            m[n] = percore[c][n] if (percore is not None and n in percore[c]) else shared[n]
        m.update(in_maps_extra[c])
        in_maps.append(m)
    res = run_bass_kernel_spmd(nc, in_maps, core_ids=list(range(NCORE)))
    return res.results


def run_phase_l1qkv(stream_bf_np, shared, percore):
    def build(cx):
        phase_l1qkv(cx, cx.dram["stream_bf"], cx.dram["qn_d"], cx.dram["kvp_d"])
    io = [("stream_bf", [D, T], BF16, "in"),
          ("qn_d", [QLR, T], BF16, "out"), ("kvp_d", [576, T], BF16, "out")]
    extra = [{"stream_bf": stream_bf_np[c]} for c in range(NCORE)]
    res = _run_simple_phase(build, QKV_NAMES, io, extra, shared, percore)
    return [r["qn_d"] for r in res], [r["kvp_d"] for r in res]


def run_phase_l1attn(qn_np, kvall_np, shared, percore):
    def build(cx):
        phase_l1attn(cx, cx.dram["qn_d"], cx.dram["kvall_d"], cx.dram["attn_d"])
    io = [("qn_d", [QLR, T], BF16, "in"), ("kvall_d", [4 * 576, T], BF16, "in"),
          ("attn_d", [D, T], BF16, "out")]
    extra = [{"qn_d": qn_np[c], "kvall_d": kvall_np[c]} for c in range(NCORE)]
    res = _run_simple_phase(build, ATTN_NAMES, io, extra, shared, percore)
    return [r["attn_d"] for r in res]


def run_phase_l1wo(attn_np, stream_f32_np, shared):
    def build(cx):
        phase_l1wo(cx, cx.dram["attn_d"], cx.dram["stream_f32"],
                   cx.dram["out_f32"], cx.dram["out_bf"])
    io = [("attn_d", [D, T], BF16, "in"), ("stream_f32", [D, T], F32, "in"),
          ("out_f32", [D, T], F32, "out"), ("out_bf", [D, T], BF16, "out")]
    extra = [{"attn_d": attn_np[c], "stream_f32": stream_f32_np[c]} for c in range(NCORE)]
    res = _run_simple_phase(build, WO_NAMES, io, extra, shared)
    return [r["out_f32"] for r in res]




L0A_NAMES = ("c_idf32", "c_idbf", "c_ones_1x128_f", "c_ones_128x1_f", "c_scalarpool",
             "c_seldt", "c_scanmask", "w_winT", "w_alog", "w_dtb",
             "w_cwx", "w_cwb", "w_cwc")
L0B_NAMES = ("c_idbf", "c_ones_1x128_f", "c_ones_128x1_f", "c_scalarpool",
             "c_wcoef", "c_jmask", "w_woutT")


def _launch(build_fn, names, io_spec, extra, shared, percore):
    nc = bass.Bass("TRN2", target_bir_lowering=False)
    with contextlib.ExitStack() as es:
        cx = Ctx(nc, es)
        for n in names:
            cx.dram_in(n, *SHAPES[n])
        for (n, shape, dtype, kind) in io_spec:
            if kind == "in":
                cx.dram_in(n, shape, dtype)
            else:
                cx.dram_out(n, shape, dtype)
        for n in names:
            if n.startswith("c_") and n not in ("c_cosq", "c_sinq", "c_onesrow_b",
                                                "c_wcoef", "c_jmask"):
                cx.load_const(n)
        cx.register_scalar_pool()
        build_fn(cx)
        cx.p.barrier_wait("gpsimd")
        cx.p.barrier_wait("sync")
        blk = es.enter_context(nc.Block())
        cx.p.emit(blk)
    in_maps = []
    for c in range(NCORE):
        m = {}
        for n in names:
            m[n] = percore[c][n] if (percore is not None and n in percore[c]) else shared[n]
        m.update(extra[c])
        in_maps.append(m)
    res = run_bass_kernel_spmd(nc, in_maps, core_ids=list(range(NCORE)))
    return res.results




FUSED = True
PERSISTENT_CONSTS = ("c_idf32", "c_idbf", "c_ones_1x128_f", "c_ones_1x128_b",
                     "c_ones_128x1_b", "c_ones_128x1_f", "c_blk2", "c_scalarpool")
ARENA_BYTES = 210600


def kernel_fused(inputs):
    shared, percore = host_prep(inputs)
    nc = bass.Bass("TRN2", target_bir_lowering=False)
    allnames = set(PERSISTENT_CONSTS)
    for ns in (L0A_NAMES, L0B_NAMES, MOE_NAMES, QKV_NAMES, ATTN_NAMES, WO_NAMES):
        allnames |= set(ns)
    allnames |= {"w_upqT", "w_upsT", "w_bup", "w_upla", "w_uplb",
                 "w_dnqT", "w_dnsT", "w_bdn", "w_dnla", "w_dnlb"}
    percore_names = ("x_fm", "c_cosq", "c_sinq", "c_wcoef", "c_jmask")
    allnames |= set(percore_names)
    with contextlib.ExitStack() as es:
        cx = Ctx(nc, es)
        p = cx.p
        for n in sorted(allnames):
            cx.dram_in(n, *SHAPES[n])
        out = cx.dram_out("out_f32", [D, T], F32)
        y_raw = cx.dram_tmp("y_raw", [T, D], BF16)
        sz_tm = cx.dram_tmp("sz_tm", [T, D], BF16)
        Cc_d = cx.dram_tmp("Cc_d", [4096, T], BF16)
        S_out = cx.dram_tmp("S_out", [4096, 64], F32)
        dectot = cx.dram_tmp("dectot", [32, 1], F32)
        expSgT = cx.dram_tmp("expSgT", [128, 512], F32)
        S_all = cx.dram_tmp("S_all", [4 * 4096, 64], F32)
        dect_all = cx.dram_tmp("dect_all", [128, 1], F32)
        s1f = cx.dram_tmp("s1f", [D, T], F32)
        s1b = cx.dram_tmp("s1b", [D, T], BF16)
        s2f = cx.dram_tmp("s2f", [D, T], F32)
        s2b = cx.dram_tmp("s2b", [D, T], BF16)
        qn_d = cx.dram_tmp("qn_d", [QLR, T], BF16)
        kvp_d = cx.dram_tmp("kvp_d", [576, T], BF16)
        kvall_d = cx.dram_tmp("kvall_d", [4 * 576, T], BF16)
        attn_d = cx.dram_tmp("attn_d", [D, T], BF16)
        s3f = cx.dram_tmp("s3f", [D, T], F32)
        s3b = cx.dram_tmp("s3b", [D, T], BF16)

        for n in PERSISTENT_CONSTS:
            cx.load_const(n)
        cx.register_scalar_pool()
        cx.begin_arena(ARENA_BYTES)

        phase_l0a(cx, cx.dram["x_fm"], y_raw, sz_tm, Cc_d, S_out, dectot, expSgT)
        p.phase_barrier()
        cx.arena_reset()
        p.op("gpsimd", lambda e: e.collective_compute(
            "AllGather", ALU.bypass, replica_groups=[[0, 1, 2, 3], [4, 5, 6, 7]],
            ins=[S_out.opt()], outs=[S_all.opt()]), inc="cc")
        p.op("gpsimd", lambda e: e.collective_compute(
            "AllGather", ALU.bypass, replica_groups=[[0, 1, 2, 3], [4, 5, 6, 7]],
            ins=[dectot.opt()], outs=[dect_all.opt()]), inc="cc")
        p.phase_barrier()

        phase_l0b(cx, S_all, dect_all, y_raw, sz_tm, Cc_d, expSgT,
                  cx.dram["x_fm"], s1f, s1b)
        p.phase_barrier()
        cx.arena_reset()

        phase_moe(cx, s1f, s1b, s2f, s2b)
        p.phase_barrier()
        cx.arena_reset()

        phase_l1qkv(cx, s2b, qn_d, kvp_d)
        p.phase_barrier()
        cx.arena_reset()
        p.op("gpsimd", lambda e: e.collective_compute(
            "AllGather", ALU.bypass, replica_groups=[[0, 1, 2, 3], [4, 5, 6, 7]],
            ins=[kvp_d.opt()], outs=[kvall_d.opt()]), inc="cc")
        p.phase_barrier()

        phase_l1attn(cx, qn_d, kvall_d, attn_d)
        p.phase_barrier()
        cx.arena_reset()

        phase_l1wo(cx, attn_d, s2f, s3f, s3b)
        p.phase_barrier()
        cx.arena_reset()

        phase_l1ffn(cx, s3f, s3b, out)
        p.barrier_wait("gpsimd")
        p.barrier_wait("sync")
        nsem = len(p.sem_handles)
        blk = es.enter_context(nc.Block())
        p.emit(blk)
    in_maps = []
    for c in range(NCORE):
        m = {}
        for n in sorted(allnames):
            m[n] = percore[c][n] if n in percore[c] else shared[n]
        in_maps.append(m)
    res = run_bass_kernel_spmd(nc, in_maps, core_ids=list(range(NCORE)))
    outfm = [r["out_f32"] for r in res.results]
    outa = np.empty((B, S, D), np.float32)
    for c in range(NCORE):
        b, q = c // 4, c % 4
        outa[b, q * T:(q + 1) * T, :] = outfm[c].T
    return outa


_DBG = {}


def kernel(**inputs):
    if FUSED:
        return kernel_fused(inputs)
    shared, percore = host_prep(inputs)

    def bA(cx):
        phase_l0a(cx, cx.dram["x_fm"], cx.dram["y_raw"], cx.dram["sz_tm"],
                  cx.dram["Cc_d"], cx.dram["S_out"], cx.dram["dectot"], cx.dram["expSgT"])
    ioA = [("x_fm", [D, T + 3], F32, "in"),
           ("y_raw", [T, D], BF16, "out"), ("sz_tm", [T, D], BF16, "out"),
           ("Cc_d", [4096, T], BF16, "out"), ("S_out", [4096, 64], F32, "out"),
           ("dectot", [32, 1], F32, "out"), ("expSgT", [128, 512], F32, "out")]
    extA = [{"x_fm": percore[c]["x_fm"]} for c in range(NCORE)]
    rA = _launch(bA, L0A_NAMES, ioA, extA, shared, percore)
    _DBG["rA"] = rA

    S_alls, de_alls = [], []
    for c in range(NCORE):
        grp = range(0, 4) if c < 4 else range(4, 8)
        S_alls.append(np.ascontiguousarray(np.concatenate([rA[j]["S_out"] for j in grp], axis=0)))
        de_alls.append(np.ascontiguousarray(np.concatenate([rA[j]["dectot"] for j in grp], axis=0)))

    def bB(cx):
        phase_l0b(cx, cx.dram["S_all"], cx.dram["dect_all"], cx.dram["y_raw"],
                  cx.dram["sz_tm"], cx.dram["Cc_d"], cx.dram["expSgT"],
                  cx.dram["x_fm"], cx.dram["out_f32"], cx.dram["out_bf"])
    ioB = [("S_all", [4 * 4096, 64], F32, "in"), ("dect_all", [128, 1], F32, "in"),
           ("y_raw", [T, D], BF16, "in"), ("sz_tm", [T, D], BF16, "in"),
           ("Cc_d", [4096, T], BF16, "in"), ("expSgT", [128, 512], F32, "in"),
           ("x_fm", [D, T + 3], F32, "in"),
           ("out_f32", [D, T], F32, "out"), ("out_bf", [D, T], BF16, "out")]
    extB = [{"S_all": S_alls[c], "dect_all": de_alls[c], "y_raw": rA[c]["y_raw"],
             "sz_tm": rA[c]["sz_tm"], "Cc_d": rA[c]["Cc_d"], "expSgT": rA[c]["expSgT"],
             "x_fm": percore[c]["x_fm"]} for c in range(NCORE)]
    rB = _launch(bB, L0B_NAMES, ioB, extB, shared, percore)
    _DBG["rB"] = rB

    def bC(cx):
        phase_moe(cx, cx.dram["stream_f32"], cx.dram["stream_bf"],
                  cx.dram["out_f32"], cx.dram["out_bf"])
    ioC = [("stream_f32", [D, T], F32, "in"), ("stream_bf", [D, T], BF16, "in"),
           ("out_f32", [D, T], F32, "out"), ("out_bf", [D, T], BF16, "out")]
    extC = [{"stream_f32": rB[c]["out_f32"], "stream_bf": rB[c]["out_bf"]}
            for c in range(NCORE)]
    rC = _launch(bC, MOE_NAMES, ioC, extC, shared, percore)
    _DBG["rC"] = rC

    def bD(cx):
        phase_l1qkv(cx, cx.dram["stream_bf"], cx.dram["qn_d"], cx.dram["kvp_d"])
    ioD = [("stream_bf", [D, T], BF16, "in"),
           ("qn_d", [QLR, T], BF16, "out"), ("kvp_d", [576, T], BF16, "out")]
    extD = [{"stream_bf": rC[c]["out_bf"]} for c in range(NCORE)]
    rD = _launch(bD, QKV_NAMES, ioD, extD, shared, percore)
    _DBG["rD"] = rD

    kvalls = []
    for c in range(NCORE):
        grp = range(0, 4) if c < 4 else range(4, 8)
        kvalls.append(np.ascontiguousarray(np.concatenate([rD[j]["kvp_d"] for j in grp], axis=0)))

    def bE(cx):
        phase_l1attn(cx, cx.dram["qn_d"], cx.dram["kvall_d"], cx.dram["attn_d"])
    ioE = [("qn_d", [QLR, T], BF16, "in"), ("kvall_d", [4 * 576, T], BF16, "in"),
           ("attn_d", [D, T], BF16, "out")]
    extE = [{"qn_d": rD[c]["qn_d"], "kvall_d": kvalls[c]} for c in range(NCORE)]
    rE = _launch(bE, ATTN_NAMES, ioE, extE, shared, percore)
    _DBG["rE"] = rE

    def bF(cx):
        phase_l1wo(cx, cx.dram["attn_d"], cx.dram["stream_f32"],
                   cx.dram["out_f32"], cx.dram["out_bf"])
    ioF = [("attn_d", [D, T], BF16, "in"), ("stream_f32", [D, T], F32, "in"),
           ("out_f32", [D, T], F32, "out"), ("out_bf", [D, T], BF16, "out")]
    extF = [{"attn_d": rE[c]["attn_d"], "stream_f32": rC[c]["out_f32"]}
            for c in range(NCORE)]
    rF = _launch(bF, WO_NAMES, ioF, extF, shared, percore)
    _DBG["rF"] = rF

    def bG(cx):
        phase_l1ffn(cx, cx.dram["stream_f32"], cx.dram["stream_bf"], cx.dram["out_f32"])
    FFN_NAMES = ("c_blk2", "c_ones_128x1_f", "c_ones_1x128_f", "c_scalarpool",
                 "w_upqT", "w_upsT", "w_bup", "w_upla", "w_uplb",
                 "w_dnqT", "w_dnsT", "w_bdn", "w_dnla", "w_dnlb")
    ioG = [("stream_f32", [D, T], F32, "in"), ("stream_bf", [D, T], BF16, "in"),
           ("out_f32", [D, T], F32, "out")]
    extG = [{"stream_f32": rF[c]["out_f32"], "stream_bf": rF[c]["out_bf"]}
            for c in range(NCORE)]
    rG = _launch(bG, FFN_NAMES, ioG, extG, shared, percore)

    out = np.empty((B, S, D), np.float32)
    for c in range(NCORE):
        b, q = c // 4, c % 4
        out[b, q * T:(q + 1) * T, :] = rG[c]["out_f32"].T
    return out



# version 07
# speedup vs baseline: 1.0552x; 1.0143x over previous
import contextlib
import numpy as np
import ml_dtypes
import concourse.bass as bass
import concourse.mybir as mybir
from concourse.bass_utils import run_bass_kernel_spmd

F32 = mybir.dt.float32
BF16 = mybir.dt.bfloat16
I8 = mybir.dt.int8
AF = mybir.ActivationFunctionType
ALU = mybir.AluOpType
BF = ml_dtypes.bfloat16

D = 2048
T = 2048
B, S = 2, 8192
NCORE = 8
H, PP, NN, K4 = 32, 64, 128, 4
PROJ = 12320
QC = 128
NCH = T // QC
EXP, FF = 8, 512
NH, QLR, KVLR, DN, DR, DV = 16, 1536, 512, 128, 64, 128
DQK = DN + DR
HF, RLORA = 8192, 16
LNEPS = 1e-5

ENGS = ("sync", "gpsimd", "vector", "scalar", "tensor")

SCALAR_CONSTS = [1e-5, 1e-6, -0.5, 1.0, -1.0, -7.5, 0.07216878364870323, 0.5,
                 2.0, -40.0, 0.015625, 1.0 / 2048.0, 1.0 / 1536.0, 1.0 / 512.0,
                 1.0 / 64.0]


class Prog:

    def __init__(self, nc, es):
        self.nc = nc
        self.es = es
        self.steps = {e: [] for e in ENGS}
        self.sem_handles = {}
        self.sem_counts = {}

    def sem(self, name):
        if name not in self.sem_handles:
            self.sem_handles[name] = self.es.enter_context(self.nc.semaphore(name))
            self.sem_counts[name] = 0
        return name

    def count(self, name):
        self.sem(name)
        return self.sem_counts[name]

    def op(self, eng, fn, waits=(), inc=None, amt=1):
        w = [(s, v) for (s, v) in waits if v > 0]
        post = None
        if inc is not None:
            self.sem(inc)
            self.sem_counts[inc] += amt
            post = self.sem_counts[inc]
        self.steps[eng].append((fn, w, inc, amt if inc else 0))
        return post

    def dma(self, eng, fn, key, waits=()):
        s = "d_" + key
        v = self.op(eng, fn, waits=waits, inc=s, amt=16)
        return (s, v)

    def wait_only(self, eng, waits):
        self.steps[eng].append((None, [(s, v) for (s, v) in waits if v > 0], None, 0))

    def phase_barrier(self):
        for eng in ENGS:
            self.barrier_wait(eng)

    def barrier_wait(self, eng, sems=None):
        names = sems if sems is not None else list(self.sem_counts)
        self.steps[eng].append((None, [(s, self.sem_counts[s]) for s in names
                                       if self.sem_counts[s] > 0], None, 0))

    def emit(self, block):
        handles = self.sem_handles

        def mk(name):
            def run(e):
                for fn, waits, inc, amt in self.steps[name]:
                    for (s, v) in waits:
                        e.wait_ge(handles[s], v)
                    if fn is not None:
                        inst = fn(e)
                        if inc is not None:
                            inst.then_inc(handles[inc], amt)
            return run

        block.sync(mk("sync"))
        block.gpsimd(mk("gpsimd"))
        block.vector(mk("vector"))
        block.scalar(mk("scalar"))
        block.tensor(mk("tensor"))



def host_prep(inp):
    shared = {}
    shared["c_idf32"] = np.eye(128, dtype=np.float32)
    shared["c_idbf"] = np.eye(128, dtype=np.float32).astype(BF)
    shared["c_ones_1x128_f"] = np.ones((1, 128), np.float32)
    shared["c_ones_1x128_b"] = np.ones((1, 128), np.float32).astype(BF)
    shared["c_ones_128x1_b"] = np.ones((128, 1), np.float32).astype(BF)
    shared["c_ones_128x1_f"] = np.ones((128, 1), np.float32)
    jj, ii = np.meshgrid(np.arange(QC), np.arange(QC), indexing="ij")
    shared["c_scanmask"] = np.where(jj > ii, -40.0, 0.0).astype(np.float32)
    sel = np.zeros((32, 128 * 16), np.float32)
    for g in range(16):
        for p in range(128):
            sel[2 * g + p // 64, g * 128 + p] = 1.0
    shared["c_seldt"] = sel
    blk2 = np.zeros((2, 128), np.float32)
    blk2[0, 0:64] = 1.0
    blk2[1, 64:128] = 1.0
    shared["c_blk2"] = blk2
    shared["c_scalarpool"] = np.tile(np.array(SCALAR_CONSTS, np.float32), (128, 1))
    shared["c_onesrow_b"] = np.ones((1, S), np.float32).astype(BF)
    inv_freq = 1.0 / (10000.0 ** (np.arange(0, DR, 2, dtype=np.float64) / DR))
    t = np.arange(S, dtype=np.float64)
    emb = np.concatenate([np.outer(t, inv_freq)] * 2, axis=1)
    shared["c_cosT"] = np.ascontiguousarray(np.cos(emb).T.astype(np.float32))
    shared["c_sinT"] = np.ascontiguousarray(np.sin(emb).T.astype(np.float32))

    Win = np.asarray(inp["L0_attn_Win"])
    order = []
    for g in range(16):
        order += list(range(g * 128, g * 128 + 128))
        order += list(range(H * PP + g * 128, H * PP + g * 128 + 128))
        order += list(range(2 * H * PP + g * 256, 2 * H * PP + g * 256 + 256))
        order += list(range(2 * H * PP + H * NN + g * 256, 2 * H * PP + H * NN + g * 256 + 256))
    order += list(range(2 * H * PP + 2 * H * NN, PROJ))
    shared["w_winT"] = np.ascontiguousarray(Win[order, :].T)
    shared["w_woutT"] = np.ascontiguousarray(np.asarray(inp["L0_attn_Wout"]).T)
    shared["w_alog"] = np.asarray(inp["L0_attn_A_log"]).reshape(32, 1).astype(np.float32)
    shared["w_dtb"] = np.asarray(inp["L0_attn_dt_bias"]).reshape(32, 1).astype(np.float32)
    shared["w_cwx"] = np.ascontiguousarray(np.asarray(inp["L0_attn_conv_w_x"]).reshape(2048, 4))
    shared["w_cwb"] = np.ascontiguousarray(np.asarray(inp["L0_attn_conv_w_b"]).reshape(4096, 4))
    shared["w_cwc"] = np.ascontiguousarray(np.asarray(inp["L0_attn_conv_w_c"]).reshape(4096, 4))

    shared["w_routerT"] = np.ascontiguousarray(np.asarray(inp["L0_ffn_router_w"]).T)
    shared["w_moe1"] = np.ascontiguousarray(np.asarray(inp["L0_ffn_w1"]))
    shared["w_moe3"] = np.ascontiguousarray(np.asarray(inp["L0_ffn_w3"]))
    shared["w_moe2"] = np.ascontiguousarray(np.asarray(inp["L0_ffn_w2"]))

    shared["w_qaT"] = np.ascontiguousarray(np.asarray(inp["L1_attn_wq_a_w"]).T)
    shared["w_qab"] = np.ascontiguousarray(np.asarray(inp["L1_attn_wq_a_b"]).reshape(12, 128).T)
    shared["w_qnorm"] = np.ascontiguousarray(np.asarray(inp["L1_attn_q_norm_w"]).reshape(12, 128).T)
    shared["w_qbT"] = np.ascontiguousarray(np.asarray(inp["L1_attn_wq_b_w"]).T)
    qbb = np.zeros((128, 32), np.float32)
    _b = np.asarray(inp["L1_attn_wq_b_b"])
    for hh in range(16):
        qbb[:, 2 * hh] = _b[hh * 192:hh * 192 + 128]
        qbb[0:64, 2 * hh + 1] = _b[hh * 192 + 128:hh * 192 + 192]
    shared["w_qbb"] = np.ascontiguousarray(qbb)
    shared["w_kvaT"] = np.ascontiguousarray(np.asarray(inp["L1_attn_wkv_a_w"]).T)
    kvab = np.zeros((128, 5), np.float32)
    kvab[:, 0:4] = np.asarray(inp["L1_attn_wkv_a_b"])[0:512].reshape(4, 128).T
    kvab[0:64, 4] = np.asarray(inp["L1_attn_wkv_a_b"])[512:576]
    shared["w_kvab"] = np.ascontiguousarray(kvab)
    shared["w_kvnorm"] = np.ascontiguousarray(np.asarray(inp["L1_attn_kv_norm_w"]).reshape(4, 128).T)
    shared["w_kvbT"] = np.ascontiguousarray(np.asarray(inp["L1_attn_wkv_b_w"]).T)
    shared["w_kvbb"] = np.ascontiguousarray(np.asarray(inp["L1_attn_wkv_b_b"]).reshape(32, 128).T)
    shared["w_woT"] = np.ascontiguousarray(np.asarray(inp["L1_attn_wo_w"]).T)
    shared["w_wob"] = np.ascontiguousarray(np.asarray(inp["L1_attn_wo_b"]).reshape(16, 128).T)

    shared["w_upqT"] = np.ascontiguousarray(np.asarray(inp["L1_ffn_w_up_q"]).T.astype(np.int8))
    shared["w_upsT"] = np.ascontiguousarray(np.asarray(inp["L1_ffn_w_up_scale"]).T)
    shared["w_bup"] = np.ascontiguousarray(np.asarray(inp["L1_ffn_b_up"]).reshape(64, 128).T)
    shared["w_upla"] = np.ascontiguousarray(np.asarray(inp["L1_ffn_w_up_lora_a"]))
    shared["w_uplb"] = np.ascontiguousarray(np.asarray(inp["L1_ffn_w_up_lora_b"]))
    shared["w_dnqT"] = np.ascontiguousarray(np.asarray(inp["L1_ffn_w_down_q"]).T.astype(np.int8))
    shared["w_dnsT"] = np.ascontiguousarray(np.asarray(inp["L1_ffn_w_down_scale"]).T)
    shared["w_bdn"] = np.ascontiguousarray(np.asarray(inp["L1_ffn_b_down"]).reshape(16, 128).T)
    shared["w_dnla"] = np.ascontiguousarray(np.asarray(inp["L1_ffn_w_down_lora_a"]))
    shared["w_dnlb"] = np.ascontiguousarray(np.asarray(inp["L1_ffn_w_down_lora_b"]))

    x = np.asarray(inp["L0_attn_x"])
    percore = []
    for c in range(NCORE):
        b, q = c // 4, c % 4
        lo = q * T
        sl = x[b, max(lo - 3, 0):lo + T, :]
        if lo == 0:
            sl = np.concatenate([np.zeros((3, D), x.dtype), sl], axis=0)
        wc = np.zeros((4, 4), np.float32)
        jm = np.zeros((4, 1), np.float32)
        for j in range(4):
            if j < q:
                jm[j, 0] = 1.0
                for l in range(j + 1, q):
                    wc[l, j] = 1.0
        percore.append({
            "x_fm": np.ascontiguousarray(sl.T),
            "c_cosq": np.ascontiguousarray(shared["c_cosT"][:, lo:lo + T].astype(BF)),
            "c_sinq": np.ascontiguousarray(shared["c_sinT"][:, lo:lo + T].astype(BF)),
            "c_wcoef": wc, "c_jmask": jm,
        })
    return shared, percore


SHAPES = {
    "c_idf32": ([128, 128], F32), "c_idbf": ([128, 128], BF16),
    "c_ones_1x128_f": ([1, 128], F32), "c_ones_1x128_b": ([1, 128], BF16),
    "c_ones_128x1_b": ([128, 1], BF16), "c_ones_128x1_f": ([128, 1], F32),
    "c_scanmask": ([128, 128], F32), "c_seldt": ([32, 2048], F32),
    "c_blk2": ([2, 128], F32), "c_cosT": ([64, S], F32), "c_sinT": ([64, S], F32),
    "c_scalarpool": ([128, len(SCALAR_CONSTS)], F32),
    "w_winT": ([D, PROJ], F32), "w_woutT": ([D, D], F32),
    "w_alog": ([32, 1], F32), "w_dtb": ([32, 1], F32),
    "w_cwx": ([2048, 4], F32), "w_cwb": ([4096, 4], F32), "w_cwc": ([4096, 4], F32),
    "w_routerT": ([D, 8], F32),
    "w_moe1": ([EXP, D, FF], F32), "w_moe3": ([EXP, D, FF], F32), "w_moe2": ([EXP, FF, D], F32),
    "w_qaT": ([D, QLR], F32), "w_qab": ([128, 12], F32), "w_qnorm": ([128, 12], F32),
    "w_qbT": ([QLR, NH * DQK], F32), "w_qbb": ([128, 32], F32),
    "w_kvaT": ([D, KVLR + DR], F32), "w_kvab": ([128, 5], F32),
    "w_kvnorm": ([128, 4], F32),
    "w_kvbT": ([KVLR, NH * (DN + DV)], F32), "w_kvbb": ([128, 32], F32),
    "w_woT": ([D, D], F32), "w_wob": ([128, 16], F32),
    "w_upqT": ([D, HF], I8), "w_upsT": ([32, HF], F32), "w_bup": ([128, 64], F32),
    "w_upla": ([D, RLORA], F32), "w_uplb": ([RLORA, HF], F32),
    "w_dnqT": ([HF, D], I8), "w_dnsT": ([128, D], F32), "w_bdn": ([128, 16], F32),
    "w_dnla": ([HF, RLORA], F32), "w_dnlb": ([RLORA, D], F32),
    "x_fm": ([D, T + 3], F32),
    "c_cosq": ([64, T], BF16), "c_sinq": ([64, T], BF16),
    "c_wcoef": ([4, 4], F32), "c_jmask": ([4, 1], F32),
    "c_onesrow_b": ([1, S], BF16),
}


class Ctx:
    def __init__(self, nc, es):
        self.nc = nc
        self.es = es
        self.p = Prog(nc, es)
        self.dram = {}
        self.sbuf = {}
        self._cw = {"consts": ("d_consts", 0), "constsg": ("d_constsg", 0)}

    @property
    def consts_w(self):
        return [v for v in self._cw.values() if v[1] > 0]

    def dram_in(self, name, shape, dtype):
        ap = self.nc.dram_tensor(name, list(shape), dtype, kind="ExternalInput").ap()
        self.dram[name] = ap
        return ap

    def dram_out(self, name, shape, dtype):
        ap = self.nc.dram_tensor(name, list(shape), dtype, kind="ExternalOutput").ap()
        self.dram[name] = ap
        return ap

    def dram_tmp(self, name, shape, dtype):
        ap = self.nc.dram_tensor(name, list(shape), dtype, kind="Internal").ap()
        self.dram[name] = ap
        return ap

    def begin_arena(self, nbytes):
        self._arena = self.es.enter_context(
            self.nc.sbuf_tensor("ARENA", [128, nbytes], I8))
        self._aoff = 0
        self._arena_names = []

    def arena_reset(self):
        self._aoff = 0
        for n in self._arena_names:
            self.sbuf.pop(n, None)
        self._arena_names = []

    def sb(self, name, shape, dtype):
        if getattr(self, "_arena", None) is not None:
            size = {F32: 4, BF16: 2, I8: 1}[dtype]
            nb = shape[1] * size
            nb_al = (nb + 63) & ~63
            off = self._aoff
            self._aoff += nb_al
            assert self._aoff <= self._arena.shape[1], \
                "arena overflow at %s (%d)" % (name, self._aoff)
            v = self._arena[:, off:off + nb].bitcast(dtype)
            if shape[0] != 128:
                v = v[0:shape[0], :]
            self.sbuf[name] = v
            self._arena_names.append(name)
            return v
        h = self.es.enter_context(self.nc.sbuf_tensor(name, list(shape), dtype))
        self.sbuf[name] = h
        return h

    def ps(self, name, shape, dtype=F32):
        return self.es.enter_context(self.nc.psum_tensor(name, list(shape), dtype))

    def pbank(self, i):
        if not hasattr(self, "_pdbl"):
            self._pdbl = [self.ps("pd%d" % k, [128, 1024]) for k in range(4)]
        return self._pdbl[i // 2][:, (i % 2) * 512:(i % 2 + 1) * 512]

    def pdouble(self, j):
        self.pbank(0)
        return self._pdbl[j]

    def load_const(self, dname, cast=False):
        if dname in self.sbuf:
            return self.sbuf[dname]
        shape, dtype = SHAPES[dname]
        sbdt = dtype if not cast else BF16
        if getattr(self, "_arena", None) is not None:
            h = self.sb(dname, shape, sbdt)
            w0 = [("pe", self.p.count("pe")), ("dve", self.p.count("dve"))]
        else:
            h = self.es.enter_context(self.nc.sbuf_tensor("sb_" + dname, list(shape), sbdt))
            self.sbuf[dname] = h
            w0 = []
        eng = "gpsimd" if cast else "sync"
        key = "constsg" if cast else "consts"
        self._cw[key] = self.p.dma(eng, lambda e, d=h, a=self.dram[dname]:
                                   e.dma_start(out=d[:, :], in_=a), key, waits=w0)
        return h

    def register_scalar_pool(self):
        pool = self.sbuf["c_scalarpool"]
        for i, v in enumerate(SCALAR_CONSTS):
            self.nc.const_aps.register(F32, v, pool[:, i:i + 1])



def emit_ln_store(cx, stg, tqw, out_f32, out_bf, tcol0, bank, uid):
    p = cx.p
    pss, psq, psb1, psb2 = bank["pss"], bank["psq"], bank["psb1"], bank["psb2"]
    sqt, mrow, vrow, oout = bank["sqt"], bank["mrow"], bank["vrow"], bank["oout"]
    onesf = cx.sbuf["c_ones_128x1_f"]
    ones1x = cx.sbuf["c_ones_1x128_f"]
    stg_ready = [("dve", p.count("dve"))] + cx.consts_w
    prev_psq_mm = [0, 0]
    for dt in range(16):
        par = dt % 2
        tile = stg[:, dt * tqw:(dt + 1) * tqw]
        sq_h = sqt[:, par * tqw:(par + 1) * tqw]
        p.op("tensor", lambda e, t=tile, k=dt: e.matmul(
            out=pss[:, :], lhsT=onesf[:, :], rhs=t, start=(k == 0), stop=(k == 15)),
            waits=(stg_ready if dt == 0 else []), inc="pe")
        av = p.op("scalar", lambda e, t=tile, s=sq_h: e.activation(
            out=s, in_=t, func=AF.Square),
            waits=(stg_ready if dt == 0 else [("pe", prev_psq_mm[par])]), inc="act")
        prev_psq_mm[par] = p.op("tensor", lambda e, k=dt, s=sq_h: e.matmul(
            out=psq[:, :], lhsT=onesf[:, :], rhs=s, start=(k == 0), stop=(k == 15)),
            waits=[("act", av)], inc="pe")
    pe_stats = p.count("pe")
    mv = p.op("vector", lambda e: e.tensor_scalar(out=mrow[:, :], in0=pss[0:1, :],
              scalar1=1.0 / D, scalar2=None, op0=ALU.mult), waits=[("pe", pe_stats)], inc="dve")
    vv = p.op("vector", lambda e: e.tensor_scalar(out=vrow[:, :], in0=psq[0:1, :],
              scalar1=1.0 / D, scalar2=None, op0=ALU.mult), waits=[("pe", pe_stats)], inc="dve")
    m2 = p.op("scalar", lambda e: e.activation(out=sqt[0:1, 0:tqw], in_=mrow[:, :],
              func=AF.Square), waits=[("dve", mv)], inc="act")
    vv2 = p.op("vector", lambda e: e.tensor_tensor(out=vrow[:, :], in0=vrow[:, :],
               in1=sqt[0:1, 0:tqw], op=ALU.subtract), waits=[("act", m2), ("dve", vv)], inc="dve")
    l1 = p.op("scalar", lambda e: e.activation(out=vrow[:, :], in_=vrow[:, :],
              func=AF.Ln, bias=LNEPS), waits=[("dve", vv2)], inc="act")
    l2 = p.op("scalar", lambda e: e.activation(out=vrow[:, :], in_=vrow[:, :],
              func=AF.Exp, scale=-0.5), waits=[("act", l1)], inc="act")
    pb1 = p.op("tensor", lambda e: e.matmul(out=psb1[:, 0:tqw], lhsT=ones1x[:, :],
               rhs=mrow[:, :], start=True, stop=True), waits=[("dve", mv), ("act", l2)], inc="pe")
    pb2 = p.op("tensor", lambda e: e.matmul(out=psb2[:, 0:tqw], lhsT=ones1x[:, :],
               rhs=vrow[:, :], start=True, stop=True), waits=[("act", l2)], inc="pe")
    w1 = bank.get("last_w1")
    w2 = bank.get("last_w2")
    for dt in range(16):
        tile = stg[:, dt * tqw:(dt + 1) * tqw]
        gate = ([w1] if w1 else []) + ([w2] if w2 else [])
        sv = p.op("vector", lambda e, t=tile: e.tensor_tensor(
            out=sqt[:, 0:tqw], in0=t, in1=psb1[:, 0:tqw], op=ALU.subtract),
            waits=[("pe", pb1)] + gate, inc="dve")
        ov = p.op("vector", lambda e: e.tensor_tensor(
            out=oout[:, :], in0=sqt[:, 0:tqw], in1=psb2[:, 0:tqw], op=ALU.mult),
            waits=[("dve", sv), ("pe", pb2)], inc="dve")
        w1 = p.dma("gpsimd", lambda e, d=dt: e.dma_start(
            out=out_f32[d * 128:(d + 1) * 128, tcol0:tcol0 + tqw], in_=oout[:, :]),
            "st_" + uid + "_f", waits=[("dve", ov)])
        if out_bf is not None:
            bv = p.op("vector", lambda e: e.tensor_copy(out=bank["obf"][:, :], in_=oout[:, :]),
                      waits=[("dve", ov)], inc="dve")
            w2 = p.dma("gpsimd", lambda e, d=dt: e.dma_start(
                out=out_bf[d * 128:(d + 1) * 128, tcol0:tcol0 + tqw], in_=bank["obf"][:, :]),
                "st_" + uid + "_b", waits=[("dve", bv)])
    bank["last_w1"] = w1
    bank["last_w2"] = w2
    return [w1] + ([w2] if out_bf is not None else [])



def phase_l1ffn(cx, stream_f32, stream_bf, out_f32):
    nc, p = cx.nc, cx.p
    TQ = 512

    wup_bf = cx.dram_tmp("wup_bf", [D, HF], BF16)
    wdn_bf = cx.dram_tmp("wdn_bf", [HF, D], BF16)

    xs = cx.sb("xs", [128, 16 * TQ], BF16)
    x2 = cx.sb("x2", [128, 64 * TQ], BF16)
    stg = cx.sb("stg", [128, 16 * TQ], F32)
    wb1 = cx.sb("wb1", [128, 2 * 2048], BF16)
    wb2 = cx.sb("wb2", [128, 8192], BF16)
    xres = cx.sb("xres", [128, 2 * TQ], F32)
    labuf = cx.sb("labuf", [16, TQ], BF16)
    la2buf = cx.sb("la2buf", [16, TQ], BF16)
    sqt = cx.sb("sqt", [128, 2 * TQ], F32)
    mrow = cx.sb("mrow", [1, TQ], F32)
    vrow = cx.sb("vrow", [1, TQ], F32)
    oout = cx.sb("oout", [128, TQ], F32)
    dqf = cx.sb("dqf", [128, 512], F32)
    scbuf = cx.sb("scbuf", [2, 4096], F32)

    bup = cx.load_const("w_bup")
    bdn = cx.load_const("w_bdn")
    la_up = cx.sb("la_up", [128, 16 * 16], BF16)
    lb_up = cx.sb("lb_up", [16, HF], BF16)
    la_dn = cx.sb("la_dn", [128, 64 * 16], BF16)
    lb_dn = cx.sb("lb_dn", [16, D], BF16)
    p.dma("gpsimd", lambda e: e.dma_start(
        out=la_up[:, :].rearrange("p (a r) -> p a r", r=RLORA),
        in_=cx.dram["w_upla"].rearrange("(a p) r -> p a r", p=128)), "constsg")
    p.dma("gpsimd", lambda e: e.dma_start(out=lb_up[:, :], in_=cx.dram["w_uplb"]), "constsg")
    p.dma("gpsimd", lambda e: e.dma_start(
        out=la_dn[:, :].rearrange("p (a r) -> p a r", r=RLORA),
        in_=cx.dram["w_dnla"].rearrange("(a p) r -> p a r", p=128)), "constsg")
    cx._cw["constsg"] = p.dma("gpsimd", lambda e: e.dma_start(
        out=lb_dn[:, :], in_=cx.dram["w_dnlb"]), "constsg")

    blk2 = cx.sbuf["c_blk2"]

    psy0 = cx.pbank(0)
    psy1 = cx.pbank(1)
    psl = cx.pbank(2)[0:16, :]
    pss = cx.pbank(3)[0:1, :]
    psq = cx.pbank(4)[0:1, :]
    psb1 = cx.pbank(5)
    psb2 = cx.pbank(6)
    psc = cx.pbank(7)

    bank = dict(pss=pss, psq=psq, psb1=psb1, psb2=psb2, sqt=sqt, mrow=mrow,
                vrow=vrow, oout=oout, obf=None)

    q8v = [x2[:, 0:2048].bitcast(I8), x2[:, 2048:4096].bitcast(I8)]
    dqo = [x2[:, 4096:8192], x2[:, 8192:12288]]

    stw = {0: ("d_dq0", 0), 1: ("d_dq1", 0)}

    def dequant_matrix(qT, scT_dram, out_dram, nrow, ncol, tag):
        cw = 4096 if ncol % 4096 == 0 else ncol
        ncs = ncol // cw
        it = 0
        last_scmm = p.count("pe")
        for rt in range(nrow // 128):
            for cs in range(ncs):
                par = it % 2
                it += 1
                buf, obuf = q8v[par], dqo[par]
                ldw = p.dma("sync", lambda e, b=buf, r=rt, c=cs, cw=cw: e.dma_start(
                    out=b[:, 0:cw], in_=qT[r * 128:(r + 1) * 128, c * cw:(c + 1) * cw]),
                    "q8_%d" % par, waits=[stw[par], ("act", p.count("act"))])
                r0 = (rt * 128) // 64
                scw = p.dma("sync", lambda e, rr=r0, c=cs, cw=cw: e.dma_start(
                    out=scbuf[:, 0:cw], in_=scT_dram[rr:rr + 2, c * cw:(c + 1) * cw]),
                    "sc", waits=[("pe", last_scmm)])
                for j in range(cw // 512):
                    pev = p.op("tensor", lambda e, j=j: e.matmul(
                        out=psc[:, :], lhsT=blk2[:, :], rhs=scbuf[:, j * 512:(j + 1) * 512],
                        start=True, stop=True),
                        waits=[scw] + cx.consts_w + [("dve", p.count("dve"))], inc="pe")
                    acv = p.op("scalar", lambda e, b=buf, j=j: e.activation(
                        out=dqf[:, :], in_=b[:, j * 512:(j + 1) * 512], func=AF.Copy,
                        bias=-7.5, scale=1.0), waits=[ldw, ("dve", p.count("dve"))], inc="act")
                    p.op("vector", lambda e, o=obuf, j=j: e.tensor_tensor(
                        out=o[:, j * 512:(j + 1) * 512], in0=dqf[:, :], in1=psc[:, :], op=ALU.mult),
                        waits=[("pe", pev), ("act", acv)], inc="dve")
                last_scmm = p.count("pe")
                stw[par] = p.dma("gpsimd", lambda e, o=obuf, r=rt, c=cs, cw=cw: e.dma_start(
                    out=out_dram[r * 128:(r + 1) * 128, c * cw:(c + 1) * cw], in_=o[:, 0:cw]),
                    "dq%d" % par, waits=[("dve", p.count("dve"))])
        return [stw[0], stw[1]]

    deq_done = dequant_matrix(cx.dram["w_upqT"], cx.dram["w_upsT"], wup_bf, D, HF, "u")
    deq_done += dequant_matrix(cx.dram["w_dnqT"], cx.dram["w_dnsT"], wdn_bf, HF, D, "d")
    deq_dve = p.count("dve")

    last_pe_user_xs = 0
    wlast = None
    evac_dve2 = [deq_dve, deq_dve]
    for tq in range(4):
        tqs = slice(tq * TQ, (tq + 1) * TQ)
        ldx = p.dma("sync", lambda e, s=tqs: e.dma_start(
            out=xs[:, :].rearrange("p (a t) -> p a t", t=TQ),
            in_=stream_bf.rearrange("(a p) t -> p a t", p=128)[:, :, s]),
            "xs", waits=[("pe", last_pe_user_xs)] + (deq_done + [("dve", deq_dve)] if tq == 0 else []))

        lpe = None
        for kt in range(16):
            lpe = p.op("tensor", lambda e, k=kt: e.matmul(
                out=psl[:, :], lhsT=la_up[:, k * 16:(k + 1) * 16], rhs=xs[:, k * TQ:(k + 1) * TQ],
                start=(k == 0), stop=(k == 15)),
                waits=([ldx] + cx.consts_w + [("dve", p.count("dve"))] if kt == 0 else []),
                inc=("pe" if kt == 15 else None))
        lav = p.op("vector", lambda e: e.tensor_copy(out=labuf[:, :], in_=psl[:, :]),
                   waits=[("pe", lpe)], inc="dve")

        wld = [None, None]
        use_pe = [0, 0]
        use_act = [p.count("act"), p.count("act")]
        for ht in range(64):
            par = ht % 2
            wtile = wb1[:, par * 2048:(par + 1) * 2048]
            wld[par] = p.dma("sync", lambda e, w=wtile, h=ht: e.dma_start(
                out=w.rearrange("p (a h) -> p a h", h=128),
                in_=wup_bf.rearrange("(a p) h -> p a h", p=128)[:, :, h * 128:(h + 1) * 128]),
                "wb1_%d" % par, waits=[("pe", use_pe[par])])
            ps = psy0 if par == 0 else psy1
            for kt in range(16):
                p.op("tensor", lambda e, k=kt, w=wtile, ps=ps: e.matmul(
                    out=ps[:, :], lhsT=w[:, k * 128:(k + 1) * 128], rhs=xs[:, k * TQ:(k + 1) * TQ],
                    start=(k == 0), stop=False),
                    waits=([wld[par], ("act", use_act[par]), ("dve", lav)] if kt == 0 else []),
                    inc=None)
            pev = p.op("tensor", lambda e, h=ht, ps=ps: e.matmul(
                out=ps[:, :], lhsT=lb_up[:, h * 128:(h + 1) * 128], rhs=labuf[:, :],
                start=False, stop=True), inc="pe")
            use_pe[par] = pev
            use_act[par] = p.op("scalar", lambda e, h=ht, ps=ps: e.activation(
                out=x2[:, h * TQ:(h + 1) * TQ], in_=ps[:, :], func=AF.Relu,
                bias=bup[:, h:h + 1], scale=1.0), waits=[("pe", pev)], inc="act")
        y1_act = p.count("act")

        lpe2 = None
        for kt in range(64):
            lpe2 = p.op("tensor", lambda e, k=kt: e.matmul(
                out=psl[:, :], lhsT=la_dn[:, k * 16:(k + 1) * 16], rhs=x2[:, k * TQ:(k + 1) * TQ],
                start=(k == 0), stop=(k == 63)),
                waits=([("act", y1_act), ("dve", lav)] if kt == 0 else []),
                inc=("pe" if kt == 63 else None))
        la2v = p.op("vector", lambda e: e.tensor_copy(out=la2buf[:, :], in_=psl[:, :]),
                    waits=[("pe", lpe2)], inc="dve")

        half_done = 0
        grp_done = 0
        for dt in range(16):
            par = dt % 2
            wld2a = p.dma("sync", lambda e, d=dt: e.dma_start(
                out=wb2[:, 0:4096].rearrange("p (a h) -> p a h", h=128),
                in_=wdn_bf.rearrange("(a p) h -> p a h", p=128)[:, 0:32, d * 128:(d + 1) * 128]),
                "wb2a", waits=[("pe", half_done)] + (deq_done if (tq == 0 and dt == 0) else []))
            wld2b = p.dma("sync", lambda e, d=dt: e.dma_start(
                out=wb2[:, 4096:8192].rearrange("p (a h) -> p a h", h=128),
                in_=wdn_bf.rearrange("(a p) h -> p a h", p=128)[:, 32:64, d * 128:(d + 1) * 128]),
                "wb2b", waits=[("pe", grp_done)])
            ldr = p.dma("sync", lambda e, d=dt, s=tq, par=par: e.dma_start(
                out=xres[:, par * TQ:(par + 1) * TQ],
                in_=stream_f32[d * 128:(d + 1) * 128, s * TQ:(s + 1) * TQ]),
                "xres%d" % par, waits=[("dve", evac_dve2[par])])
            ps = psy0 if par == 0 else psy1
            for kt in range(64):
                inc = "pe" if kt == 31 else None
                v = p.op("tensor", lambda e, k=kt, ps=ps: e.matmul(
                    out=ps[:, :], lhsT=wb2[:, k * 128:(k + 1) * 128], rhs=x2[:, k * TQ:(k + 1) * TQ],
                    start=(k == 0), stop=False),
                    waits=([wld2a, ("dve", evac_dve2[par])] if kt == 0 else
                           ([wld2b] if kt == 32 else [])), inc=inc)
                if kt == 31:
                    half_done = v
            pev = p.op("tensor", lambda e, d=dt, ps=ps: e.matmul(
                out=ps[:, :], lhsT=lb_dn[:, d * 128:(d + 1) * 128], rhs=la2buf[:, :],
                start=False, stop=True), waits=[("dve", la2v)], inc="pe")
            grp_done = pev
            evac_dve2[par] = p.op("vector", lambda e, d=dt, ps=ps: e.scalar_tensor_tensor(
                out=stg[:, d * TQ:(d + 1) * TQ], in0=ps[:, :], scalar=bdn[:, d:d + 1],
                in1=xres[:, (d % 2) * TQ:((d % 2) + 1) * TQ], op0=ALU.add, op1=ALU.add),
                waits=[("pe", pev), ldr], inc="dve")
        last_pe_user_xs = p.count("pe")

        wlast = emit_ln_store(cx, stg, TQ, out_f32, None, tq * TQ, bank, "ffn")
    return wlast





def phase_moe(cx, stream_f32, stream_bf, out_f32, out_bf):
    nc, p = cx.nc, cx.p
    TW = 512

    g_dram = cx.dram_tmp("g_dram", [8, T], F32)

    xs = cx.sb("m_xs", [128, 16 * TW], BF16)
    Hp = cx.sb("m_Hp", [128, 32 * TW], BF16)
    stg = cx.sb("m_stg", [128, 16 * 512], F32)
    wq1 = cx.sb("m_wq1", [128, 2 * 2048], BF16)
    wq3 = cx.sb("m_wq3", [128, 2 * 2048], BF16)
    wq2 = cx.sb("m_wq2", [128, 2 * 4096], BF16)
    xres = cx.sb("m_xres", [128, 2 * 512], F32)
    gtm = cx.sb("m_gtm", [128, 64], F32)
    gfm = cx.sb("m_gfm", [8, TW], F32)
    grow = cx.sb("m_grow", [1, 8 * TW], F32)
    sc1 = cx.sb("m_sc1", [128, 8], F32)
    sc2 = cx.sb("m_sc2", [128, 8], F32)
    m1c = cx.sb("m_m1", [128, 4], F32)
    xsi = cx.sb("m_xsi", [128, 512], BF16)
    hh = cx.sb("m_hh", [128, 512], BF16)
    sqt = cx.sb("m_sqt", [128, 2 * 512], F32)
    mrow = cx.sb("m_mrow", [1, 512], F32)
    vrow = cx.sb("m_vrow", [1, 512], F32)
    oout = cx.sb("m_oout", [128, 512], F32)
    obf = cx.sb("m_obf", [128, 512], BF16)

    routT = cx.sb("m_routT", [128, 16 * 8], BF16)
    cx._cw["constsg"] = p.dma("gpsimd", lambda e: e.dma_start(
        out=routT[:, :].rearrange("p (a r) -> p a r", r=8),
        in_=cx.dram["w_routerT"].rearrange("(a p) r -> p a r", p=128)), "constsg")
    idf = cx.sbuf["c_idf32"]
    ones1x = cx.sbuf["c_ones_1x128_f"]

    pb = [cx.pbank(i) for i in range(8)]
    bank = dict(pss=pb[3][0:1, :], psq=pb[4][0:1, :], psb1=pb[5], psb2=pb[6],
                sqt=sqt, mrow=mrow, vrow=vrow, oout=oout, obf=obf)

    w1d = cx.dram["w_moe1"].rearrange("e (a p) f -> p e a f", p=128)
    w3d = cx.dram["w_moe3"].rearrange("e (a p) f -> p e a f", p=128)
    w2d = cx.dram["w_moe2"].rearrange("e (k p) d -> p e k d", p=128)

    wlast = None
    for th in range(T // TW):
        ths = slice(th * TW, (th + 1) * TW)
        ldx = p.dma("sync", lambda e, s=ths: e.dma_start(
            out=xs[:, :].rearrange("p (a t) -> p a t", t=TW),
            in_=stream_bf.rearrange("(a p) t -> p a t", p=128)[:, :, s]),
            "m_xs", waits=[("pe", p.count("pe"))])

        gtr_evac = None
        for tt in range(TW // 128):
            pr = pb[7][:, 0:8]
            lpe = None
            for kt in range(16):
                lpe = p.op("tensor", lambda e, k=kt, t=tt: e.matmul(
                    out=pr, lhsT=xs[:, k * TW + t * 128:k * TW + (t + 1) * 128],
                    rhs=routT[:, k * 8:(k + 1) * 8], start=(k == 0), stop=(k == 15)),
                    waits=([ldx] + cx.consts_w + [("dve", p.count("dve"))] if kt == 0 else []),
                    inc=("pe" if kt == 15 else None))
            c1 = p.op("vector", lambda e: e.tensor_copy(out=sc1[:, :], in_=pr),
                      waits=[("pe", lpe)], inc="dve")
            c2 = p.op("vector", lambda e: e.reduce_max(out=m1c[:, 0:1], in_=sc1[:, :],
                      axis=mybir.AxisListType.X), waits=[("dve", c1)], inc="dve")
            c3 = p.op("vector", lambda e: e.tensor_scalar(out=m1c[:, 1:2], in0=m1c[:, 0:1],
                      scalar1=-1.0, scalar2=None, op0=ALU.mult), waits=[("dve", c2)], inc="dve")
            c4 = p.op("vector", lambda e: e.tensor_scalar(out=sc2[:, :], in0=sc1[:, :],
                      scalar1=m1c[:, 0:1], scalar2=None, op0=ALU.is_ge),
                      waits=[("dve", c3)], inc="dve")
            c5 = p.op("vector", lambda e: e.scalar_tensor_tensor(out=sc2[:, :], in0=sc2[:, :],
                      scalar=-1e30, in1=sc1[:, :], op0=ALU.mult, op1=ALU.add),
                      waits=[("dve", c4)], inc="dve")
            c6 = p.op("vector", lambda e: e.reduce_max(out=m1c[:, 2:3], in_=sc2[:, :],
                      axis=mybir.AxisListType.X), waits=[("dve", c5)], inc="dve")
            a1 = p.op("scalar", lambda e: e.activation(out=sc2[:, :], in_=sc1[:, :],
                      func=AF.Exp, bias=m1c[:, 1:2], scale=1.0),
                      waits=[("dve", c6)], inc="act")
            c7 = p.op("vector", lambda e: e.tensor_scalar(out=sc1[:, :], in0=sc1[:, :],
                      scalar1=m1c[:, 2:3], scalar2=None, op0=ALU.is_ge),
                      waits=[("act", a1)], inc="dve")
            c8 = p.op("vector", lambda e: e.tensor_tensor(out=sc2[:, :], in0=sc2[:, :],
                      in1=sc1[:, :], op=ALU.mult), waits=[("dve", c7)], inc="dve")
            c9 = p.op("vector", lambda e: e.reduce_sum(out=m1c[:, 3:4], in_=sc2[:, :],
                      axis=mybir.AxisListType.X), waits=[("dve", c8)], inc="dve")
            c10 = p.op("vector", lambda e: e.reciprocal(out=m1c[:, 3:4], in_=m1c[:, 3:4]),
                       waits=[("dve", c9)], inc="dve")
            c11 = p.op("vector", lambda e, t=tt: e.tensor_scalar(out=gtm[:, t * 8:(t + 1) * 8],
                       in0=sc2[:, :], scalar1=m1c[:, 3:4], scalar2=None, op0=ALU.mult),
                       waits=[("dve", c10)], inc="dve")
            tpe = p.op("tensor", lambda e, t=tt: e.transpose(
                out=pb[6][0:8, 0:128], in_=gtm[:, t * 8:(t + 1) * 8], identity=idf[:, :]),
                waits=[("dve", c11)], inc="pe")
            gtr_evac = p.op("vector", lambda e, t=tt: e.tensor_copy(
                out=gfm[:, t * 128:(t + 1) * 128], in_=pb[6][0:8, 0:128]),
                waits=[("pe", tpe)], inc="dve")
        stg_w = p.dma("gpsimd", lambda e, h=th: e.dma_start(
            out=g_dram[:, h * TW:(h + 1) * TW], in_=gfm[:, :]),
            "m_g%d" % th, waits=[("dve", gtr_evac)])
        groww = None
        for ee in range(8):
            groww = p.dma("sync", lambda e, ee=ee, h=th: e.dma_start(
                out=grow[0:1, ee * TW:(ee + 1) * TW], in_=g_dram[ee:ee + 1, h * TW:(h + 1) * TW]),
                "m_grow%d" % th, waits=([stg_w, ("pe", p.count("pe"))] if ee == 0 else []))

        w1w = [None, None]
        w3w = [None, None]
        w_use = [0, 0]
        for e_i in range(EXP):
            gmm = p.op("tensor", lambda e, ei=e_i: e.matmul(
                out=pb[2][:, :], lhsT=ones1x[:, :],
                rhs=grow[0:1, ei * TW:ei * TW + 512],
                start=True, stop=True),
                waits=[groww, ("dve", p.count("dve"))], inc="pe")
            for mt in range(4):
                wpar = mt % 2
                w1w[wpar] = p.dma("gpsimd", lambda e, ei=e_i, m=mt, wpar=wpar: e.dma_start(
                    out=wq1[:, wpar * 2048:(wpar + 1) * 2048].rearrange("p (a f) -> p a f", f=128),
                    in_=w1d[:, ei, :, m * 128:(m + 1) * 128]),
                    "m_w1%d" % wpar, waits=[("pe", w_use[wpar])])
                w3w[wpar] = p.dma("gpsimd", lambda e, ei=e_i, m=mt, wpar=wpar: e.dma_start(
                    out=wq3[:, wpar * 2048:(wpar + 1) * 2048].rearrange("p (a f) -> p a f", f=128),
                    in_=w3d[:, ei, :, m * 128:(m + 1) * 128]),
                    "m_w3%d" % wpar)
                x1pe = x3pe = None
                actw = p.count("act")
                dvew = p.count("dve")
                for kt in range(16):
                    x1pe = p.op("tensor", lambda e, k=kt, wpar=wpar: e.matmul(
                        out=pb[0][:, :], lhsT=wq1[:, wpar * 2048 + k * 128:wpar * 2048 + (k + 1) * 128],
                        rhs=xs[:, k * TW:k * TW + 512],
                        start=(k == 0), stop=(k == 15)),
                        waits=([w1w[wpar], ("act", actw)] if kt == 0 else []),
                        inc=("pe" if kt == 15 else None))
                for kt in range(16):
                    x3pe = p.op("tensor", lambda e, k=kt, wpar=wpar: e.matmul(
                        out=pb[1][:, :], lhsT=wq3[:, wpar * 2048 + k * 128:wpar * 2048 + (k + 1) * 128],
                        rhs=xs[:, k * TW:k * TW + 512],
                        start=(k == 0), stop=(k == 15)),
                        waits=([w3w[wpar], ("dve", dvew)] if kt == 0 else []),
                        inc=("pe" if kt == 15 else None))
                w_use[wpar] = x3pe
                av = p.op("scalar", lambda e: e.activation(out=xsi[:, :], in_=pb[0][:, :],
                          func=AF.Silu), waits=[("pe", x1pe), ("dve", p.count("dve"))], inc="act")
                hv = p.op("vector", lambda e: e.tensor_tensor(out=hh[:, :], in0=xsi[:, :],
                          in1=pb[1][:, :], op=ALU.mult),
                          waits=[("act", av), ("pe", x3pe)], inc="dve")
                p.op("vector", lambda e, ei=e_i, m=mt: e.tensor_tensor(
                    out=Hp[:, (ei * 4 + m) * TW:(ei * 4 + m) * TW + 512],
                    in0=hh[:, :], in1=pb[2][:, :], op=ALU.mult),
                    waits=[("dve", hv), ("pe", gmm)], inc="dve")
        hp_dve = p.count("dve")

        for tc in range(TW // 512):
            w2w = [None, None]
            w2use = [0, 0]
            evac_dve = [p.count("dve"), p.count("dve")]
            for dt in range(16):
                par = dt % 2
                w2w[par] = p.dma("gpsimd", lambda e, d=dt, par=par: e.dma_start(
                    out=wq2[:, par * 4096:(par + 1) * 4096].rearrange("p (ek d) -> p ek d", d=128),
                    in_=w2d[:, :, :, d * 128:(d + 1) * 128].rearrange("p e k d -> p (e k) d")),
                    "m_w2%d" % par, waits=[("pe", w2use[par])])
                ldr = p.dma("sync", lambda e, d=dt, c=tc, h=th, par=par: e.dma_start(
                    out=xres[:, par * 512:(par + 1) * 512],
                    in_=stream_f32[d * 128:(d + 1) * 128, h * TW + c * 512:h * TW + (c + 1) * 512]),
                    "m_xr%d" % par, waits=[("dve", p.count("dve"))])
                ps = pb[0] if par == 0 else pb[1]
                ype = None
                act_war = p.count("act")
                for kt in range(32):
                    ype = p.op("tensor", lambda e, k=kt, c=tc, par=par, ps=ps: e.matmul(
                        out=ps[:, :], lhsT=wq2[:, par * 4096 + k * 128:par * 4096 + (k + 1) * 128],
                        rhs=Hp[:, k * TW + c * 512:k * TW + (c + 1) * 512],
                        start=(k == 0), stop=(k == 31)),
                        waits=([w2w[par], ("dve", max(hp_dve, evac_dve[par])),
                                ("act", act_war)] if kt == 0 else []),
                        inc=("pe" if kt == 31 else None))
                w2use[par] = ype
                evac_dve[par] = p.op("vector", lambda e, d=dt, ps=ps, par=par: e.tensor_tensor(
                    out=stg[:, d * 512:(d + 1) * 512], in0=ps[:, :],
                    in1=xres[:, par * 512:(par + 1) * 512], op=ALU.add),
                    waits=[("pe", ype), ldr], inc="dve")
            wlast = emit_ln_store(cx, stg, 512, out_f32, out_bf, th * TW + tc * 512, bank, "moe")
    return wlast





def rms_bcast(cx, psq_ap, inv_n_const, pbb, tag):
    p = cx.p
    vr = cx.sbuf["vrow_s"]
    w = psq_ap.shape[1]
    v1 = p.op("vector", lambda e: e.tensor_scalar(out=vr[:, 0:w], in0=psq_ap,
              scalar1=inv_n_const, scalar2=None, op0=ALU.mult),
              waits=[("pe", p.count("pe"))], inc="dve")
    a1 = p.op("scalar", lambda e: e.activation(out=vr[:, 0:w], in_=vr[:, 0:w],
              func=AF.Ln, bias=1e-6), waits=[("dve", v1)], inc="act")
    a2 = p.op("scalar", lambda e: e.activation(out=vr[:, 0:w], in_=vr[:, 0:w],
              func=AF.Exp, scale=-0.5), waits=[("act", a1)], inc="act")
    pbv = p.op("tensor", lambda e: e.matmul(out=pbb[:, 0:w], lhsT=cx.sbuf["c_ones_1x128_f"][:, :],
               rhs=vr[:, 0:w], start=True, stop=True),
               waits=[("act", a2), ("dve", p.count("dve"))], inc="pe")
    return pbv


def phase_l1qkv(cx, stream_bf, qn_dram, kvp_dram):
    nc, p = cx.nc, cx.p
    TC = 512

    xs = cx.sb("a_xs", [128, 16 * T], BF16)
    qraw = cx.sb("a_qraw", [128, 12 * TC], F32)
    kvraw = cx.sb("a_kvraw", [128, 5 * TC], F32)
    sq = cx.sb("a_sq", [128, TC], F32)
    vrow_s = cx.sb("vrow_s", [1, TC], F32)
    cx.sbuf["vrow_s"] = vrow_s
    obf = cx.sb("a_obf", [128, TC], BF16)
    rot = cx.sb("a_rot", [64, TC], F32)
    rop1 = cx.sb("a_rop1", [64, TC], F32)
    wqa = cx.sb("a_wqa", [128, 2 * 2048], BF16)
    wkva = cx.sb("a_wkva", [128, 2 * 2048], BF16)

    qab = cx.load_const("w_qab")
    qnw = cx.load_const("w_qnorm")
    kvab = cx.load_const("w_kvab")
    kvnw = cx.load_const("w_kvnorm")
    cosq = cx.load_const("c_cosq")
    sinq = cx.load_const("c_sinq")

    pb = [cx.pbank(i) for i in range(8)]

    ldx = p.dma("sync", lambda e: e.dma_start(
        out=xs[:, :].rearrange("p (a t) -> p a t", t=T),
        in_=stream_bf.rearrange("(a p) t -> p a t", p=128)), "a_xs",
        waits=[("pe", p.count("pe"))])

    qaT = cx.dram["w_qaT"].rearrange("(a p) n -> p a n", p=128)
    kvaT = cx.dram["w_kvaT"].rearrange("(a p) n -> p a n", p=128)

    obf_gate = None
    for tc in range(4):
        wq = [None, None]
        ev = [0, 0]
        wquse = [0, 0]
        for mt in range(12):
            par = mt % 2
            wq[par] = p.dma("gpsimd", lambda e, m=mt, par=par: e.dma_start(
                out=wqa[:, par * 2048:(par + 1) * 2048].rearrange("p (a n) -> p a n", n=128),
                in_=qaT[:, :, m * 128:(m + 1) * 128]), "a_wqa%d" % par,
                waits=[("pe", wquse[par])])
            ps = pb[par]
            qpe = None
            for kt in range(16):
                qpe = p.op("tensor", lambda e, k=kt, par=par, c=tc, ps=ps: e.matmul(
                    out=ps[:, :], lhsT=wqa[:, par * 2048 + k * 128:par * 2048 + (k + 1) * 128],
                    rhs=xs[:, k * T + c * TC:k * T + (c + 1) * TC],
                    start=(k == 0), stop=(k == 15)),
                    waits=([wq[par], ldx, ("act", ev[par])] + cx.consts_w if kt == 0 else []),
                    inc=("pe" if kt == 15 else None))
            wquse[par] = qpe
            ev[par] = p.op("scalar", lambda e, m=mt, ps=ps: e.activation(
                out=qraw[:, m * TC:(m + 1) * TC], in_=ps[:, :], func=AF.Identity,
                bias=qab[:, m:m + 1], scale=1.0), waits=[("pe", qpe)], inc="act")
        qraw_act = p.count("act")
        prevmm = 0
        for mt in range(12):
            av = p.op("scalar", lambda e, m=mt: e.activation(
                out=sq[:, :], in_=qraw[:, m * TC:(m + 1) * TC], func=AF.Square),
                waits=([("act", qraw_act), ("pe", prevmm), ("dve", p.count("dve"))]), inc="act")
            prevmm = p.op("tensor", lambda e, m=mt: e.matmul(
                out=pb[2][0:1, :], lhsT=cx.sbuf["c_ones_128x1_f"][:, :], rhs=sq[:, :],
                start=(m == 0), stop=(m == 11)), waits=[("act", av)], inc="pe")
        pbv = rms_bcast(cx, pb[2][0:1, 0:TC], 1.0 / 1536.0, pb[3], "q")
        for mt in range(12):
            ov = p.op("vector", lambda e, m=mt: e.scalar_tensor_tensor(
                out=obf[:, :], in0=qraw[:, m * TC:(m + 1) * TC], scalar=qnw[:, m:m + 1],
                in1=pb[3][:, 0:TC], op0=ALU.mult, op1=ALU.mult),
                waits=[("pe", pbv)] + ([obf_gate] if obf_gate else []), inc="dve")
            obf_gate = p.dma("gpsimd", lambda e, m=mt, c=tc: e.dma_start(
                out=qn_dram[m * 128:(m + 1) * 128, c * TC:(c + 1) * TC], in_=obf[:, :]),
                "a_qn", waits=[("dve", ov)])

        wk = [None, None]
        evk = [qraw_act, qraw_act]
        wkuse = [0, 0]
        for mt in range(5):
            par = mt % 2
            mw = 128 if mt < 4 else 64
            wk[par] = p.dma("gpsimd", lambda e, m=mt, par=par, mw=mw: e.dma_start(
                out=wkva[:, par * 2048:par * 2048 + 16 * mw].rearrange("p (a n) -> p a n", n=mw),
                in_=kvaT[:, :, m * 128:m * 128 + mw]), "a_wkva%d" % par,
                waits=[("pe", wkuse[par])])
            ps = pb[par]
            kpe_mm = None
            for kt in range(16):
                kpe_mm = p.op("tensor", lambda e, k=kt, par=par, c=tc, ps=ps, mw=mw: e.matmul(
                    out=ps[0:mw, 0:TC], lhsT=wkva[:, par * 2048 + k * mw:par * 2048 + (k + 1) * mw],
                    rhs=xs[:, k * T + c * TC:k * T + (c + 1) * TC],
                    start=(k == 0), stop=(k == 15)),
                    waits=([wk[par], ("act", evk[par])] if kt == 0 else []),
                    inc=("pe" if kt == 15 else None))
            wkuse[par] = kpe_mm
            evk[par] = p.op("scalar", lambda e, m=mt, ps=ps, mw=mw: e.activation(
                out=kvraw[0:mw, m * TC:(m + 1) * TC], in_=ps[0:mw, 0:TC], func=AF.Identity,
                bias=kvab[0:mw, m:m + 1], scale=1.0), waits=[("pe", kpe_mm)], inc="act")
        kvraw_act = p.count("act")
        prevmm = 0
        for mt in range(4):
            av = p.op("scalar", lambda e, m=mt: e.activation(
                out=sq[:, :], in_=kvraw[:, m * TC:(m + 1) * TC], func=AF.Square),
                waits=[("act", kvraw_act), ("pe", prevmm), ("dve", p.count("dve"))], inc="act")
            prevmm = p.op("tensor", lambda e, m=mt: e.matmul(
                out=pb[2][0:1, :], lhsT=cx.sbuf["c_ones_128x1_f"][:, :], rhs=sq[:, :],
                start=(m == 0), stop=(m == 3)), waits=[("act", av)], inc="pe")
        pbv = rms_bcast(cx, pb[2][0:1, 0:TC], 1.0 / 512.0, pb[3], "kv")
        for mt in range(4):
            ov = p.op("vector", lambda e, m=mt: e.scalar_tensor_tensor(
                out=obf[:, :], in0=kvraw[:, m * TC:(m + 1) * TC], scalar=kvnw[:, m:m + 1],
                in1=pb[3][:, 0:TC], op0=ALU.mult, op1=ALU.mult),
                waits=[("pe", pbv)] + ([obf_gate] if obf_gate else []), inc="dve")
            obf_gate = p.dma("gpsimd", lambda e, m=mt, c=tc: e.dma_start(
                out=kvp_dram[m * 128:(m + 1) * 128, c * TC:(c + 1) * TC], in_=obf[:, :]),
                "a_kv", waits=[("dve", ov)])
        kp = kvraw[0:64, 4 * TC:5 * TC]
        r1 = p.op("vector", lambda e: e.tensor_scalar(
            out=rot[0:32, :], in0=kvraw[32:64, 4 * TC:5 * TC], scalar1=-1.0, scalar2=None,
            op0=ALU.mult), waits=[("act", kvraw_act), ("dve", p.count("dve"))], inc="dve")
        r2 = p.op("vector", lambda e: e.tensor_copy(
            out=rot[32:64, :], in_=kvraw[0:32, 4 * TC:5 * TC]), waits=[("dve", r1)], inc="dve")
        r3 = p.op("vector", lambda e, c=tc: e.tensor_tensor(
            out=rop1[:, :], in0=kp, in1=cosq[:, c * TC:(c + 1) * TC], op=ALU.mult),
            waits=[("dve", r2)] + cx.consts_w, inc="dve")
        r4 = p.op("vector", lambda e, c=tc: e.tensor_tensor(
            out=rot[:, :], in0=rot[:, :], in1=sinq[:, c * TC:(c + 1) * TC], op=ALU.mult),
            waits=[("dve", r3)], inc="dve")
        r5 = p.op("vector", lambda e: e.tensor_tensor(
            out=obf[0:64, :], in0=rop1[:, :], in1=rot[:, :], op=ALU.add),
            waits=[("dve", r4)] + ([obf_gate] if obf_gate else []), inc="dve")
        obf_gate = p.dma("gpsimd", lambda e, c=tc: e.dma_start(
            out=kvp_dram[512:576, c * TC:(c + 1) * TC], in_=obf[0:64, :]),
            "a_kv", waits=[("dve", r5)])
    return [obf_gate]



def phase_l1attn(cx, qn_dram, kvall_dram, attn_dram):
    nc, p = cx.nc, cx.p
    TC = 512
    SCALE = 0.07216878364870323

    qn = cx.sb("t_qn", [128, 12 * T], BF16)
    kvn = cx.sb("t_kvn", [128, 4 * S], BF16)
    kB = cx.sb("t_kB", [65, S], BF16)
    kp2m = cx.sb("t_kp2m", [1, 1], F32)
    kA = cx.sb("t_kA", [128, S], BF16)
    VT = cx.sb("t_VT", [128, S], BF16)
    vtmp = cx.sb("t_vtmp", [128, TC], BF16)
    wqb_ = cx.sb("t_wqb", [128, 2 * 12 * 192], BF16)
    wkvb_ = cx.sb("t_wkvb", [128, 4 * 256], BF16)
    qA = cx.sb("t_qA", [128, T], BF16)
    qB = cx.sb("t_qB", [65, T], BF16)
    qpef = cx.sb("t_qpef", [64, TC], F32)
    rot = cx.sb("t_rot", [64, TC], F32)
    sq = cx.sb("t_sq", [128, TC], F32)
    kmaxr = cx.sb("t_kmaxr", [1, TC], F32)
    km2 = cx.sb("t_km2", [1, 1], F32)
    brow = cx.sb("t_brow", [1, TC], F32)
    PT = cx.sb("t_PT", [128, 2 * 1024], BF16)
    rb_sb = cx.sb("t_rb", [128, TC], BF16)
    oot = cx.sb("t_oot", [128, TC], BF16)
    cosq = cx.load_const("c_cosq")
    sinq = cx.load_const("c_sinq")
    qbb = cx.load_const("w_qbb")
    kvbb = cx.load_const("w_kvbb")
    idb = cx.sbuf["c_idbf"]
    onesf = cx.sbuf["c_ones_128x1_f"]
    ones64 = onesf[0:64, :]
    onesb = cx.sbuf["c_ones_128x1_b"]

    pb = [cx.pbank(i) for i in range(8)]

    ldq = p.dma("sync", lambda e: e.dma_start(
        out=qn[:, :].rearrange("p (a t) -> p a t", t=T),
        in_=qn_dram.rearrange("(a p) t -> p a t", p=128)), "t_qn",
        waits=[("pe", p.count("pe"))])
    ldkv = None
    for r in range(4):
        for kvt in range(4):
            ldkv = p.dma("sync", lambda e, r=r, k=kvt: e.dma_start(
                out=kvn[:, k * S + r * T:k * S + (r + 1) * T],
                in_=kvall_dram[r * 576 + k * 128:r * 576 + (k + 1) * 128, :]), "t_kvn",
                waits=([("pe", p.count("pe"))] if (r == 0 and kvt == 0) else []))
        ldkv = p.dma("sync", lambda e, r=r: e.dma_start(
            out=kB[0:64, r * T:(r + 1) * T],
            in_=kvall_dram[r * 576 + 512:r * 576 + 576, :]), "t_kvn")
    ldone = p.dma("sync", lambda e: e.dma_start(
        out=kB[64:65, :], in_=cx.dram["c_onesrow_b"]), "t_kvn")
    kvn_w = ldone
    kp2mm = None
    kmx = None
    for c in range(16):
        a1 = p.op("scalar", lambda e, c=c: e.activation(
            out=sq[0:64, :], in_=kB[0:64, c * TC:(c + 1) * TC], func=AF.Square),
            waits=[kvn_w, ("pe", kp2mm or 0)], inc="act")
        kp2mm = p.op("tensor", lambda e, c=c: e.matmul(
            out=pb[2][0:1, :], lhsT=ones64, rhs=sq[0:64, :], start=True, stop=True),
            waits=[("act", a1), ("dve", p.count("dve"))] + cx.consts_w, inc="pe")
        if c == 0:
            kmx = p.op("vector", lambda e: e.tensor_copy(out=kmaxr[:, :], in_=pb[2][0:1, :]),
                       waits=[("pe", kp2mm)], inc="dve")
        else:
            kmx = p.op("vector", lambda e: e.tensor_tensor(out=kmaxr[:, :], in0=kmaxr[:, :],
                       in1=pb[2][0:1, :], op=ALU.max), waits=[("pe", kp2mm)], inc="dve")
    kmx = p.op("vector", lambda e: e.reduce_max(out=kp2m[:, :], in_=kmaxr[:, :],
               axis=mybir.AxisListType.X), waits=[("dve", kmx)], inc="dve")
    kpe2_dve = p.count("dve")

    qbT = cx.dram["w_qbT"].rearrange("(a p) n -> p a n", p=128)
    kvbT = cx.dram["w_kvbT"].rearrange("(a p) n -> p a n", p=128)

    attn_st = None
    huse = [0, 0]
    for h in range(NH):
        hp_ = h % 2
        wqb = wqb_[:, hp_ * 2304:(hp_ + 1) * 2304]
        wkvb = wkvb_
        wqw = p.dma("gpsimd", lambda e, h=h, w=wqb: e.dma_start(
            out=w.rearrange("p (a n) -> p a n", n=192),
            in_=qbT[:, :, h * 192:(h + 1) * 192]), "t_wqb%d" % hp_,
            waits=[("pe", huse[hp_])])
        wkw = p.dma("gpsimd", lambda e, h=h, w=wkvb: e.dma_start(
            out=w.rearrange("p (a n) -> p a n", n=256),
            in_=kvbT[:, :, h * 256:(h + 1) * 256]), "t_wkvb",
            waits=[("pe", huse[1 - hp_])])

        kmax_dve = None
        for c in range(16):
            kmm = None
            act_war = p.count("act")
            for a in range(4):
                kmm = p.op("tensor", lambda e, a=a, c=c, wkvb=wkvb: e.matmul(
                    out=pb[0][:, :], lhsT=wkvb[:, a * 256:a * 256 + 128],
                    rhs=kvn[:, a * S + c * TC:a * S + (c + 1) * TC],
                    start=(a == 0), stop=(a == 3)),
                    waits=([wkw, kvn_w, ("act", act_war), ("dve", kpe2_dve)] if a == 0 else []),
                    inc=("pe" if a == 3 else None))
            av = p.op("scalar", lambda e, c=c, h=h: e.activation(
                out=kA[:, c * TC:(c + 1) * TC], in_=pb[0][:, :], func=AF.Identity,
                bias=kvbb[:, 2 * h:2 * h + 1], scale=1.0), waits=[("pe", kmm)], inc="act")
            a2 = p.op("scalar", lambda e, c=c: e.activation(
                out=sq[:, :], in_=kA[:, c * TC:(c + 1) * TC], func=AF.Square),
                waits=[("act", av), ("pe", p.count("pe"))], inc="act")
            nmm = p.op("tensor", lambda e: e.matmul(
                out=pb[2][0:1, :], lhsT=onesf[:, :], rhs=sq[:, :], start=True, stop=True),
                waits=[("act", a2), ("dve", p.count("dve"))] + cx.consts_w, inc="pe")
            if c == 0:
                kmax_dve = p.op("vector", lambda e: e.tensor_copy(out=kmaxr[:, :],
                                in_=pb[2][0:1, :]), waits=[("pe", nmm)], inc="dve")
            else:
                kmax_dve = p.op("vector", lambda e: e.tensor_tensor(out=kmaxr[:, :],
                                in0=kmaxr[:, :], in1=pb[2][0:1, :], op=ALU.max),
                                waits=[("pe", nmm)], inc="dve")
            vmm = None
            for a in range(4):
                vmm = p.op("tensor", lambda e, a=a, c=c, wkvb=wkvb: e.matmul(
                    out=pb[1][:, :], lhsT=wkvb[:, a * 256 + 128:a * 256 + 256],
                    rhs=kvn[:, a * S + c * TC:a * S + (c + 1) * TC],
                    start=(a == 0), stop=(a == 3)),
                    waits=([("dve", p.count("dve"))] if a == 0 else []),
                    inc=("pe" if a == 3 else None))
            vv = p.op("scalar", lambda e, c=c, h=h: e.activation(
                out=vtmp[:, :], in_=pb[1][:, :], func=AF.Identity,
                bias=kvbb[:, 2 * h + 1:2 * h + 2], scale=1.0), waits=[("pe", vmm)], inc="act")
            for s4 in range(4):
                tmm = p.op("tensor", lambda e, s4=s4: e.transpose(
                    out=pb[3][:, 0:64].bitcast(BF16), in_=vtmp[:, s4 * 128:(s4 + 1) * 128],
                    identity=idb[:, :]),
                    waits=[("act", vv), ("dve", p.count("dve"))] + cx.consts_w, inc="pe")
                p.op("vector", lambda e, c=c, s4=s4: e.tensor_copy(
                    out=VT[:, (c * 4 + s4) * 128:(c * 4 + s4 + 1) * 128],
                    in_=pb[3][:, 0:64].bitcast(BF16)), waits=[("pe", tmm)], inc="dve")
        km = p.op("vector", lambda e: e.reduce_max(out=km2[:, :], in_=kmaxr[:, :],
                  axis=mybir.AxisListType.X), waits=[("dve", kmax_dve)], inc="dve")
        km = p.op("vector", lambda e: e.tensor_tensor(out=km2[:, :], in0=km2[:, :],
                  in1=kp2m[:, :], op=ALU.add), waits=[("dve", km)], inc="dve")

        on_prev = 0
        for c in range(4):
            qmm = None
            for a in range(12):
                qmm = p.op("tensor", lambda e, a=a, c=c, wqb=wqb: e.matmul(
                    out=pb[0][:, :], lhsT=wqb[:, a * 192:a * 192 + 128],
                    rhs=qn[:, a * T + c * TC:a * T + (c + 1) * TC],
                    start=(a == 0), stop=(a == 11)),
                    waits=([wqw, ldq, ("act", p.count("act")), ("dve", p.count("dve"))]
                           if a == 0 else []), inc=("pe" if a == 11 else None))
            qav = p.op("scalar", lambda e, c=c, h=h: e.activation(
                out=qA[:, c * TC:(c + 1) * TC], in_=pb[0][:, :], func=AF.Identity,
                bias=qbb[:, 2 * h:2 * h + 1], scale=1.0), waits=[("pe", qmm)], inc="act")
            qpmm = None
            for a in range(12):
                qpmm = p.op("tensor", lambda e, a=a, c=c, wqb=wqb: e.matmul(
                    out=pb[1][0:64, :], lhsT=wqb[:, a * 192 + 128:a * 192 + 192],
                    rhs=qn[:, a * T + c * TC:a * T + (c + 1) * TC],
                    start=(a == 0), stop=(a == 11)),
                    waits=([("dve", p.count("dve"))] if a == 0 else []),
                    inc=("pe" if a == 11 else None))
            qpv = p.op("scalar", lambda e, c=c, h=h: e.activation(
                out=qpef[:, :], in_=pb[1][0:64, :], func=AF.Identity,
                bias=qbb[0:64, 2 * h + 1:2 * h + 2], scale=1.0), waits=[("pe", qpmm)], inc="act")
            r1 = p.op("vector", lambda e: e.tensor_scalar(out=rot[0:32, :], in0=qpef[32:64, :],
                      scalar1=-1.0, scalar2=None, op0=ALU.mult), waits=[("act", qpv)], inc="dve")
            r2 = p.op("vector", lambda e: e.tensor_copy(out=rot[32:64, :], in_=qpef[0:32, :]),
                      waits=[("dve", r1)], inc="dve")
            r3 = p.op("vector", lambda e, c=c: e.tensor_tensor(out=qpef[:, :], in0=qpef[:, :],
                      in1=cosq[:, c * TC:(c + 1) * TC], op=ALU.mult),
                      waits=[("dve", r2)] + cx.consts_w, inc="dve")
            r4 = p.op("vector", lambda e, c=c: e.tensor_tensor(out=rot[:, :], in0=rot[:, :],
                      in1=sinq[:, c * TC:(c + 1) * TC], op=ALU.mult), waits=[("dve", r3)], inc="dve")
            r5 = p.op("vector", lambda e, c=c: e.tensor_tensor(out=qB[0:64, c * TC:(c + 1) * TC],
                      in0=qpef[:, :], in1=rot[:, :], op=ALU.add), waits=[("dve", r4)], inc="dve")
            s1 = p.op("scalar", lambda e, c=c: e.activation(out=sq[:, :],
                      in_=qA[:, c * TC:(c + 1) * TC], func=AF.Square),
                      waits=[("act", qav), ("pe", p.count("pe"))], inc="act")
            b1 = p.op("tensor", lambda e: e.matmul(out=pb[2][0:1, :], lhsT=onesf[:, :],
                      rhs=sq[:, :], start=True, stop=False),
                      waits=[("act", s1)] + cx.consts_w, inc="pe")
            s2 = p.op("scalar", lambda e, c=c: e.activation(out=sq[0:64, :],
                      in_=qB[0:64, c * TC:(c + 1) * TC], func=AF.Square),
                      waits=[("dve", r5), ("pe", b1)], inc="act")
            b2 = p.op("tensor", lambda e: e.matmul(out=pb[2][0:1, :], lhsT=ones64,
                      rhs=sq[0:64, :], start=False, stop=True), waits=[("act", s2)], inc="pe")
            b3 = p.op("vector", lambda e: e.tensor_scalar(out=brow[:, :], in0=pb[2][0:1, :],
                      scalar1=km2[0:1, 0:1], scalar2=None, op0=ALU.mult),
                      waits=[("pe", b2), ("dve", km)], inc="dve")
            b4 = p.op("scalar", lambda e: e.activation(out=brow[:, :], in_=brow[:, :],
                      func=AF.Ln, bias=1e-6), waits=[("dve", b3)], inc="act")
            b5 = p.op("scalar", lambda e, c=c: e.activation(out=qB[64:65, c * TC:(c + 1) * TC],
                      in_=brow[:, :], func=AF.Exp, scale=0.5), waits=[("act", b4)], inc="act")
            b6 = p.op("vector", lambda e, c=c: e.tensor_scalar(
                out=qB[64:65, c * TC:(c + 1) * TC], in0=qB[64:65, c * TC:(c + 1) * TC],
                scalar1=-1.0, scalar2=None, op0=ALU.mult), waits=[("act", b5)], inc="dve")

            expw = [0, 0]
            opv = None
            dnm = None
            for kt2 in range(32):
                par = kt2 % 2
                std = cx.pdouble(par)
                mm1 = mm2 = None
                for half in range(2):
                    kt = kt2 * 2 + half
                    tgt = std[:, half * 512:(half + 1) * 512]
                    mm1 = p.op("tensor", lambda e, kt=kt, c=c, tgt=tgt: e.matmul(
                        out=tgt, lhsT=kA[:, kt * 128:(kt + 1) * 128],
                        rhs=qA[:, c * TC:(c + 1) * TC], start=True, stop=False),
                        waits=([("act", expw[par]), ("dve", b6)] if half == 0 else []),
                        inc=None)
                    mm2 = p.op("tensor", lambda e, kt=kt, c=c, tgt=tgt: e.matmul(
                        out=tgt, lhsT=kB[:, kt * 128:(kt + 1) * 128],
                        rhs=qB[:, c * TC:(c + 1) * TC], start=False, stop=True),
                        inc=("pe" if half == 1 else None))
                expw[par] = p.op("scalar", lambda e, par=par, std=std: e.activation(
                    out=PT[:, par * 1024:(par + 1) * 1024], in_=std[:, 0:1024],
                    func=AF.Exp, scale=SCALE), waits=[("pe", mm2)], inc="act")
                for half in range(2):
                    kt = kt2 * 2 + half
                    opv = p.op("tensor", lambda e, kt=kt, par=par, half=half: e.matmul(
                        out=pb[4][:, :], lhsT=VT[:, kt * 128:(kt + 1) * 128],
                        rhs=PT[:, par * 1024 + half * 512:par * 1024 + (half + 1) * 512],
                        start=(kt == 0), stop=(kt == 63)),
                        waits=(([("act", expw[par]), ("dve", on_prev)]) if half == 0 else []),
                        inc=("pe" if kt == 63 else None))
                    dnm = p.op("tensor", lambda e, kt=kt, par=par, half=half: e.matmul(
                        out=pb[5][0:1, :], lhsT=onesb[:, :],
                        rhs=PT[:, par * 1024 + half * 512:par * 1024 + (half + 1) * 512],
                        start=(kt == 0), stop=(kt == 63)),
                        waits=((cx.consts_w + [("dve", on_prev)]) if kt == 0 else []),
                        inc=("pe" if kt == 63 else None))
            rc = p.op("vector", lambda e: e.reciprocal(out=brow[:, :], in_=pb[5][0:1, :]),
                      waits=[("pe", dnm)], inc="dve")
            rbc = p.op("tensor", lambda e: e.matmul(out=pb[6][:, :],
                       lhsT=cx.sbuf["c_ones_1x128_f"][:, :], rhs=brow[:, :],
                       start=True, stop=True), waits=[("dve", rc)] + cx.consts_w, inc="pe")
            rv = p.op("vector", lambda e: e.tensor_copy(out=rb_sb[:, :], in_=pb[6][:, :]),
                      waits=[("pe", rbc)], inc="dve")
            on = p.op("vector", lambda e: e.tensor_tensor(out=oot[:, :], in0=rb_sb[:, :],
                      in1=pb[4][:, :], op=ALU.mult),
                      waits=[("dve", rv), ("pe", opv)] + ([attn_st] if attn_st else []), inc="dve")
            on_prev = on
            attn_st = p.dma("gpsimd", lambda e, h=h, c=c: e.dma_start(
                out=attn_dram[h * 128:(h + 1) * 128, c * TC:(c + 1) * TC], in_=oot[:, :]),
                "t_attn", waits=[("dve", on)])
        huse[hp_] = p.count("pe")
    return [attn_st]



def phase_l1wo(cx, attn_dram, stream_f32, out_f32, out_bf):
    nc, p = cx.nc, cx.p
    TC = 512
    av_ = cx.sb("w_av", [128, 16 * TC], BF16)
    stg = cx.sb("w_stg", [128, 16 * TC], F32)
    wwo = cx.sb("w_wwo", [128, 2 * 2048], BF16)
    xres = cx.sb("w_xres", [128, 2 * TC], F32)
    sqt = cx.sb("w_sqt", [128, 2 * TC], F32)
    mrow = cx.sb("w_mrow", [1, TC], F32)
    vrow = cx.sb("w_vrow", [1, TC], F32)
    oout = cx.sb("w_oout", [128, TC], F32)
    obf = cx.sb("w_obf", [128, TC], BF16)
    wob = cx.load_const("w_wob")
    pb = [cx.pbank(i) for i in range(8)]
    bank = dict(pss=pb[3][0:1, :], psq=pb[4][0:1, :], psb1=pb[5], psb2=pb[6],
                sqt=sqt, mrow=mrow, vrow=vrow, oout=oout, obf=obf)
    woT = cx.dram["w_woT"].rearrange("(a p) n -> p a n", p=128)
    wlast = None
    wuse = [0, 0]
    evac_dve = [p.count("dve"), p.count("dve")]
    wld = [None, None]
    for tc in range(4):
        lda = p.dma("sync", lambda e, c=tc: e.dma_start(
            out=av_[:, :].rearrange("p (a t) -> p a t", t=TC),
            in_=attn_dram.rearrange("(a p) t -> p a t", p=128)[:, :, c * TC:(c + 1) * TC]),
            "w_av", waits=[("pe", p.count("pe"))])
        for dt in range(16):
            par = dt % 2
            wld[par] = p.dma("gpsimd", lambda e, d=dt, par=par: e.dma_start(
                out=wwo[:, par * 2048:(par + 1) * 2048].rearrange("p (a n) -> p a n", n=128),
                in_=woT[:, :, d * 128:(d + 1) * 128]), "w_wwo%d" % par,
                waits=[("pe", wuse[par])])
            ldr = p.dma("sync", lambda e, d=dt, c=tc, par=par: e.dma_start(
                out=xres[:, par * TC:(par + 1) * TC],
                in_=stream_f32[d * 128:(d + 1) * 128, c * TC:(c + 1) * TC]),
                "w_xr%d" % par, waits=[("dve", evac_dve[par])])
            ps = pb[par]
            ype = None
            for kt in range(16):
                ype = p.op("tensor", lambda e, k=kt, ps=ps, par=par: e.matmul(
                    out=ps[:, :], lhsT=wwo[:, par * 2048 + k * 128:par * 2048 + (k + 1) * 128],
                    rhs=av_[:, k * TC:(k + 1) * TC], start=(k == 0), stop=(k == 15)),
                    waits=([wld[par], lda, ("dve", evac_dve[par])] + cx.consts_w
                           if kt == 0 else []), inc=("pe" if kt == 15 else None))
            wuse[par] = ype
            evac_dve[par] = p.op("vector", lambda e, d=dt, ps=ps, par=par: e.scalar_tensor_tensor(
                out=stg[:, d * TC:(d + 1) * TC], in0=ps[:, :], scalar=wob[:, d:d + 1],
                in1=xres[:, par * TC:(par + 1) * TC], op0=ALU.add, op1=ALU.add),
                waits=[("pe", ype), ldr], inc="dve")
        wlast = emit_ln_store(cx, stg, TC, out_f32, out_bf, tc * TC, bank, "wo")
    return wlast





def phase_l0a(cx, x_fm, y_raw, sz_tm, Cc_d, S_out, dectot_d, expSgT_d):
    nc, p = cx.nc, cx.p
    TH = T + 3

    xT = cx.sb("l_xT", [128, 16 * TH], BF16)
    winb = cx.sb("l_winb", [128, 2 * 2048], BF16)
    wdt = cx.sb("l_wdt", [128, 16 * 32], BF16)
    xssm = cx.sb("l_xssm", [128, TH], BF16)
    Braw = cx.sb("l_Braw", [128, 2 * TH], BF16)
    Craw = cx.sb("l_Craw", [128, 2 * TH], BF16)
    cacc = cx.sb("l_cacc", [128, T], F32)
    xc = cx.sb("l_xc", [128, T], F32)
    Bc = cx.sb("l_Bc", [128, 2 * T], BF16)
    Cc = cx.sb("l_Cc", [128, 2 * T], BF16)
    dtx = cx.sb("l_dtx", [128, T], BF16)
    dtb_ = cx.sb("l_dt", [32, T], F32)
    sR = cx.sb("l_sR", [32, T], F32)
    sL = cx.sb("l_sL", [32, T], F32)
    sTl = cx.sb("l_sTl", [128, 512], F32)
    sgT = cx.sb("l_sgT", [128, 512], F32)
    eSl = cx.sb("l_eSl", [128, 512], F32)
    wsT = cx.sb("l_wsT", [128, 512], F32)
    erep = cx.sb("l_erep", [128, 512], F32)
    eend = cx.sb("l_eend", [32, 16], F32)
    erow = cx.sb("l_erow", [1, 512], F32)
    srow = cx.sb("l_srow", [1, 2 * 128], F32)
    tA = cx.sb("l_tA", [32, 512], F32)
    tBq = cx.sb("l_tB", [32, 512], F32)
    cwx = cx.sb("l_cwx", [128, 4], F32)
    cwb = cx.sb("l_cwb", [128, 2 * 4], F32)
    cwc = cx.sb("l_cwc", [128, 2 * 4], F32)
    Dm = cx.sb("l_Dm", [128, 2 * 128], F32)
    eD = cx.sb("l_eD", [128, 2 * 128], F32)
    PT = cx.sb("l_PT", [128, 2 * 128], BF16)
    dtxT = cx.sb("l_dtxT", [128, 2 * 64], BF16)
    BcTs = cx.sb("l_BcTs", [128, 2 * 128], BF16)
    ytmp = cx.sb("l_ytmp", [128, 2 * 64], F32)
    yg = cx.sb("l_yg", [128, 16 * 128], BF16)
    zg = cx.sb("l_zg", [128, 16 * 128], BF16)
    hT = cx.sb("l_hT", [128, 2 * 2 * 64], F32)
    hTb = cx.sb("l_hTb", [128, 2 * 64], BF16)

    alog = cx.load_const("w_alog")
    dtbias = cx.load_const("w_dtb")
    idf = cx.sbuf["c_idf32"]
    idb = cx.sbuf["c_idbf"]
    seldt = cx.load_const("c_seldt")
    scanmask = cx.load_const("c_scanmask")
    ones1x = cx.sbuf["c_ones_1x128_f"]
    eb_dram = cx.dram_tmp("l_eend_d", [16, 32], F32)

    pb = [cx.pbank(i) for i in range(8)]

    winT = cx.dram["w_winT"].rearrange("(a p) n -> p a n", p=128)
    ldx = p.dma("gpsimd", lambda e: e.dma_start(
        out=xT[:, :].rearrange("p (a t) -> p a t", t=TH),
        in_=x_fm.rearrange("(a p) t -> p a t", p=128)), "l_xT",
        waits=[("pe", p.count("pe"))])
    ldwdt = p.dma("gpsimd", lambda e: e.dma_start(
        out=wdt[:, :].rearrange("p (a n) -> p a n", n=32),
        in_=winT[:, :, 12288:12320]), "l_wdt")

    sp_dve = 0
    for tc in range(4):
        dmm = None
        for kt in range(16):
            dmm = p.op("tensor", lambda e, k=kt, c=tc: e.matmul(
                out=pb[2][0:32, :], lhsT=wdt[:, k * 32:(k + 1) * 32],
                rhs=xT[:, k * TH + 3 + c * 512:k * TH + 3 + (c + 1) * 512],
                start=(k == 0), stop=(k == 15)),
                waits=([ldx, ldwdt, ("dve", p.count("dve")),
                        ("act", p.count("act"))] + cx.consts_w if kt == 0 else []),
                inc=("pe" if kt == 15 else None))
        a1 = p.op("scalar", lambda e: e.activation(out=tA[:, :], in_=pb[2][0:32, :],
                  func=AF.Abs, bias=dtbias[:, 0:1], scale=1.0), waits=[("pe", dmm)], inc="act")
        a2 = p.op("scalar", lambda e: e.activation(out=tA[:, :], in_=tA[:, :],
                  func=AF.Exp, scale=-1.0), waits=[("act", a1)], inc="act")
        a3 = p.op("scalar", lambda e: e.activation(out=tA[:, :], in_=tA[:, :],
                  func=AF.Ln, bias=1.0), waits=[("act", a2)], inc="act")
        a4 = p.op("scalar", lambda e: e.activation(out=tBq[:, :], in_=pb[2][0:32, :],
                  func=AF.Relu, bias=dtbias[:, 0:1], scale=1.0), waits=[("act", a3)], inc="act")
        sp_dve = p.op("vector", lambda e, c=tc: e.tensor_tensor(
            out=dtb_[:, c * 512:(c + 1) * 512], in0=tA[:, :], in1=tBq[:, :], op=ALU.add),
            waits=[("act", a4)], inc="dve")
    ae = p.op("scalar", lambda e: e.activation(out=tA[:, 0:1], in_=alog[:, 0:1],
              func=AF.Exp), waits=[("dve", sp_dve)] + cx.consts_w, inc="act")
    ne = p.op("vector", lambda e: e.tensor_scalar(out=tA[:, 0:1], in0=tA[:, 0:1],
              scalar1=-1.0, scalar2=None, op0=ALU.mult), waits=[("act", ae)], inc="dve")
    ld_ = p.op("vector", lambda e: e.tensor_scalar(out=sR[:, :], in0=dtb_[:, :],
               scalar1=tA[:, 0:1], scalar2=None, op0=ALU.mult),
               waits=[("dve", ne)], inc="dve")
    sc = p.op("vector", lambda e: e.tensor_tensor_scan(out=sR[:, :], data0=sR[:, :],
              data1=dtb_[:, :], initial=0.0, op0=ALU.add, op1=ALU.bypass),
              waits=[("dve", ld_)], inc="dve")
    dst = p.dma("gpsimd", lambda e: e.dma_start(out=dectot_d, in_=sR[:, T - 1:T]),
                "l_dec", waits=[("dve", sc)])
    slv = None
    for c in range(NCH):
        if c == 0:
            slv = p.op("vector", lambda e: e.tensor_copy(out=sL[:, 0:QC], in_=sR[:, 0:QC]),
                       waits=[("dve", sc)], inc="dve")
        else:
            slv = p.op("vector", lambda e, c=c: e.tensor_scalar(
                out=sL[:, c * QC:(c + 1) * QC], in0=sR[:, c * QC:(c + 1) * QC],
                scalar1=sR[:, c * QC - 1:c * QC], scalar2=None, op0=ALU.subtract),
                waits=[("dve", sc)], inc="dve")
    wrow = xc[0:32, :]
    wrv = None
    for c in range(NCH):
        wrv = p.op("vector", lambda e, c=c: e.tensor_scalar(
            out=wrow[:, c * QC:(c + 1) * QC], in0=sL[:, c * QC:(c + 1) * QC],
            scalar1=sL[:, c * QC + QC - 1:c * QC + QC], scalar2=None, op0=ALU.subtract),
            waits=[("dve", slv)], inc="dve")
    tpv = None
    for c in range(NCH):
        t1 = p.op("tensor", lambda e, c=c: e.transpose(
            out=pb[4][:, 0:32], in_=sL[:, c * QC:(c + 1) * QC], identity=idf[0:32, 0:32]),
            waits=[("dve", wrv), ("dve", p.count("dve"))] + cx.consts_w, inc="pe")
        v1 = p.op("vector", lambda e, c=c: e.tensor_copy(out=sTl[:, c * 32:(c + 1) * 32],
                  in_=pb[4][:, 0:32]), waits=[("pe", t1)], inc="dve")
        t2 = p.op("tensor", lambda e, c=c: e.transpose(
            out=pb[4][:, 0:32], in_=sR[:, c * QC:(c + 1) * QC], identity=idf[0:32, 0:32]),
            waits=[("dve", v1)], inc="pe")
        v2 = p.op("vector", lambda e, c=c: e.tensor_copy(out=sgT[:, c * 32:(c + 1) * 32],
                  in_=pb[4][:, 0:32]), waits=[("pe", t2)], inc="dve")
        t3 = p.op("tensor", lambda e, c=c: e.transpose(
            out=pb[4][:, 0:32], in_=wrow[:, c * QC:(c + 1) * QC], identity=idf[0:32, 0:32]),
            waits=[("dve", v2)], inc="pe")
        tpv = p.op("vector", lambda e, c=c: e.tensor_copy(out=wsT[:, c * 32:(c + 1) * 32],
                  in_=pb[4][:, 0:32]), waits=[("pe", t3)], inc="dve")
    e1 = p.op("scalar", lambda e: e.activation(out=eSl[:, :], in_=sTl[:, :], func=AF.Exp),
              waits=[("dve", tpv)], inc="act")
    e2 = p.op("scalar", lambda e: e.activation(out=wsT[:, :], in_=wsT[:, :],
              func=AF.Exp, scale=-1.0), waits=[("act", e1)], inc="act")
    e3 = p.op("scalar", lambda e: e.activation(out=sgT[:, :], in_=sgT[:, :], func=AF.Exp),
              waits=[("act", e2)], inc="act")
    est = p.dma("gpsimd", lambda e: e.dma_start(out=expSgT_d, in_=sgT[:, :]), "l_esg",
                waits=[("act", e3)])
    ee1 = p.op("scalar", lambda e: e.activation(
        out=eend[:, :], in_=sL[:, QC - 1:T:QC], func=AF.Exp),
        waits=[("dve", slv)], inc="act")
    te = p.op("tensor", lambda e: e.transpose(out=pb[5][0:16, 0:32], in_=eend[:, :],
              identity=idf[0:32, 0:32]), waits=[("act", ee1)], inc="pe")
    ve = p.op("vector", lambda e: e.tensor_copy(out=tA[0:16, 0:32], in_=pb[5][0:16, 0:32]),
              waits=[("pe", te)], inc="dve")
    we = p.dma("gpsimd", lambda e: e.dma_start(out=eb_dram, in_=tA[0:16, 0:32]), "l_eb",
               waits=[("dve", ve)])
    re = p.dma("sync", lambda e: e.dma_start(
        out=erow[:, :], in_=eb_dram.rearrange("a b -> (a b)").unsqueeze(0)), "l_er",
        waits=[we])
    me = p.op("tensor", lambda e: e.matmul(out=pb[5][:, 0:512], lhsT=ones1x[:, :],
              rhs=erow[:, :], start=True, stop=True), waits=[re] + cx.consts_w, inc="pe")
    ve2 = p.op("vector", lambda e: e.tensor_copy(out=erep[:, :], in_=pb[5][:, 0:512]),
               waits=[("pe", me)], inc="dve")
    prep_dve = p.count("dve")
    prep_act = p.count("act")

    winw = None
    zstw = None
    ystw = None
    winuse = [0, 0]
    soutw = {0: None, 1: None}
    for g in range(16):
        pe_war = p.count("pe")
        dvw = p.count("dve")
        cw1 = p.dma("sync", lambda e, g=g: e.dma_start(
            out=cwx[:, :], in_=cx.dram["w_cwx"][g * 128:(g + 1) * 128, :]), "l_cw1",
            waits=[("dve", dvw)])
        cw2 = p.dma("sync", lambda e, g=g: e.dma_start(
            out=cwb[:, :].rearrange("p (a n) -> p a n", n=4),
            in_=cx.dram["w_cwb"].rearrange("(a p) n -> p a n", p=128)[:, 2 * g:2 * g + 2, :]),
            "l_cw2", waits=[("dve", dvw)])
        cw3 = p.dma("sync", lambda e, g=g: e.dma_start(
            out=cwc[:, :].rearrange("p (a n) -> p a n", n=4),
            in_=cx.dram["w_cwc"].rearrange("(a p) n -> p a n", p=128)[:, 2 * g:2 * g + 2, :]),
            "l_cw3", waits=[("dve", dvw)])

        outs = [(0, xssm, 0), (256, Braw, 0), (384, Braw, 1), (512, Craw, 0), (640, Craw, 1)]
        chunks = [(0, 512), (512, 512), (1024, 512), (1536, 512), (2048, 3)]
        group_dve = p.count("dve")
        ev = 0
        winw2 = [None, None]
        for oi, (off, buf, half) in enumerate(outs):
            wpar = oi % 2
            winw2[wpar] = p.dma("gpsimd", lambda e, g=g, off=off, wpar=wpar: e.dma_start(
                out=winb[:, wpar * 2048:(wpar + 1) * 2048].rearrange("p (a n) -> p a n", n=128),
                in_=winT[:, :, g * 768 + off:g * 768 + off + 128]), "l_win%d" % wpar,
                waits=[("pe", winuse[wpar])])
            for ci, (c0, cwid) in enumerate(chunks):
                par = ci % 2
                ps = pb[par]
                mmv = None
                for kt in range(16):
                    mmv = p.op("tensor", lambda e, k=kt, wpar=wpar, c0=c0, cwid=cwid, ps=ps: e.matmul(
                        out=ps[:, 0:cwid],
                        lhsT=winb[:, wpar * 2048 + k * 128:wpar * 2048 + (k + 1) * 128],
                        rhs=xT[:, k * TH + c0:k * TH + c0 + cwid],
                        start=(k == 0), stop=(k == 15)),
                        waits=([winw2[wpar], ("act", p.count("act")),
                                ("dve", group_dve)] if kt == 0 else []),
                        inc=("pe" if kt == 15 else None))
                ev = p.op("scalar", lambda e, buf=buf, half=half, c0=c0, cwid=cwid, ps=ps:
                          e.activation(out=buf[:, half * TH + c0:half * TH + c0 + cwid],
                                       in_=ps[:, 0:cwid], func=AF.Copy),
                          waits=[("pe", mmv), ("dve", group_dve)], inc="act")
            winuse[wpar] = mmv
        proj_act = p.count("act")

        zwin = p.dma("gpsimd", lambda e, g=g: e.dma_start(
            out=winb[:, 0:2048].rearrange("p (a n) -> p a n", n=128),
            in_=winT[:, :, g * 768 + 128:g * 768 + 256]), "l_win0",
            waits=[("pe", winuse[0])])
        zevs = None
        for tt in range(16):
            par = tt % 2
            ps = pb[4] if par == 0 else pb[5]
            zmm = None
            for kt in range(16):
                zmm = p.op("tensor", lambda e, k=kt, t=tt, ps=ps: e.matmul(
                    out=ps[:, 0:128], lhsT=xT[:, k * TH + 3 + t * 128:k * TH + 3 + (t + 1) * 128],
                    rhs=winb[:, k * 128:(k + 1) * 128],
                    start=(k == 0), stop=(k == 15)),
                    waits=([zwin, ("act", p.count("act")),
                            ("dve", p.count("dve"))] if kt == 0 else []),
                    inc=("pe" if kt == 15 else None))
            zevs = p.op("scalar", lambda e, t=tt, ps=ps: e.activation(
                out=zg[:, t * 128:(t + 1) * 128], in_=ps[:, 0:128], func=AF.Silu),
                waits=[("pe", zmm)] + ([zstw] if (tt == 0 and zstw) else []), inc="act")
            winuse[0] = zmm
        zstw = p.dma("gpsimd", lambda e, g=g: e.dma_start(
            out=sz_tm.rearrange("(a p) hp -> p a hp", p=128)[:, :, g * 128:(g + 1) * 128],
            in_=zg[:, :].rearrange("p (a n) -> p a n", n=128)), "l_zst",
            waits=[("act", zevs)])

        def conv(buf, half, cw, cwcol, outbuf, outbf, outoff):
            v0 = p.op("vector", lambda e, buf=buf, half=half, cw=cw, cwcol=cwcol: e.tensor_scalar(
                out=cacc[:, :], in0=buf[:, half * TH + 0:half * TH + T],
                scalar1=cw[:, cwcol * 4:cwcol * 4 + 1], scalar2=None, op0=ALU.mult),
                waits=[("act", proj_act), cw1, cw2, cw3, ("act", p.count("act"))] + extra_gate,
                inc="dve")
            for k in range(1, 4):
                v0 = p.op("vector", lambda e, buf=buf, half=half, cw=cw, cwcol=cwcol, k=k:
                          e.scalar_tensor_tensor(
                              out=cacc[:, :], in0=buf[:, half * TH + k:half * TH + k + T],
                              scalar=cw[:, cwcol * 4 + k:cwcol * 4 + k + 1], in1=cacc[:, :],
                              op0=ALU.mult, op1=ALU.add), waits=[("dve", v0)], inc="dve")
            av = p.op("scalar", lambda e, outbuf=outbuf, outoff=outoff: e.activation(
                out=outbuf[:, outoff:outoff + T], in_=cacc[:, :], func=AF.Silu),
                waits=[("dve", v0), ("pe", pe_war)], inc="act")
            return av

        extra_gate = ([ccstw, ccst2w] if g > 0 else [])
        cxv = conv(xssm, 0, cwx, 0, xc, False, 0)
        cb0 = conv(Braw, 0, cwb, 0, Bc, True, 0)
        cb1 = conv(Braw, 1, cwb, 1, Bc, True, T)
        cc0 = conv(Craw, 0, cwc, 0, Cc, True, 0)
        cc1 = conv(Craw, 1, cwc, 1, Cc, True, T)
        dxv = None
        for tc4 in range(4):
            smm = p.op("tensor", lambda e, c=tc4, g=g: e.matmul(
                out=pb[6][:, :], lhsT=seldt[:, g * 128:(g + 1) * 128],
                rhs=dtb_[:, c * 512:(c + 1) * 512], start=True, stop=True),
                waits=[("dve", p.count("dve")), ("act", cxv)] + cx.consts_w, inc="pe")
            dxv = p.op("vector", lambda e, c=tc4: e.tensor_tensor(
                out=dtx[:, c * 512:(c + 1) * 512], in0=xc[:, c * 512:(c + 1) * 512],
                in1=pb[6][:, :], op=ALU.mult), waits=[("pe", smm), ("act", cxv)], inc="dve")

        ccstw = p.dma("gpsimd", lambda e, g=g: e.dma_start(
            out=Cc_d[g * 256:g * 256 + 128, :], in_=Cc[:, 0:T]), "l_cc0",
            waits=[("act", cc0)])
        ccst2w = p.dma("gpsimd", lambda e, g=g: e.dma_start(
            out=Cc_d[g * 256 + 128:g * 256 + 256, :], in_=Cc[:, T:2 * T]), "l_cc1",
            waits=[("act", cc1)])

        bc_act = {0: cb0, 1: cb1}
        cc_act = {0: cc0, 1: cc1}
        PBA = {0: pb[6], 1: pb[2]}
        PBG = {0: pb[7], 1: pb[3]}
        PBS = {0: pb[4], 1: pb[0]}
        PBY = {0: pb[5], 1: pb[1]}
        t_ = [dict(), dict()]
        for c in range(NCH):
            for hp in range(2):
                h = 2 * g + hp
                col = c * 32 + h
                t_[hp]["st1"] = p.op("tensor", lambda e, col=col, hp=hp: e.transpose(
                    out=PBA[hp][0:1, 0:128], in_=sTl[:, col:col + 1], identity=idf[:, :]),
                    waits=[("dve", p.count("dve"))], inc="pe")
            for hp in range(2):
                t_[hp]["sv1"] = p.op("vector", lambda e, hp=hp: e.tensor_copy(
                    out=srow[:, hp * 128:(hp + 1) * 128], in_=PBA[hp][0:1, 0:128]),
                    waits=[("pe", t_[hp]["st1"])], inc="dve")
            for hp in range(2):
                t_[hp]["db"] = p.op("tensor", lambda e, hp=hp: e.matmul(
                    out=PBA[hp][:, 0:128], lhsT=ones1x[:, :],
                    rhs=srow[:, hp * 128:(hp + 1) * 128], start=True, stop=True),
                    waits=[("dve", t_[hp]["sv1"])], inc="pe")
            for hp in range(2):
                col = c * 32 + 2 * g + hp
                t_[hp]["d1"] = p.op("vector", lambda e, col=col, hp=hp: e.tensor_scalar(
                    out=Dm[:, hp * 128:(hp + 1) * 128], in0=PBA[hp][:, 0:128],
                    scalar1=sTl[:, col:col + 1], scalar2=0.0,
                    op0=ALU.subtract, op1=ALU.min),
                    waits=[("pe", t_[hp]["db"])], inc="dve")
            for hp in range(2):
                t_[hp]["d2"] = p.op("vector", lambda e, hp=hp: e.tensor_tensor(
                    out=Dm[:, hp * 128:(hp + 1) * 128], in0=Dm[:, hp * 128:(hp + 1) * 128],
                    in1=scanmask[:, :], op=ALU.add), waits=[("dve", t_[hp]["d1"])], inc="dve")
            for hp in range(2):
                t_[hp]["d3"] = p.op("scalar", lambda e, hp=hp: e.activation(
                    out=eD[:, hp * 128:(hp + 1) * 128], in_=Dm[:, hp * 128:(hp + 1) * 128],
                    func=AF.Exp), waits=[("dve", t_[hp]["d2"])], inc="act")
            for hp in range(2):
                t_[hp]["gt"] = p.op("tensor", lambda e, hp=hp, c=c: e.matmul(
                    out=PBG[hp][:, 0:128], lhsT=Bc[:, hp * T + c * QC:hp * T + (c + 1) * QC],
                    rhs=Cc[:, hp * T + c * QC:hp * T + (c + 1) * QC],
                    start=True, stop=True),
                    waits=[("act", bc_act[hp]), ("dve", p.count("dve"))], inc="pe")
            for hp in range(2):
                t_[hp]["pt"] = p.op("vector", lambda e, hp=hp: e.tensor_tensor(
                    out=PT[:, hp * 128:(hp + 1) * 128], in0=eD[:, hp * 128:(hp + 1) * 128],
                    in1=PBG[hp][:, 0:128], op=ALU.mult),
                    waits=[("act", t_[hp]["d3"]), ("pe", t_[hp]["gt"])], inc="dve")
            for hp in range(2):
                t_[hp]["dt1"] = p.op("tensor", lambda e, hp=hp, c=c: e.transpose(
                    out=PBA[hp][:, 0:32].bitcast(BF16),
                    in_=dtx[hp * 64:(hp + 1) * 64, c * QC:(c + 1) * QC],
                    identity=idb[hp * 64:(hp + 1) * 64, hp * 64:(hp + 1) * 64]),
                    waits=[("dve", dxv), ("dve", p.count("dve"))], inc="pe")
            for hp in range(2):
                t_[hp]["dv1"] = p.op("vector", lambda e, hp=hp: e.tensor_copy(
                    out=dtxT[:, hp * 64:(hp + 1) * 64], in_=PBA[hp][:, 0:32].bitcast(BF16)),
                    waits=[("pe", t_[hp]["dt1"])], inc="dve")
            for hp in range(2):
                t_[hp]["bt1"] = p.op("tensor", lambda e, hp=hp, c=c: e.transpose(
                    out=PBA[hp][:, 0:64].bitcast(BF16),
                    in_=Bc[:, hp * T + c * QC:hp * T + (c + 1) * QC],
                    identity=idb[:, :]), waits=[("dve", t_[hp]["dv1"])], inc="pe")
            for hp in range(2):
                col = c * 32 + 2 * g + hp
                t_[hp]["bv1"] = p.op("vector", lambda e, col=col, hp=hp: e.tensor_scalar(
                    out=BcTs[:, hp * 128:(hp + 1) * 128], in0=PBA[hp][:, 0:64].bitcast(BF16),
                    scalar1=wsT[:, col:col + 1], scalar2=None, op0=ALU.mult),
                    waits=[("pe", t_[hp]["bt1"])], inc="dve")
            for hp in range(2):
                t_[hp]["sm"] = p.op("tensor", lambda e, hp=hp: e.matmul(
                    out=PBS[hp][:, 0:64], lhsT=BcTs[:, hp * 128:(hp + 1) * 128],
                    rhs=dtxT[:, hp * 64:(hp + 1) * 64], start=True, stop=True),
                    waits=[("dve", t_[hp]["bv1"])], inc="pe")
            for hp in range(2):
                t_[hp]["ym1"] = p.op("tensor", lambda e, hp=hp: e.matmul(
                    out=PBY[hp][:, 0:64], lhsT=PT[:, hp * 128:(hp + 1) * 128],
                    rhs=dtxT[:, hp * 64:(hp + 1) * 64], start=True, stop=True),
                    waits=[("dve", t_[hp]["pt"])], inc="pe")
            if c > 0:
                for hp in range(2):
                    t_[hp]["ym2"] = p.op("tensor", lambda e, hp=hp, c=c: e.matmul(
                        out=PBG[hp][:, 0:64], lhsT=Cc[:, hp * T + c * QC:hp * T + (c + 1) * QC],
                        rhs=hTb[:, hp * 64:(hp + 1) * 64], start=True, stop=True),
                        waits=[("dve", p.count("dve")), ("act", cc_act[hp])], inc="pe")
                for hp in range(2):
                    col = c * 32 + 2 * g + hp
                    t_[hp]["yv1"] = p.op("vector", lambda e, col=col, hp=hp: e.tensor_scalar(
                        out=ytmp[:, hp * 64:(hp + 1) * 64], in0=PBG[hp][:, 0:64],
                        scalar1=eSl[:, col:col + 1], scalar2=None, op0=ALU.mult),
                        waits=[("pe", t_[hp]["ym2"])], inc="dve")
                for hp in range(2):
                    t_[hp]["yv2"] = p.op("vector", lambda e, c=c, hp=hp: e.tensor_tensor(
                        out=yg[:, c * 128 + hp * 64:c * 128 + (hp + 1) * 64],
                        in0=PBY[hp][:, 0:64], in1=ytmp[:, hp * 64:(hp + 1) * 64], op=ALU.add),
                        waits=[("dve", t_[hp]["yv1"]), ("pe", t_[hp]["ym1"])], inc="dve")
            else:
                for hp in range(2):
                    t_[hp]["yv2"] = p.op("vector", lambda e, c=c, hp=hp: e.tensor_copy(
                        out=yg[:, c * 128 + hp * 64:c * 128 + (hp + 1) * 64],
                        in_=PBY[hp][:, 0:64]),
                        waits=[("pe", t_[hp]["ym1"])] + ([ystw] if (hp == 0 and ystw) else []),
                        inc="dve")
            par = c % 2
            for hp in range(2):
                col = c * 32 + 2 * g + hp
                if c > 0:
                    t_[hp]["hv"] = p.op("vector", lambda e, hp=hp, col=col, par=par:
                        e.scalar_tensor_tensor(
                            out=hT[:, (hp * 2 + par) * 64:(hp * 2 + par + 1) * 64],
                            in0=hT[:, (hp * 2 + 1 - par) * 64:(hp * 2 + 1 - par + 1) * 64],
                            scalar=erep[:, col:col + 1], in1=PBS[hp][:, 0:64],
                            op0=ALU.mult, op1=ALU.add),
                        waits=[("pe", t_[hp]["sm"])], inc="dve")
                else:
                    t_[hp]["hv"] = p.op("vector", lambda e, hp=hp, par=par: e.tensor_copy(
                        out=hT[:, (hp * 2 + par) * 64:(hp * 2 + par + 1) * 64],
                        in_=PBS[hp][:, 0:64]),
                        waits=[("pe", t_[hp]["sm"])] + ([soutw[hp]] if soutw[hp] else []),
                        inc="dve")
            for hp in range(2):
                h = 2 * g + hp
                if c < NCH - 1:
                    p.op("vector", lambda e, hp=hp, par=par: e.tensor_copy(
                        out=hTb[:, hp * 64:(hp + 1) * 64],
                        in_=hT[:, (hp * 2 + par) * 64:(hp * 2 + par + 1) * 64]),
                        waits=[("dve", t_[hp]["hv"])], inc="dve")
                else:
                    soutw[hp] = p.dma("gpsimd", lambda e, h=h, hp=hp, par=par: e.dma_start(
                        out=S_out[h * 128:(h + 1) * 128, :],
                        in_=hT[:, (hp * 2 + par) * 64:(hp * 2 + par + 1) * 64]),
                        "l_sout%d" % hp, waits=[("dve", t_[hp]["hv"])])
        ystw = p.dma("gpsimd", lambda e, g=g: e.dma_start(
            out=y_raw.rearrange("(a p) hp -> p a hp", p=128)[:, :, g * 128:(g + 1) * 128],
            in_=yg[:, :].rearrange("p (a n) -> p a n", n=128)), "l_yst",
            waits=[("dve", p.count("dve"))])
    return [ystw]





def phase_l0b(cx, S_all, dect_all, y_raw, sz_tm, Cc_d, expSgT_d, x_fm, out_f32, out_bf):
    nc, p = cx.nc, cx.p
    TC = 512

    wc = cx.load_const("c_wcoef")
    jm = cx.load_const("c_jmask")
    crow = cx.sb("b_crow", [1, 128], F32)
    crep = cx.sb("b_crep", [128, 128], F32)
    coefsT = cx.sb("b_coefsT", [4, 32], F32)
    dect = cx.sb("b_dect", [4, 32], F32)
    sjt = cx.sb("b_sjt", [128, 4 * 64], F32)
    hacc = cx.sb("b_hacc", [128, 2 * 64], F32)
    hin = cx.sb("b_hin", [128, 32 * 64], BF16)
    eSg = cx.sb("b_eSg", [128, 512], F32)
    Cc_sb = cx.sb("b_Cc", [128, 2 * T], BF16)
    y_sb = cx.sb("b_y", [128, 16 * 128], BF16)
    sz_sb = cx.sb("b_sz", [128, 16 * 128], BF16)
    ybat = cx.sb("b_ybat", [128, 16 * 64], F32)
    sqb = cx.sb("b_sqb", [128, 16 * 64], F32)
    rsb = cx.sb("b_rsb", [128, 16], F32)
    ygc = cx.sb("b_ygc", [128, 64], BF16)
    ygfm = cx.sb("b_ygfm", [128, 16 * T], BF16)
    stg = cx.sb("b_stg", [128, 16 * TC], F32)
    wwo = cx.sb("b_wwo", [128, 2 * 2048], BF16)
    xres = cx.sb("b_xres", [128, 2 * TC], F32)
    sqt = cx.sb("b_sqt", [128, 2 * TC], F32)
    mrow = cx.sb("b_mrow", [1, TC], F32)
    vrow = cx.sb("b_vrow", [1, TC], F32)
    oout = cx.sb("b_oout", [128, TC], F32)
    obf = cx.sb("b_obf", [128, TC], BF16)
    idb = cx.sbuf["c_idbf"]
    ones1x = cx.sbuf["c_ones_1x128_f"]
    cb_dram = cx.dram_tmp("b_cb_d", [4, 32], F32)

    pb = [cx.pbank(i) for i in range(8)]
    bank = dict(pss=pb[3][0:1, :], psq=pb[4][0:1, :], psb1=pb[5], psb2=pb[6],
                sqt=sqt, mrow=mrow, vrow=vrow, oout=oout, obf=obf)

    ldd = p.dma("sync", lambda e: e.dma_start(
        out=dect[:, :], in_=dect_all.rearrange("(j h) o -> j (h o)", j=4)), "b_dect",
        waits=[("pe", p.count("pe"))])
    cmm = p.op("tensor", lambda e: e.matmul(out=pb[0][0:4, 0:32], lhsT=wc[:, :],
               rhs=dect[:, :], start=True, stop=True),
               waits=[ldd] + cx.consts_w + [("dve", p.count("dve"))], inc="pe")
    ce = p.op("scalar", lambda e: e.activation(out=coefsT[:, :], in_=pb[0][0:4, 0:32],
              func=AF.Exp), waits=[("pe", cmm)], inc="act")
    cm = p.op("vector", lambda e: e.tensor_scalar(out=coefsT[:, :], in0=coefsT[:, :],
              scalar1=jm[:, 0:1], scalar2=None, op0=ALU.mult), waits=[("act", ce)], inc="dve")
    cw_ = p.dma("gpsimd", lambda e: e.dma_start(out=cb_dram, in_=coefsT[:, :]), "b_cb",
                waits=[("dve", cm)])
    cr = p.dma("sync", lambda e: e.dma_start(
        out=crow[:, :], in_=cb_dram.rearrange("a b -> (a b)").unsqueeze(0)), "b_cr",
        waits=[cw_])
    cbm = p.op("tensor", lambda e: e.matmul(out=pb[0][:, 0:128], lhsT=ones1x[:, :],
               rhs=crow[:, :], start=True, stop=True),
               waits=[cr, ("act", p.count("act"))], inc="pe")
    crv = p.op("vector", lambda e: e.tensor_copy(out=crep[:, :], in_=pb[0][:, 0:128]),
               waits=[("pe", cbm)], inc="dve")
    hvv = {0: crv, 1: crv}
    use_dve = {}
    for hb in range(16):
        for j in range(4):
            for s in range(2):
                h = 2 * hb + s
                slot = j % 2 + 2 * s
                ldw = p.dma("sync", lambda e, h=h, j=j, slot=slot: e.dma_start(
                    out=sjt[:, slot * 64:(slot + 1) * 64],
                    in_=S_all[j * 4096 + h * 128:j * 4096 + (h + 1) * 128, :]),
                    "b_sj%d" % slot, waits=[("dve", use_dve.get(slot, 0))])
                if j == 0:
                    hvv[s] = p.op("vector", lambda e, h=h, s=s, slot=slot: e.tensor_scalar(
                        out=hacc[:, s * 64:(s + 1) * 64],
                        in0=sjt[:, slot * 64:(slot + 1) * 64],
                        scalar1=crep[:, h:h + 1], scalar2=None, op0=ALU.mult),
                        waits=[ldw, ("dve", crv)], inc="dve")
                else:
                    hvv[s] = p.op("vector", lambda e, h=h, j=j, s=s, slot=slot:
                        e.scalar_tensor_tensor(
                            out=hacc[:, s * 64:(s + 1) * 64],
                            in0=sjt[:, slot * 64:(slot + 1) * 64],
                            scalar=crep[:, j * 32 + h:j * 32 + h + 1],
                            in1=hacc[:, s * 64:(s + 1) * 64],
                            op0=ALU.mult, op1=ALU.add), waits=[ldw], inc="dve")
                use_dve[slot] = hvv[s]
        for s in range(2):
            h = 2 * hb + s
            p.op("vector", lambda e, h=h, s=s: e.tensor_copy(
                out=hin[:, h * 64:(h + 1) * 64], in_=hacc[:, s * 64:(s + 1) * 64]),
                waits=[("dve", hvv[s])], inc="dve")
    hin_dve = p.count("dve")
    lde = p.dma("sync", lambda e: e.dma_start(out=eSg[:, :], in_=expSgT_d), "b_esg")

    ldy = ldz = ldc = None
    for g in range(16):
        pe_war = p.count("pe")
        dve_war = p.count("dve")
        ldc = p.dma("sync", lambda e, g=g: e.dma_start(
            out=Cc_sb[:, :].rearrange("p (a t) -> p a t", t=T),
            in_=Cc_d.rearrange("(a p) t -> p a t", p=128)[:, 2 * g:2 * g + 2, :]),
            "b_cc", waits=[("pe", pe_war)])
        ldy = p.dma("sync", lambda e, g=g: e.dma_start(
            out=y_sb[:, :].rearrange("p (a n) -> p a n", n=128),
            in_=y_raw.rearrange("(a p) hp -> p a hp", p=128)[:, :, g * 128:(g + 1) * 128]),
            "b_yld", waits=[("dve", dve_war)])
        ldz = p.dma("sync", lambda e, g=g: e.dma_start(
            out=sz_sb[:, :].rearrange("p (a n) -> p a n", n=128),
            in_=sz_tm.rearrange("(a p) hp -> p a hp", p=128)[:, :, g * 128:(g + 1) * 128]),
            "b_zld", waits=[("dve", dve_war)])
        for hp in range(2):
            h = 2 * g + hp
            cmm_prev = [0, 0]
            yb_dve = None
            for c in range(NCH):
                col = c * 32 + h
                par = c % 2
                cmm2 = p.op("tensor", lambda e, hp=hp, c=c, h=h, par=par: e.matmul(
                    out=pb[par][:, 0:64], lhsT=Cc_sb[:, hp * T + c * QC:hp * T + (c + 1) * QC],
                    rhs=hin[:, h * 64:(h + 1) * 64], start=True, stop=True),
                    waits=[ldc, ("dve", max(hin_dve, cmm_prev[par]))], inc="pe")
                yb_dve = p.op("vector", lambda e, col=col, c=c, hp=hp, par=par: e.scalar_tensor_tensor(
                    out=ybat[:, c * 64:(c + 1) * 64], in0=pb[par][:, 0:64],
                    scalar=eSg[:, col:col + 1],
                    in1=y_sb[:, c * 128 + hp * 64:c * 128 + (hp + 1) * 64],
                    op0=ALU.mult, op1=ALU.add),
                    waits=[("pe", cmm2), ldy, lde], inc="dve")
                cmm_prev[par] = yb_dve
            s1 = p.op("scalar", lambda e: e.activation(out=sqb[:, :], in_=ybat[:, :],
                      func=AF.Square), waits=[("dve", yb_dve)], inc="act")
            s2 = p.op("vector", lambda e: e.reduce_sum(
                out=rsb[:, :], in_=sqb[:, :].rearrange("p (c v) -> p c v", v=64),
                axis=mybir.AxisListType.X), waits=[("act", s1)], inc="dve")
            s3 = p.op("vector", lambda e: e.tensor_scalar(out=rsb[:, :], in0=rsb[:, :],
                      scalar1=0.015625, scalar2=1e-6, op0=ALU.mult, op1=ALU.add),
                      waits=[("dve", s2)], inc="dve")
            s4 = p.op("scalar", lambda e: e.activation(out=rsb[:, :], in_=rsb[:, :],
                      func=AF.Ln), waits=[("dve", s3)], inc="act")
            s5 = p.op("scalar", lambda e: e.activation(out=rsb[:, :], in_=rsb[:, :],
                      func=AF.Exp, scale=-0.5), waits=[("act", s4)], inc="act")
            tp_prev = 0
            for c in range(NCH):
                g1 = p.op("vector", lambda e, c=c: e.tensor_scalar(
                    out=ybat[:, c * 64:(c + 1) * 64], in0=ybat[:, c * 64:(c + 1) * 64],
                    scalar1=rsb[:, c:c + 1], scalar2=None, op0=ALU.mult),
                    waits=([("act", s5)] if c == 0 else []), inc="dve")
                g2 = p.op("vector", lambda e, c=c, hp=hp: e.tensor_tensor(
                    out=ygc[:, :], in0=ybat[:, c * 64:(c + 1) * 64],
                    in1=sz_sb[:, c * 128 + hp * 64:c * 128 + (hp + 1) * 64], op=ALU.mult),
                    waits=[("dve", g1), ldz, ("pe", tp_prev)], inc="dve")
                tp_prev = p.op("tensor", lambda e: e.transpose(
                    out=pb[2][0:64, 0:64].bitcast(BF16), in_=ygc[:, :], identity=idb[:, :]),
                    waits=[("dve", g2)] + cx.consts_w, inc="pe")
                p.op("vector", lambda e, g=g, hp=hp, c=c: e.tensor_copy(
                    out=ygfm[hp * 64:(hp + 1) * 64, g * T + c * QC:g * T + (c + 1) * QC],
                    in_=pb[2][0:64, 0:64].bitcast(BF16)), waits=[("pe", tp_prev)], inc="dve")
    yg_dve = p.count("dve")

    woutT = cx.dram["w_woutT"].rearrange("(a p) n -> p a n", p=128)
    wlast = None
    wuse = [0, 0]
    evac_dve = [yg_dve, yg_dve]
    for tc in range(4):
        wld = [None, None]
        for dt in range(16):
            par = dt % 2
            wld[par] = p.dma("gpsimd", lambda e, d=dt, par=par: e.dma_start(
                out=wwo[:, par * 2048:(par + 1) * 2048].rearrange("p (a n) -> p a n", n=128),
                in_=woutT[:, :, d * 128:(d + 1) * 128]), "b_wo%d" % par,
                waits=[("pe", wuse[par])])
            ldr = p.dma("sync", lambda e, d=dt, c=tc, par=par: e.dma_start(
                out=xres[:, par * TC:(par + 1) * TC],
                in_=x_fm[d * 128:(d + 1) * 128, 3 + c * TC:3 + (c + 1) * TC]),
                "b_xr%d" % par, waits=[("dve", evac_dve[par])])
            ps = pb[par]
            ype = None
            for kt in range(16):
                ype = p.op("tensor", lambda e, k=kt, c=tc, ps=ps, par=par: e.matmul(
                    out=ps[:, :],
                    lhsT=wwo[:, par * 2048 + k * 128:par * 2048 + (k + 1) * 128],
                    rhs=ygfm[:, k * T + c * TC:k * T + (c + 1) * TC],
                    start=(k == 0), stop=(k == 15)),
                    waits=([wld[par], ("dve", max(yg_dve, evac_dve[par]))] if kt == 0 else []),
                    inc=("pe" if kt == 15 else None))
            wuse[par] = ype
            evac_dve[par] = p.op("vector", lambda e, d=dt, ps=ps, par=par: e.tensor_tensor(
                out=stg[:, d * TC:(d + 1) * TC], in0=ps[:, :],
                in1=xres[:, par * TC:(par + 1) * TC], op=ALU.add),
                waits=[("pe", ype), ldr], inc="dve")
        wlast = emit_ln_store(cx, stg, TC, out_f32, out_bf, tc * TC, bank, "l0")
    return wlast



def run_phase_l1ffn(stream_f32_np, stream_bf_np, shared):
    nc = bass.Bass("TRN2", target_bir_lowering=False)
    names = ("c_blk2", "c_ones_128x1_f", "c_ones_1x128_f", "c_scalarpool",
             "w_upqT", "w_upsT", "w_bup", "w_upla", "w_uplb",
             "w_dnqT", "w_dnsT", "w_bdn", "w_dnla", "w_dnlb")
    with contextlib.ExitStack() as es:
        cx = Ctx(nc, es)
        for n in names:
            cx.dram_in(n, *SHAPES[n])
        sf = cx.dram_in("stream_f32", [D, T], F32)
        sb_ = cx.dram_in("stream_bf", [D, T], BF16)
        out = cx.dram_out("out_f32", [D, T], F32)
        for n in ("c_blk2", "c_ones_128x1_f", "c_ones_1x128_f", "c_scalarpool"):
            cx.load_const(n)
        cx.register_scalar_pool()
        phase_l1ffn(cx, sf, sb_, out)
        cx.p.barrier_wait("gpsimd")
        blk = es.enter_context(nc.Block())
        cx.p.emit(blk)
    in_maps = []
    for c in range(NCORE):
        m = {n: shared[n] for n in names}
        m["stream_f32"] = stream_f32_np[c]
        m["stream_bf"] = stream_bf_np[c]
        in_maps.append(m)
    res = run_bass_kernel_spmd(nc, in_maps, core_ids=list(range(NCORE)))
    return [r["out_f32"] for r in res.results]


MOE_NAMES = ("c_idf32", "c_ones_128x1_f", "c_ones_1x128_f", "c_scalarpool", "w_routerT",
             "w_moe1", "w_moe3", "w_moe2")


def run_phase_moe(stream_f32_np, stream_bf_np, shared):
    nc = bass.Bass("TRN2", target_bir_lowering=False)
    with contextlib.ExitStack() as es:
        cx = Ctx(nc, es)
        for n in MOE_NAMES:
            cx.dram_in(n, *SHAPES[n])
        sf = cx.dram_in("stream_f32", [D, T], F32)
        sb_ = cx.dram_in("stream_bf", [D, T], BF16)
        out = cx.dram_out("out_f32", [D, T], F32)
        outb = cx.dram_out("out_bf", [D, T], BF16)
        for n in ("c_idf32", "c_ones_128x1_f", "c_ones_1x128_f", "c_scalarpool"):
            cx.load_const(n)
        cx.register_scalar_pool()
        phase_moe(cx, sf, sb_, out, outb)
        cx.p.barrier_wait("gpsimd")
        blk = es.enter_context(nc.Block())
        cx.p.emit(blk)
    in_maps = []
    for c in range(NCORE):
        m = {n: shared[n] for n in MOE_NAMES}
        m["stream_f32"] = stream_f32_np[c]
        m["stream_bf"] = stream_bf_np[c]
        in_maps.append(m)
    res = run_bass_kernel_spmd(nc, in_maps, core_ids=list(range(NCORE)))
    return [r["out_f32"] for r in res.results]




QKV_NAMES = ("c_ones_128x1_f", "c_ones_1x128_f", "c_scalarpool",
             "w_qaT", "w_qab", "w_qnorm", "w_kvaT", "w_kvab", "w_kvnorm",
             "c_cosq", "c_sinq")
ATTN_NAMES = ("c_idbf", "c_ones_128x1_f", "c_ones_128x1_b", "c_ones_1x128_f",
              "c_scalarpool", "w_qbT", "w_qbb", "w_kvbT", "w_kvbb",
              "c_cosq", "c_sinq", "c_onesrow_b")
WO_NAMES = ("c_ones_128x1_f", "c_ones_1x128_f", "c_scalarpool", "w_woT", "w_wob")


def _run_simple_phase(build_fn, names, io_spec, in_maps_extra, shared, percore=None):
    nc = bass.Bass("TRN2", target_bir_lowering=False)
    with contextlib.ExitStack() as es:
        cx = Ctx(nc, es)
        for n in names:
            cx.dram_in(n, *SHAPES[n])
        for (n, shape, dtype, kind) in io_spec:
            if kind == "in":
                cx.dram_in(n, shape, dtype)
            else:
                cx.dram_out(n, shape, dtype)
        for n in names:
            if n.startswith("c_") and n not in ("c_cosq", "c_sinq", "c_onesrow_b"):
                cx.load_const(n)
        cx.register_scalar_pool()
        build_fn(cx)
        cx.p.barrier_wait("gpsimd")
        cx.p.barrier_wait("sync")
        blk = es.enter_context(nc.Block())
        cx.p.emit(blk)
    in_maps = []
    for c in range(NCORE):
        m = {}
        for n in names:
            m[n] = percore[c][n] if (percore is not None and n in percore[c]) else shared[n]
        m.update(in_maps_extra[c])
        in_maps.append(m)
    res = run_bass_kernel_spmd(nc, in_maps, core_ids=list(range(NCORE)))
    return res.results


def run_phase_l1qkv(stream_bf_np, shared, percore):
    def build(cx):
        phase_l1qkv(cx, cx.dram["stream_bf"], cx.dram["qn_d"], cx.dram["kvp_d"])
    io = [("stream_bf", [D, T], BF16, "in"),
          ("qn_d", [QLR, T], BF16, "out"), ("kvp_d", [576, T], BF16, "out")]
    extra = [{"stream_bf": stream_bf_np[c]} for c in range(NCORE)]
    res = _run_simple_phase(build, QKV_NAMES, io, extra, shared, percore)
    return [r["qn_d"] for r in res], [r["kvp_d"] for r in res]


def run_phase_l1attn(qn_np, kvall_np, shared, percore):
    def build(cx):
        phase_l1attn(cx, cx.dram["qn_d"], cx.dram["kvall_d"], cx.dram["attn_d"])
    io = [("qn_d", [QLR, T], BF16, "in"), ("kvall_d", [4 * 576, T], BF16, "in"),
          ("attn_d", [D, T], BF16, "out")]
    extra = [{"qn_d": qn_np[c], "kvall_d": kvall_np[c]} for c in range(NCORE)]
    res = _run_simple_phase(build, ATTN_NAMES, io, extra, shared, percore)
    return [r["attn_d"] for r in res]


def run_phase_l1wo(attn_np, stream_f32_np, shared):
    def build(cx):
        phase_l1wo(cx, cx.dram["attn_d"], cx.dram["stream_f32"],
                   cx.dram["out_f32"], cx.dram["out_bf"])
    io = [("attn_d", [D, T], BF16, "in"), ("stream_f32", [D, T], F32, "in"),
          ("out_f32", [D, T], F32, "out"), ("out_bf", [D, T], BF16, "out")]
    extra = [{"attn_d": attn_np[c], "stream_f32": stream_f32_np[c]} for c in range(NCORE)]
    res = _run_simple_phase(build, WO_NAMES, io, extra, shared)
    return [r["out_f32"] for r in res]




L0A_NAMES = ("c_idf32", "c_idbf", "c_ones_1x128_f", "c_ones_128x1_f", "c_scalarpool",
             "c_seldt", "c_scanmask", "w_winT", "w_alog", "w_dtb",
             "w_cwx", "w_cwb", "w_cwc")
L0B_NAMES = ("c_idbf", "c_ones_1x128_f", "c_ones_128x1_f", "c_scalarpool",
             "c_wcoef", "c_jmask", "w_woutT")


def _launch(build_fn, names, io_spec, extra, shared, percore):
    nc = bass.Bass("TRN2", target_bir_lowering=False)
    with contextlib.ExitStack() as es:
        cx = Ctx(nc, es)
        for n in names:
            cx.dram_in(n, *SHAPES[n])
        for (n, shape, dtype, kind) in io_spec:
            if kind == "in":
                cx.dram_in(n, shape, dtype)
            else:
                cx.dram_out(n, shape, dtype)
        for n in names:
            if n.startswith("c_") and n not in ("c_cosq", "c_sinq", "c_onesrow_b",
                                                "c_wcoef", "c_jmask"):
                cx.load_const(n)
        cx.register_scalar_pool()
        build_fn(cx)
        cx.p.barrier_wait("gpsimd")
        cx.p.barrier_wait("sync")
        blk = es.enter_context(nc.Block())
        cx.p.emit(blk)
    in_maps = []
    for c in range(NCORE):
        m = {}
        for n in names:
            m[n] = percore[c][n] if (percore is not None and n in percore[c]) else shared[n]
        m.update(extra[c])
        in_maps.append(m)
    res = run_bass_kernel_spmd(nc, in_maps, core_ids=list(range(NCORE)))
    return res.results




FUSED = True
PERSISTENT_CONSTS = ("c_idf32", "c_idbf", "c_ones_1x128_f", "c_ones_1x128_b",
                     "c_ones_128x1_b", "c_ones_128x1_f", "c_blk2", "c_scalarpool")
ARENA_BYTES = 210600


def kernel_fused(inputs):
    shared, percore = host_prep(inputs)
    nc = bass.Bass("TRN2", target_bir_lowering=False)
    allnames = set(PERSISTENT_CONSTS)
    for ns in (L0A_NAMES, L0B_NAMES, MOE_NAMES, QKV_NAMES, ATTN_NAMES, WO_NAMES):
        allnames |= set(ns)
    allnames |= {"w_upqT", "w_upsT", "w_bup", "w_upla", "w_uplb",
                 "w_dnqT", "w_dnsT", "w_bdn", "w_dnla", "w_dnlb"}
    percore_names = ("x_fm", "c_cosq", "c_sinq", "c_wcoef", "c_jmask")
    allnames |= set(percore_names)
    with contextlib.ExitStack() as es:
        cx = Ctx(nc, es)
        p = cx.p
        for n in sorted(allnames):
            cx.dram_in(n, *SHAPES[n])
        out = cx.dram_out("out_f32", [D, T], F32)
        y_raw = cx.dram_tmp("y_raw", [T, D], BF16)
        sz_tm = cx.dram_tmp("sz_tm", [T, D], BF16)
        Cc_d = cx.dram_tmp("Cc_d", [4096, T], BF16)
        S_out = cx.dram_tmp("S_out", [4096, 64], F32)
        dectot = cx.dram_tmp("dectot", [32, 1], F32)
        expSgT = cx.dram_tmp("expSgT", [128, 512], F32)
        S_all = cx.dram_tmp("S_all", [4 * 4096, 64], F32)
        dect_all = cx.dram_tmp("dect_all", [128, 1], F32)
        s1f = cx.dram_tmp("s1f", [D, T], F32)
        s1b = cx.dram_tmp("s1b", [D, T], BF16)
        s2f = cx.dram_tmp("s2f", [D, T], F32)
        s2b = cx.dram_tmp("s2b", [D, T], BF16)
        qn_d = cx.dram_tmp("qn_d", [QLR, T], BF16)
        kvp_d = cx.dram_tmp("kvp_d", [576, T], BF16)
        kvall_d = cx.dram_tmp("kvall_d", [4 * 576, T], BF16)
        attn_d = cx.dram_tmp("attn_d", [D, T], BF16)
        s3f = cx.dram_tmp("s3f", [D, T], F32)
        s3b = cx.dram_tmp("s3b", [D, T], BF16)

        for n in PERSISTENT_CONSTS:
            cx.load_const(n)
        cx.register_scalar_pool()
        cx.begin_arena(ARENA_BYTES)

        phase_l0a(cx, cx.dram["x_fm"], y_raw, sz_tm, Cc_d, S_out, dectot, expSgT)
        p.phase_barrier()
        cx.arena_reset()
        p.op("gpsimd", lambda e: e.collective_compute(
            "AllGather", ALU.bypass, replica_groups=[[0, 1, 2, 3], [4, 5, 6, 7]],
            ins=[S_out.opt()], outs=[S_all.opt()]), inc="cc")
        p.op("gpsimd", lambda e: e.collective_compute(
            "AllGather", ALU.bypass, replica_groups=[[0, 1, 2, 3], [4, 5, 6, 7]],
            ins=[dectot.opt()], outs=[dect_all.opt()]), inc="cc")
        p.phase_barrier()

        phase_l0b(cx, S_all, dect_all, y_raw, sz_tm, Cc_d, expSgT,
                  cx.dram["x_fm"], s1f, s1b)
        p.phase_barrier()
        cx.arena_reset()

        phase_moe(cx, s1f, s1b, s2f, s2b)
        p.phase_barrier()
        cx.arena_reset()

        phase_l1qkv(cx, s2b, qn_d, kvp_d)
        p.phase_barrier()
        cx.arena_reset()
        p.op("gpsimd", lambda e: e.collective_compute(
            "AllGather", ALU.bypass, replica_groups=[[0, 1, 2, 3], [4, 5, 6, 7]],
            ins=[kvp_d.opt()], outs=[kvall_d.opt()]), inc="cc")
        p.phase_barrier()

        phase_l1attn(cx, qn_d, kvall_d, attn_d)
        p.phase_barrier()
        cx.arena_reset()

        phase_l1wo(cx, attn_d, s2f, s3f, s3b)
        p.phase_barrier()
        cx.arena_reset()

        phase_l1ffn(cx, s3f, s3b, out)
        p.barrier_wait("gpsimd")
        p.barrier_wait("sync")
        nsem = len(p.sem_handles)
        blk = es.enter_context(nc.Block())
        p.emit(blk)
    in_maps = []
    for c in range(NCORE):
        m = {}
        for n in sorted(allnames):
            m[n] = percore[c][n] if n in percore[c] else shared[n]
        in_maps.append(m)
    res = run_bass_kernel_spmd(nc, in_maps, core_ids=list(range(NCORE)))
    outfm = [r["out_f32"] for r in res.results]
    outa = np.empty((B, S, D), np.float32)
    for c in range(NCORE):
        b, q = c // 4, c % 4
        outa[b, q * T:(q + 1) * T, :] = outfm[c].T
    return outa


_DBG = {}


def kernel(**inputs):
    if FUSED:
        return kernel_fused(inputs)
    shared, percore = host_prep(inputs)

    def bA(cx):
        phase_l0a(cx, cx.dram["x_fm"], cx.dram["y_raw"], cx.dram["sz_tm"],
                  cx.dram["Cc_d"], cx.dram["S_out"], cx.dram["dectot"], cx.dram["expSgT"])
    ioA = [("x_fm", [D, T + 3], F32, "in"),
           ("y_raw", [T, D], BF16, "out"), ("sz_tm", [T, D], BF16, "out"),
           ("Cc_d", [4096, T], BF16, "out"), ("S_out", [4096, 64], F32, "out"),
           ("dectot", [32, 1], F32, "out"), ("expSgT", [128, 512], F32, "out")]
    extA = [{"x_fm": percore[c]["x_fm"]} for c in range(NCORE)]
    rA = _launch(bA, L0A_NAMES, ioA, extA, shared, percore)
    _DBG["rA"] = rA

    S_alls, de_alls = [], []
    for c in range(NCORE):
        grp = range(0, 4) if c < 4 else range(4, 8)
        S_alls.append(np.ascontiguousarray(np.concatenate([rA[j]["S_out"] for j in grp], axis=0)))
        de_alls.append(np.ascontiguousarray(np.concatenate([rA[j]["dectot"] for j in grp], axis=0)))

    def bB(cx):
        phase_l0b(cx, cx.dram["S_all"], cx.dram["dect_all"], cx.dram["y_raw"],
                  cx.dram["sz_tm"], cx.dram["Cc_d"], cx.dram["expSgT"],
                  cx.dram["x_fm"], cx.dram["out_f32"], cx.dram["out_bf"])
    ioB = [("S_all", [4 * 4096, 64], F32, "in"), ("dect_all", [128, 1], F32, "in"),
           ("y_raw", [T, D], BF16, "in"), ("sz_tm", [T, D], BF16, "in"),
           ("Cc_d", [4096, T], BF16, "in"), ("expSgT", [128, 512], F32, "in"),
           ("x_fm", [D, T + 3], F32, "in"),
           ("out_f32", [D, T], F32, "out"), ("out_bf", [D, T], BF16, "out")]
    extB = [{"S_all": S_alls[c], "dect_all": de_alls[c], "y_raw": rA[c]["y_raw"],
             "sz_tm": rA[c]["sz_tm"], "Cc_d": rA[c]["Cc_d"], "expSgT": rA[c]["expSgT"],
             "x_fm": percore[c]["x_fm"]} for c in range(NCORE)]
    rB = _launch(bB, L0B_NAMES, ioB, extB, shared, percore)
    _DBG["rB"] = rB

    def bC(cx):
        phase_moe(cx, cx.dram["stream_f32"], cx.dram["stream_bf"],
                  cx.dram["out_f32"], cx.dram["out_bf"])
    ioC = [("stream_f32", [D, T], F32, "in"), ("stream_bf", [D, T], BF16, "in"),
           ("out_f32", [D, T], F32, "out"), ("out_bf", [D, T], BF16, "out")]
    extC = [{"stream_f32": rB[c]["out_f32"], "stream_bf": rB[c]["out_bf"]}
            for c in range(NCORE)]
    rC = _launch(bC, MOE_NAMES, ioC, extC, shared, percore)
    _DBG["rC"] = rC

    def bD(cx):
        phase_l1qkv(cx, cx.dram["stream_bf"], cx.dram["qn_d"], cx.dram["kvp_d"])
    ioD = [("stream_bf", [D, T], BF16, "in"),
           ("qn_d", [QLR, T], BF16, "out"), ("kvp_d", [576, T], BF16, "out")]
    extD = [{"stream_bf": rC[c]["out_bf"]} for c in range(NCORE)]
    rD = _launch(bD, QKV_NAMES, ioD, extD, shared, percore)
    _DBG["rD"] = rD

    kvalls = []
    for c in range(NCORE):
        grp = range(0, 4) if c < 4 else range(4, 8)
        kvalls.append(np.ascontiguousarray(np.concatenate([rD[j]["kvp_d"] for j in grp], axis=0)))

    def bE(cx):
        phase_l1attn(cx, cx.dram["qn_d"], cx.dram["kvall_d"], cx.dram["attn_d"])
    ioE = [("qn_d", [QLR, T], BF16, "in"), ("kvall_d", [4 * 576, T], BF16, "in"),
           ("attn_d", [D, T], BF16, "out")]
    extE = [{"qn_d": rD[c]["qn_d"], "kvall_d": kvalls[c]} for c in range(NCORE)]
    rE = _launch(bE, ATTN_NAMES, ioE, extE, shared, percore)
    _DBG["rE"] = rE

    def bF(cx):
        phase_l1wo(cx, cx.dram["attn_d"], cx.dram["stream_f32"],
                   cx.dram["out_f32"], cx.dram["out_bf"])
    ioF = [("attn_d", [D, T], BF16, "in"), ("stream_f32", [D, T], F32, "in"),
           ("out_f32", [D, T], F32, "out"), ("out_bf", [D, T], BF16, "out")]
    extF = [{"attn_d": rE[c]["attn_d"], "stream_f32": rC[c]["out_f32"]}
            for c in range(NCORE)]
    rF = _launch(bF, WO_NAMES, ioF, extF, shared, percore)
    _DBG["rF"] = rF

    def bG(cx):
        phase_l1ffn(cx, cx.dram["stream_f32"], cx.dram["stream_bf"], cx.dram["out_f32"])
    FFN_NAMES = ("c_blk2", "c_ones_128x1_f", "c_ones_1x128_f", "c_scalarpool",
                 "w_upqT", "w_upsT", "w_bup", "w_upla", "w_uplb",
                 "w_dnqT", "w_dnsT", "w_bdn", "w_dnla", "w_dnlb")
    ioG = [("stream_f32", [D, T], F32, "in"), ("stream_bf", [D, T], BF16, "in"),
           ("out_f32", [D, T], F32, "out")]
    extG = [{"stream_f32": rF[c]["out_f32"], "stream_bf": rF[c]["out_bf"]}
            for c in range(NCORE)]
    rG = _launch(bG, FFN_NAMES, ioG, extG, shared, percore)

    out = np.empty((B, S, D), np.float32)
    for c in range(NCORE):
        b, q = c // 4, c % 4
        out[b, q * T:(q + 1) * T, :] = rG[c]["out_f32"].T
    return out

